# version 04
# baseline (speedup 1.0000x reference)
import contextlib
import numpy as np
import concourse.bass as bass
import concourse.mybir as mybir
from concourse.bass_utils import run_bass_kernel_spmd

F32 = mybir.dt.float32
BF16 = mybir.dt.bfloat16
I32 = mybir.dt.int32
AF = mybir.ActivationFunctionType
ALU = mybir.AluOpType
AX = mybir.AxisListType

S, D, H, DH = 2048, 1024, 16, 64
TILE = 128
NTS = S // TILE
NT = 2 * NTS
NQG = NTS // 4
INNER = 4096
EPS = 1e-5
DPAD = 3

N_A1 = NT
N_QG = NQG
N_QUAD = NT // 4
SPILL_OUT = False
DBG_A1 = False


def host_tables():
    import jax
    with jax.default_device(jax.devices("cpu")[0]):
        import jax.numpy as jnp
        expoent = jnp.arange(0, DH, 2, dtype=jnp.float32) / DH
        angle = 10.0 ** (-4.0 * expoent)
        pos = jnp.arange(0, S, dtype=jnp.float32)
        nangle = pos[:, None] * angle[None, :]
        sin = np.asarray(jnp.sin(nangle))
        cos = np.asarray(jnp.cos(nangle))
        scale = 1.0 - 2.0 ** (-5.0 - jnp.arange(H, dtype=jnp.float32))
        logs = jnp.log(scale)
        delta = jnp.arange(S, dtype=jnp.float32)
        logd = delta[None, :] * logs[:, None]
        d32 = np.asarray(jnp.exp(logd))

    ctbl = np.zeros((TILE, NTS * DH), np.float32)
    stbl = np.zeros((TILE, NTS * DH), np.float32)
    for st in range(NTS):
        c = cos[st * TILE:(st + 1) * TILE]
        s = sin[st * TILE:(st + 1) * TILE]
        ctbl[:, st * DH:(st + 1) * DH] = np.repeat(c, 2, axis=1)
        blk = np.empty((TILE, DH), np.float32)
        blk[:, 0::2] = -s
        blk[:, 1::2] = s
        stbl[:, st * DH:(st + 1) * DH] = blk

    ndt = DPAD + NTS
    dmt = np.zeros((H, TILE, ndt * TILE), np.float32)
    jj = np.arange(TILE)[:, None]
    ii = np.arange(TILE)[None, :]
    for t in range(ndt):
        dlt = (t - DPAD) * TILE + ii - jj
        valid = (dlt >= 0) & (dlt < S)
        dl = np.clip(dlt, 0, S - 1)
        for h in range(H):
            dmt[h, :, t * TILE:(t + 1) * TILE] = d32[h][dl] * valid
    ident = np.eye(TILE, dtype=np.float32)
    return dict(ctbl=ctbl, stbl=stbl, dmt=dmt, ident=ident)


class Tr:
    ENG = ("tensor", "vector", "scalar", "gpsimd", "sync")
    DMA_Q = {"cx": "sync", "ct": "sync", "cw": "gpsimd", "cs": "gpsimd", "co": "gpsimd"}
    SELFWAIT = {"vector", "scalar", "gpsimd"}

    def __init__(self):
        self.recs = []
        self.lastw = {}
        self.readers = {}
        self.openg = []
        self.opkey = []

    PSPLIT = {"PA": ("PA0", "PA1"), "PB": ("PB0", "PB1"), "PC": ("PC0", "PC1")}

    def _norm(self, bufs):
        out = []
        for b in bufs:
            if b in self.PSPLIT:
                out.extend(self.PSPLIT[b])
            else:
                out.append(b)
        return out

    def op(self, eng, fn, reads=(), writes=(), dma=None):
        reads = self._norm(reads)
        writes = self._norm(writes)
        deps = set()
        myid = len(self.recs)

        def need(pid):
            if pid is None:
                return
            peng = self.openg[pid]
            pkey = self.opkey[pid]
            if pkey is None and peng == eng and eng not in self.SELFWAIT:
                return
            deps.add(pid)

        for b in reads:
            need(self.lastw.get(b))
        for b in writes:
            need(self.lastw.get(b))
            for r in self.readers.get(b, ()):
                need(r)
        key = None
        if dma is not None:
            assert self.DMA_Q[dma] == eng
            slot = tuple(sorted(writes)) if writes else ("st",) + tuple(sorted(reads))
            key = ("D", dma, slot)
        self.openg.append(eng)
        self.opkey.append(key)
        for b in writes:
            self.lastw[b] = myid
            self.readers[b] = []
        for b in reads:
            self.readers.setdefault(b, []).append(myid)
        self.recs.append((eng, fn, key, deps))

    def finalize(self):
        n = len(self.recs)
        needed = [False] * n
        for (_, _, key, deps) in self.recs:
            for d in deps:
                needed[d] = True
        val = [0] * n
        ecnt = {e: 0 for e in self.ENG}
        dcnt = {}
        for i, (eng, fn, key, deps) in enumerate(self.recs):
            if key is not None:
                dcnt[key] = dcnt.get(key, 0) + 16
                val[i] = dcnt[key]
            elif needed[i]:
                ecnt[eng] += 1
                val[i] = ecnt[eng]
        ops = {e: [] for e in self.ENG}
        waited = {e: {} for e in self.ENG}
        for i, (eng, fn, key, deps) in enumerate(self.recs):
            waits = {}
            for d in deps:
                dk = self.opkey[d]
                sk = dk if dk is not None else ("E", self.openg[d])
                v = val[d]
                if waited[eng].get(sk, 0) >= v:
                    continue
                if waits.get(sk, 0) < v:
                    waits[sk] = v
            for k2, v2 in waits.items():
                waited[eng][k2] = v2
            if key is not None:
                inc = (key, 16)
            elif needed[i]:
                inc = (("E", eng), 1)
            else:
                inc = None
            ops[eng].append((sorted(waits.items()), fn, inc))
        self.dcnt = dcnt
        return ops


def build():
    tabs = host_tables()
    nc = bass.Bass("TRN2", target_bir_lowering=False, detect_race_conditions=False)
    stack = contextlib.ExitStack()
    dr = {}

    def din(name, shape, dt=F32):
        dr[name] = nc.dram_tensor(name, list(shape), dt, kind="ExternalInput").ap()

    def dint(name, shape, dt):
        kind = "ExternalOutput" if SPILL_OUT else "Internal"
        dr[name] = nc.dram_tensor(name, list(shape), dt, kind=kind).ap()

    din("x", [2, S, D])
    dr["out"] = nc.dram_tensor("out", [2, S, D], F32, kind="ExternalOutput").ap()
    din("cF", [TILE, 16])
    din("wmod4", [12, 8, TILE, 512])
    din("wqkv4", [8, TILE, 3 * D])
    din("wo4", [8, TILE, D])
    din("gammaF", [TILE, 8])
    din("betaF", [TILE, 8])
    din("wg4", [8, TILE, INNER])
    din("wu4", [8, TILE, INNER])
    din("wd4", [32, TILE, D])
    din("ctbl", [TILE, NTS * DH])
    din("stbl", [TILE, NTS * DH])
    din("dmt", [H, TILE, (DPAD + NTS) * TILE])
    din("identf", [TILE, TILE])
    dint("sp_q", [NT, TILE, D], BF16)
    dint("sp_k", [NT, TILE, D], BF16)
    dint("sp_v", [NT, TILE, D], BF16)
    dint("sp_r", [NT, H, DH, TILE], F32)
    dint("modv", [2, 6 * D], F32)
    if DBG_A1:
        for i in (1, 2, 3):
            dint(f"dbg{i}", [TILE, D], F32)
        for i in (4, 5, 6, 7, 8):
            dint(f"dbg{i}", [TILE, D], F32)
        for i in (9, 10, 11):
            dint(f"dbg{i}", [TILE, INNER], F32)
        dint("dbg12", [TILE, 16], F32)
    dint("tg", [8, 8, TILE, 512], BF16)
    dint("tu", [8, 8, TILE, 512], BF16)
    dint("td", [32, TILE, D], BF16)

    sb = {}

    def sbuf(name, shape, dt):
        sb[name] = stack.enter_context(nc.sbuf_tensor("sb_" + name, list(shape), dt))

    sbuf("identf", [TILE, TILE], F32)
    sbuf("ctbl", [TILE, NTS * DH], F32)
    sbuf("stbl", [TILE, NTS * DH], F32)
    sbuf("ones_bf", [1, 512], BF16)
    sbuf("onesf_col", [TILE, 1], F32)
    sbuf("onesf_row", [1, TILE], F32)
    sbuf("cF", [TILE, 16], F32)
    sbuf("silu_cF", [TILE, 16], F32)
    sbuf("gammaF", [TILE, 8], F32)
    sbuf("betaF_bf", [TILE, 8], BF16)
    sbuf("modF", [TILE, 32], F32)
    sbuf("saF_bf", [TILE, 16], BF16)
    sbuf("caF1p", [TILE, 16], F32)
    sbuf("saw_bf", [1, 3 * D], BF16)
    sbuf("bwo_bf", [1, D], BF16)
    sbuf("scw", [TILE, 3], F32)
    sbuf("invsw", [TILE, 3], F32)
    sbuf("accw", [TILE, 48], F32)
    sbuf("stat", [TILE, 64], F32)
    sbuf("ln2rstd", [TILE, 4], F32)
    sbuf("ln2nmr", [TILE, 4], F32)
    sbuf("epsc", [TILE, 1], F32)
    sbuf("x1q", [TILE, 4 * D], F32)
    sbuf("PF", [TILE, 16384], F32)
    sbuf("PBF", [TILE, 47104], BF16)
    sbuf("PI", [TILE, 1024], I32)

    SEG = {"PF": 512, "PBF": 1024, "PI": 512}

    def carve(pool, off, n, p0=0, p1=TILE):
        ap = sb[pool][p0:p1, off:off + n]
        s0, s1 = off // SEG[pool], (off + n - 1) // SEG[pool]
        names = [f"{pool}:{i}" for i in range(s0, s1 + 1)]
        return ap, names

    ps = {}
    for n, shp in (("PA", [TILE, 1024]), ("PB", [TILE, 1024]), ("PC", [TILE, 1024]),
                   ("PD1", [TILE, 512]), ("PD2", [TILE, 512])):
        ps[n] = stack.enter_context(nc.psum_tensor(n, shp, F32))

    tr = Tr()
    sems = {}
    for e in Tr.ENG:
        sems[("E", e)] = stack.enter_context(nc.semaphore("se_" + e))

    x_h, out_h = dr["x"], dr["out"]

    def dma(chan, fn, r=(), w=()):
        tr.op(Tr.DMA_Q[chan], fn, r, w, dma=chan)

    def rnd_nearest(src_ap, src_n, w_slices, rbufs):
        n = src_n
        for lo in range(0, n, 1024):
            wdt = min(1024, n - lo)
            sl = src_ap[:, lo:lo + wdt]
            ri, rin = carve("PI", 0, wdt)
            dv, dvn = carve("PF", FD_RND, wdt)
            bv, bvn = carve("PF", FD_RND + 1024, wdt)
            tr.op("vector", lambda E, sl=sl, ri=ri: E.tensor_copy(ri, sl), w_slices, rin)
            tr.op("vector", lambda E, sl=sl, ri=ri, dv=dv: E.tensor_sub(dv, sl, ri), list(w_slices) + rin, dvn)
            tr.op("vector", lambda E, dv=dv, bv=bv: E.tensor_single_scalar(bv, dv, -0.5, ALU.is_lt), dvn, bvn)
            tr.op("vector", lambda E, sl=sl, dv=dv, ri=ri: E.scalar_tensor_tensor(sl, dv, 0.5, ri, ALU.is_gt, ALU.add),
                  dvn + rin, w_slices)
            tr.op("vector", lambda E, sl=sl, bv=bv: E.tensor_sub(sl, sl, bv), list(w_slices) + bvn, w_slices)

    FD_RND = 14336

    def prep():
        for nm in ("identf", "ctbl", "stbl", "cF", "gammaF"):
            dma("cx", lambda E, nm=nm: E.dma_start(out=sb[nm][:], in_=dr[nm]), w=[nm])
        dma("cw", lambda E: E.dma_start(out=sb["betaF_bf"][:], in_=dr["betaF"]), w=["betaF_bf"])
        tr.op("vector", lambda E: E.memset(sb["ones_bf"][:], 1.0), (), ["ones_bf"])
        tr.op("vector", lambda E: E.memset(sb["onesf_col"][:], 1.0), (), ["onesf_col"])
        tr.op("vector", lambda E: E.memset(sb["onesf_row"][:], 1.0), (), ["onesf_row"])
        tr.op("vector", lambda E: E.memset(sb["epsc"][:], EPS), (), ["epsc"])
        tr.op("scalar", lambda E: E.activation(sb["silu_cF"][:], sb["cF"][:], AF.Silu), ["cF"], ["silu_cF"])

        def mtmp(lo, n):
            if lo < 4096:
                assert lo + n <= 4096
                return carve("PF", lo, n, 0, 2)
            return carve("PF", 4096 + (lo - 4096), n, 0, 2)

        wmb, wmbn = carve("PF", 10240, 4096)
        for n in range(12):
            pn, pb = (ps["PA"], "PA") if n % 2 == 0 else (ps["PB"], "PB")
            dma("ct", lambda E, n=n, wmb=wmb: E.dma_start(out=wmb.rearrange("p (c m) -> p c m", m=512), in_=dr["wmod4"][n].transpose([1, 0, 2])),
                r=wmbn, w=wmbn)
            for c in range(8):
                tr.op("tensor", lambda E, c=c, pn=pn, wmb=wmb: E.matmul(
                    pn[0:2, 0:512], sb["silu_cF"][:, 2 * c:2 * c + 2], wmb[:, 512 * c:512 * (c + 1)],
                    start=(c == 0), stop=(c == 7)), ["silu_cF"] + wmbn, [pb])
            mt, mtn = mtmp(512 * n, 512)
            tr.op("vector", lambda E, pn=pn, mt=mt: E.tensor_copy(mt, pn[0:2, 0:512]), [pb], mtn)

        ct, ctn = carve("PF", 8192, 1024, 0, 2)
        cm, cmn = carve("PF", 9216, 1024, 0, 2)
        for lo in (2048, 5120):
            mt, mtn = mtmp(lo, 1024)
            tr.op("vector", lambda E, mt=mt, ct=ct: E.tensor_single_scalar(ct, mt, 0.0, ALU.is_ge), mtn, ctn)
            tr.op("vector", lambda E, ct=ct: E.tensor_scalar(ct, ct, 2.0, -1.0, ALU.mult, ALU.add), ctn, ctn)
            tr.op("scalar", lambda E, mt=mt, cm=cm: E.activation(cm, mt, AF.Abs), mtn, cmn)
            tr.op("vector", lambda E, cm=cm: E.tensor_single_scalar(cm, cm, 0.25, ALU.max), cmn, cmn)
            tr.op("vector", lambda E, mt=mt, ct=ct, cm=cm: E.tensor_mul(mt, ct, cm), ctn + cmn, mtn)

        for vi, base in ((0, 0), (1, 1024)):
            for c in range(8):
                mt, mtn = mtmp(base + TILE * c, TILE)
                tr.op("tensor", lambda E, mt=mt: E.transpose(ps["PC"][:, 0:2], mt, sb["identf"][0:2, 0:2]),
                      mtn + ["identf"], ["PC"])
                tr.op("vector", lambda E, vi=vi, c=c: E.tensor_copy(sb["modF"][:, 16 * vi + 2 * c:16 * vi + 2 * c + 2],
                                                                    ps["PC"][:, 0:2]), ["PC"], ["modF"])
        tr.op("vector", lambda E: E.tensor_copy(sb["saF_bf"][:], sb["modF"][:, 0:16]), ["modF"], ["saF_bf"])
        tr.op("vector", lambda E: E.tensor_scalar_add(sb["caF1p"][:], sb["modF"][:, 16:32], 1.0), ["modF"], ["caF1p"])

        for bi in range(2):
            for seg in range(3):
                mt, mtn = mtmp(2048 * seg, 2048)
                dma("cs", lambda E, bi=bi, seg=seg, mt=mt: E.dma_start(
                    out=dr["modv"][bi:bi + 1, 2048 * seg:2048 * (seg + 1)], in_=mt[bi:bi + 1, :]), r=mtn, w=["modv"])

        pass

    def quant_units():
        nch = {"g": 8, "u": 8, "d": 32}
        wsrc = {"g": "wg4", "u": "wu4", "d": "wd4"}
        wid = {"g": 0, "u": 1, "d": 2}
        ccol = {"g": 0, "u": 8, "d": 16}
        nsl = {"g": 4, "u": 4, "d": 1}
        wcs, wcsn = carve("PF", 12288, 1024)
        ab2, ab2n = carve("PF", 13312, 1024)
        qtb, qtbn = carve("PBF", 28672, 1024)

        for m in ("g", "u", "d"):
            for c in range(nch[m]):
                for sl in range(nsl[m]):
                    def u1(m=m, c=c, sl=sl):
                        dma("ct", lambda E: E.dma_start(out=wcs, in_=dr[wsrc[m]][c][:, 1024 * sl:1024 * (sl + 1)]),
                            r=wcsn, w=wcsn)
                        tr.op("scalar", lambda E: E.activation(ab2, wcs, AF.Abs), wcsn, ab2n)
                        col = 32 + sl
                        tr.op("vector", lambda E, col=col: E.reduce_sum(sb["accw"][:, col:col + 1], ab2, axis=AX.X),
                              ab2n, ["accw"])
                        if sl == nsl[m] - 1:
                            if nsl[m] > 1:
                                tr.op("vector", lambda E: E.tensor_reduce(
                                    sb["accw"][:, ccol[m] + c:ccol[m] + c + 1],
                                    sb["accw"][:, 32:32 + nsl[m]], AX.X, ALU.add), ["accw"], ["accw"])
                            else:
                                tr.op("vector", lambda E: E.tensor_copy(
                                    sb["accw"][:, ccol[m] + c:ccol[m] + c + 1], sb["accw"][:, 32:33]),
                                    ["accw"], ["accw"])
                    yield u1

        for m in ("g", "u", "d"):
            def us(m=m):
                tr.op("tensor", lambda E: E.matmul(ps["PD1"][0:1, 0:nch[m]], sb["onesf_col"][:],
                                                   sb["accw"][:, ccol[m]:ccol[m] + nch[m]], start=True, stop=True),
                      ["onesf_col", "accw"], ["PD1"])
                tr.op("vector", lambda E: E.reduce_sum(sb["stat"][0:1, 32:33], ps["PD1"][0:1, 0:nch[m]], axis=AX.X),
                      ["PD1"], ["stat"])
                tr.op("vector", lambda E: E.tensor_scalar(sb["stat"][0:1, 33:34], sb["stat"][0:1, 32:33],
                                                          1.0 / (INNER * D), 1e-5, ALU.mult, ALU.max),
                      ["stat"], ["stat"])
                tr.op("vector", lambda E: E.reciprocal(sb["stat"][0:1, 34:35], sb["stat"][0:1, 33:34]),
                      ["stat"], ["stat"])
                for (srcc, dst) in ((34, "scw"), (33, "invsw")):
                    tr.op("tensor", lambda E, srcc=srcc: E.matmul(ps["PD2"][:, 0:1], sb["onesf_row"][:],
                                                                  sb["stat"][0:1, srcc:srcc + 1], start=True, stop=True),
                          ["stat", "onesf_row"], ["PD2"])
                    tr.op("vector", lambda E, dst=dst: E.tensor_copy(sb[dst][:, wid[m]:wid[m] + 1], ps["PD2"][:, 0:1]),
                          ["PD2"], [dst])
            yield us

        for m in ("g", "u", "d"):
            for c in range(nch[m]):
                for sl in range(nsl[m]):
                    def u2(m=m, c=c, sl=sl):
                        dma("ct", lambda E: E.dma_start(out=wcs, in_=dr[wsrc[m]][c][:, 1024 * sl:1024 * (sl + 1)]),
                            r=wcsn, w=wcsn)
                        tr.op("vector", lambda E: E.tensor_scalar_mul(wcs, wcs, sb["scw"][:, wid[m]:wid[m] + 1]),
                              wcsn + ["scw"], wcsn)
                        rnd_nearest(wcs, 1024, wcsn, None)
                        tr.op("vector", lambda E: E.tensor_scalar(qtb, wcs, -1.0, 1.0, ALU.max, ALU.min),
                              wcsn, qtbn)
                        if m == "d":
                            dma("cs", lambda E: E.dma_start(out=dr["td"][c], in_=qtb), r=qtbn, w=[])
                        else:
                            tgt = "tg" if m == "g" else "tu"
                            for e2 in range(2):
                                e8 = 2 * sl + e2
                                dma("cs", lambda E, e8=e8, e2=e2, tgt=tgt: E.dma_start(
                                    out=dr[tgt][e8, c], in_=qtb[:, 512 * e2:512 * (e2 + 1)]), r=qtbn, w=[])
                    yield u2

    def load_wqkv(bi):
        for c in range(8):
            wv, wvn = carve("PBF", 3 * D * c, 3 * D)
            dma("cw", lambda E, c=c, wv=wv: E.dma_start(out=wv, in_=dr["wqkv4"][c]), r=wvn, w=wvn)
        pmap = {0: ("PA", ps["PA"]), 1: ("PB", ps["PB"]), 2: ("PC", ps["PC"])}
        for p in range(3):
            pb, pn = pmap[p]
            for half in range(2):
                for c in range(8):
                    wv, wvn = carve("PBF", 3 * D * c, 3 * D)
                    tr.op("tensor", lambda E, p=p, half=half, c=c, pn=pn, wv=wv: E.matmul(
                        pn[0:1, 512 * half:512 * (half + 1)], sb["saF_bf"][:, 2 * c + bi:2 * c + bi + 1],
                        wv[:, D * p + 512 * half:D * p + 512 * (half + 1)], start=(c == 0), stop=(c == 7)),
                        ["saF_bf"] + wvn, [pb])
                tr.op("vector", lambda E, p=p, half=half, pn=pn: E.tensor_copy(
                    sb["saw_bf"][0:1, D * p + 512 * half:D * p + 512 * (half + 1)],
                    pn[0:1, 512 * half:512 * (half + 1)]), [pb], ["saw_bf"])
        for c in range(8):
            wv, wvn = carve("PBF", 3 * D * c, 3 * D)
            tr.op("vector", lambda E, c=c, wv=wv: E.tensor_scalar_mul(wv, wv, sb["caF1p"][:, 2 * c + bi:2 * c + bi + 1]),
                  wvn + ["caF1p"], wvn)

    def load_wo():
        for c in range(8):
            wv, wvn = carve("PBF", D * c, D)
            dma("cw", lambda E, c=c, wv=wv: E.dma_start(out=wv, in_=dr["wo4"][c]), r=wvn, w=wvn)
        for half in range(2):
            for c in range(8):
                wv, wvn = carve("PBF", D * c, D)
                tr.op("tensor", lambda E, half=half, c=c, wv=wv: E.matmul(
                    ps["PD1"][0:1, 0:512], sb["betaF_bf"][:, c:c + 1],
                    wv[:, 512 * half:512 * (half + 1)], start=(c == 0), stop=(c == 7)),
                    ["betaF_bf"] + wvn, ["PD1"])
            tr.op("vector", lambda E, half=half: E.tensor_copy(sb["bwo_bf"][0:1, 512 * half:512 * (half + 1)],
                                                               ps["PD1"][0:1, 0:512]), ["PD1"], ["bwo_bf"])
        for c in range(8):
            wv, wvn = carve("PBF", D * c, D)
            tr.op("vector", lambda E, c=c, wv=wv: E.tensor_scalar_mul(wv, wv, sb["gammaF"][:, c:c + 1]),
                  wvn + ["gammaF"], wvn)

    def a1_tile(gt):
        bi, st = gt // NTS, gt % NTS
        xv, xvn = carve("PF", 10240 + 1024 * (gt % 2), 1024)
        dma("cx", lambda E, bi=bi, st=st, xv=xv: E.dma_start(out=xv, in_=x_h[bi, TILE * st:TILE * (st + 1), :]),
            r=xvn, w=xvn)
        stc = sb["stat"]
        tr.op("vector", lambda E, xv=xv: E.bn_stats(stc[:, 0:6], xv[:, 0:512]), xvn, ["stat"])
        tr.op("vector", lambda E, xv=xv: E.bn_stats(stc[:, 6:12], xv[:, 512:1024]), xvn, ["stat"])
        tr.op("vector", lambda E: E.bn_aggr(stc[:, 12:14], stc[:, 0:12]), ["stat"], ["stat"])
        tr.op("scalar", lambda E: E.activation(stc[:, 14:15], stc[:, 13:14], AF.Ln, bias=sb["epsc"][:]), ["stat", "epsc"], ["stat"])
        tr.op("scalar", lambda E: E.activation(stc[:, 15:16], stc[:, 14:15], AF.Exp, scale=-0.5), ["stat"], ["stat"])
        tr.op("vector", lambda E: E.tensor_scalar(stc[:, 16:17], stc[:, 12:13], stc[:, 15:16], -1.0, ALU.mult, ALU.mult),
              ["stat"], ["stat"])
        xn1, xn1n = carve("PF", 8192, 1024)
        tr.op("scalar", lambda E, xv=xv, xn1=xn1: E.activation(xn1, xv, AF.Identity,
                                                               bias=stc[:, 16:17], scale=stc[:, 15:16]),
              xvn + ["stat"], xn1n)
        xn1F, xn1Fn = carve("PBF", 24576, 1024)
        for c in range(8):
            pd, pb = (ps["PD1"], "PD1") if c % 2 == 0 else (ps["PD2"], "PD2")
            tr.op("tensor", lambda E, c=c, pd=pd, xn1=xn1: E.transpose(pd[:, 0:TILE], xn1[:, TILE * c:TILE * (c + 1)],
                                                                      sb["identf"][:]), xn1n + ["identf"], [pb])
            tr.op("scalar", lambda E, c=c, pd=pd, xn1F=xn1F: E.activation(xn1F[:, TILE * c:TILE * (c + 1)],
                                                                          pd[:, 0:TILE], AF.Copy), [pb], xn1Fn)
        pmap = {0: ("PA", ps["PA"]), 1: ("PB", ps["PB"]), 2: ("PC", ps["PC"])}
        for p in range(3):
            pb, pn = pmap[p]
            for half in range(2):
                for c in range(8):
                    wv, wvn = carve("PBF", 3 * D * c, 3 * D)
                    tr.op("tensor", lambda E, p=p, half=half, c=c, pn=pn, wv=wv, xn1F=xn1F: E.matmul(
                        pn[:, 512 * half:512 * (half + 1)], xn1F[:, TILE * c:TILE * (c + 1)],
                        wv[:, D * p + 512 * half:D * p + 512 * (half + 1)], start=(c == 0), stop=False),
                        xn1Fn + wvn, [pb])
                tr.op("tensor", lambda E, p=p, half=half, pn=pn: E.matmul(
                    pn[:, 512 * half:512 * (half + 1)], sb["ones_bf"][0:1, 0:TILE],
                    sb["saw_bf"][0:1, D * p + 512 * half:D * p + 512 * (half + 1)], start=False, stop=True),
                    ["ones_bf", "saw_bf"], [pb])
        if DBG_A1 and gt == 0:
            dma("cs", lambda E, xn1=xn1: E.dma_start(out=dr["dbg1"], in_=xn1), r=xn1n, w=[])
            for pbx, pnx, tag in (("PA", ps["PA"], "dbg2"), ("PB", ps["PB"], "dbg3")):
                tmpd, tmpdn = carve("PF", 2048, 1024)
                tr.op("vector", lambda E, pnx=pnx, tmpd=tmpd: E.tensor_copy(tmpd, pnx[:]), [pbx], tmpdn)
                dma("cs", lambda E, tmpd=tmpd, tag=tag: E.dma_start(out=dr[tag], in_=tmpd), r=tmpdn, w=[])
        for pb, pn, base in (("PA", ps["PA"], 0), ("PB", ps["PB"], 4096)):
            raw, rawn = carve("PF", base + 2048, 1024)
            tr.op("vector", lambda E, pn=pn, raw=raw: E.tensor_copy(raw, pn[:]), [pb], rawn)
            rot, rotn = carve("PF", base, 1024)
            tmp, tmpn = carve("PF", base + 1024, 1024)
            cs = sb["ctbl"][:, DH * st:DH * (st + 1)]
            sn = sb["stbl"][:, DH * st:DH * (st + 1)]
            p3 = raw.rearrange("p (h u) -> p h u", u=DH)
            t3 = tmp.rearrange("p (h u) -> p h u", u=DH)
            d3 = rot.rearrange("p (h u) -> p h u", u=DH)
            sne = sn[:, 0::2].unsqueeze(1).broadcast_to([TILE, H, DH // 2])
            sno = sn[:, 1::2].unsqueeze(1).broadcast_to([TILE, H, DH // 2])
            csf = cs.unsqueeze(1).broadcast_to([TILE, H, DH])
            tr.op("vector", lambda E, t3=t3, sne=sne, p3=p3: E.tensor_mul(t3[:, :, 0::2], sne, p3[:, :, 1::2]),
                  rawn + ["stbl"], tmpn)
            tr.op("vector", lambda E, t3=t3, sno=sno, p3=p3: E.tensor_mul(t3[:, :, 1::2], sno, p3[:, :, 0::2]),
                  rawn + ["stbl"], tmpn)
            tr.op("vector", lambda E, d3=d3, csf=csf, p3=p3: E.tensor_mul(d3, csf, p3), rawn + ["ctbl"], rotn)
            tr.op("vector", lambda E, rot=rot, tmp=tmp: E.tensor_add(rot, rot, tmp), rotn + tmpn, rotn)
        vbf, vbfn = carve("PBF", 25600, 1024)
        tr.op("vector", lambda E, vbf=vbf: E.tensor_copy(vbf, ps["PC"][:]), ["PC"], vbfn)
        dma("cs", lambda E, gt=gt, vbf=vbf: E.dma_start(out=dr["sp_v"][gt], in_=vbf), r=vbfn, w=["sp_v"])
        for base, dsto, tgt in ((0, 26624, "sp_q"), (4096, 27648, "sp_k")):
            rot, rotn = carve("PF", base, 1024)
            dst, dstn = carve("PBF", dsto, 1024)
            for c in range(8):
                pd, pb2 = (ps["PD1"], "PD1") if c % 2 == 0 else (ps["PD2"], "PD2")
                tr.op("tensor", lambda E, c=c, pd=pd, rot=rot: E.transpose(pd[:, 0:TILE], rot[:, TILE * c:TILE * (c + 1)],
                                                                           sb["identf"][:]), rotn + ["identf"], [pb2])
                tr.op("scalar", lambda E, c=c, pd=pd, dst=dst: E.activation(dst[:, TILE * c:TILE * (c + 1)],
                                                                            pd[:, 0:TILE], AF.Copy), [pb2], dstn)
            dma("cs", lambda E, gt=gt, dst=dst, tgt=tgt: E.dma_start(out=dr[tgt][gt], in_=dst), r=dstn, w=[tgt])

    def a2():
        for qg in range(N_QG):
            ntj = 4 * qg + 4
            qfg, qfgn = carve("PBF", 0, 8192)
            for si in range(2):
                for tq in range(4):
                    gt = si * NTS + 4 * qg + tq
                    if gt >= N_A1:
                        continue
                    sl = qfg[:, 1024 * (4 * si + tq):1024 * (4 * si + tq + 1)]
                    dma("ct", lambda E, gt=gt, sl=sl: E.dma_start(out=sl, in_=dr["sp_q"][gt]),
                        r=["sp_q"] + qfgn, w=qfgn)
            wcols = TILE * (4 * qg + 4 + DPAD)
            for h in range(H):
                dslot = h % 2
                dmv, dmvn = carve("PF", 2432 * dslot, 2432)
                dma("ct", lambda E, h=h, wcols=wcols, dmv=dmv: E.dma_start(
                    out=dmv[:, 0:wcols], in_=dr["dmt"][h][:, 0:wcols]), r=dmvn, w=dmvn)
                for si in range(2):
                    if si * NTS + 4 * qg >= N_A1:
                        continue
                    rp, rpb = (ps["PD1"], "PD1") if si == 0 else (ps["PD2"], "PD2")
                    for tj in range(ntj):
                        gt = si * NTS + tj
                        ks = tj % 2
                        kv, kvn = carve("PBF", 8192 + 1024 * ks, 1024)
                        vv, vvn = carve("PBF", 10240 + 1024 * ks, 1024)
                        dma("ct", lambda E, gt=gt, kv=kv: E.dma_start(out=kv, in_=dr["sp_k"][gt]),
                            r=["sp_k"] + kvn, w=kvn)
                        dma("ct", lambda E, gt=gt, vv=vv: E.dma_start(out=vv, in_=dr["sp_v"][gt]),
                            r=["sp_v"] + vvn, w=vvn)
                        sc = ps["PA"][:, 0:512] if tj % 2 == 0 else ps["PA"][:, 512:1024]
                        scb = "PA0" if tj % 2 == 0 else "PA1"
                        qrhs = qfg.rearrange("p (t m) -> p t m", m=1024)[
                            64 * (h % 2):64 * (h % 2) + 64, 4 * si:4 * si + 4, TILE * (h // 2):TILE * (h // 2 + 1)]
                        tr.op("tensor", lambda E, h=h, kv=kv, sc=sc, qrhs=qrhs: E.matmul(
                            sc, kv[64 * (h % 2):64 * (h % 2) + 64, TILE * (h // 2):TILE * (h // 2 + 1)], qrhs,
                            start=True, stop=True), kvn + qfgn, [scb])
                        smv, smvn = carve("PBF", 12288 + 512 * ks, 512)
                        off = TILE * (4 * qg - tj + DPAD)
                        tr.op("vector", lambda E, sc=sc, smv=smv, dmv=dmv, off=off: E.tensor_mul(
                            smv, sc, dmv[:, off:off + 512]), [scb] + dmvn, smvn)
                        tr.op("tensor", lambda E, h=h, vv=vv, rp=rp, smv=smv, tj=tj, ntj=ntj: E.matmul(
                            rp[0:DH, 0:512], vv[:, DH * h:DH * (h + 1)], smv,
                            start=(tj == 0), stop=(tj == ntj - 1)), vvn + smvn, [rpb])
                    rfv, rfvn = carve("PF", 14336 + 512 * si, 512, 0, DH)
                    tr.op("vector", lambda E, rp=rp, rfv=rfv: E.tensor_copy(rfv, rp[0:DH, 0:512]), [rpb], rfvn)
                    for tq in range(4):
                        gt = si * NTS + 4 * qg + tq
                        if gt >= N_A1:
                            continue
                        dma("cs", lambda E, gt=gt, h=h, rfv=rfv, tq=tq: E.dma_start(
                            out=dr["sp_r"][gt, h], in_=rfv[:, TILE * tq:TILE * (tq + 1)]), r=rfvn, w=["sp_r"])

    def bcast_tile(dst_off, src_b, src_lo, plus1, dstn_out):
        dst, dstn = carve("PF", dst_off, 1024)
        dma("ct", lambda E, dst=dst, src_b=src_b, src_lo=src_lo: E.dma_start(
            out=dst, in_=dr["modv"][src_b:src_b + 1, src_lo:src_lo + 1024].partition_broadcast(TILE).squeeze(1)),
            r=["modv"] + dstn, w=dstn)
        if plus1:
            tr.op("vector", lambda E, dst=dst: E.tensor_scalar_add(dst, dst, 1.0), dstn, dstn)
        dstn_out.extend(dstn)

    def a3_tile(gt, qslot):
        bi, st = gt // NTS, gt % NTS
        stc = sb["stat"]
        rfl, rfln = carve("PF", 4096, 2048, 0, DH)
        dma("ct", lambda E, gt=gt, rfl=rfl: E.dma_start(out=rfl.rearrange("d (h t) -> d h t", t=128), in_=dr["sp_r"][gt].transpose([1, 0, 2])),
            r=["sp_r"] + rfln, w=rfln)
        xv, xvn = carve("PF", 10240 + 1024 * (gt % 2), 1024)
        dma("cx", lambda E, bi=bi, st=st, xv=xv: E.dma_start(out=xv, in_=x_h[bi, TILE * st:TILE * (st + 1), :]),
            r=xvn, w=xvn)
        rt, rtn = carve("PF", 0, 1024)
        for h in range(H):
            pd, pb = (ps["PD1"], "PD1") if h % 2 == 0 else (ps["PD2"], "PD2")
            tr.op("tensor", lambda E, h=h, pd=pd, rfl=rfl: E.transpose(
                pd[:, 0:DH], rfl[0:DH, TILE * h:TILE * (h + 1)], sb["identf"][0:DH, 0:DH]),
                rfln + ["identf"], [pb])
            tr.op("vector", lambda E, h=h, pd=pd, rt=rt: E.tensor_copy(rt[:, DH * h:DH * (h + 1)], pd[:, 0:DH]),
                  [pb], rtn)
        rt3 = rt.rearrange("p (h u) -> p h u", u=DH)
        tr.op("vector", lambda E, rt3=rt3: E.tensor_reduce(stc[:, 0:16], rt3, AX.X, ALU.add), rtn, ["stat"])
        sq, sqn = carve("PF", 4096 + 2048, 1024)
        tr.op("scalar", lambda E, rt=rt, sq=sq: E.activation(sq, rt, AF.Square), rtn, sqn)
        tr.op("vector", lambda E, sq=sq: E.tensor_reduce(stc[:, 16:32], sq.rearrange("p (h u) -> p h u", u=DH),
                                                         AX.X, ALU.add), sqn, ["stat"])
        tr.op("vector", lambda E: E.tensor_scalar_mul(stc[:, 0:16], stc[:, 0:16], 1.0 / DH), ["stat"], ["stat"])
        tr.op("vector", lambda E: E.tensor_scalar_mul(stc[:, 16:32], stc[:, 16:32], 1.0 / DH), ["stat"], ["stat"])
        tr.op("vector", lambda E: E.tensor_mul(stc[:, 32:48], stc[:, 0:16], stc[:, 0:16]), ["stat"], ["stat"])
        tr.op("vector", lambda E: E.tensor_sub(stc[:, 16:32], stc[:, 16:32], stc[:, 32:48]), ["stat"], ["stat"])
        tr.op("scalar", lambda E: E.activation(stc[:, 16:32], stc[:, 16:32], AF.Ln, bias=sb["epsc"][:]), ["stat", "epsc"], ["stat"])
        tr.op("scalar", lambda E: E.activation(stc[:, 16:32], stc[:, 16:32], AF.Exp, scale=-0.5), ["stat"], ["stat"])
        if DBG_A1 and gt == 0:
            dma("cs", lambda E, rt=rt: E.dma_start(out=dr["dbg4"], in_=rt), r=rtn, w=[])
        mu_rep = stc[:, 0:16].unsqueeze(2).broadcast_to([TILE, 16, DH])
        rs_rep = stc[:, 16:32].unsqueeze(2).broadcast_to([TILE, 16, DH])
        gn, gnn = carve("PF", 4096 + 3072, 1024)
        gn3 = gn.rearrange("p (h u) -> p h u", u=DH)
        tr.op("vector", lambda E, gn3=gn3, rt3=rt3, mu_rep=mu_rep: E.tensor_sub(gn3, rt3, mu_rep),
              rtn + ["stat"], gnn)
        tr.op("vector", lambda E, gn3=gn3, rs_rep=rs_rep: E.tensor_mul(gn3, gn3, rs_rep), gnn + ["stat"], gnn)
        if DBG_A1 and gt == 0:
            dma("cs", lambda E, gn=gn: E.dma_start(out=dr["dbg5"], in_=gn), r=gnn, w=[])
        gnF, gnFn = carve("PBF", 8192, 1024)
        for c in range(8):
            pd, pb = (ps["PD1"], "PD1") if c % 2 == 0 else (ps["PD2"], "PD2")
            tr.op("tensor", lambda E, c=c, pd=pd, gn=gn: E.transpose(pd[:, 0:TILE], gn[:, TILE * c:TILE * (c + 1)],
                                                                     sb["identf"][:]), gnn + ["identf"], [pb])
            tr.op("scalar", lambda E, c=c, pd=pd, gnF=gnF: E.activation(gnF[:, TILE * c:TILE * (c + 1)],
                                                                        pd[:, 0:TILE], AF.Copy), [pb], gnFn)
        for half in range(2):
            for c in range(8):
                wv, wvn = carve("PBF", D * c, D)
                tr.op("tensor", lambda E, half=half, c=c, wv=wv, gnF=gnF: E.matmul(
                    ps["PA"][:, 512 * half:512 * (half + 1)], gnF[:, TILE * c:TILE * (c + 1)],
                    wv[:, 512 * half:512 * (half + 1)], start=(c == 0), stop=False), gnFn + wvn, ["PA"])
            tr.op("tensor", lambda E, half=half: E.matmul(
                ps["PA"][:, 512 * half:512 * (half + 1)], sb["ones_bf"][0:1, 0:TILE],
                sb["bwo_bf"][0:1, 512 * half:512 * (half + 1)], start=False, stop=True),
                ["ones_bf", "bwo_bf"], ["PA"])
        x1 = sb["x1q"][:, D * qslot:D * (qslot + 1)]
        x1b = f"x1q{qslot}"
        gax, gaxn = carve("PF", 8192, 1024)
        bc1, bc1n = carve("PF", 12288, 1024)
        tr.op("vector", lambda E, gax=gax, bc1=bc1: E.tensor_mul(gax, ps["PA"][:], bc1), ["PA"] + bc1n, gaxn)
        tr.op("vector", lambda E, x1=x1, xv=xv, gax=gax: E.tensor_add(x1, xv, gax), xvn + gaxn, [x1b])
        if DBG_A1 and gt == 0:
            dma("cs", lambda E, x1=x1: E.dma_start(out=dr["dbg6"], in_=x1), r=[x1b], w=[])
        tr.op("vector", lambda E, x1=x1: E.bn_stats(stc[:, 0:6], x1[:, 0:512]), [x1b], ["stat"])
        tr.op("vector", lambda E, x1=x1: E.bn_stats(stc[:, 6:12], x1[:, 512:1024]), [x1b], ["stat"])
        tr.op("vector", lambda E: E.bn_aggr(stc[:, 12:14], stc[:, 0:12]), ["stat"], ["stat"])
        tr.op("scalar", lambda E: E.activation(stc[:, 14:15], stc[:, 13:14], AF.Ln, bias=sb["epsc"][:]), ["stat", "epsc"], ["stat"])
        tr.op("scalar", lambda E, qslot=qslot: E.activation(sb["ln2rstd"][:, qslot:qslot + 1], stc[:, 14:15],
                                                            AF.Exp, scale=-0.5), ["stat"], ["ln2rstd"])
        tr.op("vector", lambda E, qslot=qslot: E.tensor_scalar(sb["ln2nmr"][:, qslot:qslot + 1], stc[:, 12:13],
                                                               sb["ln2rstd"][:, qslot:qslot + 1], -1.0,
                                                               ALU.mult, ALU.mult), ["stat", "ln2rstd"], ["ln2nmr"])

    def b_quad(quad):
        bi = (4 * quad) // NTS
        bc1, bc1n = carve("PF", 12288, 1024)
        bc2, bc2n = carve("PF", 13312, 1024)
        _ = []
        bcast_tile(12288, bi, 4 * D, True, _)
        bcast_tile(13312, bi, 3 * D, False, _)

        stc = sb["stat"]
        for tq in range(4):
            x1 = sb["x1q"][:, D * tq:D * (tq + 1)]
            x1b = f"x1q{tq}"
            xn2, xn2n = carve("PF", 8192, 1024)
            tr.op("scalar", lambda E, x1=x1, tq=tq, xn2=xn2: E.activation(
                xn2, x1, AF.Identity, bias=sb["ln2nmr"][:, tq:tq + 1], scale=sb["ln2rstd"][:, tq:tq + 1]),
                [x1b, "ln2rstd", "ln2nmr"], xn2n)
            h2, h2n = carve("PF", 9216, 1024)
            tr.op("vector", lambda E, xn2=xn2, h2=h2, bc1=bc1: E.tensor_mul(h2, xn2, bc1), xn2n + bc1n, h2n)
            tr.op("vector", lambda E, h2=h2, bc2=bc2: E.tensor_add(h2, h2, bc2), h2n + bc2n, h2n)
            tr.op("vector", lambda E, h2=h2: E.reduce_max(stc[:, 0:1], h2, axis=AX.X, apply_absolute_value=True),
                  h2n, ["stat"])
            tr.op("vector", lambda E: E.tensor_single_scalar(stc[:, 1:2], stc[:, 0:1], 1e-5, ALU.max),
                  ["stat"], ["stat"])
            tr.op("vector", lambda E: E.reciprocal(stc[:, 2:3], stc[:, 1:2]), ["stat"], ["stat"])
            tr.op("vector", lambda E: E.tensor_scalar_mul(stc[:, 3:4], stc[:, 2:3], 127.0), ["stat"], ["stat"])
            tr.op("vector", lambda E: E.tensor_scalar_mul(stc[:, 4:5], stc[:, 1:2], 1.0 / 127.0), ["stat"], ["stat"])
            tr.op("vector", lambda E, tq=tq: E.tensor_mul(stc[:, 8 + tq:9 + tq], stc[:, 4:5], sb["invsw"][:, 0:1]),
                  ["stat", "invsw"], ["stat"])
            tr.op("vector", lambda E, tq=tq: E.tensor_mul(stc[:, 12 + tq:13 + tq], stc[:, 4:5], sb["invsw"][:, 1:2]),
                  ["stat", "invsw"], ["stat"])
            if DBG_A1 and quad == 0 and tq == 0:
                dma("cs", lambda E, h2=h2: E.dma_start(out=dr["dbg7"], in_=h2), r=h2n, w=[])
            q1, q1n = carve("PF", 0, 1024)
            tr.op("vector", lambda E, h2=h2, q1=q1: E.tensor_scalar_mul(q1, h2, stc[:, 3:4]),
                  h2n + ["stat"], q1n)
            rnd_nearest(q1, 1024, q1n, None)
            tr.op("vector", lambda E, q1=q1: E.tensor_scalar(q1, q1, -128.0, 127.0, ALU.max, ALU.min), q1n, q1n)
            if DBG_A1 and quad == 0 and tq == 0:
                dma("cs", lambda E, q1=q1: E.dma_start(out=dr["dbg8"], in_=q1), r=q1n, w=[])
            xqF, xqFn = carve("PBF", 32768 + 1024 * tq, 1024)
            for c in range(8):
                pd, pb = (ps["PD1"], "PD1") if c % 2 == 0 else (ps["PD2"], "PD2")
                tr.op("tensor", lambda E, c=c, pd=pd, q1=q1: E.transpose(pd[:, 0:TILE], q1[:, TILE * c:TILE * (c + 1)],
                                                                        sb["identf"][:]), q1n + ["identf"], [pb])
                tr.op("scalar", lambda E, c=c, pd=pd, xqF=xqF: E.activation(xqF[:, TILE * c:TILE * (c + 1)],
                                                                            pd[:, 0:TILE], AF.Copy), [pb], xqFn)

        for e8 in range(8):
            ks = e8 % 2
            tgv, tgvn = carve("PBF", 0 + 4096 * ks, 4096)
            tuv, tuvn = carve("PBF", 8192 + 4096 * ks, 4096)
            dma("ct", lambda E, e8=e8, tgv=tgv: E.dma_start(out=tgv.rearrange("p (c m) -> p c m", m=512), in_=dr["tg"][e8].transpose([1, 0, 2])),
                r=tgvn, w=tgvn)
            dma("ct", lambda E, e8=e8, tuv=tuv: E.dma_start(out=tuv.rearrange("p (c m) -> p c m", m=512), in_=dr["tu"][e8].transpose([1, 0, 2])),
                r=tuvn, w=tuvn)
            for tq in range(4):
                xqF, xqFn = carve("PBF", 32768 + 1024 * tq, 1024)
                par = (4 * e8 + tq) % 2
                pg, pgb = (ps["PA"], "PA") if par == 0 else (ps["PB"], "PB")
                for c in range(8):
                    tr.op("tensor", lambda E, c=c, tgv=tgv, xqF=xqF, pg=pg: E.matmul(
                        pg[:, 0:512], xqF[:, TILE * c:TILE * (c + 1)], tgv[:, 512 * c:512 * (c + 1)],
                        start=(c == 0), stop=(c == 7)), xqFn + tgvn, [pgb + "0"])
                    tr.op("tensor", lambda E, c=c, tuv=tuv, xqF=xqF, pg=pg: E.matmul(
                        pg[:, 512:1024], xqF[:, TILE * c:TILE * (c + 1)], tuv[:, 512 * c:512 * (c + 1)],
                        start=(c == 0), stop=(c == 7)), xqFn + tuvn, [pgb + "1"])
                sg, sgn = carve("PF", 8192 + 512 * par, 512)
                tr.op("scalar", lambda E, tq=tq, sg=sg, pg=pg: E.activation(sg, pg[:, 0:512], AF.Silu,
                                                                     scale=stc[:, 8 + tq:9 + tq]),
                      [pgb + "0", "stat"], sgn)
                a2v, a2vn = carve("PBF", 16384 + 4096 * tq + 512 * e8, 512)
                tr.op("vector", lambda E, tq=tq, a2v=a2v, sg=sg, pg=pg: E.scalar_tensor_tensor(
                    a2v, pg[:, 512:1024], stc[:, 12 + tq:13 + tq], sg, ALU.mult, ALU.mult),
                    [pgb + "1", "stat"] + sgn, a2vn)
                tr.op("vector", lambda E, tq=tq, e8=e8, a2v=a2v: E.reduce_max(
                    stc[:, 20 + tq:21 + tq] if e8 == 0 else stc[:, 24 + tq:25 + tq], a2v,
                    axis=AX.X, apply_absolute_value=True), a2vn, ["stat"])
                if e8 > 0:
                    tr.op("vector", lambda E, tq=tq: E.tensor_max(stc[:, 20 + tq:21 + tq], stc[:, 20 + tq:21 + tq],
                                                                  stc[:, 24 + tq:25 + tq]), ["stat"], ["stat"])

        if DBG_A1 and quad == 0:
            dma("cs", lambda E: E.dma_start(out=dr["dbg12"], in_=sb["stat"][:, 16:32]), r=["stat"], w=[])
            a2full, a2fn = carve("PBF", 16384, 4096)
            dma("cs", lambda E, a2full=a2full: E.dma_start(out=dr["dbg9"], in_=a2full),
                r=[f"a2_0"] + a2fn, w=[])
        for pair in range(2):
            for tq2 in range(2):
                tq = 2 * pair + tq2
                tr.op("vector", lambda E, tq=tq: E.tensor_single_scalar(stc[:, 1:2], stc[:, 20 + tq:21 + tq],
                                                                        1e-5, ALU.max), ["stat"], ["stat"])
                tr.op("vector", lambda E: E.reciprocal(stc[:, 2:3], stc[:, 1:2]), ["stat"], ["stat"])
                tr.op("vector", lambda E: E.tensor_scalar_mul(stc[:, 3:4], stc[:, 2:3], 127.0), ["stat"], ["stat"])
                tr.op("vector", lambda E: E.tensor_scalar_mul(stc[:, 4:5], stc[:, 1:2], 1.0 / 127.0), ["stat"], ["stat"])
                tr.op("vector", lambda E, tq=tq: E.tensor_mul(stc[:, 36 + tq:37 + tq], stc[:, 4:5], sb["invsw"][:, 2:3]),
                      ["stat", "invsw"], ["stat"])
                xq2F, xq2Fn = carve("PBF", 36864 + 4096 * tq2, 4096)
                for qtr in range(4):
                    a2v, a2vn = carve("PBF", 16384 + 4096 * tq + 1024 * qtr, 1024)
                    q1, q1n = carve("PF", 0, 1024)
                    tr.op("vector", lambda E, a2v=a2v, q1=q1: E.tensor_scalar_mul(q1, a2v, stc[:, 3:4]),
                          a2vn + ["stat"], q1n)
                    if DBG_A1 and quad == 0 and tq == 0:
                        dma("cs", lambda E, q1=q1, qtr=qtr: E.dma_start(
                            out=dr["dbg11"][:, 1024 * qtr:1024 * (qtr + 1)], in_=q1), r=q1n, w=[])
                    rnd_nearest(q1, 1024, q1n, None)
                    tr.op("vector", lambda E, q1=q1: E.tensor_scalar(q1, q1, -128.0, 127.0, ALU.max, ALU.min),
                          q1n, q1n)
                    if DBG_A1 and quad == 0 and tq == 0:
                        dma("cs", lambda E, q1=q1, qtr=qtr: E.dma_start(
                            out=dr["dbg10"][:, 1024 * qtr:1024 * (qtr + 1)], in_=q1), r=q1n, w=[])
                    for c in range(8):
                        pd, pb = (ps["PD1"], "PD1") if c % 2 == 0 else (ps["PD2"], "PD2")
                        tr.op("tensor", lambda E, c=c, pd=pd, q1=q1: E.transpose(
                            pd[:, 0:TILE], q1[:, TILE * c:TILE * (c + 1)], sb["identf"][:]), q1n + ["identf"], [pb])
                        tr.op("scalar", lambda E, c=c, qtr=qtr, pd=pd, xq2F=xq2F: E.activation(
                            xq2F[:, 1024 * qtr + TILE * c:1024 * qtr + TILE * (c + 1)], pd[:, 0:TILE], AF.Copy),
                            [pb], xq2Fn)
            for k in range(32):
                wv, wvn = carve("PBF", 45056 + 1024 * (k % 2), 1024)
                dma("ct", lambda E, k=k, wv=wv: E.dma_start(out=wv, in_=dr["td"][k]), r=wvn, w=wvn)
                for tq2 in range(2):
                    xq2F, xq2Fn = carve("PBF", 36864 + 4096 * tq2, 4096)
                    pn, pb = (ps["PA"], "PA") if tq2 == 0 else (ps["PB"], "PB")
                    for half in range(2):
                        tr.op("tensor", lambda E, k=k, half=half, pn=pn, xq2F=xq2F, wv=wv: E.matmul(
                            pn[:, 512 * half:512 * (half + 1)], xq2F[:, TILE * k:TILE * (k + 1)],
                            wv[:, 512 * half:512 * (half + 1)], start=(k == 0), stop=(k == 31)),
                            xq2Fn + wvn, [pb + str(half)])
            gb, gbn = carve("PF", 13312, 1024)
            _ = []
            bcast_tile(13312, bi, 5 * D, False, _)
            for tq2 in range(2):
                tq = 2 * pair + tq2
                gt = 4 * quad + tq
                bi3, st3 = gt // NTS, gt % NTS
                pn, pb = (ps["PA"], "PA") if tq2 == 0 else (ps["PB"], "PB")
                x1 = sb["x1q"][:, D * tq:D * (tq + 1)]
                x1b = f"x1q{tq}"
                tmp, tmpn = carve("PF", 8192, 1024)
                for half in range(2):
                    tr.op("vector", lambda E, pn=pn, half=half, tq=tq, tmp=tmp, gb=gb: E.scalar_tensor_tensor(
                        tmp[:, 512 * half:512 * (half + 1)], pn[:, 512 * half:512 * (half + 1)],
                        stc[:, 36 + tq:37 + tq], gb[:, 512 * half:512 * (half + 1)], ALU.mult, ALU.mult),
                        [pb + str(half), "stat"] + gbn, tmpn)
                tr.op("vector", lambda E, x1=x1, tmp=tmp: E.tensor_add(x1, x1, tmp), [x1b] + tmpn, [x1b])
                dma("co", lambda E, bi3=bi3, st3=st3, x1=x1: E.dma_start(
                    out=out_h[bi3, TILE * st3:TILE * (st3 + 1), :], in_=x1), r=[x1b], w=[])

    prep()
    load_wqkv(0)
    qgen = quant_units()
    qdone = False
    for gt in range(N_A1):
        if gt == NTS:
            load_wqkv(1)
        a1_tile(gt)
        for _ in range(7):
            try:
                next(qgen)()
            except StopIteration:
                qdone = True
                break
    while not qdone:
        try:
            next(qgen)()
        except StopIteration:
            qdone = True
    a2()
    for quad in range(N_QUAD):
        load_wo()
        bi = (4 * quad) // NTS
        _ = []
        bcast_tile(12288, bi, 2 * D, False, _)
        for tq in range(4):
            a3_tile(4 * quad + tq, tq)
        b_quad(quad)

    ops = tr.finalize()
    co_waits = [(k, v) for k, v in tr.dcnt.items() if k[1] == "co"]
    if co_waits:
        ops["sync"].append((sorted(co_waits), lambda E: E.nop(), None))
    nds = 0
    for k in tr.dcnt:
        if k not in sems:
            sems[k] = stack.enter_context(nc.semaphore(f"sd{nds}"))
            nds += 1
    assert nds + len(Tr.ENG) <= 100, f"too many semaphores: {nds}"

    with nc.Block() as blk0:
        def clear(E):
            for s in sems.values():
                E.sem_clear(s)
        blk0.sync(clear)

    with nc.Block() as blk:
        def mk(ename):
            myops = ops[ename]

            def run(E):
                for waits, fn, inc in myops:
                    for k, v in waits:
                        E.wait_ge(sems[k], v)
                    inst = fn(E)
                    if inc is not None:
                        inst.then_inc(sems[inc[0]], inc[1])
            return run
        blk.tensor(mk("tensor"))
        blk.vector(mk("vector"))
        blk.scalar(mk("scalar"))
        blk.gpsimd(mk("gpsimd"))
        blk.sync(mk("sync"))

    n_inst = sum(len(v) for v in ops.values())
    return nc, stack, n_inst


def make_inputs(full):
    tabs = host_tables()
    WQ, WK, WV = full["L0_attn_WQ"], full["L0_attn_WK"], full["L0_attn_WV"]
    wqkvT = np.concatenate([w.transpose(2, 0, 1).reshape(D, D) for w in (WQ, WK, WV)], axis=1)
    wqkv4 = np.ascontiguousarray(wqkvT.reshape(8, TILE, 3 * D))
    wo4 = np.ascontiguousarray(full["L0_attn_WO"].reshape(8, TILE, D))
    wmod4 = np.ascontiguousarray(full["L0_wmod"].reshape(8, TILE, 12, 512).transpose(2, 0, 1, 3))
    wg4 = np.ascontiguousarray(full["L0_ffn_w_gate"].T.reshape(8, TILE, INNER))
    wu4 = np.ascontiguousarray(full["L0_ffn_w_up"].T.reshape(8, TILE, INNER))
    wd4 = np.ascontiguousarray(full["L0_ffn_w_down"].T.reshape(32, TILE, D))
    gammaF = np.ascontiguousarray(full["L0_attn_gamma"].reshape(8, TILE).T)
    betaF = np.ascontiguousarray(full["L0_attn_beta"].reshape(8, TILE).T)
    x = full["L0_attn_x"]
    c = full["L0_c"]
    shared = dict(wqkv4=wqkv4, wo4=wo4, wmod4=wmod4, wg4=wg4, wu4=wu4, wd4=wd4,
                  gammaF=gammaF, betaF=betaF, ctbl=tabs["ctbl"], stbl=tabs["stbl"],
                  dmt=tabs["dmt"], identf=tabs["ident"])
    in_maps = []
    for core in range(8):
        cs = c[2 * core:2 * core + 2]
        cF = np.ascontiguousarray(cs.reshape(2, 8, TILE).transpose(2, 1, 0).reshape(TILE, 16))
        m = dict(shared)
        m["x"] = np.ascontiguousarray(x[2 * core:2 * core + 2])
        m["cF"] = cF
        in_maps.append(m)
    return in_maps


def kernel(**inputs):
    in_maps = make_inputs(inputs)
    nc, stack, n_inst = build()
    res = run_bass_kernel_spmd(nc, in_maps, core_ids=list(range(8)))
    outs = [res.results[i]["out"] for i in range(8)]
    full = np.concatenate(outs, axis=0).astype(np.float32)
    if SPILL_OUT:
        return full, res
    stack.close()
    return full




# version 08
# speedup vs baseline: 1.0249x; 1.0249x over previous
import contextlib
import numpy as np
import concourse.bass as bass
import concourse.mybir as mybir
from concourse.bass_utils import run_bass_kernel_spmd

F32 = mybir.dt.float32
BF16 = mybir.dt.bfloat16
I32 = mybir.dt.int32
AF = mybir.ActivationFunctionType
ALU = mybir.AluOpType
AX = mybir.AxisListType

S, D, H, DH = 2048, 1024, 16, 64
TILE = 128
NTS = S // TILE
NT = 2 * NTS
NQG = NTS // 4
INNER = 4096
EPS = 1e-5
DPAD = 3

N_A1 = NT
N_QG = NQG
N_QUAD = NT // 4
SPILL_OUT = False
DBG_A1 = False


def host_tables():
    import jax
    with jax.default_device(jax.devices("cpu")[0]):
        import jax.numpy as jnp
        expoent = jnp.arange(0, DH, 2, dtype=jnp.float32) / DH
        angle = 10.0 ** (-4.0 * expoent)
        pos = jnp.arange(0, S, dtype=jnp.float32)
        nangle = pos[:, None] * angle[None, :]
        sin = np.asarray(jnp.sin(nangle))
        cos = np.asarray(jnp.cos(nangle))
        scale = 1.0 - 2.0 ** (-5.0 - jnp.arange(H, dtype=jnp.float32))
        logs = jnp.log(scale)
        delta = jnp.arange(S, dtype=jnp.float32)
        logd = delta[None, :] * logs[:, None]
        d32 = np.asarray(jnp.exp(logd))

    ctbl = np.zeros((TILE, NTS * DH), np.float32)
    stbl = np.zeros((TILE, NTS * DH), np.float32)
    for st in range(NTS):
        c = cos[st * TILE:(st + 1) * TILE]
        s = sin[st * TILE:(st + 1) * TILE]
        ctbl[:, st * DH:(st + 1) * DH] = np.repeat(c, 2, axis=1)
        blk = np.empty((TILE, DH), np.float32)
        blk[:, 0::2] = -s
        blk[:, 1::2] = s
        stbl[:, st * DH:(st + 1) * DH] = blk

    ndt = DPAD + NTS
    dmt = np.zeros((H, TILE, ndt * TILE), np.float32)
    jj = np.arange(TILE)[:, None]
    ii = np.arange(TILE)[None, :]
    for t in range(ndt):
        dlt = (t - DPAD) * TILE + ii - jj
        valid = (dlt >= 0) & (dlt < S)
        dl = np.clip(dlt, 0, S - 1)
        for h in range(H):
            dmt[h, :, t * TILE:(t + 1) * TILE] = d32[h][dl] * valid
    ident = np.eye(TILE, dtype=np.float32)
    return dict(ctbl=ctbl, stbl=stbl, dmt=dmt, ident=ident)


class Tr:
    ENG = ("tensor", "vector", "scalar", "gpsimd", "sync")
    DMA_Q = {"cx": "sync", "ct": "sync", "ck": "scalar", "cw": "gpsimd", "cs": "gpsimd", "co": "gpsimd"}
    SELFWAIT = {"vector", "scalar", "gpsimd"}

    def __init__(self):
        self.recs = []
        self.lastw = {}
        self.readers = {}
        self.openg = []
        self.opkey = []

    PSPLIT = {"PA": ("PA0", "PA1"), "PB": ("PB0", "PB1"), "PC": ("PC0", "PC1")}

    def _norm(self, bufs):
        out = []
        for b in bufs:
            if b in self.PSPLIT:
                out.extend(self.PSPLIT[b])
            else:
                out.append(b)
        return out

    def op(self, eng, fn, reads=(), writes=(), dma=None):
        reads = self._norm(reads)
        writes = self._norm(writes)
        deps = set()
        myid = len(self.recs)

        def need(pid):
            if pid is None:
                return
            peng = self.openg[pid]
            pkey = self.opkey[pid]
            if pkey is None and peng == eng and eng not in self.SELFWAIT:
                return
            deps.add(pid)

        for b in reads:
            need(self.lastw.get(b))
        for b in writes:
            need(self.lastw.get(b))
            for r in self.readers.get(b, ()):
                need(r)
        key = None
        if dma is not None:
            assert self.DMA_Q[dma] == eng
            slot = tuple(sorted(writes)) if writes else ("st",) + tuple(sorted(reads))
            key = ("D", dma, slot)
        self.openg.append(eng)
        self.opkey.append(key)
        for b in writes:
            self.lastw[b] = myid
            self.readers[b] = []
        for b in reads:
            self.readers.setdefault(b, []).append(myid)
        self.recs.append((eng, fn, key, deps))

    def finalize(self):
        n = len(self.recs)
        needed = [False] * n
        for (_, _, key, deps) in self.recs:
            for d in deps:
                needed[d] = True
        val = [0] * n
        ecnt = {e: 0 for e in self.ENG}
        dcnt = {}
        for i, (eng, fn, key, deps) in enumerate(self.recs):
            if key is not None:
                dcnt[key] = dcnt.get(key, 0) + 16
                val[i] = dcnt[key]
            elif needed[i]:
                ecnt[eng] += 1
                val[i] = ecnt[eng]
        ops = {e: [] for e in self.ENG}
        waited = {e: {} for e in self.ENG}
        for i, (eng, fn, key, deps) in enumerate(self.recs):
            waits = {}
            for d in deps:
                dk = self.opkey[d]
                sk = dk if dk is not None else ("E", self.openg[d])
                v = val[d]
                if waited[eng].get(sk, 0) >= v:
                    continue
                if waits.get(sk, 0) < v:
                    waits[sk] = v
            for k2, v2 in waits.items():
                waited[eng][k2] = v2
            if key is not None:
                inc = (key, 16)
            elif needed[i]:
                inc = (("E", eng), 1)
            else:
                inc = None
            ops[eng].append((sorted(waits.items()), fn, inc))
        self.dcnt = dcnt
        return ops


def build():
    tabs = host_tables()
    nc = bass.Bass("TRN2", target_bir_lowering=False, detect_race_conditions=False)
    stack = contextlib.ExitStack()
    dr = {}

    def din(name, shape, dt=F32):
        dr[name] = nc.dram_tensor(name, list(shape), dt, kind="ExternalInput").ap()

    def dint(name, shape, dt):
        kind = "ExternalOutput" if SPILL_OUT else "Internal"
        dr[name] = nc.dram_tensor(name, list(shape), dt, kind=kind).ap()

    din("x", [2, S, D])
    dr["out"] = nc.dram_tensor("out", [2, S, D], F32, kind="ExternalOutput").ap()
    din("cF", [TILE, 16])
    din("wmod4", [12, 8, TILE, 512])
    din("wqkv4", [8, TILE, 3 * D])
    din("wo4", [8, TILE, D])
    din("gammaF", [TILE, 8])
    din("betaF", [TILE, 8])
    din("wg4", [8, TILE, INNER])
    din("wu4", [8, TILE, INNER])
    din("wd4", [32, TILE, D])
    din("ctbl", [TILE, NTS * DH])
    din("stbl", [TILE, NTS * DH])
    din("dmt", [H, TILE, (DPAD + NTS) * TILE])
    din("identf", [TILE, TILE])
    dint("sp_q", [NT, TILE, D], BF16)
    dint("sp_kv", [NT, 2, TILE, D], BF16)
    dint("sp_r", [NT, H, DH, TILE], F32)
    dint("modv", [2, 6 * D], F32)
    if DBG_A1:
        for i in (1, 2, 3):
            dint(f"dbg{i}", [TILE, D], F32)
        for i in (4, 5, 6, 7, 8):
            dint(f"dbg{i}", [TILE, D], F32)
        for i in (9, 10, 11):
            dint(f"dbg{i}", [TILE, INNER], F32)
        dint("dbg12", [TILE, 16], F32)
    dint("tg", [8, 8, TILE, 512], BF16)
    dint("tu", [8, 8, TILE, 512], BF16)
    dint("td", [32, TILE, D], BF16)

    sb = {}

    def sbuf(name, shape, dt):
        sb[name] = stack.enter_context(nc.sbuf_tensor("sb_" + name, list(shape), dt))

    sbuf("identf", [TILE, TILE], F32)
    sbuf("ctbl", [TILE, NTS * DH], F32)
    sbuf("stbl", [TILE, NTS * DH], F32)
    sbuf("ones_bf", [1, 512], BF16)
    sbuf("onesf_col", [TILE, 1], F32)
    sbuf("onesf_row", [1, TILE], F32)
    sbuf("cF", [TILE, 16], F32)
    sbuf("silu_cF", [TILE, 16], F32)
    sbuf("gammaF", [TILE, 8], F32)
    sbuf("betaF_bf", [TILE, 8], BF16)
    sbuf("modF", [TILE, 32], F32)
    sbuf("saF_bf", [TILE, 16], BF16)
    sbuf("caF1p", [TILE, 16], F32)
    sbuf("saw_bf", [1, 3 * D], BF16)
    sbuf("bwo_bf", [1, D], BF16)
    sbuf("scw", [TILE, 3], F32)
    sbuf("invsw", [TILE, 3], F32)
    sbuf("accw", [TILE, 48], F32)
    sbuf("stat", [TILE, 64], F32)
    sbuf("ln2rstd", [TILE, 4], F32)
    sbuf("ln2nmr", [TILE, 4], F32)
    sbuf("epsc", [TILE, 1], F32)
    sbuf("x1q", [TILE, 4 * D], F32)
    sbuf("PF", [TILE, 16384], F32)
    sbuf("PBF", [TILE, 47104], BF16)
    sbuf("PI", [TILE, 1024], I32)

    SEG = {"PF": 512, "PBF": 1024, "PI": 512}

    def carve(pool, off, n, p0=0, p1=TILE):
        ap = sb[pool][p0:p1, off:off + n]
        s0, s1 = off // SEG[pool], (off + n - 1) // SEG[pool]
        names = [f"{pool}:{i}" for i in range(s0, s1 + 1)]
        return ap, names

    ps = {}
    for n, shp in (("PA", [TILE, 1024]), ("PB", [TILE, 1024]), ("PC", [TILE, 1024]),
                   ("PD1", [TILE, 512]), ("PD2", [TILE, 512])):
        ps[n] = stack.enter_context(nc.psum_tensor(n, shp, F32))

    tr = Tr()
    sems = {}
    for e in Tr.ENG:
        sems[("E", e)] = stack.enter_context(nc.semaphore("se_" + e))

    x_h, out_h = dr["x"], dr["out"]

    def dma(chan, fn, r=(), w=()):
        tr.op(Tr.DMA_Q[chan], fn, r, w, dma=chan)

    def rnd_nearest(src_ap, src_n, w_slices, rbufs):
        n = src_n
        for lo in range(0, n, 1024):
            wdt = min(1024, n - lo)
            sl = src_ap[:, lo:lo + wdt]
            ri, rin = carve("PI", 0, wdt)
            dv, dvn = carve("PF", FD_RND, wdt)
            bv, bvn = carve("PF", FD_RND + 1024, wdt)
            tr.op("vector", lambda E, sl=sl, ri=ri: E.tensor_copy(ri, sl), w_slices, rin)
            tr.op("vector", lambda E, sl=sl, ri=ri, dv=dv: E.tensor_sub(dv, sl, ri), list(w_slices) + rin, dvn)
            tr.op("vector", lambda E, dv=dv, bv=bv: E.tensor_single_scalar(bv, dv, -0.5, ALU.is_lt), dvn, bvn)
            tr.op("vector", lambda E, sl=sl, dv=dv, ri=ri: E.scalar_tensor_tensor(sl, dv, 0.5, ri, ALU.is_gt, ALU.add),
                  dvn + rin, w_slices)
            tr.op("vector", lambda E, sl=sl, bv=bv: E.tensor_sub(sl, sl, bv), list(w_slices) + bvn, w_slices)

    FD_RND = 14336

    def prep():
        for nm in ("identf", "ctbl", "stbl", "cF", "gammaF"):
            dma("cx", lambda E, nm=nm: E.dma_start(out=sb[nm][:], in_=dr[nm]), w=[nm])
        dma("cw", lambda E: E.dma_start(out=sb["betaF_bf"][:], in_=dr["betaF"]), w=["betaF_bf"])
        tr.op("vector", lambda E: E.memset(sb["ones_bf"][:], 1.0), (), ["ones_bf"])
        tr.op("vector", lambda E: E.memset(sb["onesf_col"][:], 1.0), (), ["onesf_col"])
        tr.op("vector", lambda E: E.memset(sb["onesf_row"][:], 1.0), (), ["onesf_row"])
        tr.op("vector", lambda E: E.memset(sb["epsc"][:], EPS), (), ["epsc"])
        tr.op("scalar", lambda E: E.activation(sb["silu_cF"][:], sb["cF"][:], AF.Silu), ["cF"], ["silu_cF"])

        def mtmp(lo, n):
            if lo < 4096:
                assert lo + n <= 4096
                return carve("PF", lo, n, 0, 2)
            return carve("PF", 4096 + (lo - 4096), n, 0, 2)

        wmb, wmbn = carve("PF", 10240, 4096)
        for n in range(12):
            pn, pb = (ps["PA"], "PA") if n % 2 == 0 else (ps["PB"], "PB")
            dma("ct", lambda E, n=n, wmb=wmb: E.dma_start(out=wmb.rearrange("p (c m) -> p c m", m=512), in_=dr["wmod4"][n].transpose([1, 0, 2])),
                r=wmbn, w=wmbn)
            for c in range(8):
                tr.op("tensor", lambda E, c=c, pn=pn, wmb=wmb: E.matmul(
                    pn[0:2, 0:512], sb["silu_cF"][:, 2 * c:2 * c + 2], wmb[:, 512 * c:512 * (c + 1)],
                    start=(c == 0), stop=(c == 7)), ["silu_cF"] + wmbn, [pb])
            mt, mtn = mtmp(512 * n, 512)
            tr.op("vector", lambda E, pn=pn, mt=mt: E.tensor_copy(mt, pn[0:2, 0:512]), [pb], mtn)

        ct, ctn = carve("PF", 8192, 1024, 0, 2)
        cm, cmn = carve("PF", 9216, 1024, 0, 2)
        for lo in (2048, 5120):
            mt, mtn = mtmp(lo, 1024)
            tr.op("vector", lambda E, mt=mt, ct=ct: E.tensor_single_scalar(ct, mt, 0.0, ALU.is_ge), mtn, ctn)
            tr.op("vector", lambda E, ct=ct: E.tensor_scalar(ct, ct, 2.0, -1.0, ALU.mult, ALU.add), ctn, ctn)
            tr.op("scalar", lambda E, mt=mt, cm=cm: E.activation(cm, mt, AF.Abs), mtn, cmn)
            tr.op("vector", lambda E, cm=cm: E.tensor_single_scalar(cm, cm, 0.25, ALU.max), cmn, cmn)
            tr.op("vector", lambda E, mt=mt, ct=ct, cm=cm: E.tensor_mul(mt, ct, cm), ctn + cmn, mtn)

        for vi, base in ((0, 0), (1, 1024)):
            for c in range(8):
                mt, mtn = mtmp(base + TILE * c, TILE)
                tr.op("tensor", lambda E, mt=mt: E.transpose(ps["PC"][:, 0:2], mt, sb["identf"][0:2, 0:2]),
                      mtn + ["identf"], ["PC"])
                tr.op("vector", lambda E, vi=vi, c=c: E.tensor_copy(sb["modF"][:, 16 * vi + 2 * c:16 * vi + 2 * c + 2],
                                                                    ps["PC"][:, 0:2]), ["PC"], ["modF"])
        tr.op("vector", lambda E: E.tensor_copy(sb["saF_bf"][:], sb["modF"][:, 0:16]), ["modF"], ["saF_bf"])
        tr.op("vector", lambda E: E.tensor_scalar_add(sb["caF1p"][:], sb["modF"][:, 16:32], 1.0), ["modF"], ["caF1p"])

        for bi in range(2):
            for seg in range(3):
                mt, mtn = mtmp(2048 * seg, 2048)
                dma("cs", lambda E, bi=bi, seg=seg, mt=mt: E.dma_start(
                    out=dr["modv"][bi:bi + 1, 2048 * seg:2048 * (seg + 1)], in_=mt[bi:bi + 1, :]), r=mtn, w=["modv"])

        pass

    def quant_units():
        nch = {"g": 8, "u": 8, "d": 32}
        wsrc = {"g": "wg4", "u": "wu4", "d": "wd4"}
        wid = {"g": 0, "u": 1, "d": 2}
        ccol = {"g": 0, "u": 8, "d": 16}
        nsl = {"g": 4, "u": 4, "d": 1}
        wcs, wcsn = carve("PF", 12288, 1024)
        ab2, ab2n = carve("PF", 13312, 1024)
        qtb, qtbn = carve("PBF", 28672, 1024)

        for m in ("g", "u", "d"):
            for c in range(nch[m]):
                for sl in range(nsl[m]):
                    def u1(m=m, c=c, sl=sl):
                        dma("ct", lambda E: E.dma_start(out=wcs, in_=dr[wsrc[m]][c][:, 1024 * sl:1024 * (sl + 1)]),
                            r=wcsn, w=wcsn)
                        tr.op("scalar", lambda E: E.activation(ab2, wcs, AF.Abs), wcsn, ab2n)
                        col = 32 + sl
                        tr.op("vector", lambda E, col=col: E.reduce_sum(sb["accw"][:, col:col + 1], ab2, axis=AX.X),
                              ab2n, ["accw"])
                        if sl == nsl[m] - 1:
                            if nsl[m] > 1:
                                tr.op("vector", lambda E: E.tensor_reduce(
                                    sb["accw"][:, ccol[m] + c:ccol[m] + c + 1],
                                    sb["accw"][:, 32:32 + nsl[m]], AX.X, ALU.add), ["accw"], ["accw"])
                            else:
                                tr.op("vector", lambda E: E.tensor_copy(
                                    sb["accw"][:, ccol[m] + c:ccol[m] + c + 1], sb["accw"][:, 32:33]),
                                    ["accw"], ["accw"])
                    yield u1

        for m in ("g", "u", "d"):
            def us(m=m):
                tr.op("tensor", lambda E: E.matmul(ps["PD1"][0:1, 0:nch[m]], sb["onesf_col"][:],
                                                   sb["accw"][:, ccol[m]:ccol[m] + nch[m]], start=True, stop=True),
                      ["onesf_col", "accw"], ["PD1"])
                tr.op("vector", lambda E: E.reduce_sum(sb["stat"][0:1, 32:33], ps["PD1"][0:1, 0:nch[m]], axis=AX.X),
                      ["PD1"], ["stat"])
                tr.op("vector", lambda E: E.tensor_scalar(sb["stat"][0:1, 33:34], sb["stat"][0:1, 32:33],
                                                          1.0 / (INNER * D), 1e-5, ALU.mult, ALU.max),
                      ["stat"], ["stat"])
                tr.op("vector", lambda E: E.reciprocal(sb["stat"][0:1, 34:35], sb["stat"][0:1, 33:34]),
                      ["stat"], ["stat"])
                for (srcc, dst) in ((34, "scw"), (33, "invsw")):
                    tr.op("tensor", lambda E, srcc=srcc: E.matmul(ps["PD2"][:, 0:1], sb["onesf_row"][:],
                                                                  sb["stat"][0:1, srcc:srcc + 1], start=True, stop=True),
                          ["stat", "onesf_row"], ["PD2"])
                    tr.op("vector", lambda E, dst=dst: E.tensor_copy(sb[dst][:, wid[m]:wid[m] + 1], ps["PD2"][:, 0:1]),
                          ["PD2"], [dst])
            yield us

        for m in ("g", "u", "d"):
            for c in range(nch[m]):
                for sl in range(nsl[m]):
                    def u2(m=m, c=c, sl=sl):
                        dma("ct", lambda E: E.dma_start(out=wcs, in_=dr[wsrc[m]][c][:, 1024 * sl:1024 * (sl + 1)]),
                            r=wcsn, w=wcsn)
                        tr.op("vector", lambda E: E.tensor_scalar_mul(wcs, wcs, sb["scw"][:, wid[m]:wid[m] + 1]),
                              wcsn + ["scw"], wcsn)
                        rnd_nearest(wcs, 1024, wcsn, None)
                        tr.op("vector", lambda E: E.tensor_scalar(qtb, wcs, -1.0, 1.0, ALU.max, ALU.min),
                              wcsn, qtbn)
                        if m == "d":
                            dma("cs", lambda E: E.dma_start(out=dr["td"][c], in_=qtb), r=qtbn, w=[])
                        else:
                            tgt = "tg" if m == "g" else "tu"
                            for e2 in range(2):
                                e8 = 2 * sl + e2
                                dma("cs", lambda E, e8=e8, e2=e2, tgt=tgt: E.dma_start(
                                    out=dr[tgt][e8, c], in_=qtb[:, 512 * e2:512 * (e2 + 1)]), r=qtbn, w=[])
                    yield u2

    def load_wqkv(bi):
        for c in range(8):
            wv, wvn = carve("PBF", 3 * D * c, 3 * D)
            dma("cw", lambda E, c=c, wv=wv: E.dma_start(out=wv, in_=dr["wqkv4"][c]), r=wvn, w=wvn)
        pmap = {0: ("PA", ps["PA"]), 1: ("PB", ps["PB"]), 2: ("PC", ps["PC"])}
        for p in range(3):
            pb, pn = pmap[p]
            for half in range(2):
                for c in range(8):
                    wv, wvn = carve("PBF", 3 * D * c, 3 * D)
                    tr.op("tensor", lambda E, p=p, half=half, c=c, pn=pn, wv=wv: E.matmul(
                        pn[0:1, 512 * half:512 * (half + 1)], sb["saF_bf"][:, 2 * c + bi:2 * c + bi + 1],
                        wv[:, D * p + 512 * half:D * p + 512 * (half + 1)], start=(c == 0), stop=(c == 7)),
                        ["saF_bf"] + wvn, [pb])
                tr.op("vector", lambda E, p=p, half=half, pn=pn: E.tensor_copy(
                    sb["saw_bf"][0:1, D * p + 512 * half:D * p + 512 * (half + 1)],
                    pn[0:1, 512 * half:512 * (half + 1)]), [pb], ["saw_bf"])
        for c in range(8):
            wv, wvn = carve("PBF", 3 * D * c, 3 * D)
            tr.op("vector", lambda E, c=c, wv=wv: E.tensor_scalar_mul(wv, wv, sb["caF1p"][:, 2 * c + bi:2 * c + bi + 1]),
                  wvn + ["caF1p"], wvn)

    def load_wo():
        for c in range(8):
            wv, wvn = carve("PBF", D * c, D)
            dma("cw", lambda E, c=c, wv=wv: E.dma_start(out=wv, in_=dr["wo4"][c]), r=wvn, w=wvn)
        for half in range(2):
            for c in range(8):
                wv, wvn = carve("PBF", D * c, D)
                tr.op("tensor", lambda E, half=half, c=c, wv=wv: E.matmul(
                    ps["PD1"][0:1, 0:512], sb["betaF_bf"][:, c:c + 1],
                    wv[:, 512 * half:512 * (half + 1)], start=(c == 0), stop=(c == 7)),
                    ["betaF_bf"] + wvn, ["PD1"])
            tr.op("vector", lambda E, half=half: E.tensor_copy(sb["bwo_bf"][0:1, 512 * half:512 * (half + 1)],
                                                               ps["PD1"][0:1, 0:512]), ["PD1"], ["bwo_bf"])
        for c in range(8):
            wv, wvn = carve("PBF", D * c, D)
            tr.op("vector", lambda E, c=c, wv=wv: E.tensor_scalar_mul(wv, wv, sb["gammaF"][:, c:c + 1]),
                  wvn + ["gammaF"], wvn)

    def a1_tile(gt):
        bi, st = gt // NTS, gt % NTS
        xv, xvn = carve("PF", 10240 + 1024 * (gt % 2), 1024)
        dma("cx", lambda E, bi=bi, st=st, xv=xv: E.dma_start(out=xv, in_=x_h[bi, TILE * st:TILE * (st + 1), :]),
            r=xvn, w=xvn)
        stc = sb["stat"]
        tr.op("vector", lambda E, xv=xv: E.bn_stats(stc[:, 0:6], xv[:, 0:512]), xvn, ["stat"])
        tr.op("vector", lambda E, xv=xv: E.bn_stats(stc[:, 6:12], xv[:, 512:1024]), xvn, ["stat"])
        tr.op("vector", lambda E: E.bn_aggr(stc[:, 12:14], stc[:, 0:12]), ["stat"], ["stat"])
        tr.op("scalar", lambda E: E.activation(stc[:, 14:15], stc[:, 13:14], AF.Ln, bias=sb["epsc"][:]), ["stat", "epsc"], ["stat"])
        tr.op("scalar", lambda E: E.activation(stc[:, 15:16], stc[:, 14:15], AF.Exp, scale=-0.5), ["stat"], ["stat"])
        tr.op("vector", lambda E: E.tensor_scalar(stc[:, 16:17], stc[:, 12:13], stc[:, 15:16], -1.0, ALU.mult, ALU.mult),
              ["stat"], ["stat"])
        xn1, xn1n = carve("PF", 8192, 1024)
        tr.op("scalar", lambda E, xv=xv, xn1=xn1: E.activation(xn1, xv, AF.Identity,
                                                               bias=stc[:, 16:17], scale=stc[:, 15:16]),
              xvn + ["stat"], xn1n)
        xn1F, xn1Fn = carve("PBF", 24576, 1024)
        for c in range(8):
            pd, pb = (ps["PD1"], "PD1") if c % 2 == 0 else (ps["PD2"], "PD2")
            tr.op("tensor", lambda E, c=c, pd=pd, xn1=xn1: E.transpose(pd[:, 0:TILE], xn1[:, TILE * c:TILE * (c + 1)],
                                                                      sb["identf"][:]), xn1n + ["identf"], [pb])
            tr.op("scalar", lambda E, c=c, pd=pd, xn1F=xn1F: E.activation(xn1F[:, TILE * c:TILE * (c + 1)],
                                                                          pd[:, 0:TILE], AF.Copy), [pb], xn1Fn)
        pmap = {0: ("PA", ps["PA"]), 1: ("PB", ps["PB"]), 2: ("PC", ps["PC"])}
        for p in range(3):
            pb, pn = pmap[p]
            for half in range(2):
                for c in range(8):
                    wv, wvn = carve("PBF", 3 * D * c, 3 * D)
                    tr.op("tensor", lambda E, p=p, half=half, c=c, pn=pn, wv=wv, xn1F=xn1F: E.matmul(
                        pn[:, 512 * half:512 * (half + 1)], xn1F[:, TILE * c:TILE * (c + 1)],
                        wv[:, D * p + 512 * half:D * p + 512 * (half + 1)], start=(c == 0), stop=False),
                        xn1Fn + wvn, [pb])
                tr.op("tensor", lambda E, p=p, half=half, pn=pn: E.matmul(
                    pn[:, 512 * half:512 * (half + 1)], sb["ones_bf"][0:1, 0:TILE],
                    sb["saw_bf"][0:1, D * p + 512 * half:D * p + 512 * (half + 1)], start=False, stop=True),
                    ["ones_bf", "saw_bf"], [pb])
        if DBG_A1 and gt == 0:
            dma("cs", lambda E, xn1=xn1: E.dma_start(out=dr["dbg1"], in_=xn1), r=xn1n, w=[])
            for pbx, pnx, tag in (("PA", ps["PA"], "dbg2"), ("PB", ps["PB"], "dbg3")):
                tmpd, tmpdn = carve("PF", 2048, 1024)
                tr.op("vector", lambda E, pnx=pnx, tmpd=tmpd: E.tensor_copy(tmpd, pnx[:]), [pbx], tmpdn)
                dma("cs", lambda E, tmpd=tmpd, tag=tag: E.dma_start(out=dr[tag], in_=tmpd), r=tmpdn, w=[])
        for pb, pn, base in (("PA", ps["PA"], 0), ("PB", ps["PB"], 4096)):
            raw, rawn = carve("PF", base + 2048, 1024)
            tr.op("vector", lambda E, pn=pn, raw=raw: E.tensor_copy(raw, pn[:]), [pb], rawn)
            rot, rotn = carve("PF", base, 1024)
            tmp, tmpn = carve("PF", base + 1024, 1024)
            cs = sb["ctbl"][:, DH * st:DH * (st + 1)]
            sn = sb["stbl"][:, DH * st:DH * (st + 1)]
            p3 = raw.rearrange("p (h u) -> p h u", u=DH)
            t3 = tmp.rearrange("p (h u) -> p h u", u=DH)
            d3 = rot.rearrange("p (h u) -> p h u", u=DH)
            sne = sn[:, 0::2].unsqueeze(1).broadcast_to([TILE, H, DH // 2])
            sno = sn[:, 1::2].unsqueeze(1).broadcast_to([TILE, H, DH // 2])
            csf = cs.unsqueeze(1).broadcast_to([TILE, H, DH])
            tr.op("vector", lambda E, t3=t3, sne=sne, p3=p3: E.tensor_mul(t3[:, :, 0::2], sne, p3[:, :, 1::2]),
                  rawn + ["stbl"], tmpn)
            tr.op("vector", lambda E, t3=t3, sno=sno, p3=p3: E.tensor_mul(t3[:, :, 1::2], sno, p3[:, :, 0::2]),
                  rawn + ["stbl"], tmpn)
            tr.op("vector", lambda E, d3=d3, csf=csf, p3=p3: E.tensor_mul(d3, csf, p3), rawn + ["ctbl"], rotn)
            tr.op("vector", lambda E, rot=rot, tmp=tmp: E.tensor_add(rot, rot, tmp), rotn + tmpn, rotn)
        vbf, vbfn = carve("PBF", 25600, 1024)
        tr.op("vector", lambda E, vbf=vbf: E.tensor_copy(vbf, ps["PC"][:]), ["PC"], vbfn)
        dma("cs", lambda E, gt=gt, vbf=vbf: E.dma_start(out=dr["sp_kv"][gt, 1], in_=vbf), r=vbfn, w=["sp_kv"])
        for base, dsto, tgt in ((0, 26624, "sp_q"), (4096, 27648, "sp_kv")):
            rot, rotn = carve("PF", base, 1024)
            dst, dstn = carve("PBF", dsto, 1024)
            for c in range(8):
                pd, pb2 = (ps["PD1"], "PD1") if c % 2 == 0 else (ps["PD2"], "PD2")
                tr.op("tensor", lambda E, c=c, pd=pd, rot=rot: E.transpose(pd[:, 0:TILE], rot[:, TILE * c:TILE * (c + 1)],
                                                                           sb["identf"][:]), rotn + ["identf"], [pb2])
                tr.op("scalar", lambda E, c=c, pd=pd, dst=dst: E.activation(dst[:, TILE * c:TILE * (c + 1)],
                                                                            pd[:, 0:TILE], AF.Copy), [pb2], dstn)
            if tgt == "sp_q":
                dma("cs", lambda E, gt=gt, dst=dst: E.dma_start(out=dr["sp_q"][gt], in_=dst), r=dstn, w=["sp_q"])
            else:
                dma("cs", lambda E, gt=gt, dst=dst: E.dma_start(out=dr["sp_kv"][gt, 0], in_=dst), r=dstn, w=["sp_kv"])

    def a2():
        for qg in range(N_QG):
            ntj = 4 * qg + 4
            qfg, qfgn = carve("PBF", 0, 8192)
            for si in range(2):
                for tq in range(4):
                    gt = si * NTS + 4 * qg + tq
                    if gt >= N_A1:
                        continue
                    sl = qfg[:, 1024 * (4 * si + tq):1024 * (4 * si + tq + 1)]
                    dma("ct", lambda E, gt=gt, sl=sl: E.dma_start(out=sl, in_=dr["sp_q"][gt]),
                        r=["sp_q"] + qfgn, w=qfgn)
            wcols = TILE * (4 * qg + 4 + DPAD)
            for hp in range(H // 2):
                h0, h1 = 2 * hp, 2 * hp + 1
                ds = 2 * (hp % 2)
                dmv0, dmv0n = carve("PF", 2432 * ds, 2432)
                dmv1, dmv1n = carve("PF", 2432 * (ds + 1), 2432)
                dma("ct", lambda E, h0=h0, wcols=wcols, dmv0=dmv0: E.dma_start(
                    out=dmv0[:, 0:wcols], in_=dr["dmt"][h0][:, 0:wcols]), r=dmv0n, w=dmv0n)
                dma("ct", lambda E, h1=h1, wcols=wcols, dmv1=dmv1: E.dma_start(
                    out=dmv1[:, 0:wcols], in_=dr["dmt"][h1][:, 0:wcols]), r=dmv1n, w=dmv1n)
                for si in range(2):
                    if si * NTS + 4 * qg >= N_A1:
                        continue
                    rp0 = (ps["PD1"] if si == 0 else ps["PD2"])[0:DH, 0:512]
                    rp0b = "PD1" if si == 0 else "PD2"
                    rp1 = ps["PC"][0:DH, 0:512] if si == 0 else ps["PC"][0:DH, 512:1024]
                    rp1b = "PC0" if si == 0 else "PC1"
                    for tj in range(ntj):
                        gt = si * NTS + tj
                        ks = tj % 4
                        kvv, kvvn = carve("PBF", 8192 + 2048 * ks, 2048)
                        kv, vv = kvv[:, 0:1024], kvv[:, 1024:2048]
                        kvn = vvn = kvvn
                        dma("ck", lambda E, gt=gt, kvv=kvv: E.dma_start(
                            out=kvv.rearrange("p (a m) -> p a m", m=1024), in_=dr["sp_kv"][gt].transpose([1, 0, 2])),
                            r=["sp_kv"] + kvvn, w=kvvn)
                        off = TILE * (4 * qg - tj + DPAD)
                        for hi, (h, dmv, dmvn, rp, rpb) in enumerate(
                                ((h0, dmv0, dmv0n, rp0, rp0b), (h1, dmv1, dmv1n, rp1, rp1b))):
                            kp = tj % 2
                            sc = (ps["PA"] if hi == 0 else ps["PB"])[:, 512 * kp:512 * (kp + 1)]
                            scb = ("PA" if hi == 0 else "PB") + str(kp)
                            qrhs = qfg.rearrange("p (t m) -> p t m", m=1024)[
                                64 * (h % 2):64 * (h % 2) + 64, 4 * si:4 * si + 4,
                                TILE * (h // 2):TILE * (h // 2 + 1)]
                            tr.op("tensor", lambda E, h=h, kv=kv, sc=sc, qrhs=qrhs: E.matmul(
                                sc, kv[64 * (h % 2):64 * (h % 2) + 64, TILE * (h // 2):TILE * (h // 2 + 1)],
                                qrhs, start=True, stop=True), kvn + qfgn, [scb])
                            smv, smvn = carve("PBF", 16384 + 512 * (2 * (tj % 2) + hi), 512)
                            tr.op("vector", lambda E, sc=sc, smv=smv, dmv=dmv, off=off: E.tensor_mul(
                                smv, sc, dmv[:, off:off + 512]), [scb] + dmvn, smvn)
                            tr.op("tensor", lambda E, h=h, vv=vv, rp=rp, smv=smv, tj=tj, ntj=ntj: E.matmul(
                                rp, vv[:, DH * h:DH * (h + 1)], smv,
                                start=(tj == 0), stop=(tj == ntj - 1)), vvn + smvn, [rpb])
                    for hi, (h, rp, rpb) in enumerate(((h0, rp0, rp0b), (h1, rp1, rp1b))):
                        rfv, rfvn = carve("PF", 14336 + 512 * (2 * si + hi), 512, 0, DH)
                        tr.op("vector", lambda E, rp=rp, rfv=rfv: E.tensor_copy(rfv, rp), [rpb], rfvn)
                        for tq in range(4):
                            gt = si * NTS + 4 * qg + tq
                            if gt >= N_A1:
                                continue
                            dma("cs", lambda E, gt=gt, h=h, rfv=rfv, tq=tq: E.dma_start(
                                out=dr["sp_r"][gt, h], in_=rfv[:, TILE * tq:TILE * (tq + 1)]), r=rfvn, w=["sp_r"])

    def bcast_tile(dst_off, src_b, src_lo, plus1, dstn_out):
        dst, dstn = carve("PF", dst_off, 1024)
        dma("ct", lambda E, dst=dst, src_b=src_b, src_lo=src_lo: E.dma_start(
            out=dst, in_=dr["modv"][src_b:src_b + 1, src_lo:src_lo + 1024].partition_broadcast(TILE).squeeze(1)),
            r=["modv"] + dstn, w=dstn)
        if plus1:
            tr.op("vector", lambda E, dst=dst: E.tensor_scalar_add(dst, dst, 1.0), dstn, dstn)
        dstn_out.extend(dstn)

    def a3_tile(gt, qslot):
        bi, st = gt // NTS, gt % NTS
        stc = sb["stat"]
        rfl, rfln = carve("PF", 4096, 2048, 0, DH)
        dma("ct", lambda E, gt=gt, rfl=rfl: E.dma_start(out=rfl.rearrange("d (h t) -> d h t", t=128), in_=dr["sp_r"][gt].transpose([1, 0, 2])),
            r=["sp_r"] + rfln, w=rfln)
        xv, xvn = carve("PF", 10240 + 1024 * (gt % 2), 1024)
        dma("cx", lambda E, bi=bi, st=st, xv=xv: E.dma_start(out=xv, in_=x_h[bi, TILE * st:TILE * (st + 1), :]),
            r=xvn, w=xvn)
        rt, rtn = carve("PF", 0, 1024)
        for h in range(H):
            pd, pb = (ps["PD1"], "PD1") if h % 2 == 0 else (ps["PD2"], "PD2")
            tr.op("tensor", lambda E, h=h, pd=pd, rfl=rfl: E.transpose(
                pd[:, 0:DH], rfl[0:DH, TILE * h:TILE * (h + 1)], sb["identf"][0:DH, 0:DH]),
                rfln + ["identf"], [pb])
            tr.op("vector", lambda E, h=h, pd=pd, rt=rt: E.tensor_copy(rt[:, DH * h:DH * (h + 1)], pd[:, 0:DH]),
                  [pb], rtn)
        rt3 = rt.rearrange("p (h u) -> p h u", u=DH)
        tr.op("vector", lambda E, rt3=rt3: E.tensor_reduce(stc[:, 0:16], rt3, AX.X, ALU.add), rtn, ["stat"])
        sq, sqn = carve("PF", 4096 + 2048, 1024)
        tr.op("scalar", lambda E, rt=rt, sq=sq: E.activation(sq, rt, AF.Square), rtn, sqn)
        tr.op("vector", lambda E, sq=sq: E.tensor_reduce(stc[:, 16:32], sq.rearrange("p (h u) -> p h u", u=DH),
                                                         AX.X, ALU.add), sqn, ["stat"])
        tr.op("vector", lambda E: E.tensor_scalar_mul(stc[:, 0:16], stc[:, 0:16], 1.0 / DH), ["stat"], ["stat"])
        tr.op("vector", lambda E: E.tensor_scalar_mul(stc[:, 16:32], stc[:, 16:32], 1.0 / DH), ["stat"], ["stat"])
        tr.op("vector", lambda E: E.tensor_mul(stc[:, 32:48], stc[:, 0:16], stc[:, 0:16]), ["stat"], ["stat"])
        tr.op("vector", lambda E: E.tensor_sub(stc[:, 16:32], stc[:, 16:32], stc[:, 32:48]), ["stat"], ["stat"])
        tr.op("scalar", lambda E: E.activation(stc[:, 16:32], stc[:, 16:32], AF.Ln, bias=sb["epsc"][:]), ["stat", "epsc"], ["stat"])
        tr.op("scalar", lambda E: E.activation(stc[:, 16:32], stc[:, 16:32], AF.Exp, scale=-0.5), ["stat"], ["stat"])
        if DBG_A1 and gt == 0:
            dma("cs", lambda E, rt=rt: E.dma_start(out=dr["dbg4"], in_=rt), r=rtn, w=[])
        mu_rep = stc[:, 0:16].unsqueeze(2).broadcast_to([TILE, 16, DH])
        rs_rep = stc[:, 16:32].unsqueeze(2).broadcast_to([TILE, 16, DH])
        gn, gnn = carve("PF", 4096 + 3072, 1024)
        gn3 = gn.rearrange("p (h u) -> p h u", u=DH)
        tr.op("vector", lambda E, gn3=gn3, rt3=rt3, mu_rep=mu_rep: E.tensor_sub(gn3, rt3, mu_rep),
              rtn + ["stat"], gnn)
        tr.op("vector", lambda E, gn3=gn3, rs_rep=rs_rep: E.tensor_mul(gn3, gn3, rs_rep), gnn + ["stat"], gnn)
        if DBG_A1 and gt == 0:
            dma("cs", lambda E, gn=gn: E.dma_start(out=dr["dbg5"], in_=gn), r=gnn, w=[])
        gnF, gnFn = carve("PBF", 8192, 1024)
        for c in range(8):
            pd, pb = (ps["PD1"], "PD1") if c % 2 == 0 else (ps["PD2"], "PD2")
            tr.op("tensor", lambda E, c=c, pd=pd, gn=gn: E.transpose(pd[:, 0:TILE], gn[:, TILE * c:TILE * (c + 1)],
                                                                     sb["identf"][:]), gnn + ["identf"], [pb])
            tr.op("scalar", lambda E, c=c, pd=pd, gnF=gnF: E.activation(gnF[:, TILE * c:TILE * (c + 1)],
                                                                        pd[:, 0:TILE], AF.Copy), [pb], gnFn)
        for half in range(2):
            for c in range(8):
                wv, wvn = carve("PBF", D * c, D)
                tr.op("tensor", lambda E, half=half, c=c, wv=wv, gnF=gnF: E.matmul(
                    ps["PA"][:, 512 * half:512 * (half + 1)], gnF[:, TILE * c:TILE * (c + 1)],
                    wv[:, 512 * half:512 * (half + 1)], start=(c == 0), stop=False), gnFn + wvn, ["PA"])
            tr.op("tensor", lambda E, half=half: E.matmul(
                ps["PA"][:, 512 * half:512 * (half + 1)], sb["ones_bf"][0:1, 0:TILE],
                sb["bwo_bf"][0:1, 512 * half:512 * (half + 1)], start=False, stop=True),
                ["ones_bf", "bwo_bf"], ["PA"])
        x1 = sb["x1q"][:, D * qslot:D * (qslot + 1)]
        x1b = f"x1q{qslot}"
        gax, gaxn = carve("PF", 8192, 1024)
        bc1, bc1n = carve("PF", 12288, 1024)
        tr.op("vector", lambda E, gax=gax, bc1=bc1: E.tensor_mul(gax, ps["PA"][:], bc1), ["PA"] + bc1n, gaxn)
        tr.op("vector", lambda E, x1=x1, xv=xv, gax=gax: E.tensor_add(x1, xv, gax), xvn + gaxn, [x1b])
        if DBG_A1 and gt == 0:
            dma("cs", lambda E, x1=x1: E.dma_start(out=dr["dbg6"], in_=x1), r=[x1b], w=[])
        tr.op("vector", lambda E, x1=x1: E.bn_stats(stc[:, 0:6], x1[:, 0:512]), [x1b], ["stat"])
        tr.op("vector", lambda E, x1=x1: E.bn_stats(stc[:, 6:12], x1[:, 512:1024]), [x1b], ["stat"])
        tr.op("vector", lambda E: E.bn_aggr(stc[:, 12:14], stc[:, 0:12]), ["stat"], ["stat"])
        tr.op("scalar", lambda E: E.activation(stc[:, 14:15], stc[:, 13:14], AF.Ln, bias=sb["epsc"][:]), ["stat", "epsc"], ["stat"])
        tr.op("scalar", lambda E, qslot=qslot: E.activation(sb["ln2rstd"][:, qslot:qslot + 1], stc[:, 14:15],
                                                            AF.Exp, scale=-0.5), ["stat"], ["ln2rstd"])
        tr.op("vector", lambda E, qslot=qslot: E.tensor_scalar(sb["ln2nmr"][:, qslot:qslot + 1], stc[:, 12:13],
                                                               sb["ln2rstd"][:, qslot:qslot + 1], -1.0,
                                                               ALU.mult, ALU.mult), ["stat", "ln2rstd"], ["ln2nmr"])

    def b_quad(quad):
        bi = (4 * quad) // NTS
        bc1, bc1n = carve("PF", 12288, 1024)
        bc2, bc2n = carve("PF", 13312, 1024)
        _ = []
        bcast_tile(12288, bi, 4 * D, True, _)
        bcast_tile(13312, bi, 3 * D, False, _)

        stc = sb["stat"]
        for tq in range(4):
            x1 = sb["x1q"][:, D * tq:D * (tq + 1)]
            x1b = f"x1q{tq}"
            xn2, xn2n = carve("PF", 8192, 1024)
            tr.op("scalar", lambda E, x1=x1, tq=tq, xn2=xn2: E.activation(
                xn2, x1, AF.Identity, bias=sb["ln2nmr"][:, tq:tq + 1], scale=sb["ln2rstd"][:, tq:tq + 1]),
                [x1b, "ln2rstd", "ln2nmr"], xn2n)
            h2, h2n = carve("PF", 9216, 1024)
            tr.op("vector", lambda E, xn2=xn2, h2=h2, bc1=bc1: E.tensor_mul(h2, xn2, bc1), xn2n + bc1n, h2n)
            tr.op("vector", lambda E, h2=h2, bc2=bc2: E.tensor_add(h2, h2, bc2), h2n + bc2n, h2n)
            tr.op("vector", lambda E, h2=h2: E.reduce_max(stc[:, 0:1], h2, axis=AX.X, apply_absolute_value=True),
                  h2n, ["stat"])
            tr.op("vector", lambda E: E.tensor_single_scalar(stc[:, 1:2], stc[:, 0:1], 1e-5, ALU.max),
                  ["stat"], ["stat"])
            tr.op("vector", lambda E: E.reciprocal(stc[:, 2:3], stc[:, 1:2]), ["stat"], ["stat"])
            tr.op("vector", lambda E: E.tensor_scalar_mul(stc[:, 3:4], stc[:, 2:3], 127.0), ["stat"], ["stat"])
            tr.op("vector", lambda E: E.tensor_scalar_mul(stc[:, 4:5], stc[:, 1:2], 1.0 / 127.0), ["stat"], ["stat"])
            tr.op("vector", lambda E, tq=tq: E.tensor_mul(stc[:, 8 + tq:9 + tq], stc[:, 4:5], sb["invsw"][:, 0:1]),
                  ["stat", "invsw"], ["stat"])
            tr.op("vector", lambda E, tq=tq: E.tensor_mul(stc[:, 12 + tq:13 + tq], stc[:, 4:5], sb["invsw"][:, 1:2]),
                  ["stat", "invsw"], ["stat"])
            if DBG_A1 and quad == 0 and tq == 0:
                dma("cs", lambda E, h2=h2: E.dma_start(out=dr["dbg7"], in_=h2), r=h2n, w=[])
            q1, q1n = carve("PF", 0, 1024)
            tr.op("vector", lambda E, h2=h2, q1=q1: E.tensor_scalar_mul(q1, h2, stc[:, 3:4]),
                  h2n + ["stat"], q1n)
            rnd_nearest(q1, 1024, q1n, None)
            tr.op("vector", lambda E, q1=q1: E.tensor_scalar(q1, q1, -128.0, 127.0, ALU.max, ALU.min), q1n, q1n)
            if DBG_A1 and quad == 0 and tq == 0:
                dma("cs", lambda E, q1=q1: E.dma_start(out=dr["dbg8"], in_=q1), r=q1n, w=[])
            xqF, xqFn = carve("PBF", 32768 + 1024 * tq, 1024)
            for c in range(8):
                pd, pb = (ps["PD1"], "PD1") if c % 2 == 0 else (ps["PD2"], "PD2")
                tr.op("tensor", lambda E, c=c, pd=pd, q1=q1: E.transpose(pd[:, 0:TILE], q1[:, TILE * c:TILE * (c + 1)],
                                                                        sb["identf"][:]), q1n + ["identf"], [pb])
                tr.op("scalar", lambda E, c=c, pd=pd, xqF=xqF: E.activation(xqF[:, TILE * c:TILE * (c + 1)],
                                                                            pd[:, 0:TILE], AF.Copy), [pb], xqFn)

        for e8 in range(8):
            ks = e8 % 2
            tgv, tgvn = carve("PBF", 0 + 4096 * ks, 4096)
            tuv, tuvn = carve("PBF", 8192 + 4096 * ks, 4096)
            dma("ct", lambda E, e8=e8, tgv=tgv: E.dma_start(out=tgv.rearrange("p (c m) -> p c m", m=512), in_=dr["tg"][e8].transpose([1, 0, 2])),
                r=tgvn, w=tgvn)
            dma("ct", lambda E, e8=e8, tuv=tuv: E.dma_start(out=tuv.rearrange("p (c m) -> p c m", m=512), in_=dr["tu"][e8].transpose([1, 0, 2])),
                r=tuvn, w=tuvn)
            for tq in range(4):
                xqF, xqFn = carve("PBF", 32768 + 1024 * tq, 1024)
                par = (4 * e8 + tq) % 2
                pg, pgb = (ps["PA"], "PA") if par == 0 else (ps["PB"], "PB")
                for c in range(8):
                    tr.op("tensor", lambda E, c=c, tgv=tgv, xqF=xqF, pg=pg: E.matmul(
                        pg[:, 0:512], xqF[:, TILE * c:TILE * (c + 1)], tgv[:, 512 * c:512 * (c + 1)],
                        start=(c == 0), stop=(c == 7)), xqFn + tgvn, [pgb + "0"])
                    tr.op("tensor", lambda E, c=c, tuv=tuv, xqF=xqF, pg=pg: E.matmul(
                        pg[:, 512:1024], xqF[:, TILE * c:TILE * (c + 1)], tuv[:, 512 * c:512 * (c + 1)],
                        start=(c == 0), stop=(c == 7)), xqFn + tuvn, [pgb + "1"])
                sg, sgn = carve("PF", 8192 + 512 * par, 512)
                tr.op("scalar", lambda E, tq=tq, sg=sg, pg=pg: E.activation(sg, pg[:, 0:512], AF.Silu,
                                                                     scale=stc[:, 8 + tq:9 + tq]),
                      [pgb + "0", "stat"], sgn)
                a2v, a2vn = carve("PBF", 16384 + 4096 * tq + 512 * e8, 512)
                tr.op("vector", lambda E, tq=tq, a2v=a2v, sg=sg, pg=pg: E.scalar_tensor_tensor(
                    a2v, pg[:, 512:1024], stc[:, 12 + tq:13 + tq], sg, ALU.mult, ALU.mult),
                    [pgb + "1", "stat"] + sgn, a2vn)
                tr.op("vector", lambda E, tq=tq, e8=e8, a2v=a2v: E.reduce_max(
                    stc[:, 20 + tq:21 + tq] if e8 == 0 else stc[:, 24 + tq:25 + tq], a2v,
                    axis=AX.X, apply_absolute_value=True), a2vn, ["stat"])
                if e8 > 0:
                    tr.op("vector", lambda E, tq=tq: E.tensor_max(stc[:, 20 + tq:21 + tq], stc[:, 20 + tq:21 + tq],
                                                                  stc[:, 24 + tq:25 + tq]), ["stat"], ["stat"])

        if DBG_A1 and quad == 0:
            dma("cs", lambda E: E.dma_start(out=dr["dbg12"], in_=sb["stat"][:, 16:32]), r=["stat"], w=[])
            a2full, a2fn = carve("PBF", 16384, 4096)
            dma("cs", lambda E, a2full=a2full: E.dma_start(out=dr["dbg9"], in_=a2full),
                r=[f"a2_0"] + a2fn, w=[])
        for pair in range(2):
            for tq2 in range(2):
                tq = 2 * pair + tq2
                tr.op("vector", lambda E, tq=tq: E.tensor_single_scalar(stc[:, 1:2], stc[:, 20 + tq:21 + tq],
                                                                        1e-5, ALU.max), ["stat"], ["stat"])
                tr.op("vector", lambda E: E.reciprocal(stc[:, 2:3], stc[:, 1:2]), ["stat"], ["stat"])
                tr.op("vector", lambda E: E.tensor_scalar_mul(stc[:, 3:4], stc[:, 2:3], 127.0), ["stat"], ["stat"])
                tr.op("vector", lambda E: E.tensor_scalar_mul(stc[:, 4:5], stc[:, 1:2], 1.0 / 127.0), ["stat"], ["stat"])
                tr.op("vector", lambda E, tq=tq: E.tensor_mul(stc[:, 36 + tq:37 + tq], stc[:, 4:5], sb["invsw"][:, 2:3]),
                      ["stat", "invsw"], ["stat"])
                xq2F, xq2Fn = carve("PBF", 36864 + 4096 * tq2, 4096)
                for qtr in range(4):
                    a2v, a2vn = carve("PBF", 16384 + 4096 * tq + 1024 * qtr, 1024)
                    q1, q1n = carve("PF", 0, 1024)
                    tr.op("vector", lambda E, a2v=a2v, q1=q1: E.tensor_scalar_mul(q1, a2v, stc[:, 3:4]),
                          a2vn + ["stat"], q1n)
                    if DBG_A1 and quad == 0 and tq == 0:
                        dma("cs", lambda E, q1=q1, qtr=qtr: E.dma_start(
                            out=dr["dbg11"][:, 1024 * qtr:1024 * (qtr + 1)], in_=q1), r=q1n, w=[])
                    rnd_nearest(q1, 1024, q1n, None)
                    tr.op("vector", lambda E, q1=q1: E.tensor_scalar(q1, q1, -128.0, 127.0, ALU.max, ALU.min),
                          q1n, q1n)
                    if DBG_A1 and quad == 0 and tq == 0:
                        dma("cs", lambda E, q1=q1, qtr=qtr: E.dma_start(
                            out=dr["dbg10"][:, 1024 * qtr:1024 * (qtr + 1)], in_=q1), r=q1n, w=[])
                    for c in range(8):
                        pd, pb = (ps["PD1"], "PD1") if c % 2 == 0 else (ps["PD2"], "PD2")
                        tr.op("tensor", lambda E, c=c, pd=pd, q1=q1: E.transpose(
                            pd[:, 0:TILE], q1[:, TILE * c:TILE * (c + 1)], sb["identf"][:]), q1n + ["identf"], [pb])
                        tr.op("scalar", lambda E, c=c, qtr=qtr, pd=pd, xq2F=xq2F: E.activation(
                            xq2F[:, 1024 * qtr + TILE * c:1024 * qtr + TILE * (c + 1)], pd[:, 0:TILE], AF.Copy),
                            [pb], xq2Fn)
            for k in range(32):
                wv, wvn = carve("PBF", 45056 + 1024 * (k % 2), 1024)
                dma("ct", lambda E, k=k, wv=wv: E.dma_start(out=wv, in_=dr["td"][k]), r=wvn, w=wvn)
                for tq2 in range(2):
                    xq2F, xq2Fn = carve("PBF", 36864 + 4096 * tq2, 4096)
                    pn, pb = (ps["PA"], "PA") if tq2 == 0 else (ps["PB"], "PB")
                    for half in range(2):
                        tr.op("tensor", lambda E, k=k, half=half, pn=pn, xq2F=xq2F, wv=wv: E.matmul(
                            pn[:, 512 * half:512 * (half + 1)], xq2F[:, TILE * k:TILE * (k + 1)],
                            wv[:, 512 * half:512 * (half + 1)], start=(k == 0), stop=(k == 31)),
                            xq2Fn + wvn, [pb + str(half)])
            gb, gbn = carve("PF", 13312, 1024)
            _ = []
            bcast_tile(13312, bi, 5 * D, False, _)
            for tq2 in range(2):
                tq = 2 * pair + tq2
                gt = 4 * quad + tq
                bi3, st3 = gt // NTS, gt % NTS
                pn, pb = (ps["PA"], "PA") if tq2 == 0 else (ps["PB"], "PB")
                x1 = sb["x1q"][:, D * tq:D * (tq + 1)]
                x1b = f"x1q{tq}"
                tmp, tmpn = carve("PF", 8192, 1024)
                for half in range(2):
                    tr.op("vector", lambda E, pn=pn, half=half, tq=tq, tmp=tmp, gb=gb: E.scalar_tensor_tensor(
                        tmp[:, 512 * half:512 * (half + 1)], pn[:, 512 * half:512 * (half + 1)],
                        stc[:, 36 + tq:37 + tq], gb[:, 512 * half:512 * (half + 1)], ALU.mult, ALU.mult),
                        [pb + str(half), "stat"] + gbn, tmpn)
                tr.op("vector", lambda E, x1=x1, tmp=tmp: E.tensor_add(x1, x1, tmp), [x1b] + tmpn, [x1b])
                dma("co", lambda E, bi3=bi3, st3=st3, x1=x1: E.dma_start(
                    out=out_h[bi3, TILE * st3:TILE * (st3 + 1), :], in_=x1), r=[x1b], w=[])

    prep()
    load_wqkv(0)
    qgen = quant_units()
    qdone = False
    for gt in range(N_A1):
        if gt == NTS:
            load_wqkv(1)
        a1_tile(gt)
        for _ in range(7):
            try:
                next(qgen)()
            except StopIteration:
                qdone = True
                break
    while not qdone:
        try:
            next(qgen)()
        except StopIteration:
            qdone = True
    a2()
    for quad in range(N_QUAD):
        load_wo()
        bi = (4 * quad) // NTS
        _ = []
        bcast_tile(12288, bi, 2 * D, False, _)
        for tq in range(4):
            a3_tile(4 * quad + tq, tq)
        b_quad(quad)

    ops = tr.finalize()
    co_waits = [(k, v) for k, v in tr.dcnt.items() if k[1] == "co"]
    if co_waits:
        ops["sync"].append((sorted(co_waits), lambda E: E.nop(), None))
    nds = 0
    for k in tr.dcnt:
        if k not in sems:
            sems[k] = stack.enter_context(nc.semaphore(f"sd{nds}"))
            nds += 1
    assert nds + len(Tr.ENG) <= 100, f"too many semaphores: {nds}"

    with nc.Block() as blk0:
        def clear(E):
            for s in sems.values():
                E.sem_clear(s)
        blk0.sync(clear)

    with nc.Block() as blk:
        def mk(ename):
            myops = ops[ename]

            def run(E):
                for waits, fn, inc in myops:
                    for k, v in waits:
                        E.wait_ge(sems[k], v)
                    inst = fn(E)
                    if inc is not None:
                        inst.then_inc(sems[inc[0]], inc[1])
            return run
        blk.tensor(mk("tensor"))
        blk.vector(mk("vector"))
        blk.scalar(mk("scalar"))
        blk.gpsimd(mk("gpsimd"))
        blk.sync(mk("sync"))

    n_inst = sum(len(v) for v in ops.values())
    return nc, stack, n_inst


def make_inputs(full):
    tabs = host_tables()
    WQ, WK, WV = full["L0_attn_WQ"], full["L0_attn_WK"], full["L0_attn_WV"]
    wqkvT = np.concatenate([w.transpose(2, 0, 1).reshape(D, D) for w in (WQ, WK, WV)], axis=1)
    wqkv4 = np.ascontiguousarray(wqkvT.reshape(8, TILE, 3 * D))
    wo4 = np.ascontiguousarray(full["L0_attn_WO"].reshape(8, TILE, D))
    wmod4 = np.ascontiguousarray(full["L0_wmod"].reshape(8, TILE, 12, 512).transpose(2, 0, 1, 3))
    wg4 = np.ascontiguousarray(full["L0_ffn_w_gate"].T.reshape(8, TILE, INNER))
    wu4 = np.ascontiguousarray(full["L0_ffn_w_up"].T.reshape(8, TILE, INNER))
    wd4 = np.ascontiguousarray(full["L0_ffn_w_down"].T.reshape(32, TILE, D))
    gammaF = np.ascontiguousarray(full["L0_attn_gamma"].reshape(8, TILE).T)
    betaF = np.ascontiguousarray(full["L0_attn_beta"].reshape(8, TILE).T)
    x = full["L0_attn_x"]
    c = full["L0_c"]
    shared = dict(wqkv4=wqkv4, wo4=wo4, wmod4=wmod4, wg4=wg4, wu4=wu4, wd4=wd4,
                  gammaF=gammaF, betaF=betaF, ctbl=tabs["ctbl"], stbl=tabs["stbl"],
                  dmt=tabs["dmt"], identf=tabs["ident"])
    in_maps = []
    for core in range(8):
        cs = c[2 * core:2 * core + 2]
        cF = np.ascontiguousarray(cs.reshape(2, 8, TILE).transpose(2, 1, 0).reshape(TILE, 16))
        m = dict(shared)
        m["x"] = np.ascontiguousarray(x[2 * core:2 * core + 2])
        m["cF"] = cF
        in_maps.append(m)
    return in_maps


def kernel(**inputs):
    in_maps = make_inputs(inputs)
    nc, stack, n_inst = build()
    res = run_bass_kernel_spmd(nc, in_maps, core_ids=list(range(8)))
    outs = [res.results[i]["out"] for i in range(8)]
    full = np.concatenate(outs, axis=0).astype(np.float32)
    if SPILL_OUT:
        return full, res
    stack.close()
    return full




# version 09
# speedup vs baseline: 1.1248x; 1.0975x over previous
import contextlib
import numpy as np
import concourse.bass as bass
import concourse.mybir as mybir
from concourse.bass_utils import run_bass_kernel_spmd

F32 = mybir.dt.float32
BF16 = mybir.dt.bfloat16
I32 = mybir.dt.int32
AF = mybir.ActivationFunctionType
ALU = mybir.AluOpType
AX = mybir.AxisListType

S, D, H, DH = 2048, 1024, 16, 64
TILE = 128
NTS = S // TILE
NT = 2 * NTS
NQG = NTS // 4
INNER = 4096
EPS = 1e-5
DPAD = 3

N_A1 = NT
N_QG = NQG
N_QUAD = NT // 4
SPILL_OUT = False
DBG_A1 = False


def host_tables():
    import jax
    with jax.default_device(jax.devices("cpu")[0]):
        import jax.numpy as jnp
        expoent = jnp.arange(0, DH, 2, dtype=jnp.float32) / DH
        angle = 10.0 ** (-4.0 * expoent)
        pos = jnp.arange(0, S, dtype=jnp.float32)
        nangle = pos[:, None] * angle[None, :]
        sin = np.asarray(jnp.sin(nangle))
        cos = np.asarray(jnp.cos(nangle))
        scale = 1.0 - 2.0 ** (-5.0 - jnp.arange(H, dtype=jnp.float32))
        logs = jnp.log(scale)
        delta = jnp.arange(S, dtype=jnp.float32)
        logd = delta[None, :] * logs[:, None]
        d32 = np.asarray(jnp.exp(logd))

    ctbl = np.zeros((TILE, NTS * DH), np.float32)
    stbl = np.zeros((TILE, NTS * DH), np.float32)
    for st in range(NTS):
        c = cos[st * TILE:(st + 1) * TILE]
        s = sin[st * TILE:(st + 1) * TILE]
        ctbl[:, st * DH:(st + 1) * DH] = np.repeat(c, 2, axis=1)
        blk = np.empty((TILE, DH), np.float32)
        blk[:, 0::2] = -s
        blk[:, 1::2] = s
        stbl[:, st * DH:(st + 1) * DH] = blk

    ndt = DPAD + NTS
    dmt = np.zeros((H, TILE, ndt * TILE), np.float32)
    jj = np.arange(TILE)[:, None]
    ii = np.arange(TILE)[None, :]
    for t in range(ndt):
        dlt = (t - DPAD) * TILE + ii - jj
        valid = (dlt >= 0) & (dlt < S)
        dl = np.clip(dlt, 0, S - 1)
        for h in range(H):
            dmt[h, :, t * TILE:(t + 1) * TILE] = d32[h][dl] * valid
    ident = np.eye(TILE, dtype=np.float32)
    return dict(ctbl=ctbl, stbl=stbl, dmt=dmt, ident=ident)


class Tr:
    ENG = ("tensor", "vector", "scalar", "gpsimd", "sync")
    DMA_Q = {"cx": "sync", "ct": "sync", "ck": "scalar", "cw": "gpsimd", "cs": "gpsimd", "co": "gpsimd"}
    SELFWAIT = {"vector", "scalar", "gpsimd"}

    def __init__(self):
        self.recs = []
        self.lastw = {}
        self.readers = {}
        self.openg = []
        self.opkey = []

    PSPLIT = {"PA": ("PA0", "PA1"), "PB": ("PB0", "PB1"), "PC": ("PC0", "PC1")}

    def _norm(self, bufs):
        out = []
        for b in bufs:
            if b in self.PSPLIT:
                out.extend(self.PSPLIT[b])
            else:
                out.append(b)
        return out

    def op(self, eng, fn, reads=(), writes=(), dma=None):
        reads = self._norm(reads)
        writes = self._norm(writes)
        deps = set()
        myid = len(self.recs)

        def need(pid):
            if pid is None:
                return
            peng = self.openg[pid]
            pkey = self.opkey[pid]
            if pkey is None and peng == eng and eng not in self.SELFWAIT:
                return
            deps.add(pid)

        for b in reads:
            need(self.lastw.get(b))
        for b in writes:
            need(self.lastw.get(b))
            for r in self.readers.get(b, ()):
                need(r)
        key = None
        if dma is not None:
            assert self.DMA_Q[dma] == eng
            slot = tuple(sorted(writes)) if writes else ("st",) + tuple(sorted(reads))
            key = ("D", dma, slot)
        self.openg.append(eng)
        self.opkey.append(key)
        for b in writes:
            self.lastw[b] = myid
            self.readers[b] = []
        for b in reads:
            self.readers.setdefault(b, []).append(myid)
        self.recs.append((eng, fn, key, deps))

    def finalize(self):
        n = len(self.recs)
        needed = [False] * n
        for (_, _, key, deps) in self.recs:
            for d in deps:
                needed[d] = True
        val = [0] * n
        ecnt = {e: 0 for e in self.ENG}
        dcnt = {}
        for i, (eng, fn, key, deps) in enumerate(self.recs):
            if key is not None:
                dcnt[key] = dcnt.get(key, 0) + 16
                val[i] = dcnt[key]
            elif needed[i]:
                ecnt[eng] += 1
                val[i] = ecnt[eng]
        ops = {e: [] for e in self.ENG}
        waited = {e: {} for e in self.ENG}
        for i, (eng, fn, key, deps) in enumerate(self.recs):
            waits = {}
            for d in deps:
                dk = self.opkey[d]
                sk = dk if dk is not None else ("E", self.openg[d])
                v = val[d]
                if waited[eng].get(sk, 0) >= v:
                    continue
                if waits.get(sk, 0) < v:
                    waits[sk] = v
            for k2, v2 in waits.items():
                waited[eng][k2] = v2
            if key is not None:
                inc = (key, 16)
            elif needed[i]:
                inc = (("E", eng), 1)
            else:
                inc = None
            ops[eng].append((sorted(waits.items()), fn, inc))
        self.dcnt = dcnt
        return ops


def build():
    tabs = host_tables()
    nc = bass.Bass("TRN2", target_bir_lowering=False, detect_race_conditions=False)
    stack = contextlib.ExitStack()
    dr = {}

    def din(name, shape, dt=F32):
        dr[name] = nc.dram_tensor(name, list(shape), dt, kind="ExternalInput").ap()

    def dint(name, shape, dt):
        kind = "ExternalOutput" if SPILL_OUT else "Internal"
        dr[name] = nc.dram_tensor(name, list(shape), dt, kind=kind).ap()

    din("x", [2, S, D])
    dr["out"] = nc.dram_tensor("out", [2, S, D], F32, kind="ExternalOutput").ap()
    din("cF", [TILE, 16])
    din("wmod4", [12, 8, TILE, 512])
    din("wqkv4", [8, TILE, 3 * D])
    din("wo4", [8, TILE, D])
    din("gammaF", [TILE, 8])
    din("betaF", [TILE, 8])
    din("wg4", [8, TILE, INNER])
    din("wu4", [8, TILE, INNER])
    din("wd4", [32, TILE, D])
    din("ctbl", [TILE, NTS * DH])
    din("stbl", [TILE, NTS * DH])
    din("dmt", [H, TILE, (DPAD + NTS) * TILE])
    din("identf", [TILE, TILE])
    dint("sp_q", [NT, TILE, D], BF16)
    dint("sp_kv", [NT, 2, TILE, D], BF16)
    dint("sp_r", [NT, H, DH, TILE], F32)
    dint("modv", [2, 6 * D], F32)
    if DBG_A1:
        for i in (1, 2, 3):
            dint(f"dbg{i}", [TILE, D], F32)
        for i in (4, 5, 6, 7, 8):
            dint(f"dbg{i}", [TILE, D], F32)
        for i in (9, 10, 11):
            dint(f"dbg{i}", [TILE, INNER], F32)
        dint("dbg12", [TILE, 16], F32)
    dint("tg", [8, 8, TILE, 512], BF16)
    dint("tu", [8, 8, TILE, 512], BF16)
    dint("td", [32, TILE, D], BF16)

    sb = {}

    def sbuf(name, shape, dt):
        sb[name] = stack.enter_context(nc.sbuf_tensor("sb_" + name, list(shape), dt))

    sbuf("identf", [TILE, TILE], F32)
    sbuf("ctbl", [TILE, NTS * DH], F32)
    sbuf("stbl", [TILE, NTS * DH], F32)
    sbuf("ones_bf", [1, 512], BF16)
    sbuf("onesf_col", [TILE, 1], F32)
    sbuf("onesf_row", [1, TILE], F32)
    sbuf("cF", [TILE, 16], F32)
    sbuf("silu_cF", [TILE, 16], F32)
    sbuf("gammaF", [TILE, 8], F32)
    sbuf("betaF_bf", [TILE, 8], BF16)
    sbuf("modF", [TILE, 32], F32)
    sbuf("saF_bf", [TILE, 16], BF16)
    sbuf("caF1p", [TILE, 16], F32)
    sbuf("saw_bf", [1, 3 * D], BF16)
    sbuf("bwo_bf", [1, D], BF16)
    sbuf("scw", [TILE, 3], F32)
    sbuf("invsw", [TILE, 3], F32)
    sbuf("accw", [TILE, 48], F32)
    sbuf("stat", [TILE, 96], F32)
    sbuf("ln2rstd", [TILE, 4], F32)
    sbuf("ln2nmr", [TILE, 4], F32)
    sbuf("epsc", [TILE, 1], F32)
    sbuf("x1q", [TILE, 4 * D], F32)
    sbuf("PF", [TILE, 16384], F32)
    sbuf("PBF", [TILE, 47104], BF16)
    sbuf("PI", [TILE, 1024], I32)

    SEG = {"PF": 512, "PBF": 1024, "PI": 512}

    def carve(pool, off, n, p0=0, p1=TILE):
        ap = sb[pool][p0:p1, off:off + n]
        s0, s1 = off // SEG[pool], (off + n - 1) // SEG[pool]
        names = [f"{pool}:{i}" for i in range(s0, s1 + 1)]
        return ap, names

    ps = {}
    for n, shp in (("PA", [TILE, 1024]), ("PB", [TILE, 1024]), ("PC", [TILE, 1024]),
                   ("PD1", [TILE, 512]), ("PD2", [TILE, 512])):
        ps[n] = stack.enter_context(nc.psum_tensor(n, shp, F32))

    tr = Tr()
    sems = {}
    for e in Tr.ENG:
        sems[("E", e)] = stack.enter_context(nc.semaphore("se_" + e))

    x_h, out_h = dr["x"], dr["out"]

    def dma(chan, fn, r=(), w=()):
        tr.op(Tr.DMA_Q[chan], fn, r, w, dma=chan)

    def rnd_nearest(src_ap, src_n, w_slices, rbufs):
        n = src_n
        for lo in range(0, n, 1024):
            wdt = min(1024, n - lo)
            sl = src_ap[:, lo:lo + wdt]
            ri, rin = carve("PI", 0, wdt)
            dv, dvn = carve("PF", FD_RND, wdt)
            bv, bvn = carve("PF", FD_RND + 1024, wdt)
            tr.op("vector", lambda E, sl=sl, ri=ri: E.tensor_copy(ri, sl), w_slices, rin)
            tr.op("vector", lambda E, sl=sl, ri=ri, dv=dv: E.tensor_sub(dv, sl, ri), list(w_slices) + rin, dvn)
            tr.op("vector", lambda E, dv=dv, bv=bv: E.tensor_single_scalar(bv, dv, -0.5, ALU.is_lt), dvn, bvn)
            tr.op("vector", lambda E, sl=sl, dv=dv, ri=ri: E.scalar_tensor_tensor(sl, dv, 0.5, ri, ALU.is_gt, ALU.add),
                  dvn + rin, w_slices)
            tr.op("vector", lambda E, sl=sl, bv=bv: E.tensor_sub(sl, sl, bv), list(w_slices) + bvn, w_slices)

    FD_RND = 14336

    def prep():
        for nm in ("identf", "ctbl", "stbl", "cF", "gammaF"):
            dma("cx", lambda E, nm=nm: E.dma_start(out=sb[nm][:], in_=dr[nm]), w=[nm])
        dma("cw", lambda E: E.dma_start(out=sb["betaF_bf"][:], in_=dr["betaF"]), w=["betaF_bf"])
        tr.op("vector", lambda E: E.memset(sb["ones_bf"][:], 1.0), (), ["ones_bf"])
        tr.op("vector", lambda E: E.memset(sb["onesf_col"][:], 1.0), (), ["onesf_col"])
        tr.op("vector", lambda E: E.memset(sb["onesf_row"][:], 1.0), (), ["onesf_row"])
        tr.op("vector", lambda E: E.memset(sb["epsc"][:], EPS), (), ["epsc"])
        tr.op("scalar", lambda E: E.activation(sb["silu_cF"][:], sb["cF"][:], AF.Silu), ["cF"], ["silu_cF"])

        def mtmp(lo, n):
            if lo < 4096:
                assert lo + n <= 4096
                return carve("PF", lo, n, 0, 2)
            return carve("PF", 4096 + (lo - 4096), n, 0, 2)

        wmb, wmbn = carve("PF", 10240, 4096)
        for n in range(12):
            pn, pb = (ps["PA"], "PA") if n % 2 == 0 else (ps["PB"], "PB")
            dma("ct", lambda E, n=n, wmb=wmb: E.dma_start(out=wmb.rearrange("p (c m) -> p c m", m=512), in_=dr["wmod4"][n].transpose([1, 0, 2])),
                r=wmbn, w=wmbn)
            for c in range(8):
                tr.op("tensor", lambda E, c=c, pn=pn, wmb=wmb: E.matmul(
                    pn[0:2, 0:512], sb["silu_cF"][:, 2 * c:2 * c + 2], wmb[:, 512 * c:512 * (c + 1)],
                    start=(c == 0), stop=(c == 7)), ["silu_cF"] + wmbn, [pb])
            mt, mtn = mtmp(512 * n, 512)
            tr.op("vector", lambda E, pn=pn, mt=mt: E.tensor_copy(mt, pn[0:2, 0:512]), [pb], mtn)

        ct, ctn = carve("PF", 8192, 1024, 0, 2)
        cm, cmn = carve("PF", 9216, 1024, 0, 2)
        for lo in (2048, 5120):
            mt, mtn = mtmp(lo, 1024)
            tr.op("vector", lambda E, mt=mt, ct=ct: E.tensor_single_scalar(ct, mt, 0.0, ALU.is_ge), mtn, ctn)
            tr.op("vector", lambda E, ct=ct: E.tensor_scalar(ct, ct, 2.0, -1.0, ALU.mult, ALU.add), ctn, ctn)
            tr.op("scalar", lambda E, mt=mt, cm=cm: E.activation(cm, mt, AF.Abs), mtn, cmn)
            tr.op("vector", lambda E, cm=cm: E.tensor_single_scalar(cm, cm, 0.25, ALU.max), cmn, cmn)
            tr.op("vector", lambda E, mt=mt, ct=ct, cm=cm: E.tensor_mul(mt, ct, cm), ctn + cmn, mtn)

        for vi, base in ((0, 0), (1, 1024)):
            for c in range(8):
                mt, mtn = mtmp(base + TILE * c, TILE)
                tr.op("tensor", lambda E, mt=mt: E.transpose(ps["PC"][:, 0:2], mt, sb["identf"][0:2, 0:2]),
                      mtn + ["identf"], ["PC"])
                tr.op("vector", lambda E, vi=vi, c=c: E.tensor_copy(sb["modF"][:, 16 * vi + 2 * c:16 * vi + 2 * c + 2],
                                                                    ps["PC"][:, 0:2]), ["PC"], ["modF"])
        tr.op("vector", lambda E: E.tensor_copy(sb["saF_bf"][:], sb["modF"][:, 0:16]), ["modF"], ["saF_bf"])
        tr.op("vector", lambda E: E.tensor_scalar_add(sb["caF1p"][:], sb["modF"][:, 16:32], 1.0), ["modF"], ["caF1p"])

        for bi in range(2):
            for seg in range(3):
                mt, mtn = mtmp(2048 * seg, 2048)
                dma("cs", lambda E, bi=bi, seg=seg, mt=mt: E.dma_start(
                    out=dr["modv"][bi:bi + 1, 2048 * seg:2048 * (seg + 1)], in_=mt[bi:bi + 1, :]), r=mtn, w=["modv"])

        pass

    def quant_units():
        nch = {"g": 8, "u": 8, "d": 32}
        wsrc = {"g": "wg4", "u": "wu4", "d": "wd4"}
        wid = {"g": 0, "u": 1, "d": 2}
        ccol = {"g": 0, "u": 8, "d": 16}
        nsl = {"g": 4, "u": 4, "d": 1}
        wcs, wcsn = carve("PF", 12288, 1024)
        ab2, ab2n = carve("PF", 13312, 1024)
        qtb, qtbn = carve("PBF", 28672, 1024)

        for m in ("g", "u", "d"):
            for c in range(nch[m]):
                for sl in range(nsl[m]):
                    def u1(m=m, c=c, sl=sl):
                        dma("ct", lambda E: E.dma_start(out=wcs, in_=dr[wsrc[m]][c][:, 1024 * sl:1024 * (sl + 1)]),
                            r=wcsn, w=wcsn)
                        tr.op("scalar", lambda E: E.activation(ab2, wcs, AF.Abs), wcsn, ab2n)
                        col = 32 + sl
                        tr.op("vector", lambda E, col=col: E.reduce_sum(sb["accw"][:, col:col + 1], ab2, axis=AX.X),
                              ab2n, ["accw"])
                        if sl == nsl[m] - 1:
                            if nsl[m] > 1:
                                tr.op("vector", lambda E: E.tensor_reduce(
                                    sb["accw"][:, ccol[m] + c:ccol[m] + c + 1],
                                    sb["accw"][:, 32:32 + nsl[m]], AX.X, ALU.add), ["accw"], ["accw"])
                            else:
                                tr.op("vector", lambda E: E.tensor_copy(
                                    sb["accw"][:, ccol[m] + c:ccol[m] + c + 1], sb["accw"][:, 32:33]),
                                    ["accw"], ["accw"])
                    yield u1

        for m in ("g", "u", "d"):
            def us(m=m):
                tr.op("tensor", lambda E: E.matmul(ps["PD1"][0:1, 0:nch[m]], sb["onesf_col"][:],
                                                   sb["accw"][:, ccol[m]:ccol[m] + nch[m]], start=True, stop=True),
                      ["onesf_col", "accw"], ["PD1"])
                tr.op("vector", lambda E: E.reduce_sum(sb["stat"][0:1, 32:33], ps["PD1"][0:1, 0:nch[m]], axis=AX.X),
                      ["PD1"], ["stat"])
                tr.op("vector", lambda E: E.tensor_scalar(sb["stat"][0:1, 33:34], sb["stat"][0:1, 32:33],
                                                          1.0 / (INNER * D), 1e-5, ALU.mult, ALU.max),
                      ["stat"], ["stat"])
                tr.op("vector", lambda E: E.reciprocal(sb["stat"][0:1, 34:35], sb["stat"][0:1, 33:34]),
                      ["stat"], ["stat"])
                for (srcc, dst) in ((34, "scw"), (33, "invsw")):
                    tr.op("tensor", lambda E, srcc=srcc: E.matmul(ps["PD2"][:, 0:1], sb["onesf_row"][:],
                                                                  sb["stat"][0:1, srcc:srcc + 1], start=True, stop=True),
                          ["stat", "onesf_row"], ["PD2"])
                    tr.op("vector", lambda E, dst=dst: E.tensor_copy(sb[dst][:, wid[m]:wid[m] + 1], ps["PD2"][:, 0:1]),
                          ["PD2"], [dst])
            yield us

        for m in ("g", "u", "d"):
            for c in range(nch[m]):
                for sl in range(nsl[m]):
                    def u2(m=m, c=c, sl=sl):
                        dma("ct", lambda E: E.dma_start(out=wcs, in_=dr[wsrc[m]][c][:, 1024 * sl:1024 * (sl + 1)]),
                            r=wcsn, w=wcsn)
                        tr.op("vector", lambda E: E.tensor_scalar(wcs, wcs, sb["scw"][:, wid[m]:wid[m] + 1],
                                                                  12582912.0, ALU.mult, ALU.add),
                              wcsn + ["scw"], wcsn)
                        tr.op("vector", lambda E: E.tensor_scalar_add(wcs, wcs, -12582912.0), wcsn, wcsn)
                        tr.op("vector", lambda E: E.tensor_scalar(qtb, wcs, -1.0, 1.0, ALU.max, ALU.min),
                              wcsn, qtbn)
                        if m == "d":
                            dma("cs", lambda E: E.dma_start(out=dr["td"][c], in_=qtb), r=qtbn, w=[])
                        else:
                            tgt = "tg" if m == "g" else "tu"
                            for e2 in range(2):
                                e8 = 2 * sl + e2
                                dma("cs", lambda E, e8=e8, e2=e2, tgt=tgt: E.dma_start(
                                    out=dr[tgt][e8, c], in_=qtb[:, 512 * e2:512 * (e2 + 1)]), r=qtbn, w=[])
                    yield u2

    def load_wqkv(bi):
        for c in range(8):
            wv, wvn = carve("PBF", 3 * D * c, 3 * D)
            dma("cw", lambda E, c=c, wv=wv: E.dma_start(out=wv, in_=dr["wqkv4"][c]), r=wvn, w=wvn)
        pmap = {0: ("PA", ps["PA"]), 1: ("PB", ps["PB"]), 2: ("PC", ps["PC"])}
        for p in range(3):
            pb, pn = pmap[p]
            for half in range(2):
                for c in range(8):
                    wv, wvn = carve("PBF", 3 * D * c, 3 * D)
                    tr.op("tensor", lambda E, p=p, half=half, c=c, pn=pn, wv=wv: E.matmul(
                        pn[0:1, 512 * half:512 * (half + 1)], sb["saF_bf"][:, 2 * c + bi:2 * c + bi + 1],
                        wv[:, D * p + 512 * half:D * p + 512 * (half + 1)], start=(c == 0), stop=(c == 7)),
                        ["saF_bf"] + wvn, [pb])
                tr.op("vector", lambda E, p=p, half=half, pn=pn: E.tensor_copy(
                    sb["saw_bf"][0:1, D * p + 512 * half:D * p + 512 * (half + 1)],
                    pn[0:1, 512 * half:512 * (half + 1)]), [pb], ["saw_bf"])
        for c in range(8):
            wv, wvn = carve("PBF", 3 * D * c, 3 * D)
            tr.op("vector", lambda E, c=c, wv=wv: E.tensor_scalar_mul(wv, wv, sb["caF1p"][:, 2 * c + bi:2 * c + bi + 1]),
                  wvn + ["caF1p"], wvn)

    def load_wo():
        for c in range(8):
            wv, wvn = carve("PBF", D * c, D)
            dma("cw", lambda E, c=c, wv=wv: E.dma_start(out=wv, in_=dr["wo4"][c]), r=wvn, w=wvn)
        for half in range(2):
            for c in range(8):
                wv, wvn = carve("PBF", D * c, D)
                tr.op("tensor", lambda E, half=half, c=c, wv=wv: E.matmul(
                    ps["PD1"][0:1, 0:512], sb["betaF_bf"][:, c:c + 1],
                    wv[:, 512 * half:512 * (half + 1)], start=(c == 0), stop=(c == 7)),
                    ["betaF_bf"] + wvn, ["PD1"])
            tr.op("vector", lambda E, half=half: E.tensor_copy(sb["bwo_bf"][0:1, 512 * half:512 * (half + 1)],
                                                               ps["PD1"][0:1, 0:512]), ["PD1"], ["bwo_bf"])
        for c in range(8):
            wv, wvn = carve("PBF", D * c, D)
            tr.op("vector", lambda E, c=c, wv=wv: E.tensor_scalar_mul(wv, wv, sb["gammaF"][:, c:c + 1]),
                  wvn + ["gammaF"], wvn)

    def a1_tile(gt):
        bi, st = gt // NTS, gt % NTS
        xv, xvn = carve("PF", 10240 + 1024 * (gt % 2), 1024)
        dma("cx", lambda E, bi=bi, st=st, xv=xv: E.dma_start(out=xv, in_=x_h[bi, TILE * st:TILE * (st + 1), :]),
            r=xvn, w=xvn)
        stc = sb["stat"]
        tr.op("vector", lambda E, xv=xv: E.bn_stats(stc[:, 0:6], xv[:, 0:512]), xvn, ["stat"])
        tr.op("vector", lambda E, xv=xv: E.bn_stats(stc[:, 6:12], xv[:, 512:1024]), xvn, ["stat"])
        tr.op("vector", lambda E: E.bn_aggr(stc[:, 12:14], stc[:, 0:12]), ["stat"], ["stat"])
        tr.op("scalar", lambda E: E.activation(stc[:, 14:15], stc[:, 13:14], AF.Ln, bias=sb["epsc"][:]), ["stat", "epsc"], ["stat"])
        tr.op("scalar", lambda E: E.activation(stc[:, 15:16], stc[:, 14:15], AF.Exp, scale=-0.5), ["stat"], ["stat"])
        tr.op("vector", lambda E: E.tensor_scalar(stc[:, 16:17], stc[:, 12:13], stc[:, 15:16], -1.0, ALU.mult, ALU.mult),
              ["stat"], ["stat"])
        xn1, xn1n = carve("PF", 8192, 1024)
        tr.op("scalar", lambda E, xv=xv, xn1=xn1: E.activation(xn1, xv, AF.Identity,
                                                               bias=stc[:, 16:17], scale=stc[:, 15:16]),
              xvn + ["stat"], xn1n)
        xn1F, xn1Fn = carve("PBF", 24576, 1024)
        for c in range(8):
            pd, pb = (ps["PD1"], "PD1") if c % 2 == 0 else (ps["PD2"], "PD2")
            tr.op("tensor", lambda E, c=c, pd=pd, xn1=xn1: E.transpose(pd[:, 0:TILE], xn1[:, TILE * c:TILE * (c + 1)],
                                                                      sb["identf"][:]), xn1n + ["identf"], [pb])
            tr.op("scalar", lambda E, c=c, pd=pd, xn1F=xn1F: E.activation(xn1F[:, TILE * c:TILE * (c + 1)],
                                                                          pd[:, 0:TILE], AF.Copy), [pb], xn1Fn)
        pmap = {0: ("PA", ps["PA"]), 1: ("PB", ps["PB"]), 2: ("PC", ps["PC"])}
        for p in range(3):
            pb, pn = pmap[p]
            for half in range(2):
                for c in range(8):
                    wv, wvn = carve("PBF", 3 * D * c, 3 * D)
                    tr.op("tensor", lambda E, p=p, half=half, c=c, pn=pn, wv=wv, xn1F=xn1F: E.matmul(
                        pn[:, 512 * half:512 * (half + 1)], xn1F[:, TILE * c:TILE * (c + 1)],
                        wv[:, D * p + 512 * half:D * p + 512 * (half + 1)], start=(c == 0), stop=False),
                        xn1Fn + wvn, [pb])
                tr.op("tensor", lambda E, p=p, half=half, pn=pn: E.matmul(
                    pn[:, 512 * half:512 * (half + 1)], sb["ones_bf"][0:1, 0:TILE],
                    sb["saw_bf"][0:1, D * p + 512 * half:D * p + 512 * (half + 1)], start=False, stop=True),
                    ["ones_bf", "saw_bf"], [pb])
        if DBG_A1 and gt == 0:
            dma("cs", lambda E, xn1=xn1: E.dma_start(out=dr["dbg1"], in_=xn1), r=xn1n, w=[])
            for pbx, pnx, tag in (("PA", ps["PA"], "dbg2"), ("PB", ps["PB"], "dbg3")):
                tmpd, tmpdn = carve("PF", 2048, 1024)
                tr.op("vector", lambda E, pnx=pnx, tmpd=tmpd: E.tensor_copy(tmpd, pnx[:]), [pbx], tmpdn)
                dma("cs", lambda E, tmpd=tmpd, tag=tag: E.dma_start(out=dr[tag], in_=tmpd), r=tmpdn, w=[])
        for pb, pn, base in (("PA", ps["PA"], 0), ("PB", ps["PB"], 4096)):
            raw, rawn = carve("PF", base + 2048, 1024)
            tr.op("vector", lambda E, pn=pn, raw=raw: E.tensor_copy(raw, pn[:]), [pb], rawn)
            rot, rotn = carve("PF", base, 1024)
            tmp, tmpn = carve("PF", base + 1024, 1024)
            cs = sb["ctbl"][:, DH * st:DH * (st + 1)]
            sn = sb["stbl"][:, DH * st:DH * (st + 1)]
            p3 = raw.rearrange("p (h u) -> p h u", u=DH)
            t3 = tmp.rearrange("p (h u) -> p h u", u=DH)
            d3 = rot.rearrange("p (h u) -> p h u", u=DH)
            sne = sn[:, 0::2].unsqueeze(1).broadcast_to([TILE, H, DH // 2])
            sno = sn[:, 1::2].unsqueeze(1).broadcast_to([TILE, H, DH // 2])
            csf = cs.unsqueeze(1).broadcast_to([TILE, H, DH])
            tr.op("vector", lambda E, t3=t3, sne=sne, p3=p3: E.tensor_mul(t3[:, :, 0::2], sne, p3[:, :, 1::2]),
                  rawn + ["stbl"], tmpn)
            tr.op("vector", lambda E, t3=t3, sno=sno, p3=p3: E.tensor_mul(t3[:, :, 1::2], sno, p3[:, :, 0::2]),
                  rawn + ["stbl"], tmpn)
            tr.op("vector", lambda E, d3=d3, csf=csf, p3=p3: E.tensor_mul(d3, csf, p3), rawn + ["ctbl"], rotn)
            tr.op("vector", lambda E, rot=rot, tmp=tmp: E.tensor_add(rot, rot, tmp), rotn + tmpn, rotn)
        vbf, vbfn = carve("PBF", 25600, 1024)
        tr.op("vector", lambda E, vbf=vbf: E.tensor_copy(vbf, ps["PC"][:]), ["PC"], vbfn)
        dma("cs", lambda E, gt=gt, vbf=vbf: E.dma_start(out=dr["sp_kv"][gt, 1], in_=vbf), r=vbfn, w=["sp_kv"])
        for base, dsto, tgt in ((0, 26624, "sp_q"), (4096, 27648, "sp_kv")):
            rot, rotn = carve("PF", base, 1024)
            dst, dstn = carve("PBF", dsto, 1024)
            for c in range(8):
                pd, pb2 = (ps["PD1"], "PD1") if c % 2 == 0 else (ps["PD2"], "PD2")
                tr.op("tensor", lambda E, c=c, pd=pd, rot=rot: E.transpose(pd[:, 0:TILE], rot[:, TILE * c:TILE * (c + 1)],
                                                                           sb["identf"][:]), rotn + ["identf"], [pb2])
                tr.op("scalar", lambda E, c=c, pd=pd, dst=dst: E.activation(dst[:, TILE * c:TILE * (c + 1)],
                                                                            pd[:, 0:TILE], AF.Copy), [pb2], dstn)
            if tgt == "sp_q":
                dma("cs", lambda E, gt=gt, dst=dst: E.dma_start(out=dr["sp_q"][gt], in_=dst), r=dstn, w=["sp_q"])
            else:
                dma("cs", lambda E, gt=gt, dst=dst: E.dma_start(out=dr["sp_kv"][gt, 0], in_=dst), r=dstn, w=["sp_kv"])

    def a2():
        for qg in range(N_QG):
            ntj = 4 * qg + 4
            qfg, qfgn = carve("PBF", 0, 8192)
            for si in range(2):
                for tq in range(4):
                    gt = si * NTS + 4 * qg + tq
                    if gt >= N_A1:
                        continue
                    sl = qfg[:, 1024 * (4 * si + tq):1024 * (4 * si + tq + 1)]
                    dma("ct", lambda E, gt=gt, sl=sl: E.dma_start(out=sl, in_=dr["sp_q"][gt]),
                        r=["sp_q"] + qfgn, w=qfgn)
            wcols = TILE * (4 * qg + 4 + DPAD)
            for hp in range(H // 2):
                h0, h1 = 2 * hp, 2 * hp + 1
                ds = 2 * (hp % 2)
                dmv0, dmv0n = carve("PF", 2432 * ds, 2432)
                dmv1, dmv1n = carve("PF", 2432 * (ds + 1), 2432)
                dma("ct", lambda E, h0=h0, wcols=wcols, dmv0=dmv0: E.dma_start(
                    out=dmv0[:, 0:wcols], in_=dr["dmt"][h0][:, 0:wcols]), r=dmv0n, w=dmv0n)
                dma("ct", lambda E, h1=h1, wcols=wcols, dmv1=dmv1: E.dma_start(
                    out=dmv1[:, 0:wcols], in_=dr["dmt"][h1][:, 0:wcols]), r=dmv1n, w=dmv1n)
                for si in range(2):
                    if si * NTS + 4 * qg >= N_A1:
                        continue
                    rp0 = (ps["PD1"] if si == 0 else ps["PD2"])[0:DH, 0:512]
                    rp0b = "PD1" if si == 0 else "PD2"
                    rp1 = ps["PC"][0:DH, 0:512] if si == 0 else ps["PC"][0:DH, 512:1024]
                    rp1b = "PC0" if si == 0 else "PC1"
                    for tj in range(ntj):
                        gt = si * NTS + tj
                        ks = tj % 4
                        kvv, kvvn = carve("PBF", 8192 + 2048 * ks, 2048)
                        kv, vv = kvv[:, 0:1024], kvv[:, 1024:2048]
                        kvn = vvn = kvvn
                        dma("ck", lambda E, gt=gt, kvv=kvv: E.dma_start(
                            out=kvv.rearrange("p (a m) -> p a m", m=1024), in_=dr["sp_kv"][gt].transpose([1, 0, 2])),
                            r=["sp_kv"] + kvvn, w=kvvn)
                        off = TILE * (4 * qg - tj + DPAD)
                        for hi, (h, dmv, dmvn, rp, rpb) in enumerate(
                                ((h0, dmv0, dmv0n, rp0, rp0b), (h1, dmv1, dmv1n, rp1, rp1b))):
                            kp = tj % 2
                            sc = (ps["PA"] if hi == 0 else ps["PB"])[:, 512 * kp:512 * (kp + 1)]
                            scb = ("PA" if hi == 0 else "PB") + str(kp)
                            qrhs = qfg.rearrange("p (t m) -> p t m", m=1024)[
                                64 * (h % 2):64 * (h % 2) + 64, 4 * si:4 * si + 4,
                                TILE * (h // 2):TILE * (h // 2 + 1)]
                            tr.op("tensor", lambda E, h=h, kv=kv, sc=sc, qrhs=qrhs: E.matmul(
                                sc, kv[64 * (h % 2):64 * (h % 2) + 64, TILE * (h // 2):TILE * (h // 2 + 1)],
                                qrhs, start=True, stop=True), kvn + qfgn, [scb])
                            smv, smvn = carve("PBF", 16384 + 512 * (2 * (tj % 2) + hi), 512)
                            tr.op("vector", lambda E, sc=sc, smv=smv, dmv=dmv, off=off: E.tensor_mul(
                                smv, sc, dmv[:, off:off + 512]), [scb] + dmvn, smvn)
                            tr.op("tensor", lambda E, h=h, vv=vv, rp=rp, smv=smv, tj=tj, ntj=ntj: E.matmul(
                                rp, vv[:, DH * h:DH * (h + 1)], smv,
                                start=(tj == 0), stop=(tj == ntj - 1)), vvn + smvn, [rpb])
                    for hi, (h, rp, rpb) in enumerate(((h0, rp0, rp0b), (h1, rp1, rp1b))):
                        rfv, rfvn = carve("PF", 14336 + 512 * (2 * si + hi), 512, 0, DH)
                        tr.op("vector", lambda E, rp=rp, rfv=rfv: E.tensor_copy(rfv, rp), [rpb], rfvn)
                        for tq in range(4):
                            gt = si * NTS + 4 * qg + tq
                            if gt >= N_A1:
                                continue
                            dma("cs", lambda E, gt=gt, h=h, rfv=rfv, tq=tq: E.dma_start(
                                out=dr["sp_r"][gt, h], in_=rfv[:, TILE * tq:TILE * (tq + 1)]), r=rfvn, w=["sp_r"])

    def bcast_tile(dst_off, src_b, src_lo, plus1, dstn_out):
        dst, dstn = carve("PF", dst_off, 1024)
        dma("ct", lambda E, dst=dst, src_b=src_b, src_lo=src_lo: E.dma_start(
            out=dst, in_=dr["modv"][src_b:src_b + 1, src_lo:src_lo + 1024].partition_broadcast(TILE).squeeze(1)),
            r=["modv"] + dstn, w=dstn)
        if plus1:
            tr.op("vector", lambda E, dst=dst: E.tensor_scalar_add(dst, dst, 1.0), dstn, dstn)
        dstn_out.extend(dstn)

    def a3_tile(gt, qslot):
        bi, st = gt // NTS, gt % NTS
        stc = sb["stat"]
        rfl, rfln = carve("PF", 4096 + 2048 * (gt % 2), 2048, 0, DH)
        dma("cx", lambda E, gt=gt, rfl=rfl: E.dma_start(out=rfl.rearrange("d (h t) -> d h t", t=128), in_=dr["sp_r"][gt].transpose([1, 0, 2])),
            r=["sp_r"] + rfln, w=rfln)
        xv, xvn = carve("PF", 10240 + 1024 * (gt % 2), 1024)
        dma("cx", lambda E, bi=bi, st=st, xv=xv: E.dma_start(out=xv, in_=x_h[bi, TILE * st:TILE * (st + 1), :]),
            r=xvn, w=xvn)
        rt, rtn = carve("PF", 0, 1024)
        for h in range(H):
            pd, pb = (ps["PD1"], "PD1") if h % 2 == 0 else (ps["PD2"], "PD2")
            tr.op("tensor", lambda E, h=h, pd=pd, rfl=rfl: E.transpose(
                pd[:, 0:DH], rfl[0:DH, TILE * h:TILE * (h + 1)], sb["identf"][0:DH, 0:DH]),
                rfln + ["identf"], [pb])
            tr.op("vector", lambda E, h=h, pd=pd, rt=rt: E.tensor_copy(rt[:, DH * h:DH * (h + 1)], pd[:, 0:DH]),
                  [pb], rtn)
        rt3 = rt.rearrange("p (h u) -> p h u", u=DH)
        tr.op("vector", lambda E, rt3=rt3: E.tensor_reduce(stc[:, 0:16], rt3, AX.X, ALU.add), rtn, ["stat"])
        sq, sqn = carve("PF", 8192, 1024)
        tr.op("scalar", lambda E, rt=rt, sq=sq: E.activation(sq, rt, AF.Square), rtn, sqn)
        tr.op("vector", lambda E, sq=sq: E.tensor_reduce(stc[:, 16:32], sq.rearrange("p (h u) -> p h u", u=DH),
                                                         AX.X, ALU.add), sqn, ["stat"])
        tr.op("vector", lambda E: E.tensor_scalar_mul(stc[:, 0:16], stc[:, 0:16], 1.0 / DH), ["stat"], ["stat"])
        tr.op("vector", lambda E: E.tensor_scalar_mul(stc[:, 16:32], stc[:, 16:32], 1.0 / DH), ["stat"], ["stat"])
        tr.op("vector", lambda E: E.tensor_mul(stc[:, 32:48], stc[:, 0:16], stc[:, 0:16]), ["stat"], ["stat"])
        tr.op("vector", lambda E: E.tensor_sub(stc[:, 16:32], stc[:, 16:32], stc[:, 32:48]), ["stat"], ["stat"])
        tr.op("scalar", lambda E: E.activation(stc[:, 16:32], stc[:, 16:32], AF.Ln, bias=sb["epsc"][:]), ["stat", "epsc"], ["stat"])
        tr.op("scalar", lambda E: E.activation(stc[:, 16:32], stc[:, 16:32], AF.Exp, scale=-0.5), ["stat"], ["stat"])
        if DBG_A1 and gt == 0:
            dma("cs", lambda E, rt=rt: E.dma_start(out=dr["dbg4"], in_=rt), r=rtn, w=[])
        mu_rep = stc[:, 0:16].unsqueeze(2).broadcast_to([TILE, 16, DH])
        rs_rep = stc[:, 16:32].unsqueeze(2).broadcast_to([TILE, 16, DH])
        gn, gnn = carve("PF", 9216, 1024)
        gn3 = gn.rearrange("p (h u) -> p h u", u=DH)
        tr.op("vector", lambda E, gn3=gn3, rt3=rt3, mu_rep=mu_rep: E.tensor_sub(gn3, rt3, mu_rep),
              rtn + ["stat"], gnn)
        tr.op("vector", lambda E, gn3=gn3, rs_rep=rs_rep: E.tensor_mul(gn3, gn3, rs_rep), gnn + ["stat"], gnn)
        if DBG_A1 and gt == 0:
            dma("cs", lambda E, gn=gn: E.dma_start(out=dr["dbg5"], in_=gn), r=gnn, w=[])
        gnF, gnFn = carve("PBF", 8192, 1024)
        for c in range(8):
            pd, pb = (ps["PD1"], "PD1") if c % 2 == 0 else (ps["PD2"], "PD2")
            tr.op("tensor", lambda E, c=c, pd=pd, gn=gn: E.transpose(pd[:, 0:TILE], gn[:, TILE * c:TILE * (c + 1)],
                                                                     sb["identf"][:]), gnn + ["identf"], [pb])
            tr.op("scalar", lambda E, c=c, pd=pd, gnF=gnF: E.activation(gnF[:, TILE * c:TILE * (c + 1)],
                                                                        pd[:, 0:TILE], AF.Copy), [pb], gnFn)
        for half in range(2):
            for c in range(8):
                wv, wvn = carve("PBF", D * c, D)
                tr.op("tensor", lambda E, half=half, c=c, wv=wv, gnF=gnF: E.matmul(
                    ps["PA"][:, 512 * half:512 * (half + 1)], gnF[:, TILE * c:TILE * (c + 1)],
                    wv[:, 512 * half:512 * (half + 1)], start=(c == 0), stop=False), gnFn + wvn, ["PA"])
            tr.op("tensor", lambda E, half=half: E.matmul(
                ps["PA"][:, 512 * half:512 * (half + 1)], sb["ones_bf"][0:1, 0:TILE],
                sb["bwo_bf"][0:1, 512 * half:512 * (half + 1)], start=False, stop=True),
                ["ones_bf", "bwo_bf"], ["PA"])
        x1 = sb["x1q"][:, D * qslot:D * (qslot + 1)]
        x1b = f"x1q{qslot}"
        gax, gaxn = carve("PF", 8192, 1024)
        bc1, bc1n = carve("PF", 12288, 1024)
        tr.op("vector", lambda E, gax=gax, bc1=bc1: E.tensor_mul(gax, ps["PA"][:], bc1), ["PA"] + bc1n, gaxn)
        tr.op("vector", lambda E, x1=x1, xv=xv, gax=gax: E.tensor_add(x1, xv, gax), xvn + gaxn, [x1b])
        if DBG_A1 and gt == 0:
            dma("cs", lambda E, x1=x1: E.dma_start(out=dr["dbg6"], in_=x1), r=[x1b], w=[])
        tr.op("vector", lambda E, x1=x1: E.bn_stats(stc[:, 0:6], x1[:, 0:512]), [x1b], ["stat"])
        tr.op("vector", lambda E, x1=x1: E.bn_stats(stc[:, 6:12], x1[:, 512:1024]), [x1b], ["stat"])
        tr.op("vector", lambda E: E.bn_aggr(stc[:, 12:14], stc[:, 0:12]), ["stat"], ["stat"])
        tr.op("scalar", lambda E: E.activation(stc[:, 14:15], stc[:, 13:14], AF.Ln, bias=sb["epsc"][:]), ["stat", "epsc"], ["stat"])
        tr.op("scalar", lambda E, qslot=qslot: E.activation(sb["ln2rstd"][:, qslot:qslot + 1], stc[:, 14:15],
                                                            AF.Exp, scale=-0.5), ["stat"], ["ln2rstd"])
        tr.op("vector", lambda E, qslot=qslot: E.tensor_scalar(sb["ln2nmr"][:, qslot:qslot + 1], stc[:, 12:13],
                                                               sb["ln2rstd"][:, qslot:qslot + 1], -1.0,
                                                               ALU.mult, ALU.mult), ["stat", "ln2rstd"], ["ln2nmr"])

    MAGIC = 12582912.0

    def b1_tile(quad, tq):
        stc = sb["stat"]
        bc1, bc1n = carve("PF", 3072, 1024)
        bc2, bc2n = carve("PF", 13312, 1024)
        x1 = sb["x1q"][:, D * tq:D * (tq + 1)]
        x1b = f"x1q{tq}"
        xn2, xn2n = carve("PF", 2048, 1024)
        tr.op("scalar", lambda E, x1=x1, tq=tq, xn2=xn2: E.activation(
            xn2, x1, AF.Identity, bias=sb["ln2nmr"][:, tq:tq + 1], scale=sb["ln2rstd"][:, tq:tq + 1]),
            [x1b, "ln2rstd", "ln2nmr"], xn2n)
        tr.op("vector", lambda E, xn2=xn2, bc1=bc1: E.tensor_mul(xn2, xn2, bc1), xn2n + bc1n, xn2n)
        tr.op("vector", lambda E, xn2=xn2, bc2=bc2: E.tensor_add(xn2, xn2, bc2), xn2n + bc2n, xn2n)
        tr.op("vector", lambda E, xn2=xn2: E.reduce_max(stc[:, 48:49], xn2, axis=AX.X, apply_absolute_value=True),
              xn2n, ["statb"])
        tr.op("vector", lambda E: E.tensor_single_scalar(stc[:, 49:50], stc[:, 48:49], 1e-5, ALU.max),
              ["statb"], ["statb"])
        tr.op("vector", lambda E: E.reciprocal(stc[:, 50:51], stc[:, 49:50]), ["statb"], ["statb"])
        tr.op("vector", lambda E: E.tensor_scalar_mul(stc[:, 51:52], stc[:, 50:51], 127.0), ["statb"], ["statb"])
        tr.op("vector", lambda E: E.tensor_scalar_mul(stc[:, 52:53], stc[:, 49:50], 1.0 / 127.0), ["statb"], ["statb"])
        tr.op("vector", lambda E, tq=tq: E.tensor_mul(stc[:, 56 + tq:57 + tq], stc[:, 52:53], sb["invsw"][:, 0:1]),
              ["statb", "invsw"], ["esg"])
        tr.op("vector", lambda E, tq=tq: E.tensor_mul(stc[:, 60 + tq:61 + tq], stc[:, 52:53], sb["invsw"][:, 1:2]),
              ["statb", "invsw"], ["esg"])
        if DBG_A1 and quad == 0 and tq == 0:
            dma("cs", lambda E, xn2=xn2: E.dma_start(out=dr["dbg7"], in_=xn2), r=xn2n, w=[])
        q1, q1n = carve("PF", 1024, 1024)
        tr.op("vector", lambda E, xn2=xn2, q1=q1: E.tensor_scalar(q1, xn2, stc[:, 51:52], MAGIC, ALU.mult, ALU.add),
              xn2n + ["statb"], q1n)
        tr.op("vector", lambda E, q1=q1: E.tensor_scalar_add(q1, q1, -MAGIC), q1n, q1n)
        tr.op("vector", lambda E, q1=q1: E.tensor_scalar(q1, q1, -128.0, 127.0, ALU.max, ALU.min), q1n, q1n)
        if DBG_A1 and quad == 0 and tq == 0:
            dma("cs", lambda E, q1=q1: E.dma_start(out=dr["dbg8"], in_=q1), r=q1n, w=[])
        xqF, xqFn = carve("PBF", 32768 + 1024 * tq, 1024)
        for c in range(8):
            po, pb = (0, "PC0") if c % 2 == 0 else (512, "PC1")
            tr.op("tensor", lambda E, c=c, po=po, q1=q1: E.transpose(
                ps["PC"][:, po:po + TILE], q1[:, TILE * c:TILE * (c + 1)], sb["identf"][:]), q1n + ["identf"], [pb])
            tr.op("scalar", lambda E, c=c, po=po, xqF=xqF: E.activation(xqF[:, TILE * c:TILE * (c + 1)],
                                                                        ps["PC"][:, po:po + TILE], AF.Copy), [pb], xqFn)

    def b_quad(quad):
        bi = (4 * quad) // NTS
        stc = sb["stat"]

        for e8 in range(8):
            ks = e8 % 2
            tgv, tgvn = carve("PBF", 0 + 4096 * ks, 4096)
            tuv, tuvn = carve("PBF", 8192 + 4096 * ks, 4096)
            dma("ct", lambda E, e8=e8, tgv=tgv: E.dma_start(out=tgv.rearrange("p (c m) -> p c m", m=512), in_=dr["tg"][e8].transpose([1, 0, 2])),
                r=tgvn, w=tgvn)
            dma("ct", lambda E, e8=e8, tuv=tuv: E.dma_start(out=tuv.rearrange("p (c m) -> p c m", m=512), in_=dr["tu"][e8].transpose([1, 0, 2])),
                r=tuvn, w=tuvn)
            for tq in range(4):
                xqF, xqFn = carve("PBF", 32768 + 1024 * tq, 1024)
                par = (4 * e8 + tq) % 2
                pg, pgb = (ps["PA"], "PA") if par == 0 else (ps["PB"], "PB")
                for c in range(8):
                    tr.op("tensor", lambda E, c=c, tgv=tgv, xqF=xqF, pg=pg: E.matmul(
                        pg[:, 0:512], xqF[:, TILE * c:TILE * (c + 1)], tgv[:, 512 * c:512 * (c + 1)],
                        start=(c == 0), stop=(c == 7)), xqFn + tgvn, [pgb + "0"])
                    tr.op("tensor", lambda E, c=c, tuv=tuv, xqF=xqF, pg=pg: E.matmul(
                        pg[:, 512:1024], xqF[:, TILE * c:TILE * (c + 1)], tuv[:, 512 * c:512 * (c + 1)],
                        start=(c == 0), stop=(c == 7)), xqFn + tuvn, [pgb + "1"])
                sg, sgn = carve("PF", 8192 + 512 * par, 512)
                tr.op("scalar", lambda E, tq=tq, sg=sg, pg=pg: E.activation(sg, pg[:, 0:512], AF.Silu,
                                                                     scale=stc[:, 56 + tq:57 + tq]),
                      [pgb + "0", "esg"], sgn)
                a2v, a2vn = carve("PBF", 16384 + 4096 * tq + 512 * e8, 512)
                tr.op("vector", lambda E, tq=tq, a2v=a2v, sg=sg, pg=pg: E.scalar_tensor_tensor(
                    a2v, pg[:, 512:1024], stc[:, 60 + tq:61 + tq], sg, ALU.mult, ALU.mult),
                    [pgb + "1", "esg"] + sgn, a2vn)
                tr.op("vector", lambda E, tq=tq, e8=e8, a2v=a2v: E.reduce_max(
                    stc[:, 64 + tq:65 + tq] if e8 == 0 else stc[:, 68 + tq:69 + tq], a2v,
                    axis=AX.X, apply_absolute_value=True), a2vn, ["amx"])
                if e8 > 0:
                    tr.op("vector", lambda E, tq=tq: E.tensor_max(stc[:, 64 + tq:65 + tq], stc[:, 64 + tq:65 + tq],
                                                                  stc[:, 68 + tq:69 + tq]), ["amx"], ["amx"])

        if DBG_A1 and quad == 0:
            dma("cs", lambda E: E.dma_start(out=dr["dbg12"], in_=sb["stat"][:, 64:80]), r=["amx"], w=[])
            a2full, a2fn = carve("PBF", 16384, 4096)
            dma("cs", lambda E, a2full=a2full: E.dma_start(out=dr["dbg9"], in_=a2full),
                r=[f"a2_0"] + a2fn, w=[])
        for pair in range(2):
            for tq2 in range(2):
                tq = 2 * pair + tq2
                tr.op("vector", lambda E, tq=tq: E.tensor_single_scalar(stc[:, 72:73], stc[:, 64 + tq:65 + tq],
                                                                        1e-5, ALU.max), ["amx"], ["statq"])
                tr.op("vector", lambda E: E.reciprocal(stc[:, 73:74], stc[:, 72:73]), ["statq"], ["statq"])
                tr.op("vector", lambda E: E.tensor_scalar_mul(stc[:, 74:75], stc[:, 73:74], 127.0), ["statq"], ["statq"])
                tr.op("vector", lambda E: E.tensor_scalar_mul(stc[:, 75:76], stc[:, 72:73], 1.0 / 127.0), ["statq"], ["statq"])
                tr.op("vector", lambda E, tq=tq: E.tensor_mul(stc[:, 80 + tq:81 + tq], stc[:, 75:76], sb["invsw"][:, 2:3]),
                      ["statq", "invsw"], ["es2q"])
                xq2F, xq2Fn = carve("PBF", 36864 + 4096 * tq2, 4096)
                for qtr in range(4):
                    a2v, a2vn = carve("PBF", 16384 + 4096 * tq + 1024 * qtr, 1024)
                    q1, q1n = carve("PF", 1024, 1024)
                    tr.op("vector", lambda E, a2v=a2v, q1=q1: E.tensor_scalar(
                        q1, a2v, stc[:, 74:75], MAGIC, ALU.mult, ALU.add), a2vn + ["statq"], q1n)
                    if DBG_A1 and quad == 0 and tq == 0:
                        dma("cs", lambda E, q1=q1, qtr=qtr: E.dma_start(
                            out=dr["dbg11"][:, 1024 * qtr:1024 * (qtr + 1)], in_=q1), r=q1n, w=[])
                    tr.op("vector", lambda E, q1=q1: E.tensor_scalar_add(q1, q1, -MAGIC), q1n, q1n)
                    tr.op("vector", lambda E, q1=q1: E.tensor_scalar(q1, q1, -128.0, 127.0, ALU.max, ALU.min),
                          q1n, q1n)
                    if DBG_A1 and quad == 0 and tq == 0:
                        dma("cs", lambda E, q1=q1, qtr=qtr: E.dma_start(
                            out=dr["dbg10"][:, 1024 * qtr:1024 * (qtr + 1)], in_=q1), r=q1n, w=[])
                    for c in range(8):
                        pd, pb = (ps["PD1"], "PD1") if c % 2 == 0 else (ps["PD2"], "PD2")
                        tr.op("tensor", lambda E, c=c, pd=pd, q1=q1: E.transpose(
                            pd[:, 0:TILE], q1[:, TILE * c:TILE * (c + 1)], sb["identf"][:]), q1n + ["identf"], [pb])
                        tr.op("scalar", lambda E, c=c, qtr=qtr, pd=pd, xq2F=xq2F: E.activation(
                            xq2F[:, 1024 * qtr + TILE * c:1024 * qtr + TILE * (c + 1)], pd[:, 0:TILE], AF.Copy),
                            [pb], xq2Fn)
            for k in range(32):
                wv, wvn = carve("PBF", 45056 + 1024 * (k % 2), 1024)
                dma("ck", lambda E, k=k, wv=wv: E.dma_start(out=wv, in_=dr["td"][k]), r=wvn, w=wvn)
                for tq2 in range(2):
                    xq2F, xq2Fn = carve("PBF", 36864 + 4096 * tq2, 4096)
                    pn, pb = (ps["PA"], "PA") if tq2 == 0 else (ps["PB"], "PB")
                    for half in range(2):
                        tr.op("tensor", lambda E, k=k, half=half, pn=pn, xq2F=xq2F, wv=wv: E.matmul(
                            pn[:, 512 * half:512 * (half + 1)], xq2F[:, TILE * k:TILE * (k + 1)],
                            wv[:, 512 * half:512 * (half + 1)], start=(k == 0), stop=(k == 31)),
                            xq2Fn + wvn, [pb + str(half)])
            gb, gbn = carve("PF", 14336, 1024)
            _ = []
            bcast_tile(14336, bi, 5 * D, False, _)
            for tq2 in range(2):
                tq = 2 * pair + tq2
                gt = 4 * quad + tq
                bi3, st3 = gt // NTS, gt % NTS
                pn, pb = (ps["PA"], "PA") if tq2 == 0 else (ps["PB"], "PB")
                x1 = sb["x1q"][:, D * tq:D * (tq + 1)]
                x1b = f"x1q{tq}"
                tmp, tmpn = carve("PF", 2048, 1024)
                for half in range(2):
                    tr.op("vector", lambda E, pn=pn, half=half, tq=tq, tmp=tmp, gb=gb: E.scalar_tensor_tensor(
                        tmp[:, 512 * half:512 * (half + 1)], pn[:, 512 * half:512 * (half + 1)],
                        stc[:, 80 + tq:81 + tq], gb[:, 512 * half:512 * (half + 1)], ALU.mult, ALU.mult),
                        [pb + str(half), "es2q"] + gbn, tmpn)
                tr.op("vector", lambda E, x1=x1, tmp=tmp: E.tensor_add(x1, x1, tmp), [x1b] + tmpn, [x1b])
                dma("co", lambda E, bi3=bi3, st3=st3, x1=x1: E.dma_start(
                    out=out_h[bi3, TILE * st3:TILE * (st3 + 1), :], in_=x1), r=[x1b], w=[])

    prep()
    load_wqkv(0)
    qgen = quant_units()
    qdone = False
    for gt in range(N_A1):
        if gt == NTS:
            load_wqkv(1)
        a1_tile(gt)
        for _ in range(7):
            try:
                next(qgen)()
            except StopIteration:
                qdone = True
                break
    while not qdone:
        try:
            next(qgen)()
        except StopIteration:
            qdone = True
    a2()
    for quad in range(N_QUAD):
        load_wo()
        bi = (4 * quad) // NTS
        _ = []
        bcast_tile(12288, bi, 2 * D, False, _)
        bcast_tile(3072, bi, 4 * D, True, _)
        bcast_tile(13312, bi, 3 * D, False, _)
        for tq in range(4):
            a3_tile(4 * quad + tq, tq)
            b1_tile(quad, tq)
        b_quad(quad)

    ops = tr.finalize()
    co_waits = [(k, v) for k, v in tr.dcnt.items() if k[1] == "co"]
    if co_waits:
        ops["sync"].append((sorted(co_waits), lambda E: E.nop(), None))
    nds = 0
    for k in tr.dcnt:
        if k not in sems:
            sems[k] = stack.enter_context(nc.semaphore(f"sd{nds}"))
            nds += 1
    assert nds + len(Tr.ENG) <= 100, f"too many semaphores: {nds}"

    with nc.Block() as blk0:
        def clear(E):
            for s in sems.values():
                E.sem_clear(s)
        blk0.sync(clear)

    with nc.Block() as blk:
        def mk(ename):
            myops = ops[ename]

            def run(E):
                for waits, fn, inc in myops:
                    for k, v in waits:
                        E.wait_ge(sems[k], v)
                    inst = fn(E)
                    if inc is not None:
                        inst.then_inc(sems[inc[0]], inc[1])
            return run
        blk.tensor(mk("tensor"))
        blk.vector(mk("vector"))
        blk.scalar(mk("scalar"))
        blk.gpsimd(mk("gpsimd"))
        blk.sync(mk("sync"))

    n_inst = sum(len(v) for v in ops.values())
    return nc, stack, n_inst


def make_inputs(full):
    tabs = host_tables()
    WQ, WK, WV = full["L0_attn_WQ"], full["L0_attn_WK"], full["L0_attn_WV"]
    wqkvT = np.concatenate([w.transpose(2, 0, 1).reshape(D, D) for w in (WQ, WK, WV)], axis=1)
    wqkv4 = np.ascontiguousarray(wqkvT.reshape(8, TILE, 3 * D))
    wo4 = np.ascontiguousarray(full["L0_attn_WO"].reshape(8, TILE, D))
    wmod4 = np.ascontiguousarray(full["L0_wmod"].reshape(8, TILE, 12, 512).transpose(2, 0, 1, 3))
    wg4 = np.ascontiguousarray(full["L0_ffn_w_gate"].T.reshape(8, TILE, INNER))
    wu4 = np.ascontiguousarray(full["L0_ffn_w_up"].T.reshape(8, TILE, INNER))
    wd4 = np.ascontiguousarray(full["L0_ffn_w_down"].T.reshape(32, TILE, D))
    gammaF = np.ascontiguousarray(full["L0_attn_gamma"].reshape(8, TILE).T)
    betaF = np.ascontiguousarray(full["L0_attn_beta"].reshape(8, TILE).T)
    x = full["L0_attn_x"]
    c = full["L0_c"]
    shared = dict(wqkv4=wqkv4, wo4=wo4, wmod4=wmod4, wg4=wg4, wu4=wu4, wd4=wd4,
                  gammaF=gammaF, betaF=betaF, ctbl=tabs["ctbl"], stbl=tabs["stbl"],
                  dmt=tabs["dmt"], identf=tabs["ident"])
    in_maps = []
    for core in range(8):
        cs = c[2 * core:2 * core + 2]
        cF = np.ascontiguousarray(cs.reshape(2, 8, TILE).transpose(2, 1, 0).reshape(TILE, 16))
        m = dict(shared)
        m["x"] = np.ascontiguousarray(x[2 * core:2 * core + 2])
        m["cF"] = cF
        in_maps.append(m)
    return in_maps


def kernel(**inputs):
    in_maps = make_inputs(inputs)
    nc, stack, n_inst = build()
    res = run_bass_kernel_spmd(nc, in_maps, core_ids=list(range(8)))
    outs = [res.results[i]["out"] for i in range(8)]
    full = np.concatenate(outs, axis=0).astype(np.float32)
    if SPILL_OUT:
        return full, res
    stack.close()
    return full



# version 10
# speedup vs baseline: 1.1746x; 1.0442x over previous
import contextlib
import numpy as np
import concourse.bass as bass
import concourse.mybir as mybir
from concourse.bass_utils import run_bass_kernel_spmd

F32 = mybir.dt.float32
BF16 = mybir.dt.bfloat16
I32 = mybir.dt.int32
AF = mybir.ActivationFunctionType
ALU = mybir.AluOpType
AX = mybir.AxisListType

S, D, H, DH = 2048, 1024, 16, 64
TILE = 128
NTS = S // TILE
NT = 2 * NTS
NQG = NTS // 4
INNER = 4096
EPS = 1e-5
DPAD = 3

N_A1 = NT
N_QG = NQG
N_QUAD = NT // 4
SPILL_OUT = False
DBG_A1 = False


def host_tables():
    import jax
    with jax.default_device(jax.devices("cpu")[0]):
        import jax.numpy as jnp
        expoent = jnp.arange(0, DH, 2, dtype=jnp.float32) / DH
        angle = 10.0 ** (-4.0 * expoent)
        pos = jnp.arange(0, S, dtype=jnp.float32)
        nangle = pos[:, None] * angle[None, :]
        sin = np.asarray(jnp.sin(nangle))
        cos = np.asarray(jnp.cos(nangle))
        scale = 1.0 - 2.0 ** (-5.0 - jnp.arange(H, dtype=jnp.float32))
        logs = jnp.log(scale)
        delta = jnp.arange(S, dtype=jnp.float32)
        logd = delta[None, :] * logs[:, None]
        d32 = np.asarray(jnp.exp(logd))

    ctbl = np.zeros((TILE, NTS * DH), np.float32)
    stbl = np.zeros((TILE, NTS * DH), np.float32)
    for st in range(NTS):
        c = cos[st * TILE:(st + 1) * TILE]
        s = sin[st * TILE:(st + 1) * TILE]
        ctbl[:, st * DH:(st + 1) * DH] = np.repeat(c, 2, axis=1)
        blk = np.empty((TILE, DH), np.float32)
        blk[:, 0::2] = -s
        blk[:, 1::2] = s
        stbl[:, st * DH:(st + 1) * DH] = blk

    ndt = DPAD + NTS
    dmt = np.zeros((H, TILE, ndt * TILE), np.float32)
    jj = np.arange(TILE)[:, None]
    ii = np.arange(TILE)[None, :]
    for t in range(ndt):
        dlt = (t - DPAD) * TILE + ii - jj
        valid = (dlt >= 0) & (dlt < S)
        dl = np.clip(dlt, 0, S - 1)
        for h in range(H):
            dmt[h, :, t * TILE:(t + 1) * TILE] = d32[h][dl] * valid
    ident = np.eye(TILE, dtype=np.float32)
    return dict(ctbl=ctbl, stbl=stbl, dmt=dmt, ident=ident)


class Tr:
    ENG = ("tensor", "vector", "scalar", "gpsimd", "sync")
    DMA_Q = {"cx": "sync", "ct": "sync", "ck": "scalar", "cw": "gpsimd", "cs": "gpsimd", "co": "gpsimd"}
    SELFWAIT = {"vector", "scalar", "gpsimd"}

    def __init__(self):
        self.recs = []
        self.lastw = {}
        self.readers = {}
        self.openg = []
        self.opkey = []

    PSPLIT = {"PA": ("PA0", "PA1"), "PB": ("PB0", "PB1"), "PC": ("PC0", "PC1")}

    def _norm(self, bufs):
        out = []
        for b in bufs:
            if b in self.PSPLIT:
                out.extend(self.PSPLIT[b])
            else:
                out.append(b)
        return out

    def op(self, eng, fn, reads=(), writes=(), dma=None):
        reads = self._norm(reads)
        writes = self._norm(writes)
        deps = set()
        myid = len(self.recs)

        def need(pid):
            if pid is None:
                return
            peng = self.openg[pid]
            pkey = self.opkey[pid]
            if pkey is None and peng == eng and eng not in self.SELFWAIT:
                return
            deps.add(pid)

        for b in reads:
            need(self.lastw.get(b))
        for b in writes:
            need(self.lastw.get(b))
            for r in self.readers.get(b, ()):
                need(r)
        key = None
        if dma is not None:
            assert self.DMA_Q[dma] == eng
            slot = tuple(sorted(writes)) if writes else ("st",) + tuple(sorted(reads))
            key = ("D", dma, slot)
        self.openg.append(eng)
        self.opkey.append(key)
        for b in writes:
            self.lastw[b] = myid
            self.readers[b] = []
        for b in reads:
            self.readers.setdefault(b, []).append(myid)
        self.recs.append((eng, fn, key, deps))

    def finalize(self):
        n = len(self.recs)
        needed = [False] * n
        for (_, _, key, deps) in self.recs:
            for d in deps:
                needed[d] = True
        val = [0] * n
        ecnt = {e: 0 for e in self.ENG}
        dcnt = {}
        for i, (eng, fn, key, deps) in enumerate(self.recs):
            if key is not None:
                dcnt[key] = dcnt.get(key, 0) + 16
                val[i] = dcnt[key]
            elif needed[i]:
                ecnt[eng] += 1
                val[i] = ecnt[eng]
        ops = {e: [] for e in self.ENG}
        waited = {e: {} for e in self.ENG}
        for i, (eng, fn, key, deps) in enumerate(self.recs):
            waits = {}
            for d in deps:
                dk = self.opkey[d]
                sk = dk if dk is not None else ("E", self.openg[d])
                v = val[d]
                if waited[eng].get(sk, 0) >= v:
                    continue
                if waits.get(sk, 0) < v:
                    waits[sk] = v
            for k2, v2 in waits.items():
                waited[eng][k2] = v2
            if key is not None:
                inc = (key, 16)
            elif needed[i]:
                inc = (("E", eng), 1)
            else:
                inc = None
            ops[eng].append((sorted(waits.items()), fn, inc))
        self.dcnt = dcnt
        return ops


def build():
    tabs = host_tables()
    nc = bass.Bass("TRN2", target_bir_lowering=False, detect_race_conditions=False)
    stack = contextlib.ExitStack()
    dr = {}

    def din(name, shape, dt=F32):
        dr[name] = nc.dram_tensor(name, list(shape), dt, kind="ExternalInput").ap()

    def dint(name, shape, dt):
        kind = "ExternalOutput" if SPILL_OUT else "Internal"
        dr[name] = nc.dram_tensor(name, list(shape), dt, kind=kind).ap()

    din("x", [2, S, D])
    dr["out"] = nc.dram_tensor("out", [2, S, D], F32, kind="ExternalOutput").ap()
    din("cF", [TILE, 16])
    din("wmod4", [12, 8, TILE, 512])
    din("wqkv4", [8, TILE, 3 * D])
    din("wo4", [8, TILE, D])
    din("gammaF", [TILE, 8])
    din("betaF", [TILE, 8])
    din("wg4", [8, TILE, INNER])
    din("wu4", [8, TILE, INNER])
    din("wd4", [32, TILE, D])
    din("ctbl", [TILE, NTS * DH])
    din("stbl", [TILE, NTS * DH])
    din("dmt", [H, TILE, (DPAD + NTS) * TILE])
    din("identf", [TILE, TILE])
    dint("sp_q", [NT, TILE, D], BF16)
    dint("sp_kv", [NT, 2, TILE, D], BF16)
    dint("sp_r", [NT, H, DH, TILE], F32)
    dint("modv", [2, 6 * D], F32)
    if DBG_A1:
        for i in (1, 2, 3):
            dint(f"dbg{i}", [TILE, D], F32)
        for i in (4, 5, 6, 7, 8):
            dint(f"dbg{i}", [TILE, D], F32)
        for i in (9, 10, 11):
            dint(f"dbg{i}", [TILE, INNER], F32)
        dint("dbg12", [TILE, 16], F32)
    dint("tg", [8, 8, TILE, 512], BF16)
    dint("tu", [8, 8, TILE, 512], BF16)
    dint("td", [32, TILE, D], BF16)
    dint("wof", [8, TILE, D], BF16)

    sb = {}

    def sbuf(name, shape, dt):
        sb[name] = stack.enter_context(nc.sbuf_tensor("sb_" + name, list(shape), dt))

    sbuf("identf", [TILE, TILE], F32)
    sbuf("ctbl", [TILE, NTS * DH], F32)
    sbuf("stbl", [TILE, NTS * DH], F32)
    sbuf("ones_bf", [1, 512], BF16)
    sbuf("onesf_col", [TILE, 1], F32)
    sbuf("onesf_row", [1, TILE], F32)
    sbuf("cF", [TILE, 16], F32)
    sbuf("silu_cF", [TILE, 16], F32)
    sbuf("gammaF", [TILE, 8], F32)
    sbuf("betaF_bf", [TILE, 8], BF16)
    sbuf("modF", [TILE, 32], F32)
    sbuf("saF_bf", [TILE, 16], BF16)
    sbuf("caF1p", [TILE, 16], F32)
    sbuf("saw_bf", [1, 3 * D], BF16)
    sbuf("bwo_bf", [1, D], BF16)
    sbuf("scw", [TILE, 3], F32)
    sbuf("invsw", [TILE, 3], F32)
    sbuf("accw", [TILE, 48], F32)
    sbuf("stat", [TILE, 96], F32)
    sbuf("ln2rstd", [TILE, 4], F32)
    sbuf("ln2nmr", [TILE, 4], F32)
    sbuf("epsc", [TILE, 1], F32)
    sbuf("x1q", [TILE, 4 * D], F32)
    sbuf("PF", [TILE, 16384], F32)
    sbuf("identb", [TILE, TILE], BF16)
    sbuf("PBF", [TILE, 49152], BF16)
    sbuf("PI", [TILE, 1024], I32)

    SEG = {"PF": 512, "PBF": 1024, "PI": 512}

    def carve(pool, off, n, p0=0, p1=TILE):
        ap = sb[pool][p0:p1, off:off + n]
        s0, s1 = off // SEG[pool], (off + n - 1) // SEG[pool]
        names = [f"{pool}:{i}" for i in range(s0, s1 + 1)]
        return ap, names

    ps = {}
    for n, shp in (("PA", [TILE, 1024]), ("PB", [TILE, 1024]), ("PC", [TILE, 1024]),
                   ("PD1", [TILE, 512]), ("PD2", [TILE, 512])):
        ps[n] = stack.enter_context(nc.psum_tensor(n, shp, F32))

    tr = Tr()
    sems = {}
    for e in Tr.ENG:
        sems[("E", e)] = stack.enter_context(nc.semaphore("se_" + e))

    x_h, out_h = dr["x"], dr["out"]

    def dma(chan, fn, r=(), w=()):
        tr.op(Tr.DMA_Q[chan], fn, r, w, dma=chan)

    def rnd_nearest(src_ap, src_n, w_slices, rbufs):
        n = src_n
        for lo in range(0, n, 1024):
            wdt = min(1024, n - lo)
            sl = src_ap[:, lo:lo + wdt]
            ri, rin = carve("PI", 0, wdt)
            dv, dvn = carve("PF", FD_RND, wdt)
            bv, bvn = carve("PF", FD_RND + 1024, wdt)
            tr.op("vector", lambda E, sl=sl, ri=ri: E.tensor_copy(ri, sl), w_slices, rin)
            tr.op("vector", lambda E, sl=sl, ri=ri, dv=dv: E.tensor_sub(dv, sl, ri), list(w_slices) + rin, dvn)
            tr.op("vector", lambda E, dv=dv, bv=bv: E.tensor_single_scalar(bv, dv, -0.5, ALU.is_lt), dvn, bvn)
            tr.op("vector", lambda E, sl=sl, dv=dv, ri=ri: E.scalar_tensor_tensor(sl, dv, 0.5, ri, ALU.is_gt, ALU.add),
                  dvn + rin, w_slices)
            tr.op("vector", lambda E, sl=sl, bv=bv: E.tensor_sub(sl, sl, bv), list(w_slices) + bvn, w_slices)

    FD_RND = 14336

    def prep():
        for nm in ("identf", "ctbl", "stbl", "cF", "gammaF"):
            dma("cx", lambda E, nm=nm: E.dma_start(out=sb[nm][:], in_=dr[nm]), w=[nm])
        dma("cw", lambda E: E.dma_start(out=sb["betaF_bf"][:], in_=dr["betaF"]), w=["betaF_bf"])
        tr.op("vector", lambda E: E.tensor_copy(sb["identb"][:], sb["identf"][:]), ["identf"], ["identb"])
        tr.op("vector", lambda E: E.memset(sb["ones_bf"][:], 1.0), (), ["ones_bf"])
        tr.op("vector", lambda E: E.memset(sb["onesf_col"][:], 1.0), (), ["onesf_col"])
        tr.op("vector", lambda E: E.memset(sb["onesf_row"][:], 1.0), (), ["onesf_row"])
        tr.op("vector", lambda E: E.memset(sb["epsc"][:], EPS), (), ["epsc"])
        tr.op("scalar", lambda E: E.activation(sb["silu_cF"][:], sb["cF"][:], AF.Silu), ["cF"], ["silu_cF"])

        def mtmp(lo, n):
            if lo < 4096:
                assert lo + n <= 4096
                return carve("PF", lo, n, 0, 2)
            return carve("PF", 4096 + (lo - 4096), n, 0, 2)

        wmb, wmbn = carve("PF", 10240, 4096)
        for n in range(12):
            pn, pb = (ps["PA"], "PA") if n % 2 == 0 else (ps["PB"], "PB")
            dma("ct", lambda E, n=n, wmb=wmb: E.dma_start(out=wmb.rearrange("p (c m) -> p c m", m=512), in_=dr["wmod4"][n].transpose([1, 0, 2])),
                r=wmbn, w=wmbn)
            for c in range(8):
                tr.op("tensor", lambda E, c=c, pn=pn, wmb=wmb: E.matmul(
                    pn[0:2, 0:512], sb["silu_cF"][:, 2 * c:2 * c + 2], wmb[:, 512 * c:512 * (c + 1)],
                    start=(c == 0), stop=(c == 7)), ["silu_cF"] + wmbn, [pb])
            mt, mtn = mtmp(512 * n, 512)
            tr.op("vector", lambda E, pn=pn, mt=mt: E.tensor_copy(mt, pn[0:2, 0:512]), [pb], mtn)

        ct, ctn = carve("PF", 8192, 1024, 0, 2)
        cm, cmn = carve("PF", 9216, 1024, 0, 2)
        for lo in (2048, 5120):
            mt, mtn = mtmp(lo, 1024)
            tr.op("vector", lambda E, mt=mt, ct=ct: E.tensor_single_scalar(ct, mt, 0.0, ALU.is_ge), mtn, ctn)
            tr.op("vector", lambda E, ct=ct: E.tensor_scalar(ct, ct, 2.0, -1.0, ALU.mult, ALU.add), ctn, ctn)
            tr.op("scalar", lambda E, mt=mt, cm=cm: E.activation(cm, mt, AF.Abs), mtn, cmn)
            tr.op("vector", lambda E, cm=cm: E.tensor_single_scalar(cm, cm, 0.25, ALU.max), cmn, cmn)
            tr.op("vector", lambda E, mt=mt, ct=ct, cm=cm: E.tensor_mul(mt, ct, cm), ctn + cmn, mtn)

        for vi, base in ((0, 0), (1, 1024)):
            for c in range(8):
                mt, mtn = mtmp(base + TILE * c, TILE)
                tr.op("tensor", lambda E, mt=mt: E.transpose(ps["PC"][:, 0:2], mt, sb["identf"][0:2, 0:2]),
                      mtn + ["identf"], ["PC"])
                tr.op("vector", lambda E, vi=vi, c=c: E.tensor_copy(sb["modF"][:, 16 * vi + 2 * c:16 * vi + 2 * c + 2],
                                                                    ps["PC"][:, 0:2]), ["PC"], ["modF"])
        tr.op("vector", lambda E: E.tensor_copy(sb["saF_bf"][:], sb["modF"][:, 0:16]), ["modF"], ["saF_bf"])
        tr.op("vector", lambda E: E.tensor_scalar_add(sb["caF1p"][:], sb["modF"][:, 16:32], 1.0), ["modF"], ["caF1p"])

        for bi in range(2):
            for seg in range(3):
                mt, mtn = mtmp(2048 * seg, 2048)
                dma("cs", lambda E, bi=bi, seg=seg, mt=mt: E.dma_start(
                    out=dr["modv"][bi:bi + 1, 2048 * seg:2048 * (seg + 1)], in_=mt[bi:bi + 1, :]), r=mtn, w=["modv"])

        pass

    def quant_units():
        nch = {"g": 8, "u": 8, "d": 32}
        wsrc = {"g": "wg4", "u": "wu4", "d": "wd4"}
        wid = {"g": 0, "u": 1, "d": 2}
        ccol = {"g": 0, "u": 8, "d": 16}
        nsl = {"g": 4, "u": 4, "d": 1}
        wcs, wcsn = carve("PF", 12288, 1024)
        ab2, ab2n = carve("PF", 13312, 1024)
        qtb, qtbn = carve("PBF", 28672, 1024)

        for m in ("g", "u", "d"):
            for c in range(nch[m]):
                for sl in range(nsl[m]):
                    def u1(m=m, c=c, sl=sl):
                        dma("ct", lambda E: E.dma_start(out=wcs, in_=dr[wsrc[m]][c][:, 1024 * sl:1024 * (sl + 1)]),
                            r=wcsn, w=wcsn)
                        col = 32 + sl
                        if (c + sl) % 2 == 0:
                            tr.op("scalar", lambda E, col=col: E.activation(
                                ab2, wcs, AF.Abs, accum_out=sb["accw"][:, col:col + 1]),
                                wcsn, ab2n + ["accw"])
                        else:
                            tr.op("vector", lambda E, col=col: E.reduce_sum(
                                sb["accw"][:, col:col + 1], wcs, axis=AX.X, apply_absolute_value=True),
                                wcsn, ["accw"])
                        if sl == nsl[m] - 1:
                            if nsl[m] > 1:
                                tr.op("vector", lambda E: E.tensor_reduce(
                                    sb["accw"][:, ccol[m] + c:ccol[m] + c + 1],
                                    sb["accw"][:, 32:32 + nsl[m]], AX.X, ALU.add), ["accw"], ["accw"])
                            else:
                                tr.op("vector", lambda E: E.tensor_copy(
                                    sb["accw"][:, ccol[m] + c:ccol[m] + c + 1], sb["accw"][:, 32:33]),
                                    ["accw"], ["accw"])
                    yield u1

        for m in ("g", "u", "d"):
            def us(m=m):
                tr.op("tensor", lambda E: E.matmul(ps["PD1"][0:1, 0:nch[m]], sb["onesf_col"][:],
                                                   sb["accw"][:, ccol[m]:ccol[m] + nch[m]], start=True, stop=True),
                      ["onesf_col", "accw"], ["PD1"])
                tr.op("vector", lambda E: E.reduce_sum(sb["stat"][0:1, 32:33], ps["PD1"][0:1, 0:nch[m]], axis=AX.X),
                      ["PD1"], ["stat"])
                tr.op("vector", lambda E: E.tensor_scalar(sb["stat"][0:1, 33:34], sb["stat"][0:1, 32:33],
                                                          1.0 / (INNER * D), 1e-5, ALU.mult, ALU.max),
                      ["stat"], ["stat"])
                tr.op("vector", lambda E: E.reciprocal(sb["stat"][0:1, 34:35], sb["stat"][0:1, 33:34]),
                      ["stat"], ["stat"])
                for (srcc, dst) in ((34, "scw"), (33, "invsw")):
                    tr.op("tensor", lambda E, srcc=srcc: E.matmul(ps["PD2"][:, 0:1], sb["onesf_row"][:],
                                                                  sb["stat"][0:1, srcc:srcc + 1], start=True, stop=True),
                          ["stat", "onesf_row"], ["PD2"])
                    tr.op("vector", lambda E, dst=dst: E.tensor_copy(sb[dst][:, wid[m]:wid[m] + 1], ps["PD2"][:, 0:1]),
                          ["PD2"], [dst])
            yield us

        for m in ("g", "u", "d"):
            for c in range(nch[m]):
                for sl in range(nsl[m]):
                    def u2(m=m, c=c, sl=sl):
                        dma("ct", lambda E: E.dma_start(out=wcs, in_=dr[wsrc[m]][c][:, 1024 * sl:1024 * (sl + 1)]),
                            r=wcsn, w=wcsn)
                        tr.op("vector", lambda E: E.tensor_scalar(wcs, wcs, sb["scw"][:, wid[m]:wid[m] + 1],
                                                                  12582912.0, ALU.mult, ALU.add),
                              wcsn + ["scw"], wcsn)
                        tr.op("vector", lambda E: E.tensor_scalar_add(wcs, wcs, -12582912.0), wcsn, wcsn)
                        tr.op("vector", lambda E: E.tensor_scalar(qtb, wcs, -1.0, 1.0, ALU.max, ALU.min),
                              wcsn, qtbn)
                        if m == "d":
                            dma("cs", lambda E: E.dma_start(out=dr["td"][c], in_=qtb), r=qtbn, w=[])
                        else:
                            tgt = "tg" if m == "g" else "tu"
                            for e2 in range(2):
                                e8 = 2 * sl + e2
                                dma("cs", lambda E, e8=e8, e2=e2, tgt=tgt: E.dma_start(
                                    out=dr[tgt][e8, c], in_=qtb[:, 512 * e2:512 * (e2 + 1)]), r=qtbn, w=[])
                    yield u2

    def wo_prep():
        for c in range(8):
            wv, wvn = carve("PBF", 36864 + D * c, D)
            dma("cw", lambda E, c=c, wv=wv: E.dma_start(out=wv, in_=dr["wo4"][c]), r=wvn, w=wvn)
        for half in range(2):
            for c in range(8):
                wv, wvn = carve("PBF", 36864 + D * c, D)
                tr.op("tensor", lambda E, half=half, c=c, wv=wv: E.matmul(
                    ps["PD1"][0:1, 0:512], sb["betaF_bf"][:, c:c + 1],
                    wv[:, 512 * half:512 * (half + 1)], start=(c == 0), stop=(c == 7)),
                    ["betaF_bf"] + wvn, ["PD1"])
            tr.op("vector", lambda E, half=half: E.tensor_copy(sb["bwo_bf"][0:1, 512 * half:512 * (half + 1)],
                                                               ps["PD1"][0:1, 0:512]), ["PD1"], ["bwo_bf"])
        for c in range(8):
            wv, wvn = carve("PBF", 36864 + D * c, D)
            tr.op("vector", lambda E, c=c, wv=wv: E.tensor_scalar_mul(wv, wv, sb["gammaF"][:, c:c + 1]),
                  wvn + ["gammaF"], wvn)
            dma("cs", lambda E, c=c, wv=wv: E.dma_start(out=dr["wof"][c], in_=wv), r=wvn, w=["wof"])

    def load_wqkv(bi):
        for c in range(8):
            wv, wvn = carve("PBF", 3 * D * c, 3 * D)
            dma("cw", lambda E, c=c, wv=wv: E.dma_start(out=wv, in_=dr["wqkv4"][c]), r=wvn, w=wvn)
        pmap = {0: ("PA", ps["PA"]), 1: ("PB", ps["PB"]), 2: ("PC", ps["PC"])}
        for p in range(3):
            pb, pn = pmap[p]
            for half in range(2):
                for c in range(8):
                    wv, wvn = carve("PBF", 3 * D * c, 3 * D)
                    tr.op("tensor", lambda E, p=p, half=half, c=c, pn=pn, wv=wv: E.matmul(
                        pn[0:1, 512 * half:512 * (half + 1)], sb["saF_bf"][:, 2 * c + bi:2 * c + bi + 1],
                        wv[:, D * p + 512 * half:D * p + 512 * (half + 1)], start=(c == 0), stop=(c == 7)),
                        ["saF_bf"] + wvn, [pb])
                tr.op("vector", lambda E, p=p, half=half, pn=pn: E.tensor_copy(
                    sb["saw_bf"][0:1, D * p + 512 * half:D * p + 512 * (half + 1)],
                    pn[0:1, 512 * half:512 * (half + 1)]), [pb], ["saw_bf"])
        for c in range(8):
            wv, wvn = carve("PBF", 3 * D * c, 3 * D)
            tr.op("vector", lambda E, c=c, wv=wv: E.tensor_scalar_mul(wv, wv, sb["caF1p"][:, 2 * c + bi:2 * c + bi + 1]),
                  wvn + ["caF1p"], wvn)

    def load_wo():
        for c in range(8):
            wv, wvn = carve("PBF", D * c, D)
            dma("cw", lambda E, c=c, wv=wv: E.dma_start(out=wv, in_=dr["wof"][c]), r=["wof"] + wvn, w=wvn)

    def a1_tile(gt):
        bi, st = gt // NTS, gt % NTS
        xv, xvn = carve("PF", 10240 + 1024 * (gt % 2), 1024)
        dma("cx", lambda E, bi=bi, st=st, xv=xv: E.dma_start(out=xv, in_=x_h[bi, TILE * st:TILE * (st + 1), :]),
            r=xvn, w=xvn)
        stc = sb["stat"]
        tr.op("vector", lambda E, xv=xv: E.bn_stats(stc[:, 0:6], xv[:, 0:512]), xvn, ["stat"])
        tr.op("vector", lambda E, xv=xv: E.bn_stats(stc[:, 6:12], xv[:, 512:1024]), xvn, ["stat"])
        tr.op("vector", lambda E: E.bn_aggr(stc[:, 12:14], stc[:, 0:12]), ["stat"], ["stat"])
        tr.op("scalar", lambda E: E.activation(stc[:, 14:15], stc[:, 13:14], AF.Ln, bias=sb["epsc"][:]), ["stat", "epsc"], ["stat"])
        tr.op("scalar", lambda E: E.activation(stc[:, 15:16], stc[:, 14:15], AF.Exp, scale=-0.5), ["stat"], ["stat"])
        tr.op("vector", lambda E: E.tensor_scalar(stc[:, 16:17], stc[:, 12:13], stc[:, 15:16], -1.0, ALU.mult, ALU.mult),
              ["stat"], ["stat"])
        xn1, xn1n = carve("PBF", 30720, 1024)
        tr.op("scalar", lambda E, xv=xv, xn1=xn1: E.activation(xn1, xv, AF.Identity,
                                                               bias=stc[:, 16:17], scale=stc[:, 15:16]),
              xvn + ["stat"], xn1n)
        xn1F, xn1Fn = carve("PBF", 24576, 1024)
        for c in range(8):
            pd, pb = (ps["PD1"], "PD1") if c % 2 == 0 else (ps["PD2"], "PD2")
            tr.op("tensor", lambda E, c=c, pd=pd, xn1=xn1: E.transpose(
                pd.bitcast(BF16)[:, 0:TILE], xn1[:, TILE * c:TILE * (c + 1)],
                sb["identb"][:]), xn1n + ["identb"], [pb])
            tr.op("scalar", lambda E, c=c, pd=pd, xn1F=xn1F: E.activation(xn1F[:, TILE * c:TILE * (c + 1)],
                                                                          pd.bitcast(BF16)[:, 0:TILE], AF.Copy), [pb], xn1Fn)
        pmap = {0: ("PA", ps["PA"]), 1: ("PB", ps["PB"]), 2: ("PC", ps["PC"])}
        for p in range(3):
            pb, pn = pmap[p]
            for half in range(2):
                for c in range(8):
                    wv, wvn = carve("PBF", 3 * D * c, 3 * D)
                    tr.op("tensor", lambda E, p=p, half=half, c=c, pn=pn, wv=wv, xn1F=xn1F: E.matmul(
                        pn[:, 512 * half:512 * (half + 1)], xn1F[:, TILE * c:TILE * (c + 1)],
                        wv[:, D * p + 512 * half:D * p + 512 * (half + 1)], start=(c == 0), stop=False),
                        xn1Fn + wvn, [pb])
                tr.op("tensor", lambda E, p=p, half=half, pn=pn: E.matmul(
                    pn[:, 512 * half:512 * (half + 1)], sb["ones_bf"][0:1, 0:TILE],
                    sb["saw_bf"][0:1, D * p + 512 * half:D * p + 512 * (half + 1)], start=False, stop=True),
                    ["ones_bf", "saw_bf"], [pb])
        if DBG_A1 and gt == 0:
            dma("cs", lambda E, xn1=xn1: E.dma_start(out=dr["dbg1"], in_=xn1), r=xn1n, w=[])
            for pbx, pnx, tag in (("PA", ps["PA"], "dbg2"), ("PB", ps["PB"], "dbg3")):
                tmpd, tmpdn = carve("PF", 2048, 1024)
                tr.op("vector", lambda E, pnx=pnx, tmpd=tmpd: E.tensor_copy(tmpd, pnx[:]), [pbx], tmpdn)
                dma("cs", lambda E, tmpd=tmpd, tag=tag: E.dma_start(out=dr[tag], in_=tmpd), r=tmpdn, w=[])
        for pb, pn, base in (("PA", ps["PA"], 0), ("PB", ps["PB"], 4096)):
            raw, rawn = carve("PF", base + 2048, 1024)
            tr.op("scalar", lambda E, pn=pn, raw=raw: E.activation(raw, pn[:], AF.Copy), [pb], rawn)
            rot, rotn = carve("PF", base, 1024)
            tmp, tmpn = carve("PF", base + 1024, 1024)
            cs = sb["ctbl"][:, DH * st:DH * (st + 1)]
            sn = sb["stbl"][:, DH * st:DH * (st + 1)]
            p3 = raw.rearrange("p (h u) -> p h u", u=DH)
            t3 = tmp.rearrange("p (h u) -> p h u", u=DH)
            d3 = rot.rearrange("p (h u) -> p h u", u=DH)
            sne = sn[:, 0::2].unsqueeze(1).broadcast_to([TILE, H, DH // 2])
            sno = sn[:, 1::2].unsqueeze(1).broadcast_to([TILE, H, DH // 2])
            csf = cs.unsqueeze(1).broadcast_to([TILE, H, DH])
            tr.op("vector", lambda E, t3=t3, sne=sne, p3=p3: E.tensor_mul(t3[:, :, 0::2], sne, p3[:, :, 1::2]),
                  rawn + ["stbl"], tmpn)
            tr.op("vector", lambda E, t3=t3, sno=sno, p3=p3: E.tensor_mul(t3[:, :, 1::2], sno, p3[:, :, 0::2]),
                  rawn + ["stbl"], tmpn)
            tr.op("vector", lambda E, d3=d3, csf=csf, p3=p3: E.tensor_mul(d3, csf, p3), rawn + ["ctbl"], rotn)
            rotb, rotbn = carve("PBF", 31744 if base == 0 else 32768, 1024)
            tr.op("vector", lambda E, rotb=rotb, rot=rot, tmp=tmp: E.tensor_add(rotb, rot, tmp),
                  rotn + tmpn, rotbn)
        vbf, vbfn = carve("PBF", 25600, 1024)
        tr.op("scalar", lambda E, vbf=vbf: E.activation(vbf, ps["PC"][:], AF.Copy), ["PC"], vbfn)
        dma("cs", lambda E, gt=gt, vbf=vbf: E.dma_start(out=dr["sp_kv"][gt, 1], in_=vbf), r=vbfn, w=["sp_kv"])
        for base, dsto, tgt in ((0, 26624, "sp_q"), (4096, 27648, "sp_kv")):
            rotb, rotbn = carve("PBF", 31744 if base == 0 else 32768, 1024)
            dst, dstn = carve("PBF", dsto, 1024)
            for c in range(8):
                pd, pb2 = (ps["PD1"], "PD1") if c % 2 == 0 else (ps["PD2"], "PD2")
                tr.op("tensor", lambda E, c=c, pd=pd, rotb=rotb: E.transpose(
                    pd.bitcast(BF16)[:, 0:TILE], rotb[:, TILE * c:TILE * (c + 1)],
                    sb["identb"][:]), rotbn + ["identb"], [pb2])
                tr.op("vector", lambda E, c=c, pd=pd, dst=dst: E.tensor_copy(dst[:, TILE * c:TILE * (c + 1)],
                                                                             pd.bitcast(BF16)[:, 0:TILE]), [pb2], dstn)
            if tgt == "sp_q":
                dma("cs", lambda E, gt=gt, dst=dst: E.dma_start(out=dr["sp_q"][gt], in_=dst), r=dstn, w=["sp_q"])
            else:
                dma("cs", lambda E, gt=gt, dst=dst: E.dma_start(out=dr["sp_kv"][gt, 0], in_=dst), r=dstn, w=["sp_kv"])

    def a2():
        for qg in range(N_QG):
            ntj = 4 * qg + 4
            qfg, qfgn = carve("PBF", 0, 8192)
            for si in range(2):
                for tq in range(4):
                    gt = si * NTS + 4 * qg + tq
                    if gt >= N_A1:
                        continue
                    sl = qfg[:, 1024 * (4 * si + tq):1024 * (4 * si + tq + 1)]
                    dma("ct", lambda E, gt=gt, sl=sl: E.dma_start(out=sl, in_=dr["sp_q"][gt]),
                        r=["sp_q"] + qfgn, w=qfgn)
            wcols = TILE * (4 * qg + 4 + DPAD)
            for hp in range(H // 2):
                h0, h1 = 2 * hp, 2 * hp + 1
                ds = 2 * (hp % 2)
                dmv0, dmv0n = carve("PF", 2432 * ds, 2432)
                dmv1, dmv1n = carve("PF", 2432 * (ds + 1), 2432)
                dma("ct", lambda E, h0=h0, wcols=wcols, dmv0=dmv0: E.dma_start(
                    out=dmv0[:, 0:wcols], in_=dr["dmt"][h0][:, 0:wcols]), r=dmv0n, w=dmv0n)
                dma("ct", lambda E, h1=h1, wcols=wcols, dmv1=dmv1: E.dma_start(
                    out=dmv1[:, 0:wcols], in_=dr["dmt"][h1][:, 0:wcols]), r=dmv1n, w=dmv1n)
                for si in range(2):
                    if si * NTS + 4 * qg >= N_A1:
                        continue
                    rp0 = (ps["PD1"] if si == 0 else ps["PD2"])[0:DH, 0:512]
                    rp0b = "PD1" if si == 0 else "PD2"
                    rp1 = ps["PC"][0:DH, 0:512] if si == 0 else ps["PC"][0:DH, 512:1024]
                    rp1b = "PC0" if si == 0 else "PC1"
                    for tj in range(ntj):
                        gt = si * NTS + tj
                        ks = tj % 4
                        kvv, kvvn = carve("PBF", 8192 + 2048 * ks, 2048)
                        kv, vv = kvv[:, 0:1024], kvv[:, 1024:2048]
                        kvn = vvn = kvvn
                        dma("ck", lambda E, gt=gt, kvv=kvv: E.dma_start(
                            out=kvv.rearrange("p (a m) -> p a m", m=1024), in_=dr["sp_kv"][gt].transpose([1, 0, 2])),
                            r=["sp_kv"] + kvvn, w=kvvn)
                        off = TILE * (4 * qg - tj + DPAD)
                        for hi, (h, dmv, dmvn, rp, rpb) in enumerate(
                                ((h0, dmv0, dmv0n, rp0, rp0b), (h1, dmv1, dmv1n, rp1, rp1b))):
                            kp = tj % 2
                            sc = (ps["PA"] if hi == 0 else ps["PB"])[:, 512 * kp:512 * (kp + 1)]
                            scb = ("PA" if hi == 0 else "PB") + str(kp)
                            qrhs = qfg.rearrange("p (t m) -> p t m", m=1024)[
                                64 * (h % 2):64 * (h % 2) + 64, 4 * si:4 * si + 4,
                                TILE * (h // 2):TILE * (h // 2 + 1)]
                            tr.op("tensor", lambda E, h=h, kv=kv, sc=sc, qrhs=qrhs: E.matmul(
                                sc, kv[64 * (h % 2):64 * (h % 2) + 64, TILE * (h // 2):TILE * (h // 2 + 1)],
                                qrhs, start=True, stop=True), kvn + qfgn, [scb])
                            smv, smvn = carve("PBF", 16384 + 512 * (2 * (tj % 2) + hi), 512)
                            tr.op("vector", lambda E, sc=sc, smv=smv, dmv=dmv, off=off: E.tensor_mul(
                                smv, sc, dmv[:, off:off + 512]), [scb] + dmvn, smvn)
                            tr.op("tensor", lambda E, h=h, vv=vv, rp=rp, smv=smv, tj=tj, ntj=ntj: E.matmul(
                                rp, vv[:, DH * h:DH * (h + 1)], smv,
                                start=(tj == 0), stop=(tj == ntj - 1)), vvn + smvn, [rpb])
                    for hi, (h, rp, rpb) in enumerate(((h0, rp0, rp0b), (h1, rp1, rp1b))):
                        rfv, rfvn = carve("PF", 14336 + 512 * (2 * si + hi), 512, 0, DH)
                        tr.op("vector", lambda E, rp=rp, rfv=rfv: E.tensor_copy(rfv, rp), [rpb], rfvn)
                        for tq in range(4):
                            gt = si * NTS + 4 * qg + tq
                            if gt >= N_A1:
                                continue
                            dma("cs", lambda E, gt=gt, h=h, rfv=rfv, tq=tq: E.dma_start(
                                out=dr["sp_r"][gt, h], in_=rfv[:, TILE * tq:TILE * (tq + 1)]), r=rfvn, w=["sp_r"])

    def bcast_tile(dst_off, src_b, src_lo, plus1, dstn_out):
        dst, dstn = carve("PF", dst_off, 1024)
        dma("ct", lambda E, dst=dst, src_b=src_b, src_lo=src_lo: E.dma_start(
            out=dst, in_=dr["modv"][src_b:src_b + 1, src_lo:src_lo + 1024].partition_broadcast(TILE).squeeze(1)),
            r=["modv"] + dstn, w=dstn)
        if plus1:
            tr.op("vector", lambda E, dst=dst: E.tensor_scalar_add(dst, dst, 1.0), dstn, dstn)
        dstn_out.extend(dstn)

    def a3_tile(gt, qslot):
        bi, st = gt // NTS, gt % NTS
        stc = sb["stat"]
        rfl, rfln = carve("PF", 4096 + 2048 * (gt % 2), 2048, 0, DH)
        dma("cx", lambda E, gt=gt, rfl=rfl: E.dma_start(out=rfl.rearrange("d (h t) -> d h t", t=128), in_=dr["sp_r"][gt].transpose([1, 0, 2])),
            r=["sp_r"] + rfln, w=rfln)
        xv, xvn = carve("PF", 10240 + 1024 * (gt % 2), 1024)
        dma("cx", lambda E, bi=bi, st=st, xv=xv: E.dma_start(out=xv, in_=x_h[bi, TILE * st:TILE * (st + 1), :]),
            r=xvn, w=xvn)
        rt, rtn = carve("PF", 0, 1024)
        for h in range(H):
            pd, pb = (ps["PD1"], "PD1") if h % 2 == 0 else (ps["PD2"], "PD2")
            tr.op("tensor", lambda E, h=h, pd=pd, rfl=rfl: E.transpose(
                pd[:, 0:DH], rfl[0:DH, TILE * h:TILE * (h + 1)], sb["identf"][0:DH, 0:DH]),
                rfln + ["identf"], [pb])
            tr.op("vector", lambda E, h=h, pd=pd, rt=rt: E.tensor_copy(rt[:, DH * h:DH * (h + 1)], pd[:, 0:DH]),
                  [pb], rtn)
        rt3 = rt.rearrange("p (h u) -> p h u", u=DH)
        tr.op("vector", lambda E, rt3=rt3: E.tensor_reduce(stc[:, 0:16], rt3, AX.X, ALU.add), rtn, ["stat"])
        sq, sqn = carve("PF", 8192, 1024)
        tr.op("scalar", lambda E, rt=rt, sq=sq: E.activation(sq, rt, AF.Square), rtn, sqn)
        tr.op("vector", lambda E, sq=sq: E.tensor_reduce(stc[:, 16:32], sq.rearrange("p (h u) -> p h u", u=DH),
                                                         AX.X, ALU.add), sqn, ["stat"])
        tr.op("vector", lambda E: E.tensor_scalar_mul(stc[:, 0:16], stc[:, 0:16], 1.0 / DH), ["stat"], ["stat"])
        tr.op("vector", lambda E: E.tensor_scalar_mul(stc[:, 16:32], stc[:, 16:32], 1.0 / DH), ["stat"], ["stat"])
        tr.op("vector", lambda E: E.tensor_mul(stc[:, 32:48], stc[:, 0:16], stc[:, 0:16]), ["stat"], ["stat"])
        tr.op("vector", lambda E: E.tensor_sub(stc[:, 16:32], stc[:, 16:32], stc[:, 32:48]), ["stat"], ["stat"])
        tr.op("scalar", lambda E: E.activation(stc[:, 16:32], stc[:, 16:32], AF.Ln, bias=sb["epsc"][:]), ["stat", "epsc"], ["stat"])
        tr.op("scalar", lambda E: E.activation(stc[:, 16:32], stc[:, 16:32], AF.Exp, scale=-0.5), ["stat"], ["stat"])
        if DBG_A1 and gt == 0:
            dma("cs", lambda E, rt=rt: E.dma_start(out=dr["dbg4"], in_=rt), r=rtn, w=[])
        mu_rep = stc[:, 0:16].unsqueeze(2).broadcast_to([TILE, 16, DH])
        rs_rep = stc[:, 16:32].unsqueeze(2).broadcast_to([TILE, 16, DH])
        gn, gnn = carve("PF", 9216, 1024)
        gn3 = gn.rearrange("p (h u) -> p h u", u=DH)
        tr.op("vector", lambda E, gn3=gn3, rt3=rt3, mu_rep=mu_rep: E.tensor_sub(gn3, rt3, mu_rep),
              rtn + ["stat"], gnn)
        gnb, gnbn = carve("PBF", 9216, 1024)
        gnb3 = gnb.rearrange("p (h u) -> p h u", u=DH)
        tr.op("vector", lambda E, gnb3=gnb3, gn3=gn3, rs_rep=rs_rep: E.tensor_mul(gnb3, gn3, rs_rep),
              gnn + ["stat"], gnbn)
        if DBG_A1 and gt == 0:
            dma("cs", lambda E, gn=gn: E.dma_start(out=dr["dbg5"], in_=gn), r=gnn, w=[])
        gnF, gnFn = carve("PBF", 8192, 1024)
        for c in range(8):
            pd, pb = (ps["PD1"], "PD1") if c % 2 == 0 else (ps["PD2"], "PD2")
            tr.op("tensor", lambda E, c=c, pd=pd, gnb=gnb: E.transpose(
                pd.bitcast(BF16)[:, 0:TILE], gnb[:, TILE * c:TILE * (c + 1)],
                sb["identb"][:]), gnbn + ["identb"], [pb])
            tr.op("scalar", lambda E, c=c, pd=pd, gnF=gnF: E.activation(gnF[:, TILE * c:TILE * (c + 1)],
                                                                        pd.bitcast(BF16)[:, 0:TILE], AF.Copy), [pb], gnFn)
        for half in range(2):
            for c in range(8):
                wv, wvn = carve("PBF", D * c, D)
                tr.op("tensor", lambda E, half=half, c=c, wv=wv, gnF=gnF: E.matmul(
                    ps["PA"][:, 512 * half:512 * (half + 1)], gnF[:, TILE * c:TILE * (c + 1)],
                    wv[:, 512 * half:512 * (half + 1)], start=(c == 0), stop=False), gnFn + wvn, ["PA"])
            tr.op("tensor", lambda E, half=half: E.matmul(
                ps["PA"][:, 512 * half:512 * (half + 1)], sb["ones_bf"][0:1, 0:TILE],
                sb["bwo_bf"][0:1, 512 * half:512 * (half + 1)], start=False, stop=True),
                ["ones_bf", "bwo_bf"], ["PA"])
        x1 = sb["x1q"][:, D * qslot:D * (qslot + 1)]
        x1b = f"x1q{qslot}"
        gax, gaxn = carve("PF", 8192, 1024)
        bc1, bc1n = carve("PF", 12288, 1024)
        tr.op("vector", lambda E, gax=gax, bc1=bc1: E.tensor_mul(gax, ps["PA"][:], bc1), ["PA"] + bc1n, gaxn)
        tr.op("vector", lambda E, x1=x1, xv=xv, gax=gax: E.tensor_add(x1, xv, gax), xvn + gaxn, [x1b])
        if DBG_A1 and gt == 0:
            dma("cs", lambda E, x1=x1: E.dma_start(out=dr["dbg6"], in_=x1), r=[x1b], w=[])
        tr.op("vector", lambda E, x1=x1: E.bn_stats(stc[:, 0:6], x1[:, 0:512]), [x1b], ["stat"])
        tr.op("vector", lambda E, x1=x1: E.bn_stats(stc[:, 6:12], x1[:, 512:1024]), [x1b], ["stat"])
        tr.op("vector", lambda E: E.bn_aggr(stc[:, 12:14], stc[:, 0:12]), ["stat"], ["stat"])
        tr.op("scalar", lambda E: E.activation(stc[:, 14:15], stc[:, 13:14], AF.Ln, bias=sb["epsc"][:]), ["stat", "epsc"], ["stat"])
        tr.op("scalar", lambda E, qslot=qslot: E.activation(sb["ln2rstd"][:, qslot:qslot + 1], stc[:, 14:15],
                                                            AF.Exp, scale=-0.5), ["stat"], ["ln2rstd"])
        tr.op("vector", lambda E, qslot=qslot: E.tensor_scalar(sb["ln2nmr"][:, qslot:qslot + 1], stc[:, 12:13],
                                                               sb["ln2rstd"][:, qslot:qslot + 1], -1.0,
                                                               ALU.mult, ALU.mult), ["stat", "ln2rstd"], ["ln2nmr"])

    MAGIC = 12582912.0

    def b1_tile(quad, tq):
        stc = sb["stat"]
        bc1, bc1n = carve("PF", 3072, 1024)
        bc2, bc2n = carve("PF", 13312, 1024)
        x1 = sb["x1q"][:, D * tq:D * (tq + 1)]
        x1b = f"x1q{tq}"
        xn2, xn2n = carve("PF", 2048, 1024)
        tr.op("scalar", lambda E, x1=x1, tq=tq, xn2=xn2: E.activation(
            xn2, x1, AF.Identity, bias=sb["ln2nmr"][:, tq:tq + 1], scale=sb["ln2rstd"][:, tq:tq + 1]),
            [x1b, "ln2rstd", "ln2nmr"], xn2n)
        tr.op("vector", lambda E, xn2=xn2, bc1=bc1: E.tensor_mul(xn2, xn2, bc1), xn2n + bc1n, xn2n)
        tr.op("vector", lambda E, xn2=xn2, bc2=bc2: E.tensor_add(xn2, xn2, bc2), xn2n + bc2n, xn2n)
        tr.op("vector", lambda E, xn2=xn2: E.reduce_max(stc[:, 48:49], xn2, axis=AX.X, apply_absolute_value=True),
              xn2n, ["statb"])
        tr.op("vector", lambda E: E.tensor_single_scalar(stc[:, 49:50], stc[:, 48:49], 1e-5, ALU.max),
              ["statb"], ["statb"])
        tr.op("vector", lambda E: E.reciprocal(stc[:, 50:51], stc[:, 49:50]), ["statb"], ["statb"])
        tr.op("vector", lambda E: E.tensor_scalar_mul(stc[:, 51:52], stc[:, 50:51], 127.0), ["statb"], ["statb"])
        tr.op("vector", lambda E: E.tensor_scalar_mul(stc[:, 52:53], stc[:, 49:50], 1.0 / 127.0), ["statb"], ["statb"])
        tr.op("vector", lambda E, tq=tq: E.tensor_mul(stc[:, 56 + tq:57 + tq], stc[:, 52:53], sb["invsw"][:, 0:1]),
              ["statb", "invsw"], ["esg"])
        tr.op("vector", lambda E, tq=tq: E.tensor_mul(stc[:, 60 + tq:61 + tq], stc[:, 52:53], sb["invsw"][:, 1:2]),
              ["statb", "invsw"], ["esg"])
        if DBG_A1 and quad == 0 and tq == 0:
            dma("cs", lambda E, xn2=xn2: E.dma_start(out=dr["dbg7"], in_=xn2), r=xn2n, w=[])
        q1, q1n = carve("PF", 1024, 1024)
        tr.op("vector", lambda E, xn2=xn2, q1=q1: E.tensor_scalar(q1, xn2, stc[:, 51:52], MAGIC, ALU.mult, ALU.add),
              xn2n + ["statb"], q1n)
        tr.op("vector", lambda E, q1=q1: E.tensor_scalar_add(q1, q1, -MAGIC), q1n, q1n)
        q1b, q1bn = carve("PBF", 47104, 1024)
        tr.op("vector", lambda E, q1=q1, q1b=q1b: E.tensor_scalar(q1b, q1, -128.0, 127.0, ALU.max, ALU.min),
              q1n, q1bn)
        if DBG_A1 and quad == 0 and tq == 0:
            dma("cs", lambda E, q1=q1: E.dma_start(out=dr["dbg8"], in_=q1), r=q1n, w=[])
        xqF, xqFn = carve("PBF", 32768 + 1024 * tq, 1024)
        for c in range(8):
            po, pb = (0, "PC0") if c % 2 == 0 else (1024, "PC1")
            tr.op("tensor", lambda E, c=c, po=po, q1b=q1b: E.transpose(
                ps["PC"].bitcast(BF16)[:, po:po + TILE], q1b[:, TILE * c:TILE * (c + 1)],
                sb["identb"][:]), q1bn + ["identb"], [pb])
            tr.op("scalar", lambda E, c=c, po=po, xqF=xqF: E.activation(
                xqF[:, TILE * c:TILE * (c + 1)], ps["PC"].bitcast(BF16)[:, po:po + TILE], AF.Copy), [pb], xqFn)

    def b_quad(quad):
        bi = (4 * quad) // NTS
        stc = sb["stat"]

        for e8 in range(8):
            ks = e8 % 2
            tgv, tgvn = carve("PBF", 0 + 4096 * ks, 4096)
            tuv, tuvn = carve("PBF", 8192 + 4096 * ks, 4096)
            dma("ct", lambda E, e8=e8, tgv=tgv: E.dma_start(out=tgv.rearrange("p (c m) -> p c m", m=512), in_=dr["tg"][e8].transpose([1, 0, 2])),
                r=tgvn, w=tgvn)
            dma("ct", lambda E, e8=e8, tuv=tuv: E.dma_start(out=tuv.rearrange("p (c m) -> p c m", m=512), in_=dr["tu"][e8].transpose([1, 0, 2])),
                r=tuvn, w=tuvn)
            for tq in range(4):
                xqF, xqFn = carve("PBF", 32768 + 1024 * tq, 1024)
                par = (4 * e8 + tq) % 2
                pg, pgb = (ps["PA"], "PA") if par == 0 else (ps["PB"], "PB")
                for c in range(8):
                    tr.op("tensor", lambda E, c=c, tgv=tgv, xqF=xqF, pg=pg: E.matmul(
                        pg[:, 0:512], xqF[:, TILE * c:TILE * (c + 1)], tgv[:, 512 * c:512 * (c + 1)],
                        start=(c == 0), stop=(c == 7)), xqFn + tgvn, [pgb + "0"])
                    tr.op("tensor", lambda E, c=c, tuv=tuv, xqF=xqF, pg=pg: E.matmul(
                        pg[:, 512:1024], xqF[:, TILE * c:TILE * (c + 1)], tuv[:, 512 * c:512 * (c + 1)],
                        start=(c == 0), stop=(c == 7)), xqFn + tuvn, [pgb + "1"])
                sg, sgn = carve("PF", 8192 + 512 * par, 512)
                tr.op("scalar", lambda E, tq=tq, sg=sg, pg=pg: E.activation(sg, pg[:, 0:512], AF.Silu,
                                                                     scale=stc[:, 56 + tq:57 + tq]),
                      [pgb + "0", "esg"], sgn)
                a2v, a2vn = carve("PBF", 16384 + 4096 * tq + 512 * e8, 512)
                tr.op("vector", lambda E, tq=tq, a2v=a2v, sg=sg, pg=pg: E.scalar_tensor_tensor(
                    a2v, pg[:, 512:1024], stc[:, 60 + tq:61 + tq], sg, ALU.mult, ALU.mult),
                    [pgb + "1", "esg"] + sgn, a2vn)
                tr.op("vector", lambda E, tq=tq, e8=e8, a2v=a2v: E.reduce_max(
                    stc[:, 64 + tq:65 + tq] if e8 == 0 else stc[:, 68 + tq:69 + tq], a2v,
                    axis=AX.X, apply_absolute_value=True), a2vn, ["amx"])
                if e8 > 0:
                    tr.op("vector", lambda E, tq=tq: E.tensor_max(stc[:, 64 + tq:65 + tq], stc[:, 64 + tq:65 + tq],
                                                                  stc[:, 68 + tq:69 + tq]), ["amx"], ["amx"])

        if DBG_A1 and quad == 0:
            dma("cs", lambda E: E.dma_start(out=dr["dbg12"], in_=sb["stat"][:, 64:80]), r=["amx"], w=[])
            a2full, a2fn = carve("PBF", 16384, 4096)
            dma("cs", lambda E, a2full=a2full: E.dma_start(out=dr["dbg9"], in_=a2full),
                r=[f"a2_0"] + a2fn, w=[])
        for pair in range(2):
            for tq2 in range(2):
                tq = 2 * pair + tq2
                tr.op("vector", lambda E, tq=tq: E.tensor_single_scalar(stc[:, 72:73], stc[:, 64 + tq:65 + tq],
                                                                        1e-5, ALU.max), ["amx"], ["statq"])
                tr.op("vector", lambda E: E.reciprocal(stc[:, 73:74], stc[:, 72:73]), ["statq"], ["statq"])
                tr.op("vector", lambda E: E.tensor_scalar_mul(stc[:, 74:75], stc[:, 73:74], 127.0), ["statq"], ["statq"])
                tr.op("vector", lambda E: E.tensor_scalar_mul(stc[:, 75:76], stc[:, 72:73], 1.0 / 127.0), ["statq"], ["statq"])
                tr.op("vector", lambda E, tq=tq: E.tensor_mul(stc[:, 80 + tq:81 + tq], stc[:, 75:76], sb["invsw"][:, 2:3]),
                      ["statq", "invsw"], ["es2q"])
                xq2F, xq2Fn = carve("PBF", 36864 + 4096 * tq2, 4096)
                for qtr in range(4):
                    a2v, a2vn = carve("PBF", 16384 + 4096 * tq + 1024 * qtr, 1024)
                    q1, q1n = carve("PF", 1024, 1024)
                    tr.op("vector", lambda E, a2v=a2v, q1=q1: E.tensor_scalar(
                        q1, a2v, stc[:, 74:75], MAGIC, ALU.mult, ALU.add), a2vn + ["statq"], q1n)
                    if DBG_A1 and quad == 0 and tq == 0:
                        dma("cs", lambda E, q1=q1, qtr=qtr: E.dma_start(
                            out=dr["dbg11"][:, 1024 * qtr:1024 * (qtr + 1)], in_=q1), r=q1n, w=[])
                    tr.op("vector", lambda E, q1=q1: E.tensor_scalar_add(q1, q1, -MAGIC), q1n, q1n)
                    q1b, q1bn = carve("PBF", 48128, 1024)
                    tr.op("vector", lambda E, q1=q1, q1b=q1b: E.tensor_scalar(
                        q1b, q1, -128.0, 127.0, ALU.max, ALU.min), q1n, q1bn)
                    if DBG_A1 and quad == 0 and tq == 0:
                        dma("cs", lambda E, q1=q1, qtr=qtr: E.dma_start(
                            out=dr["dbg10"][:, 1024 * qtr:1024 * (qtr + 1)], in_=q1), r=q1n, w=[])
                    for c in range(8):
                        pd, pb = (ps["PD1"], "PD1") if c % 2 == 0 else (ps["PD2"], "PD2")
                        tr.op("tensor", lambda E, c=c, pd=pd, q1b=q1b: E.transpose(
                            pd.bitcast(BF16)[:, 0:TILE], q1b[:, TILE * c:TILE * (c + 1)],
                            sb["identb"][:]), q1bn + ["identb"], [pb])
                        tr.op("scalar", lambda E, c=c, qtr=qtr, pd=pd, xq2F=xq2F: E.activation(
                            xq2F[:, 1024 * qtr + TILE * c:1024 * qtr + TILE * (c + 1)],
                            pd.bitcast(BF16)[:, 0:TILE], AF.Copy), [pb], xq2Fn)
            for k in range(32):
                wv, wvn = carve("PBF", 45056 + 1024 * (k % 2), 1024)
                dma("ck", lambda E, k=k, wv=wv: E.dma_start(out=wv, in_=dr["td"][k]), r=wvn, w=wvn)
                for tq2 in range(2):
                    xq2F, xq2Fn = carve("PBF", 36864 + 4096 * tq2, 4096)
                    pn, pb = (ps["PA"], "PA") if tq2 == 0 else (ps["PB"], "PB")
                    for half in range(2):
                        tr.op("tensor", lambda E, k=k, half=half, pn=pn, xq2F=xq2F, wv=wv: E.matmul(
                            pn[:, 512 * half:512 * (half + 1)], xq2F[:, TILE * k:TILE * (k + 1)],
                            wv[:, 512 * half:512 * (half + 1)], start=(k == 0), stop=(k == 31)),
                            xq2Fn + wvn, [pb + str(half)])
            gb, gbn = carve("PF", 14336, 1024)
            _ = []
            bcast_tile(14336, bi, 5 * D, False, _)
            for tq2 in range(2):
                tq = 2 * pair + tq2
                gt = 4 * quad + tq
                bi3, st3 = gt // NTS, gt % NTS
                pn, pb = (ps["PA"], "PA") if tq2 == 0 else (ps["PB"], "PB")
                x1 = sb["x1q"][:, D * tq:D * (tq + 1)]
                x1b = f"x1q{tq}"
                tmp, tmpn = carve("PF", 2048, 1024)
                for half in range(2):
                    tr.op("vector", lambda E, pn=pn, half=half, tq=tq, tmp=tmp, gb=gb: E.scalar_tensor_tensor(
                        tmp[:, 512 * half:512 * (half + 1)], pn[:, 512 * half:512 * (half + 1)],
                        stc[:, 80 + tq:81 + tq], gb[:, 512 * half:512 * (half + 1)], ALU.mult, ALU.mult),
                        [pb + str(half), "es2q"] + gbn, tmpn)
                tr.op("vector", lambda E, x1=x1, tmp=tmp: E.tensor_add(x1, x1, tmp), [x1b] + tmpn, [x1b])
                dma("co", lambda E, bi3=bi3, st3=st3, x1=x1: E.dma_start(
                    out=out_h[bi3, TILE * st3:TILE * (st3 + 1), :], in_=x1), r=[x1b], w=[])

    prep()
    load_wqkv(0)
    wo_prep()
    qgen = quant_units()
    qdone = False
    for gt in range(N_A1):
        if gt == NTS:
            load_wqkv(1)
        a1_tile(gt)
        for _ in range(7):
            try:
                next(qgen)()
            except StopIteration:
                qdone = True
                break
    while not qdone:
        try:
            next(qgen)()
        except StopIteration:
            qdone = True
    a2()
    for quad in range(N_QUAD):
        load_wo()
        bi = (4 * quad) // NTS
        _ = []
        bcast_tile(12288, bi, 2 * D, False, _)
        bcast_tile(3072, bi, 4 * D, True, _)
        bcast_tile(13312, bi, 3 * D, False, _)
        for tq in range(4):
            a3_tile(4 * quad + tq, tq)
            b1_tile(quad, tq)
        b_quad(quad)

    ops = tr.finalize()
    co_waits = [(k, v) for k, v in tr.dcnt.items() if k[1] == "co"]
    if co_waits:
        ops["sync"].append((sorted(co_waits), lambda E: E.nop(), None))
    nds = 0
    for k in tr.dcnt:
        if k not in sems:
            sems[k] = stack.enter_context(nc.semaphore(f"sd{nds}"))
            nds += 1
    assert nds + len(Tr.ENG) <= 100, f"too many semaphores: {nds}"

    with nc.Block() as blk0:
        def clear(E):
            for s in sems.values():
                E.sem_clear(s)
        blk0.sync(clear)

    with nc.Block() as blk:
        def mk(ename):
            myops = ops[ename]

            def run(E):
                for waits, fn, inc in myops:
                    for k, v in waits:
                        E.wait_ge(sems[k], v)
                    inst = fn(E)
                    if inc is not None:
                        inst.then_inc(sems[inc[0]], inc[1])
            return run
        blk.tensor(mk("tensor"))
        blk.vector(mk("vector"))
        blk.scalar(mk("scalar"))
        blk.gpsimd(mk("gpsimd"))
        blk.sync(mk("sync"))

    n_inst = sum(len(v) for v in ops.values())
    build.sb_info = {k: v for k, v in sb.items()}
    return nc, stack, n_inst


def make_inputs(full):
    tabs = host_tables()
    WQ, WK, WV = full["L0_attn_WQ"], full["L0_attn_WK"], full["L0_attn_WV"]
    wqkvT = np.concatenate([w.transpose(2, 0, 1).reshape(D, D) for w in (WQ, WK, WV)], axis=1)
    wqkv4 = np.ascontiguousarray(wqkvT.reshape(8, TILE, 3 * D))
    wo4 = np.ascontiguousarray(full["L0_attn_WO"].reshape(8, TILE, D))
    wmod4 = np.ascontiguousarray(full["L0_wmod"].reshape(8, TILE, 12, 512).transpose(2, 0, 1, 3))
    wg4 = np.ascontiguousarray(full["L0_ffn_w_gate"].T.reshape(8, TILE, INNER))
    wu4 = np.ascontiguousarray(full["L0_ffn_w_up"].T.reshape(8, TILE, INNER))
    wd4 = np.ascontiguousarray(full["L0_ffn_w_down"].T.reshape(32, TILE, D))
    gammaF = np.ascontiguousarray(full["L0_attn_gamma"].reshape(8, TILE).T)
    betaF = np.ascontiguousarray(full["L0_attn_beta"].reshape(8, TILE).T)
    x = full["L0_attn_x"]
    c = full["L0_c"]
    shared = dict(wqkv4=wqkv4, wo4=wo4, wmod4=wmod4, wg4=wg4, wu4=wu4, wd4=wd4,
                  gammaF=gammaF, betaF=betaF, ctbl=tabs["ctbl"], stbl=tabs["stbl"],
                  dmt=tabs["dmt"], identf=tabs["ident"])
    in_maps = []
    for core in range(8):
        cs = c[2 * core:2 * core + 2]
        cF = np.ascontiguousarray(cs.reshape(2, 8, TILE).transpose(2, 1, 0).reshape(TILE, 16))
        m = dict(shared)
        m["x"] = np.ascontiguousarray(x[2 * core:2 * core + 2])
        m["cF"] = cF
        in_maps.append(m)
    return in_maps


def kernel(**inputs):
    in_maps = make_inputs(inputs)
    nc, stack, n_inst = build()
    res = run_bass_kernel_spmd(nc, in_maps, core_ids=list(range(8)))
    outs = [res.results[i]["out"] for i in range(8)]
    full = np.concatenate(outs, axis=0).astype(np.float32)
    if SPILL_OUT:
        return full, res
    stack.close()
    return full



# version 11
# speedup vs baseline: 1.1775x; 1.0024x over previous
import contextlib
import numpy as np
import concourse.bass as bass
import concourse.mybir as mybir
from concourse.bass_utils import run_bass_kernel_spmd

F32 = mybir.dt.float32
BF16 = mybir.dt.bfloat16
I32 = mybir.dt.int32
AF = mybir.ActivationFunctionType
ALU = mybir.AluOpType
AX = mybir.AxisListType

S, D, H, DH = 2048, 1024, 16, 64
TILE = 128
NTS = S // TILE
NT = 2 * NTS
NQG = NTS // 4
INNER = 4096
EPS = 1e-5
DPAD = 3

N_A1 = NT
N_QG = NQG
N_QUAD = NT // 4
SPILL_OUT = False
DBG_A1 = False


def host_tables():
    import jax
    with jax.default_device(jax.devices("cpu")[0]):
        import jax.numpy as jnp
        expoent = jnp.arange(0, DH, 2, dtype=jnp.float32) / DH
        angle = 10.0 ** (-4.0 * expoent)
        pos = jnp.arange(0, S, dtype=jnp.float32)
        nangle = pos[:, None] * angle[None, :]
        sin = np.asarray(jnp.sin(nangle))
        cos = np.asarray(jnp.cos(nangle))
        scale = 1.0 - 2.0 ** (-5.0 - jnp.arange(H, dtype=jnp.float32))
        logs = jnp.log(scale)
        delta = jnp.arange(S, dtype=jnp.float32)
        logd = delta[None, :] * logs[:, None]
        d32 = np.asarray(jnp.exp(logd))

    ctbl = np.zeros((TILE, NTS * DH), np.float32)
    stbl = np.zeros((TILE, NTS * DH), np.float32)
    for st in range(NTS):
        c = cos[st * TILE:(st + 1) * TILE]
        s = sin[st * TILE:(st + 1) * TILE]
        ctbl[:, st * DH:(st + 1) * DH] = np.repeat(c, 2, axis=1)
        blk = np.empty((TILE, DH), np.float32)
        blk[:, 0::2] = -s
        blk[:, 1::2] = s
        stbl[:, st * DH:(st + 1) * DH] = blk

    ndt = DPAD + NTS
    dmt = np.zeros((H, TILE, ndt * TILE), np.float32)
    jj = np.arange(TILE)[:, None]
    ii = np.arange(TILE)[None, :]
    for t in range(ndt):
        dlt = (t - DPAD) * TILE + ii - jj
        valid = (dlt >= 0) & (dlt < S)
        dl = np.clip(dlt, 0, S - 1)
        for h in range(H):
            dmt[h, :, t * TILE:(t + 1) * TILE] = d32[h][dl] * valid
    ident = np.eye(TILE, dtype=np.float32)
    return dict(ctbl=ctbl, stbl=stbl, dmt=dmt, ident=ident)


class Tr:
    ENG = ("tensor", "vector", "scalar", "gpsimd", "sync")
    DMA_Q = {"cx": "sync", "ct": "sync", "ck": "scalar", "cw": "gpsimd", "cs": "gpsimd", "co": "gpsimd"}
    SELFWAIT = {"vector", "scalar", "gpsimd"}

    def __init__(self):
        self.recs = []
        self.lastw = {}
        self.readers = {}
        self.openg = []
        self.opkey = []

    PSPLIT = {"PA": ("PA0", "PA1"), "PB": ("PB0", "PB1"), "PC": ("PC0", "PC1")}

    def _norm(self, bufs):
        out = []
        for b in bufs:
            if b in self.PSPLIT:
                out.extend(self.PSPLIT[b])
            else:
                out.append(b)
        return out

    def op(self, eng, fn, reads=(), writes=(), dma=None):
        reads = self._norm(reads)
        writes = self._norm(writes)
        deps = set()
        myid = len(self.recs)

        def need(pid):
            if pid is None:
                return
            peng = self.openg[pid]
            pkey = self.opkey[pid]
            if pkey is None and peng == eng and eng not in self.SELFWAIT:
                return
            deps.add(pid)

        for b in reads:
            need(self.lastw.get(b))
        for b in writes:
            need(self.lastw.get(b))
            for r in self.readers.get(b, ()):
                need(r)
        key = None
        if dma is not None:
            assert self.DMA_Q[dma] == eng
            slot = tuple(sorted(writes)) if writes else ("st",) + tuple(sorted(reads))
            key = ("D", dma, slot)
        self.openg.append(eng)
        self.opkey.append(key)
        for b in writes:
            self.lastw[b] = myid
            self.readers[b] = []
        for b in reads:
            self.readers.setdefault(b, []).append(myid)
        self.recs.append((eng, fn, key, deps))

    def finalize(self):
        n = len(self.recs)
        needed = [False] * n
        for (_, _, key, deps) in self.recs:
            for d in deps:
                needed[d] = True
        val = [0] * n
        ecnt = {e: 0 for e in self.ENG}
        dcnt = {}
        for i, (eng, fn, key, deps) in enumerate(self.recs):
            if key is not None:
                dcnt[key] = dcnt.get(key, 0) + 16
                val[i] = dcnt[key]
            elif needed[i]:
                ecnt[eng] += 1
                val[i] = ecnt[eng]
        ops = {e: [] for e in self.ENG}
        waited = {e: {} for e in self.ENG}
        for i, (eng, fn, key, deps) in enumerate(self.recs):
            waits = {}
            for d in deps:
                dk = self.opkey[d]
                sk = dk if dk is not None else ("E", self.openg[d])
                v = val[d]
                if waited[eng].get(sk, 0) >= v:
                    continue
                if waits.get(sk, 0) < v:
                    waits[sk] = v
            for k2, v2 in waits.items():
                waited[eng][k2] = v2
            if key is not None:
                inc = (key, 16)
            elif needed[i]:
                inc = (("E", eng), 1)
            else:
                inc = None
            ops[eng].append((sorted(waits.items()), fn, inc))
        self.dcnt = dcnt
        return ops


def build():
    tabs = host_tables()
    nc = bass.Bass("TRN2", target_bir_lowering=False, detect_race_conditions=False)
    stack = contextlib.ExitStack()
    dr = {}

    def din(name, shape, dt=F32):
        dr[name] = nc.dram_tensor(name, list(shape), dt, kind="ExternalInput").ap()

    def dint(name, shape, dt):
        kind = "ExternalOutput" if SPILL_OUT else "Internal"
        dr[name] = nc.dram_tensor(name, list(shape), dt, kind=kind).ap()

    din("x", [2, S, D])
    dr["out"] = nc.dram_tensor("out", [2, S, D], F32, kind="ExternalOutput").ap()
    din("cF", [TILE, 16])
    din("wmod4", [12, 8, TILE, 512])
    din("wqkv4", [8, TILE, 3 * D])
    din("wo4", [8, TILE, D])
    din("gammaF", [TILE, 8])
    din("betaF", [TILE, 8])
    din("wg4", [8, TILE, INNER])
    din("wu4", [8, TILE, INNER])
    din("wd4", [32, TILE, D])
    din("ctbl", [TILE, NTS * DH])
    din("stbl", [TILE, NTS * DH])
    din("dmt", [H, TILE, (DPAD + NTS) * TILE])
    din("identf", [TILE, TILE])
    dint("sp_q", [NT, TILE, D], BF16)
    dint("sp_kv", [NT, 2, TILE, D], BF16)
    dint("sp_r", [NT, H, DH, TILE], F32)
    dint("modv", [2, 6 * D], F32)
    if DBG_A1:
        for i in (1, 2, 3):
            dint(f"dbg{i}", [TILE, D], F32)
        for i in (4, 5, 6, 7, 8):
            dint(f"dbg{i}", [TILE, D], F32)
        for i in (9, 10, 11):
            dint(f"dbg{i}", [TILE, INNER], F32)
        dint("dbg12", [TILE, 16], F32)
    dint("tg", [8, 8, TILE, 512], BF16)
    dint("tu", [8, 8, TILE, 512], BF16)
    dint("td", [32, TILE, D], BF16)
    dint("wof", [8, TILE, D], BF16)

    sb = {}

    def sbuf(name, shape, dt):
        sb[name] = stack.enter_context(nc.sbuf_tensor("sb_" + name, list(shape), dt))

    sbuf("identf", [TILE, TILE], F32)
    sbuf("ctbl", [TILE, NTS * DH], F32)
    sbuf("stbl", [TILE, NTS * DH], F32)
    sbuf("ones_bf", [1, 512], BF16)
    sbuf("onesf_col", [TILE, 1], F32)
    sbuf("onesf_row", [1, TILE], F32)
    sbuf("cF", [TILE, 16], F32)
    sbuf("silu_cF", [TILE, 16], F32)
    sbuf("gammaF", [TILE, 8], F32)
    sbuf("betaF_bf", [TILE, 8], BF16)
    sbuf("modF", [TILE, 32], F32)
    sbuf("saF_bf", [TILE, 16], BF16)
    sbuf("caF1p", [TILE, 16], F32)
    sbuf("saw_bf", [1, 3 * D], BF16)
    sbuf("bwo_bf", [1, D], BF16)
    sbuf("scw", [TILE, 3], F32)
    sbuf("invsw", [TILE, 3], F32)
    sbuf("accw", [TILE, 48], F32)
    sbuf("stat", [TILE, 96], F32)
    sbuf("ln2rstd", [TILE, 4], F32)
    sbuf("ln2nmr", [TILE, 4], F32)
    sbuf("epsc", [TILE, 1], F32)
    sbuf("x1q", [TILE, 4 * D], F32)
    sbuf("PF", [TILE, 16384], F32)
    sbuf("identb", [TILE, TILE], BF16)
    sbuf("PBF", [TILE, 49152], BF16)
    sbuf("PI", [TILE, 1024], I32)

    SEG = {"PF": 512, "PBF": 1024, "PI": 512}

    def carve(pool, off, n, p0=0, p1=TILE):
        ap = sb[pool][p0:p1, off:off + n]
        s0, s1 = off // SEG[pool], (off + n - 1) // SEG[pool]
        names = [f"{pool}:{i}" for i in range(s0, s1 + 1)]
        return ap, names

    ps = {}
    for n, shp in (("PA", [TILE, 1024]), ("PB", [TILE, 1024]), ("PC", [TILE, 1024]),
                   ("PD1", [TILE, 512]), ("PD2", [TILE, 512])):
        ps[n] = stack.enter_context(nc.psum_tensor(n, shp, F32))

    tr = Tr()
    sems = {}
    for e in Tr.ENG:
        sems[("E", e)] = stack.enter_context(nc.semaphore("se_" + e))

    x_h, out_h = dr["x"], dr["out"]

    def dma(chan, fn, r=(), w=()):
        tr.op(Tr.DMA_Q[chan], fn, r, w, dma=chan)

    def rnd_nearest(src_ap, src_n, w_slices, rbufs):
        n = src_n
        for lo in range(0, n, 1024):
            wdt = min(1024, n - lo)
            sl = src_ap[:, lo:lo + wdt]
            ri, rin = carve("PI", 0, wdt)
            dv, dvn = carve("PF", FD_RND, wdt)
            bv, bvn = carve("PF", FD_RND + 1024, wdt)
            tr.op("vector", lambda E, sl=sl, ri=ri: E.tensor_copy(ri, sl), w_slices, rin)
            tr.op("vector", lambda E, sl=sl, ri=ri, dv=dv: E.tensor_sub(dv, sl, ri), list(w_slices) + rin, dvn)
            tr.op("vector", lambda E, dv=dv, bv=bv: E.tensor_single_scalar(bv, dv, -0.5, ALU.is_lt), dvn, bvn)
            tr.op("vector", lambda E, sl=sl, dv=dv, ri=ri: E.scalar_tensor_tensor(sl, dv, 0.5, ri, ALU.is_gt, ALU.add),
                  dvn + rin, w_slices)
            tr.op("vector", lambda E, sl=sl, bv=bv: E.tensor_sub(sl, sl, bv), list(w_slices) + bvn, w_slices)

    FD_RND = 14336

    def prep():
        for nm in ("identf", "ctbl", "stbl", "cF", "gammaF"):
            dma("cx", lambda E, nm=nm: E.dma_start(out=sb[nm][:], in_=dr[nm]), w=[nm])
        dma("cw", lambda E: E.dma_start(out=sb["betaF_bf"][:], in_=dr["betaF"]), w=["betaF_bf"])
        tr.op("vector", lambda E: E.tensor_copy(sb["identb"][:], sb["identf"][:]), ["identf"], ["identb"])
        tr.op("vector", lambda E: E.memset(sb["ones_bf"][:], 1.0), (), ["ones_bf"])
        tr.op("vector", lambda E: E.memset(sb["onesf_col"][:], 1.0), (), ["onesf_col"])
        tr.op("vector", lambda E: E.memset(sb["onesf_row"][:], 1.0), (), ["onesf_row"])
        tr.op("vector", lambda E: E.memset(sb["epsc"][:], EPS), (), ["epsc"])
        tr.op("scalar", lambda E: E.activation(sb["silu_cF"][:], sb["cF"][:], AF.Silu), ["cF"], ["silu_cF"])

        def mtmp(lo, n):
            if lo < 4096:
                assert lo + n <= 4096
                return carve("PF", lo, n, 0, 2)
            return carve("PF", 4096 + (lo - 4096), n, 0, 2)

        wmb, wmbn = carve("PF", 10240, 4096)
        for n in range(12):
            pn, pb = (ps["PA"], "PA") if n % 2 == 0 else (ps["PB"], "PB")
            dma("ct", lambda E, n=n, wmb=wmb: E.dma_start(out=wmb.rearrange("p (c m) -> p c m", m=512), in_=dr["wmod4"][n].transpose([1, 0, 2])),
                r=wmbn, w=wmbn)
            for c in range(8):
                tr.op("tensor", lambda E, c=c, pn=pn, wmb=wmb: E.matmul(
                    pn[0:2, 0:512], sb["silu_cF"][:, 2 * c:2 * c + 2], wmb[:, 512 * c:512 * (c + 1)],
                    start=(c == 0), stop=(c == 7)), ["silu_cF"] + wmbn, [pb])
            mt, mtn = mtmp(512 * n, 512)
            tr.op("vector", lambda E, pn=pn, mt=mt: E.tensor_copy(mt, pn[0:2, 0:512]), [pb], mtn)

        ct, ctn = carve("PF", 8192, 1024, 0, 2)
        cm, cmn = carve("PF", 9216, 1024, 0, 2)
        for lo in (2048, 5120):
            mt, mtn = mtmp(lo, 1024)
            tr.op("vector", lambda E, mt=mt, ct=ct: E.tensor_single_scalar(ct, mt, 0.0, ALU.is_ge), mtn, ctn)
            tr.op("vector", lambda E, ct=ct: E.tensor_scalar(ct, ct, 2.0, -1.0, ALU.mult, ALU.add), ctn, ctn)
            tr.op("scalar", lambda E, mt=mt, cm=cm: E.activation(cm, mt, AF.Abs), mtn, cmn)
            tr.op("vector", lambda E, cm=cm: E.tensor_single_scalar(cm, cm, 0.25, ALU.max), cmn, cmn)
            tr.op("vector", lambda E, mt=mt, ct=ct, cm=cm: E.tensor_mul(mt, ct, cm), ctn + cmn, mtn)

        for vi, base in ((0, 0), (1, 1024)):
            for c in range(8):
                mt, mtn = mtmp(base + TILE * c, TILE)
                tr.op("tensor", lambda E, mt=mt: E.transpose(ps["PC"][:, 0:2], mt, sb["identf"][0:2, 0:2]),
                      mtn + ["identf"], ["PC"])
                tr.op("vector", lambda E, vi=vi, c=c: E.tensor_copy(sb["modF"][:, 16 * vi + 2 * c:16 * vi + 2 * c + 2],
                                                                    ps["PC"][:, 0:2]), ["PC"], ["modF"])
        tr.op("vector", lambda E: E.tensor_copy(sb["saF_bf"][:], sb["modF"][:, 0:16]), ["modF"], ["saF_bf"])
        tr.op("vector", lambda E: E.tensor_scalar_add(sb["caF1p"][:], sb["modF"][:, 16:32], 1.0), ["modF"], ["caF1p"])

        for bi in range(2):
            for seg in range(3):
                mt, mtn = mtmp(2048 * seg, 2048)
                dma("cs", lambda E, bi=bi, seg=seg, mt=mt: E.dma_start(
                    out=dr["modv"][bi:bi + 1, 2048 * seg:2048 * (seg + 1)], in_=mt[bi:bi + 1, :]), r=mtn, w=["modv"])

        pass

    def quant_units():
        nch = {"g": 8, "u": 8, "d": 32}
        wsrc = {"g": "wg4", "u": "wu4", "d": "wd4"}
        wid = {"g": 0, "u": 1, "d": 2}
        ccol = {"g": 0, "u": 8, "d": 16}
        nsl = {"g": 4, "u": 4, "d": 1}
        wcs, wcsn = carve("PF", 12288, 1024)
        ab2, ab2n = carve("PF", 13312, 1024)
        qtb, qtbn = carve("PBF", 28672, 1024)

        for m in ("g", "u", "d"):
            for c in range(nch[m]):
                for sl in range(nsl[m]):
                    def u1(m=m, c=c, sl=sl):
                        dma("ct", lambda E: E.dma_start(out=wcs, in_=dr[wsrc[m]][c][:, 1024 * sl:1024 * (sl + 1)]),
                            r=wcsn, w=wcsn)
                        col = 32 + sl
                        if (c + sl) % 2 == 0:
                            tr.op("scalar", lambda E, col=col: E.activation(
                                ab2, wcs, AF.Abs, accum_out=sb["accw"][:, col:col + 1]),
                                wcsn, ab2n + ["accw"])
                        else:
                            tr.op("vector", lambda E, col=col: E.reduce_sum(
                                sb["accw"][:, col:col + 1], wcs, axis=AX.X, apply_absolute_value=True),
                                wcsn, ["accw"])
                        if sl == nsl[m] - 1:
                            if nsl[m] > 1:
                                tr.op("vector", lambda E: E.tensor_reduce(
                                    sb["accw"][:, ccol[m] + c:ccol[m] + c + 1],
                                    sb["accw"][:, 32:32 + nsl[m]], AX.X, ALU.add), ["accw"], ["accw"])
                            else:
                                tr.op("vector", lambda E: E.tensor_copy(
                                    sb["accw"][:, ccol[m] + c:ccol[m] + c + 1], sb["accw"][:, 32:33]),
                                    ["accw"], ["accw"])
                    yield u1

        for m in ("g", "u", "d"):
            def us(m=m):
                tr.op("tensor", lambda E: E.matmul(ps["PD1"][0:1, 0:nch[m]], sb["onesf_col"][:],
                                                   sb["accw"][:, ccol[m]:ccol[m] + nch[m]], start=True, stop=True),
                      ["onesf_col", "accw"], ["PD1"])
                tr.op("vector", lambda E: E.reduce_sum(sb["stat"][0:1, 84:85], ps["PD1"][0:1, 0:nch[m]], axis=AX.X),
                      ["PD1"], ["statp"])
                tr.op("vector", lambda E: E.tensor_scalar(sb["stat"][0:1, 85:86], sb["stat"][0:1, 84:85],
                                                          1.0 / (INNER * D), 1e-5, ALU.mult, ALU.max),
                      ["statp"], ["statp"])
                tr.op("vector", lambda E: E.reciprocal(sb["stat"][0:1, 86:87], sb["stat"][0:1, 85:86]),
                      ["statp"], ["statp"])
                for (srcc, dst) in ((86, "scw"), (85, "invsw")):
                    tr.op("tensor", lambda E, srcc=srcc: E.matmul(ps["PD2"][:, 0:1], sb["onesf_row"][:],
                                                                  sb["stat"][0:1, srcc:srcc + 1], start=True, stop=True),
                          ["statp", "onesf_row"], ["PD2"])
                    tr.op("vector", lambda E, dst=dst: E.tensor_copy(sb[dst][:, wid[m]:wid[m] + 1], ps["PD2"][:, 0:1]),
                          ["PD2"], [dst])
            yield us

        for m in ("g", "u", "d"):
            for c in range(nch[m]):
                for sl in range(nsl[m]):
                    def u2(m=m, c=c, sl=sl):
                        dma("ct", lambda E: E.dma_start(out=wcs, in_=dr[wsrc[m]][c][:, 1024 * sl:1024 * (sl + 1)]),
                            r=wcsn, w=wcsn)
                        tr.op("vector", lambda E: E.tensor_scalar(wcs, wcs, sb["scw"][:, wid[m]:wid[m] + 1],
                                                                  12582912.0, ALU.mult, ALU.add),
                              wcsn + ["scw"], wcsn)
                        tr.op("vector", lambda E: E.tensor_scalar_add(wcs, wcs, -12582912.0), wcsn, wcsn)
                        tr.op("vector", lambda E: E.tensor_scalar(qtb, wcs, -1.0, 1.0, ALU.max, ALU.min),
                              wcsn, qtbn)
                        if m == "d":
                            dma("cs", lambda E: E.dma_start(out=dr["td"][c], in_=qtb), r=qtbn, w=[])
                        else:
                            tgt = "tg" if m == "g" else "tu"
                            for e2 in range(2):
                                e8 = 2 * sl + e2
                                dma("cs", lambda E, e8=e8, e2=e2, tgt=tgt: E.dma_start(
                                    out=dr[tgt][e8, c], in_=qtb[:, 512 * e2:512 * (e2 + 1)]), r=qtbn, w=[])
                    yield u2

    def wo_prep():
        for c in range(8):
            wv, wvn = carve("PBF", 36864 + D * c, D)
            dma("cw", lambda E, c=c, wv=wv: E.dma_start(out=wv, in_=dr["wo4"][c]), r=wvn, w=wvn)
        for half in range(2):
            for c in range(8):
                wv, wvn = carve("PBF", 36864 + D * c, D)
                tr.op("tensor", lambda E, half=half, c=c, wv=wv: E.matmul(
                    ps["PD1"][0:1, 0:512], sb["betaF_bf"][:, c:c + 1],
                    wv[:, 512 * half:512 * (half + 1)], start=(c == 0), stop=(c == 7)),
                    ["betaF_bf"] + wvn, ["PD1"])
            tr.op("vector", lambda E, half=half: E.tensor_copy(sb["bwo_bf"][0:1, 512 * half:512 * (half + 1)],
                                                               ps["PD1"][0:1, 0:512]), ["PD1"], ["bwo_bf"])
        for c in range(8):
            wv, wvn = carve("PBF", 36864 + D * c, D)
            tr.op("vector", lambda E, c=c, wv=wv: E.tensor_scalar_mul(wv, wv, sb["gammaF"][:, c:c + 1]),
                  wvn + ["gammaF"], wvn)
            dma("cs", lambda E, c=c, wv=wv: E.dma_start(out=dr["wof"][c], in_=wv), r=wvn, w=["wof"])

    def load_wqkv(bi):
        for c in range(8):
            wv, wvn = carve("PBF", 3 * D * c, 3 * D)
            dma("cw", lambda E, c=c, wv=wv: E.dma_start(out=wv, in_=dr["wqkv4"][c]), r=wvn, w=wvn)
        pmap = {0: ("PA", ps["PA"]), 1: ("PB", ps["PB"]), 2: ("PC", ps["PC"])}
        for p in range(3):
            pb, pn = pmap[p]
            for half in range(2):
                for c in range(8):
                    wv, wvn = carve("PBF", 3 * D * c, 3 * D)
                    tr.op("tensor", lambda E, p=p, half=half, c=c, pn=pn, wv=wv: E.matmul(
                        pn[0:1, 512 * half:512 * (half + 1)], sb["saF_bf"][:, 2 * c + bi:2 * c + bi + 1],
                        wv[:, D * p + 512 * half:D * p + 512 * (half + 1)], start=(c == 0), stop=(c == 7)),
                        ["saF_bf"] + wvn, [pb])
                tr.op("vector", lambda E, p=p, half=half, pn=pn: E.tensor_copy(
                    sb["saw_bf"][0:1, D * p + 512 * half:D * p + 512 * (half + 1)],
                    pn[0:1, 512 * half:512 * (half + 1)]), [pb], ["saw_bf"])
        for c in range(8):
            wv, wvn = carve("PBF", 3 * D * c, 3 * D)
            tr.op("vector", lambda E, c=c, wv=wv: E.tensor_scalar_mul(wv, wv, sb["caF1p"][:, 2 * c + bi:2 * c + bi + 1]),
                  wvn + ["caF1p"], wvn)

    def load_wo():
        for c in range(8):
            wv, wvn = carve("PBF", D * c, D)
            dma("cw", lambda E, c=c, wv=wv: E.dma_start(out=wv, in_=dr["wof"][c]), r=["wof"] + wvn, w=wvn)

    def a1_tile(gt):
        bi, st = gt // NTS, gt % NTS
        xv, xvn = carve("PF", 10240 + 1024 * (gt % 2), 1024)
        dma("cx", lambda E, bi=bi, st=st, xv=xv: E.dma_start(out=xv, in_=x_h[bi, TILE * st:TILE * (st + 1), :]),
            r=xvn, w=xvn)
        cb0 = 18 * (gt % 2)
        sn = "stat" if gt % 2 == 0 else "stat2"
        stc = sb["stat"][:, cb0:cb0 + 18]
        tr.op("vector", lambda E, xv=xv: E.bn_stats(stc[:, 0:6], xv[:, 0:512]), xvn, [sn])
        tr.op("vector", lambda E, xv=xv: E.bn_stats(stc[:, 6:12], xv[:, 512:1024]), xvn, [sn])
        tr.op("vector", lambda E: E.bn_aggr(stc[:, 12:14], stc[:, 0:12]), [sn], [sn])
        tr.op("scalar", lambda E: E.activation(stc[:, 14:15], stc[:, 13:14], AF.Ln, bias=sb["epsc"][:]), [sn, "epsc"], [sn])
        tr.op("scalar", lambda E: E.activation(stc[:, 15:16], stc[:, 14:15], AF.Exp, scale=-0.5), [sn], [sn])
        tr.op("vector", lambda E: E.tensor_scalar(stc[:, 16:17], stc[:, 12:13], stc[:, 15:16], -1.0, ALU.mult, ALU.mult),
              [sn], [sn])
        xn1, xn1n = carve("PBF", 29696 + 1024 * (gt % 2), 1024)
        tr.op("scalar", lambda E, xv=xv, xn1=xn1: E.activation(xn1, xv, AF.Identity,
                                                               bias=stc[:, 16:17], scale=stc[:, 15:16]),
              xvn + [sn], xn1n)
        xn1F, xn1Fn = carve("PBF", 24576, 1024)
        for c in range(8):
            pd, pb = (ps["PD1"], "PD1") if c % 2 == 0 else (ps["PD2"], "PD2")
            tr.op("tensor", lambda E, c=c, pd=pd, xn1=xn1: E.transpose(
                pd.bitcast(BF16)[:, 0:TILE], xn1[:, TILE * c:TILE * (c + 1)],
                sb["identb"][:]), xn1n + ["identb"], [pb])
            tr.op("scalar", lambda E, c=c, pd=pd, xn1F=xn1F: E.activation(xn1F[:, TILE * c:TILE * (c + 1)],
                                                                          pd.bitcast(BF16)[:, 0:TILE], AF.Copy), [pb], xn1Fn)
        pmap = {0: ("PA", ps["PA"]), 1: ("PB", ps["PB"]), 2: ("PC", ps["PC"])}
        for p in range(3):
            pb, pn = pmap[p]
            for half in range(2):
                for c in range(8):
                    wv, wvn = carve("PBF", 3 * D * c, 3 * D)
                    tr.op("tensor", lambda E, p=p, half=half, c=c, pn=pn, wv=wv, xn1F=xn1F: E.matmul(
                        pn[:, 512 * half:512 * (half + 1)], xn1F[:, TILE * c:TILE * (c + 1)],
                        wv[:, D * p + 512 * half:D * p + 512 * (half + 1)], start=(c == 0), stop=False),
                        xn1Fn + wvn, [pb])
                tr.op("tensor", lambda E, p=p, half=half, pn=pn: E.matmul(
                    pn[:, 512 * half:512 * (half + 1)], sb["ones_bf"][0:1, 0:TILE],
                    sb["saw_bf"][0:1, D * p + 512 * half:D * p + 512 * (half + 1)], start=False, stop=True),
                    ["ones_bf", "saw_bf"], [pb])
        if DBG_A1 and gt == 0:
            dma("cs", lambda E, xn1=xn1: E.dma_start(out=dr["dbg1"], in_=xn1), r=xn1n, w=[])
            for pbx, pnx, tag in (("PA", ps["PA"], "dbg2"), ("PB", ps["PB"], "dbg3")):
                tmpd, tmpdn = carve("PF", 2048, 1024)
                tr.op("vector", lambda E, pnx=pnx, tmpd=tmpd: E.tensor_copy(tmpd, pnx[:]), [pbx], tmpdn)
                dma("cs", lambda E, tmpd=tmpd, tag=tag: E.dma_start(out=dr[tag], in_=tmpd), r=tmpdn, w=[])
        for pb, pn, base in (("PA", ps["PA"], 0), ("PB", ps["PB"], 4096)):
            raw, rawn = carve("PF", base + 2048, 1024)
            tr.op("scalar", lambda E, pn=pn, raw=raw: E.activation(raw, pn[:], AF.Copy), [pb], rawn)
            rot, rotn = carve("PF", base, 1024)
            tmp, tmpn = carve("PF", base + 1024, 1024)
            cs = sb["ctbl"][:, DH * st:DH * (st + 1)]
            sn = sb["stbl"][:, DH * st:DH * (st + 1)]
            p3 = raw.rearrange("p (h u) -> p h u", u=DH)
            t3 = tmp.rearrange("p (h u) -> p h u", u=DH)
            d3 = rot.rearrange("p (h u) -> p h u", u=DH)
            sne = sn[:, 0::2].unsqueeze(1).broadcast_to([TILE, H, DH // 2])
            sno = sn[:, 1::2].unsqueeze(1).broadcast_to([TILE, H, DH // 2])
            csf = cs.unsqueeze(1).broadcast_to([TILE, H, DH])
            tr.op("vector", lambda E, t3=t3, sne=sne, p3=p3: E.tensor_mul(t3[:, :, 0::2], sne, p3[:, :, 1::2]),
                  rawn + ["stbl"], tmpn)
            tr.op("vector", lambda E, t3=t3, sno=sno, p3=p3: E.tensor_mul(t3[:, :, 1::2], sno, p3[:, :, 0::2]),
                  rawn + ["stbl"], tmpn)
            tr.op("vector", lambda E, d3=d3, csf=csf, p3=p3: E.tensor_mul(d3, csf, p3), rawn + ["ctbl"], rotn)
            rotb, rotbn = carve("PBF", 31744 if base == 0 else 32768, 1024)
            tr.op("vector", lambda E, rotb=rotb, rot=rot, tmp=tmp: E.tensor_add(rotb, rot, tmp),
                  rotn + tmpn, rotbn)
        vbf, vbfn = carve("PBF", 25600, 1024)
        tr.op("scalar", lambda E, vbf=vbf: E.activation(vbf, ps["PC"][:], AF.Copy), ["PC"], vbfn)
        dma("cs", lambda E, gt=gt, vbf=vbf: E.dma_start(out=dr["sp_kv"][gt, 1], in_=vbf), r=vbfn, w=["sp_kv"])
        for base, dsto, tgt in ((0, 26624, "sp_q"), (4096, 27648, "sp_kv")):
            rotb, rotbn = carve("PBF", 31744 if base == 0 else 32768, 1024)
            dst, dstn = carve("PBF", dsto, 1024)
            for c in range(8):
                pd, pb2 = (ps["PD1"], "PD1") if c % 2 == 0 else (ps["PD2"], "PD2")
                tr.op("tensor", lambda E, c=c, pd=pd, rotb=rotb: E.transpose(
                    pd.bitcast(BF16)[:, 0:TILE], rotb[:, TILE * c:TILE * (c + 1)],
                    sb["identb"][:]), rotbn + ["identb"], [pb2])
                tr.op("vector", lambda E, c=c, pd=pd, dst=dst: E.tensor_copy(dst[:, TILE * c:TILE * (c + 1)],
                                                                             pd.bitcast(BF16)[:, 0:TILE]), [pb2], dstn)
            if tgt == "sp_q":
                dma("cs", lambda E, gt=gt, dst=dst: E.dma_start(out=dr["sp_q"][gt], in_=dst), r=dstn, w=["sp_q"])
            else:
                dma("cs", lambda E, gt=gt, dst=dst: E.dma_start(out=dr["sp_kv"][gt, 0], in_=dst), r=dstn, w=["sp_kv"])

    def a2():
        for qg in range(N_QG):
            ntj = 4 * qg + 4
            qfg, qfgn = carve("PBF", 0, 8192)
            for si in range(2):
                for tq in range(4):
                    gt = si * NTS + 4 * qg + tq
                    if gt >= N_A1:
                        continue
                    sl = qfg[:, 1024 * (4 * si + tq):1024 * (4 * si + tq + 1)]
                    dma("ct", lambda E, gt=gt, sl=sl: E.dma_start(out=sl, in_=dr["sp_q"][gt]),
                        r=["sp_q"] + qfgn, w=qfgn)
            wcols = TILE * (4 * qg + 4 + DPAD)
            for hp in range(H // 2):
                h0, h1 = 2 * hp, 2 * hp + 1
                ds = 2 * (hp % 2)
                dmv0, dmv0n = carve("PF", 2432 * ds, 2432)
                dmv1, dmv1n = carve("PF", 2432 * (ds + 1), 2432)
                dma("ct", lambda E, h0=h0, wcols=wcols, dmv0=dmv0: E.dma_start(
                    out=dmv0[:, 0:wcols], in_=dr["dmt"][h0][:, 0:wcols]), r=dmv0n, w=dmv0n)
                dma("ct", lambda E, h1=h1, wcols=wcols, dmv1=dmv1: E.dma_start(
                    out=dmv1[:, 0:wcols], in_=dr["dmt"][h1][:, 0:wcols]), r=dmv1n, w=dmv1n)
                for si in range(2):
                    if si * NTS + 4 * qg >= N_A1:
                        continue
                    rp0 = (ps["PD1"] if si == 0 else ps["PD2"])[0:DH, 0:512]
                    rp0b = "PD1" if si == 0 else "PD2"
                    rp1 = ps["PC"][0:DH, 0:512] if si == 0 else ps["PC"][0:DH, 512:1024]
                    rp1b = "PC0" if si == 0 else "PC1"
                    for tj in range(ntj):
                        gt = si * NTS + tj
                        ks = tj % 4
                        kvv, kvvn = carve("PBF", 8192 + 2048 * ks, 2048)
                        kv, vv = kvv[:, 0:1024], kvv[:, 1024:2048]
                        kvn = vvn = kvvn
                        dma("ck", lambda E, gt=gt, kvv=kvv: E.dma_start(
                            out=kvv.rearrange("p (a m) -> p a m", m=1024), in_=dr["sp_kv"][gt].transpose([1, 0, 2])),
                            r=["sp_kv"] + kvvn, w=kvvn)
                        off = TILE * (4 * qg - tj + DPAD)
                        for hi, (h, dmv, dmvn, rp, rpb) in enumerate(
                                ((h0, dmv0, dmv0n, rp0, rp0b), (h1, dmv1, dmv1n, rp1, rp1b))):
                            kp = tj % 2
                            sc = (ps["PA"] if hi == 0 else ps["PB"])[:, 512 * kp:512 * (kp + 1)]
                            scb = ("PA" if hi == 0 else "PB") + str(kp)
                            qrhs = qfg.rearrange("p (t m) -> p t m", m=1024)[
                                64 * (h % 2):64 * (h % 2) + 64, 4 * si:4 * si + 4,
                                TILE * (h // 2):TILE * (h // 2 + 1)]
                            tr.op("tensor", lambda E, h=h, kv=kv, sc=sc, qrhs=qrhs: E.matmul(
                                sc, kv[64 * (h % 2):64 * (h % 2) + 64, TILE * (h // 2):TILE * (h // 2 + 1)],
                                qrhs, start=True, stop=True), kvn + qfgn, [scb])
                            smv, smvn = carve("PBF", 16384 + 512 * (2 * (tj % 2) + hi), 512)
                            tr.op("vector", lambda E, sc=sc, smv=smv, dmv=dmv, off=off: E.tensor_mul(
                                smv, sc, dmv[:, off:off + 512]), [scb] + dmvn, smvn)
                            tr.op("tensor", lambda E, h=h, vv=vv, rp=rp, smv=smv, tj=tj, ntj=ntj: E.matmul(
                                rp, vv[:, DH * h:DH * (h + 1)], smv,
                                start=(tj == 0), stop=(tj == ntj - 1)), vvn + smvn, [rpb])
                    for hi, (h, rp, rpb) in enumerate(((h0, rp0, rp0b), (h1, rp1, rp1b))):
                        rfv, rfvn = carve("PF", 14336 + 512 * (2 * si + hi), 512, 0, DH)
                        tr.op("vector", lambda E, rp=rp, rfv=rfv: E.tensor_copy(rfv, rp), [rpb], rfvn)
                        for tq in range(4):
                            gt = si * NTS + 4 * qg + tq
                            if gt >= N_A1:
                                continue
                            dma("cs", lambda E, gt=gt, h=h, rfv=rfv, tq=tq: E.dma_start(
                                out=dr["sp_r"][gt, h], in_=rfv[:, TILE * tq:TILE * (tq + 1)]), r=rfvn, w=["sp_r"])

    def bcast_tile(dst_off, src_b, src_lo, plus1, dstn_out):
        dst, dstn = carve("PF", dst_off, 1024)
        dma("ct", lambda E, dst=dst, src_b=src_b, src_lo=src_lo: E.dma_start(
            out=dst, in_=dr["modv"][src_b:src_b + 1, src_lo:src_lo + 1024].partition_broadcast(TILE).squeeze(1)),
            r=["modv"] + dstn, w=dstn)
        if plus1:
            tr.op("vector", lambda E, dst=dst: E.tensor_scalar_add(dst, dst, 1.0), dstn, dstn)
        dstn_out.extend(dstn)

    def a3_tile(gt, qslot):
        bi, st = gt // NTS, gt % NTS
        stc = sb["stat"]
        rfl, rfln = carve("PF", 4096 + 2048 * (gt % 2), 2048, 0, DH)
        dma("cx", lambda E, gt=gt, rfl=rfl: E.dma_start(out=rfl.rearrange("d (h t) -> d h t", t=128), in_=dr["sp_r"][gt].transpose([1, 0, 2])),
            r=["sp_r"] + rfln, w=rfln)
        xv, xvn = carve("PF", 10240 + 1024 * (gt % 2), 1024)
        dma("cx", lambda E, bi=bi, st=st, xv=xv: E.dma_start(out=xv, in_=x_h[bi, TILE * st:TILE * (st + 1), :]),
            r=xvn, w=xvn)
        rt, rtn = carve("PF", 0, 1024)
        for h in range(H):
            pd, pb = (ps["PD1"], "PD1") if h % 2 == 0 else (ps["PD2"], "PD2")
            tr.op("tensor", lambda E, h=h, pd=pd, rfl=rfl: E.transpose(
                pd[:, 0:DH], rfl[0:DH, TILE * h:TILE * (h + 1)], sb["identf"][0:DH, 0:DH]),
                rfln + ["identf"], [pb])
            tr.op("vector", lambda E, h=h, pd=pd, rt=rt: E.tensor_copy(rt[:, DH * h:DH * (h + 1)], pd[:, 0:DH]),
                  [pb], rtn)
        rt3 = rt.rearrange("p (h u) -> p h u", u=DH)
        tr.op("vector", lambda E, rt3=rt3: E.tensor_reduce(stc[:, 0:16], rt3, AX.X, ALU.add), rtn, ["stat"])
        sq, sqn = carve("PF", 8192, 1024)
        tr.op("scalar", lambda E, rt=rt, sq=sq: E.activation(sq, rt, AF.Square), rtn, sqn)
        tr.op("vector", lambda E, sq=sq: E.tensor_reduce(stc[:, 16:32], sq.rearrange("p (h u) -> p h u", u=DH),
                                                         AX.X, ALU.add), sqn, ["stat"])
        tr.op("vector", lambda E: E.tensor_scalar_mul(stc[:, 0:16], stc[:, 0:16], 1.0 / DH), ["stat"], ["stat"])
        tr.op("vector", lambda E: E.tensor_scalar_mul(stc[:, 16:32], stc[:, 16:32], 1.0 / DH), ["stat"], ["stat"])
        tr.op("vector", lambda E: E.tensor_mul(stc[:, 32:48], stc[:, 0:16], stc[:, 0:16]), ["stat"], ["stat"])
        tr.op("vector", lambda E: E.tensor_sub(stc[:, 16:32], stc[:, 16:32], stc[:, 32:48]), ["stat"], ["stat"])
        tr.op("scalar", lambda E: E.activation(stc[:, 16:32], stc[:, 16:32], AF.Ln, bias=sb["epsc"][:]), ["stat", "epsc"], ["stat"])
        tr.op("scalar", lambda E: E.activation(stc[:, 16:32], stc[:, 16:32], AF.Exp, scale=-0.5), ["stat"], ["stat"])
        if DBG_A1 and gt == 0:
            dma("cs", lambda E, rt=rt: E.dma_start(out=dr["dbg4"], in_=rt), r=rtn, w=[])
        mu_rep = stc[:, 0:16].unsqueeze(2).broadcast_to([TILE, 16, DH])
        rs_rep = stc[:, 16:32].unsqueeze(2).broadcast_to([TILE, 16, DH])
        gn, gnn = carve("PF", 9216, 1024)
        gn3 = gn.rearrange("p (h u) -> p h u", u=DH)
        tr.op("vector", lambda E, gn3=gn3, rt3=rt3, mu_rep=mu_rep: E.tensor_sub(gn3, rt3, mu_rep),
              rtn + ["stat"], gnn)
        gnb, gnbn = carve("PBF", 9216, 1024)
        gnb3 = gnb.rearrange("p (h u) -> p h u", u=DH)
        tr.op("vector", lambda E, gnb3=gnb3, gn3=gn3, rs_rep=rs_rep: E.tensor_mul(gnb3, gn3, rs_rep),
              gnn + ["stat"], gnbn)
        if DBG_A1 and gt == 0:
            dma("cs", lambda E, gn=gn: E.dma_start(out=dr["dbg5"], in_=gn), r=gnn, w=[])
        gnF, gnFn = carve("PBF", 8192, 1024)
        for c in range(8):
            pd, pb = (ps["PD1"], "PD1") if c % 2 == 0 else (ps["PD2"], "PD2")
            tr.op("tensor", lambda E, c=c, pd=pd, gnb=gnb: E.transpose(
                pd.bitcast(BF16)[:, 0:TILE], gnb[:, TILE * c:TILE * (c + 1)],
                sb["identb"][:]), gnbn + ["identb"], [pb])
            tr.op("scalar", lambda E, c=c, pd=pd, gnF=gnF: E.activation(gnF[:, TILE * c:TILE * (c + 1)],
                                                                        pd.bitcast(BF16)[:, 0:TILE], AF.Copy), [pb], gnFn)
        for half in range(2):
            for c in range(8):
                wv, wvn = carve("PBF", D * c, D)
                tr.op("tensor", lambda E, half=half, c=c, wv=wv, gnF=gnF: E.matmul(
                    ps["PA"][:, 512 * half:512 * (half + 1)], gnF[:, TILE * c:TILE * (c + 1)],
                    wv[:, 512 * half:512 * (half + 1)], start=(c == 0), stop=False), gnFn + wvn, ["PA"])
            tr.op("tensor", lambda E, half=half: E.matmul(
                ps["PA"][:, 512 * half:512 * (half + 1)], sb["ones_bf"][0:1, 0:TILE],
                sb["bwo_bf"][0:1, 512 * half:512 * (half + 1)], start=False, stop=True),
                ["ones_bf", "bwo_bf"], ["PA"])
        x1 = sb["x1q"][:, D * qslot:D * (qslot + 1)]
        x1b = f"x1q{qslot}"
        gax, gaxn = carve("PF", 8192, 1024)
        bc1, bc1n = carve("PF", 12288, 1024)
        tr.op("vector", lambda E, gax=gax, bc1=bc1: E.tensor_mul(gax, ps["PA"][:], bc1), ["PA"] + bc1n, gaxn)
        tr.op("vector", lambda E, x1=x1, xv=xv, gax=gax: E.tensor_add(x1, xv, gax), xvn + gaxn, [x1b])
        if DBG_A1 and gt == 0:
            dma("cs", lambda E, x1=x1: E.dma_start(out=dr["dbg6"], in_=x1), r=[x1b], w=[])
        tr.op("vector", lambda E, x1=x1: E.bn_stats(stc[:, 0:6], x1[:, 0:512]), [x1b], ["stat"])
        tr.op("vector", lambda E, x1=x1: E.bn_stats(stc[:, 6:12], x1[:, 512:1024]), [x1b], ["stat"])
        tr.op("vector", lambda E: E.bn_aggr(stc[:, 12:14], stc[:, 0:12]), ["stat"], ["stat"])
        tr.op("scalar", lambda E: E.activation(stc[:, 14:15], stc[:, 13:14], AF.Ln, bias=sb["epsc"][:]), ["stat", "epsc"], ["stat"])
        tr.op("scalar", lambda E, qslot=qslot: E.activation(sb["ln2rstd"][:, qslot:qslot + 1], stc[:, 14:15],
                                                            AF.Exp, scale=-0.5), ["stat"], ["ln2rstd"])
        tr.op("vector", lambda E, qslot=qslot: E.tensor_scalar(sb["ln2nmr"][:, qslot:qslot + 1], stc[:, 12:13],
                                                               sb["ln2rstd"][:, qslot:qslot + 1], -1.0,
                                                               ALU.mult, ALU.mult), ["stat", "ln2rstd"], ["ln2nmr"])

    MAGIC = 12582912.0

    def b1_tile(quad, tq):
        stc = sb["stat"]
        bc1, bc1n = carve("PF", 3072, 1024)
        bc2, bc2n = carve("PF", 13312, 1024)
        x1 = sb["x1q"][:, D * tq:D * (tq + 1)]
        x1b = f"x1q{tq}"
        xn2, xn2n = carve("PF", 2048, 1024)
        tr.op("scalar", lambda E, x1=x1, tq=tq, xn2=xn2: E.activation(
            xn2, x1, AF.Identity, bias=sb["ln2nmr"][:, tq:tq + 1], scale=sb["ln2rstd"][:, tq:tq + 1]),
            [x1b, "ln2rstd", "ln2nmr"], xn2n)
        tr.op("vector", lambda E, xn2=xn2, bc1=bc1: E.tensor_mul(xn2, xn2, bc1), xn2n + bc1n, xn2n)
        tr.op("vector", lambda E, xn2=xn2, bc2=bc2: E.tensor_add(xn2, xn2, bc2), xn2n + bc2n, xn2n)
        tr.op("vector", lambda E, xn2=xn2: E.reduce_max(stc[:, 48:49], xn2, axis=AX.X, apply_absolute_value=True),
              xn2n, ["statb"])
        tr.op("vector", lambda E: E.tensor_single_scalar(stc[:, 49:50], stc[:, 48:49], 1e-5, ALU.max),
              ["statb"], ["statb"])
        tr.op("vector", lambda E: E.reciprocal(stc[:, 50:51], stc[:, 49:50]), ["statb"], ["statb"])
        tr.op("vector", lambda E: E.tensor_scalar_mul(stc[:, 51:52], stc[:, 50:51], 127.0), ["statb"], ["statb"])
        tr.op("vector", lambda E: E.tensor_scalar_mul(stc[:, 52:53], stc[:, 49:50], 1.0 / 127.0), ["statb"], ["statb"])
        tr.op("vector", lambda E, tq=tq: E.tensor_mul(stc[:, 56 + tq:57 + tq], stc[:, 52:53], sb["invsw"][:, 0:1]),
              ["statb", "invsw"], ["esg"])
        tr.op("vector", lambda E, tq=tq: E.tensor_mul(stc[:, 60 + tq:61 + tq], stc[:, 52:53], sb["invsw"][:, 1:2]),
              ["statb", "invsw"], ["esg"])
        if DBG_A1 and quad == 0 and tq == 0:
            dma("cs", lambda E, xn2=xn2: E.dma_start(out=dr["dbg7"], in_=xn2), r=xn2n, w=[])
        q1, q1n = carve("PF", 1024, 1024)
        tr.op("vector", lambda E, xn2=xn2, q1=q1: E.tensor_scalar(q1, xn2, stc[:, 51:52], MAGIC, ALU.mult, ALU.add),
              xn2n + ["statb"], q1n)
        tr.op("vector", lambda E, q1=q1: E.tensor_scalar_add(q1, q1, -MAGIC), q1n, q1n)
        q1b, q1bn = carve("PBF", 47104, 1024)
        tr.op("vector", lambda E, q1=q1, q1b=q1b: E.tensor_scalar(q1b, q1, -128.0, 127.0, ALU.max, ALU.min),
              q1n, q1bn)
        if DBG_A1 and quad == 0 and tq == 0:
            dma("cs", lambda E, q1=q1: E.dma_start(out=dr["dbg8"], in_=q1), r=q1n, w=[])
        xqF, xqFn = carve("PBF", 32768 + 1024 * tq, 1024)
        for c in range(8):
            po, pb = (0, "PC0") if c % 2 == 0 else (1024, "PC1")
            tr.op("tensor", lambda E, c=c, po=po, q1b=q1b: E.transpose(
                ps["PC"].bitcast(BF16)[:, po:po + TILE], q1b[:, TILE * c:TILE * (c + 1)],
                sb["identb"][:]), q1bn + ["identb"], [pb])
            tr.op("scalar", lambda E, c=c, po=po, xqF=xqF: E.activation(
                xqF[:, TILE * c:TILE * (c + 1)], ps["PC"].bitcast(BF16)[:, po:po + TILE], AF.Copy), [pb], xqFn)

    def b_quad(quad):
        bi = (4 * quad) // NTS
        stc = sb["stat"]

        for e8 in range(8):
            ks = e8 % 2
            tgv, tgvn = carve("PBF", 0 + 4096 * ks, 4096)
            tuv, tuvn = carve("PBF", 8192 + 4096 * ks, 4096)
            dma("ct", lambda E, e8=e8, tgv=tgv: E.dma_start(out=tgv.rearrange("p (c m) -> p c m", m=512), in_=dr["tg"][e8].transpose([1, 0, 2])),
                r=tgvn, w=tgvn)
            dma("ct", lambda E, e8=e8, tuv=tuv: E.dma_start(out=tuv.rearrange("p (c m) -> p c m", m=512), in_=dr["tu"][e8].transpose([1, 0, 2])),
                r=tuvn, w=tuvn)
            for tq in range(4):
                xqF, xqFn = carve("PBF", 32768 + 1024 * tq, 1024)
                par = (4 * e8 + tq) % 2
                pg, pgb = (ps["PA"], "PA") if par == 0 else (ps["PB"], "PB")
                for c in range(8):
                    tr.op("tensor", lambda E, c=c, tgv=tgv, xqF=xqF, pg=pg: E.matmul(
                        pg[:, 0:512], xqF[:, TILE * c:TILE * (c + 1)], tgv[:, 512 * c:512 * (c + 1)],
                        start=(c == 0), stop=(c == 7)), xqFn + tgvn, [pgb + "0"])
                    tr.op("tensor", lambda E, c=c, tuv=tuv, xqF=xqF, pg=pg: E.matmul(
                        pg[:, 512:1024], xqF[:, TILE * c:TILE * (c + 1)], tuv[:, 512 * c:512 * (c + 1)],
                        start=(c == 0), stop=(c == 7)), xqFn + tuvn, [pgb + "1"])
                sg, sgn = carve("PF", 8192 + 512 * par, 512)
                tr.op("scalar", lambda E, tq=tq, sg=sg, pg=pg: E.activation(sg, pg[:, 0:512], AF.Silu,
                                                                     scale=stc[:, 56 + tq:57 + tq]),
                      [pgb + "0", "esg"], sgn)
                a2v, a2vn = carve("PBF", 16384 + 4096 * tq + 512 * e8, 512)
                tr.op("vector", lambda E, tq=tq, a2v=a2v, sg=sg, pg=pg: E.scalar_tensor_tensor(
                    a2v, pg[:, 512:1024], stc[:, 60 + tq:61 + tq], sg, ALU.mult, ALU.mult),
                    [pgb + "1", "esg"] + sgn, a2vn)
                tr.op("vector", lambda E, tq=tq, e8=e8, a2v=a2v: E.reduce_max(
                    stc[:, 64 + tq:65 + tq] if e8 == 0 else stc[:, 68 + tq:69 + tq], a2v,
                    axis=AX.X, apply_absolute_value=True), a2vn, ["amx"])
                if e8 > 0:
                    tr.op("vector", lambda E, tq=tq: E.tensor_max(stc[:, 64 + tq:65 + tq], stc[:, 64 + tq:65 + tq],
                                                                  stc[:, 68 + tq:69 + tq]), ["amx"], ["amx"])

        if DBG_A1 and quad == 0:
            dma("cs", lambda E: E.dma_start(out=dr["dbg12"], in_=sb["stat"][:, 64:80]), r=["amx"], w=[])
            a2full, a2fn = carve("PBF", 16384, 4096)
            dma("cs", lambda E, a2full=a2full: E.dma_start(out=dr["dbg9"], in_=a2full),
                r=[f"a2_0"] + a2fn, w=[])
        for pair in range(2):
            for tq2 in range(2):
                tq = 2 * pair + tq2
                tr.op("vector", lambda E, tq=tq: E.tensor_single_scalar(stc[:, 72:73], stc[:, 64 + tq:65 + tq],
                                                                        1e-5, ALU.max), ["amx"], ["statq"])
                tr.op("vector", lambda E: E.reciprocal(stc[:, 73:74], stc[:, 72:73]), ["statq"], ["statq"])
                tr.op("vector", lambda E: E.tensor_scalar_mul(stc[:, 74:75], stc[:, 73:74], 127.0), ["statq"], ["statq"])
                tr.op("vector", lambda E: E.tensor_scalar_mul(stc[:, 75:76], stc[:, 72:73], 1.0 / 127.0), ["statq"], ["statq"])
                tr.op("vector", lambda E, tq=tq: E.tensor_mul(stc[:, 80 + tq:81 + tq], stc[:, 75:76], sb["invsw"][:, 2:3]),
                      ["statq", "invsw"], ["es2q"])
                xq2F, xq2Fn = carve("PBF", 36864 + 4096 * tq2, 4096)
                for qtr in range(4):
                    a2v, a2vn = carve("PBF", 16384 + 4096 * tq + 1024 * qtr, 1024)
                    q1, q1n = carve("PF", 1024, 1024)
                    tr.op("vector", lambda E, a2v=a2v, q1=q1: E.tensor_scalar(
                        q1, a2v, stc[:, 74:75], MAGIC, ALU.mult, ALU.add), a2vn + ["statq"], q1n)
                    if DBG_A1 and quad == 0 and tq == 0:
                        dma("cs", lambda E, q1=q1, qtr=qtr: E.dma_start(
                            out=dr["dbg11"][:, 1024 * qtr:1024 * (qtr + 1)], in_=q1), r=q1n, w=[])
                    tr.op("vector", lambda E, q1=q1: E.tensor_scalar_add(q1, q1, -MAGIC), q1n, q1n)
                    q1b, q1bn = carve("PBF", 48128, 1024)
                    tr.op("vector", lambda E, q1=q1, q1b=q1b: E.tensor_scalar(
                        q1b, q1, -128.0, 127.0, ALU.max, ALU.min), q1n, q1bn)
                    if DBG_A1 and quad == 0 and tq == 0:
                        dma("cs", lambda E, q1=q1, qtr=qtr: E.dma_start(
                            out=dr["dbg10"][:, 1024 * qtr:1024 * (qtr + 1)], in_=q1), r=q1n, w=[])
                    for c in range(8):
                        pd, pb = (ps["PD1"], "PD1") if c % 2 == 0 else (ps["PD2"], "PD2")
                        tr.op("tensor", lambda E, c=c, pd=pd, q1b=q1b: E.transpose(
                            pd.bitcast(BF16)[:, 0:TILE], q1b[:, TILE * c:TILE * (c + 1)],
                            sb["identb"][:]), q1bn + ["identb"], [pb])
                        tr.op("scalar", lambda E, c=c, qtr=qtr, pd=pd, xq2F=xq2F: E.activation(
                            xq2F[:, 1024 * qtr + TILE * c:1024 * qtr + TILE * (c + 1)],
                            pd.bitcast(BF16)[:, 0:TILE], AF.Copy), [pb], xq2Fn)
            for k in range(32):
                wv, wvn = carve("PBF", 45056 + 1024 * (k % 2), 1024)
                dma("ck", lambda E, k=k, wv=wv: E.dma_start(out=wv, in_=dr["td"][k]), r=wvn, w=wvn)
                for tq2 in range(2):
                    xq2F, xq2Fn = carve("PBF", 36864 + 4096 * tq2, 4096)
                    pn, pb = (ps["PA"], "PA") if tq2 == 0 else (ps["PB"], "PB")
                    for half in range(2):
                        tr.op("tensor", lambda E, k=k, half=half, pn=pn, xq2F=xq2F, wv=wv: E.matmul(
                            pn[:, 512 * half:512 * (half + 1)], xq2F[:, TILE * k:TILE * (k + 1)],
                            wv[:, 512 * half:512 * (half + 1)], start=(k == 0), stop=(k == 31)),
                            xq2Fn + wvn, [pb + str(half)])
            gb, gbn = carve("PF", 14336, 1024)
            _ = []
            bcast_tile(14336, bi, 5 * D, False, _)
            for tq2 in range(2):
                tq = 2 * pair + tq2
                gt = 4 * quad + tq
                bi3, st3 = gt // NTS, gt % NTS
                pn, pb = (ps["PA"], "PA") if tq2 == 0 else (ps["PB"], "PB")
                x1 = sb["x1q"][:, D * tq:D * (tq + 1)]
                x1b = f"x1q{tq}"
                tmp, tmpn = carve("PF", 2048, 1024)
                for half in range(2):
                    tr.op("vector", lambda E, pn=pn, half=half, tq=tq, tmp=tmp, gb=gb: E.scalar_tensor_tensor(
                        tmp[:, 512 * half:512 * (half + 1)], pn[:, 512 * half:512 * (half + 1)],
                        stc[:, 80 + tq:81 + tq], gb[:, 512 * half:512 * (half + 1)], ALU.mult, ALU.mult),
                        [pb + str(half), "es2q"] + gbn, tmpn)
                tr.op("vector", lambda E, x1=x1, tmp=tmp: E.tensor_add(x1, x1, tmp), [x1b] + tmpn, [x1b])
                dma("co", lambda E, bi3=bi3, st3=st3, x1=x1: E.dma_start(
                    out=out_h[bi3, TILE * st3:TILE * (st3 + 1), :], in_=x1), r=[x1b], w=[])

    prep()
    load_wqkv(0)
    wo_prep()
    qgen = quant_units()
    qdone = False
    for gt in range(N_A1):
        if gt == NTS:
            load_wqkv(1)
        a1_tile(gt)
        for _ in range(7):
            try:
                next(qgen)()
            except StopIteration:
                qdone = True
                break
    while not qdone:
        try:
            next(qgen)()
        except StopIteration:
            qdone = True
    a2()
    for quad in range(N_QUAD):
        load_wo()
        bi = (4 * quad) // NTS
        _ = []
        bcast_tile(12288, bi, 2 * D, False, _)
        bcast_tile(3072, bi, 4 * D, True, _)
        bcast_tile(13312, bi, 3 * D, False, _)
        for tq in range(4):
            a3_tile(4 * quad + tq, tq)
            if tq > 0:
                b1_tile(quad, tq - 1)
        b1_tile(quad, 3)
        b_quad(quad)

    ops = tr.finalize()
    co_waits = [(k, v) for k, v in tr.dcnt.items() if k[1] == "co"]
    if co_waits:
        ops["sync"].append((sorted(co_waits), lambda E: E.nop(), None))
    nds = 0
    for k in tr.dcnt:
        if k not in sems:
            sems[k] = stack.enter_context(nc.semaphore(f"sd{nds}"))
            nds += 1
    assert nds + len(Tr.ENG) <= 100, f"too many semaphores: {nds}"

    with nc.Block() as blk0:
        def clear(E):
            for s in sems.values():
                E.sem_clear(s)
        blk0.sync(clear)

    with nc.Block() as blk:
        def mk(ename):
            myops = ops[ename]

            def run(E):
                for waits, fn, inc in myops:
                    for k, v in waits:
                        E.wait_ge(sems[k], v)
                    inst = fn(E)
                    if inc is not None:
                        inst.then_inc(sems[inc[0]], inc[1])
            return run
        blk.tensor(mk("tensor"))
        blk.vector(mk("vector"))
        blk.scalar(mk("scalar"))
        blk.gpsimd(mk("gpsimd"))
        blk.sync(mk("sync"))

    n_inst = sum(len(v) for v in ops.values())
    build.sb_info = {k: v for k, v in sb.items()}
    return nc, stack, n_inst


def make_inputs(full):
    tabs = host_tables()
    WQ, WK, WV = full["L0_attn_WQ"], full["L0_attn_WK"], full["L0_attn_WV"]
    wqkvT = np.concatenate([w.transpose(2, 0, 1).reshape(D, D) for w in (WQ, WK, WV)], axis=1)
    wqkv4 = np.ascontiguousarray(wqkvT.reshape(8, TILE, 3 * D))
    wo4 = np.ascontiguousarray(full["L0_attn_WO"].reshape(8, TILE, D))
    wmod4 = np.ascontiguousarray(full["L0_wmod"].reshape(8, TILE, 12, 512).transpose(2, 0, 1, 3))
    wg4 = np.ascontiguousarray(full["L0_ffn_w_gate"].T.reshape(8, TILE, INNER))
    wu4 = np.ascontiguousarray(full["L0_ffn_w_up"].T.reshape(8, TILE, INNER))
    wd4 = np.ascontiguousarray(full["L0_ffn_w_down"].T.reshape(32, TILE, D))
    gammaF = np.ascontiguousarray(full["L0_attn_gamma"].reshape(8, TILE).T)
    betaF = np.ascontiguousarray(full["L0_attn_beta"].reshape(8, TILE).T)
    x = full["L0_attn_x"]
    c = full["L0_c"]
    shared = dict(wqkv4=wqkv4, wo4=wo4, wmod4=wmod4, wg4=wg4, wu4=wu4, wd4=wd4,
                  gammaF=gammaF, betaF=betaF, ctbl=tabs["ctbl"], stbl=tabs["stbl"],
                  dmt=tabs["dmt"], identf=tabs["ident"])
    in_maps = []
    for core in range(8):
        cs = c[2 * core:2 * core + 2]
        cF = np.ascontiguousarray(cs.reshape(2, 8, TILE).transpose(2, 1, 0).reshape(TILE, 16))
        m = dict(shared)
        m["x"] = np.ascontiguousarray(x[2 * core:2 * core + 2])
        m["cF"] = cF
        in_maps.append(m)
    return in_maps


def kernel(**inputs):
    in_maps = make_inputs(inputs)
    nc, stack, n_inst = build()
    res = run_bass_kernel_spmd(nc, in_maps, core_ids=list(range(8)))
    outs = [res.results[i]["out"] for i in range(8)]
    full = np.concatenate(outs, axis=0).astype(np.float32)
    if SPILL_OUT:
        return full, res
    stack.close()
    return full



# version 12
# speedup vs baseline: 1.2123x; 1.0295x over previous
import contextlib
import numpy as np
import concourse.bass as bass
import concourse.mybir as mybir
from concourse.bass_utils import run_bass_kernel_spmd

F32 = mybir.dt.float32
BF16 = mybir.dt.bfloat16
I32 = mybir.dt.int32
AF = mybir.ActivationFunctionType
ALU = mybir.AluOpType
AX = mybir.AxisListType

S, D, H, DH = 2048, 1024, 16, 64
TILE = 128
NTS = S // TILE
NT = 2 * NTS
NQG = NTS // 4
INNER = 4096
EPS = 1e-5
DPAD = 3

N_A1 = NT
N_QG = NQG
N_QUAD = NT // 4
SPILL_OUT = False
DBG_A1 = False


def host_tables():
    import jax
    with jax.default_device(jax.devices("cpu")[0]):
        import jax.numpy as jnp
        expoent = jnp.arange(0, DH, 2, dtype=jnp.float32) / DH
        angle = 10.0 ** (-4.0 * expoent)
        pos = jnp.arange(0, S, dtype=jnp.float32)
        nangle = pos[:, None] * angle[None, :]
        sin = np.asarray(jnp.sin(nangle))
        cos = np.asarray(jnp.cos(nangle))
        scale = 1.0 - 2.0 ** (-5.0 - jnp.arange(H, dtype=jnp.float32))
        logs = jnp.log(scale)
        delta = jnp.arange(S, dtype=jnp.float32)
        logd = delta[None, :] * logs[:, None]
        d32 = np.asarray(jnp.exp(logd))

    ctbl = np.zeros((TILE, NTS * DH), np.float32)
    stbl = np.zeros((TILE, NTS * DH), np.float32)
    for st in range(NTS):
        c = cos[st * TILE:(st + 1) * TILE]
        s = sin[st * TILE:(st + 1) * TILE]
        ctbl[:, st * DH:(st + 1) * DH] = np.repeat(c, 2, axis=1)
        blk = np.empty((TILE, DH), np.float32)
        blk[:, 0::2] = -s
        blk[:, 1::2] = s
        stbl[:, st * DH:(st + 1) * DH] = blk

    ndt = DPAD + NTS
    dmt = np.zeros((H, TILE, ndt * TILE), np.float32)
    jj = np.arange(TILE)[:, None]
    ii = np.arange(TILE)[None, :]
    for t in range(ndt):
        dlt = (t - DPAD) * TILE + ii - jj
        valid = (dlt >= 0) & (dlt < S)
        dl = np.clip(dlt, 0, S - 1)
        for h in range(H):
            dmt[h, :, t * TILE:(t + 1) * TILE] = d32[h][dl] * valid
    ident = np.eye(TILE, dtype=np.float32)
    return dict(ctbl=ctbl, stbl=stbl, dmt=dmt, ident=ident)


class Tr:
    ENG = ("tensor", "vector", "scalar", "gpsimd", "sync")
    DMA_Q = {"cx": "sync", "ct": "sync", "ck": "scalar", "cw": "gpsimd", "cs": "gpsimd", "co": "gpsimd"}
    SELFWAIT = {"vector", "scalar", "gpsimd"}

    def __init__(self):
        self.recs = []
        self.lastw = {}
        self.readers = {}
        self.openg = []
        self.opkey = []

    PSPLIT = {"PA": ("PA0", "PA1"), "PB": ("PB0", "PB1"), "PC": ("PC0", "PC1")}

    def _norm(self, bufs):
        out = []
        for b in bufs:
            if b in self.PSPLIT:
                out.extend(self.PSPLIT[b])
            else:
                out.append(b)
        return out

    def op(self, eng, fn, reads=(), writes=(), dma=None):
        reads = self._norm(reads)
        writes = self._norm(writes)
        deps = set()
        myid = len(self.recs)

        def need(pid):
            if pid is None:
                return
            peng = self.openg[pid]
            pkey = self.opkey[pid]
            if pkey is None and peng == eng and eng not in self.SELFWAIT:
                return
            deps.add(pid)

        for b in reads:
            need(self.lastw.get(b))
        for b in writes:
            need(self.lastw.get(b))
            for r in self.readers.get(b, ()):
                need(r)
        key = None
        if dma is not None:
            assert self.DMA_Q[dma] == eng
            slot = tuple(sorted(writes)) if writes else ("st",) + tuple(sorted(reads))
            key = ("D", dma, slot)
        self.openg.append(eng)
        self.opkey.append(key)
        for b in writes:
            self.lastw[b] = myid
            self.readers[b] = []
        for b in reads:
            self.readers.setdefault(b, []).append(myid)
        self.recs.append((eng, fn, key, deps))

    def finalize(self):
        n = len(self.recs)
        needed = [False] * n
        for (_, _, key, deps) in self.recs:
            for d in deps:
                needed[d] = True
        val = [0] * n
        ecnt = {e: 0 for e in self.ENG}
        dcnt = {}
        for i, (eng, fn, key, deps) in enumerate(self.recs):
            if key is not None:
                dcnt[key] = dcnt.get(key, 0) + 16
                val[i] = dcnt[key]
            elif needed[i]:
                ecnt[eng] += 1
                val[i] = ecnt[eng]
        ops = {e: [] for e in self.ENG}
        waited = {e: {} for e in self.ENG}
        for i, (eng, fn, key, deps) in enumerate(self.recs):
            waits = {}
            for d in deps:
                dk = self.opkey[d]
                sk = dk if dk is not None else ("E", self.openg[d])
                v = val[d]
                if waited[eng].get(sk, 0) >= v:
                    continue
                if waits.get(sk, 0) < v:
                    waits[sk] = v
            for k2, v2 in waits.items():
                waited[eng][k2] = v2
            if key is not None:
                inc = (key, 16)
            elif needed[i]:
                inc = (("E", eng), 1)
            else:
                inc = None
            ops[eng].append((sorted(waits.items()), fn, inc))
        self.dcnt = dcnt
        return ops


def build():
    tabs = host_tables()
    nc = bass.Bass("TRN2", target_bir_lowering=False, detect_race_conditions=False)
    stack = contextlib.ExitStack()
    dr = {}

    def din(name, shape, dt=F32):
        dr[name] = nc.dram_tensor(name, list(shape), dt, kind="ExternalInput").ap()

    def dint(name, shape, dt):
        kind = "ExternalOutput" if SPILL_OUT else "Internal"
        dr[name] = nc.dram_tensor(name, list(shape), dt, kind=kind).ap()

    din("x", [2, S, D])
    dr["out"] = nc.dram_tensor("out", [2, S, D], F32, kind="ExternalOutput").ap()
    din("cF", [TILE, 16])
    din("wmod4", [12, 8, TILE, 512])
    din("wqkv4", [8, TILE, 3 * D])
    din("wo4", [8, TILE, D])
    din("gammaF", [TILE, 8])
    din("betaF", [TILE, 8])
    din("wg4", [8, TILE, INNER])
    din("wu4", [8, TILE, INNER])
    din("wd4", [32, TILE, D])
    din("ctbl", [TILE, NTS * DH])
    din("stbl", [TILE, NTS * DH])
    din("dmt", [H, TILE, (DPAD + NTS) * TILE])
    din("identf", [TILE, TILE])
    dint("sp_q", [NT, TILE, D], BF16)
    dint("sp_kv", [NT, 2, TILE, D], BF16)
    dint("sp_r", [NT, H, DH, TILE], F32)
    dint("modv", [2, 6 * D], F32)
    if DBG_A1:
        for i in (1, 2, 3):
            dint(f"dbg{i}", [TILE, D], F32)
        for i in (4, 5, 6, 7, 8):
            dint(f"dbg{i}", [TILE, D], F32)
        for i in (9, 10, 11):
            dint(f"dbg{i}", [TILE, INNER], F32)
        dint("dbg12", [TILE, 16], F32)
    dint("tg", [8, 8, TILE, 512], BF16)
    dint("tu", [8, 8, TILE, 512], BF16)
    dint("td", [32, TILE, D], BF16)
    dint("wof", [8, TILE, D], BF16)

    sb = {}

    def sbuf(name, shape, dt):
        sb[name] = stack.enter_context(nc.sbuf_tensor("sb_" + name, list(shape), dt))

    sbuf("identf", [TILE, TILE], F32)
    sbuf("ctbl", [TILE, NTS * DH], F32)
    sbuf("stbl", [TILE, NTS * DH], F32)
    sbuf("ones_bf", [1, 512], BF16)
    sbuf("onesf_col", [TILE, 1], F32)
    sbuf("onesf_row", [1, TILE], F32)
    sbuf("cF", [TILE, 16], F32)
    sbuf("silu_cF", [TILE, 16], F32)
    sbuf("gammaF", [TILE, 8], F32)
    sbuf("betaF_bf", [TILE, 8], BF16)
    sbuf("modF", [TILE, 32], F32)
    sbuf("saF_bf", [TILE, 16], BF16)
    sbuf("caF1p", [TILE, 16], F32)
    sbuf("saw_bf", [1, 3 * D], BF16)
    sbuf("bwo_bf", [1, D], BF16)
    sbuf("scw", [TILE, 3], F32)
    sbuf("invsw", [TILE, 3], F32)
    sbuf("accw", [TILE, 48], F32)
    sbuf("stat", [TILE, 96], F32)
    sbuf("ln2rstd", [TILE, 4], F32)
    sbuf("ln2nmr", [TILE, 4], F32)
    sbuf("epsc", [TILE, 1], F32)
    sbuf("x1q", [TILE, 4 * D], F32)
    sbuf("PF", [TILE, 16384], F32)
    sbuf("identb", [TILE, TILE], BF16)
    sbuf("PBF", [TILE, 49152], BF16)
    sbuf("PI", [TILE, 1024], I32)

    SEG = {"PF": 512, "PBF": 1024, "PI": 512}

    def carve(pool, off, n, p0=0, p1=TILE):
        ap = sb[pool][p0:p1, off:off + n]
        s0, s1 = off // SEG[pool], (off + n - 1) // SEG[pool]
        names = [f"{pool}:{i}" for i in range(s0, s1 + 1)]
        return ap, names

    ps = {}
    for n, shp in (("PA", [TILE, 1024]), ("PB", [TILE, 1024]), ("PC", [TILE, 1024]),
                   ("PD1", [TILE, 512]), ("PD2", [TILE, 512])):
        ps[n] = stack.enter_context(nc.psum_tensor(n, shp, F32))

    tr = Tr()
    sems = {}
    for e in Tr.ENG:
        sems[("E", e)] = stack.enter_context(nc.semaphore("se_" + e))

    x_h, out_h = dr["x"], dr["out"]

    def dma(chan, fn, r=(), w=()):
        tr.op(Tr.DMA_Q[chan], fn, r, w, dma=chan)

    def rnd_nearest(src_ap, src_n, w_slices, rbufs):
        n = src_n
        for lo in range(0, n, 1024):
            wdt = min(1024, n - lo)
            sl = src_ap[:, lo:lo + wdt]
            ri, rin = carve("PI", 0, wdt)
            dv, dvn = carve("PF", FD_RND, wdt)
            bv, bvn = carve("PF", FD_RND + 1024, wdt)
            tr.op("vector", lambda E, sl=sl, ri=ri: E.tensor_copy(ri, sl), w_slices, rin)
            tr.op("vector", lambda E, sl=sl, ri=ri, dv=dv: E.tensor_sub(dv, sl, ri), list(w_slices) + rin, dvn)
            tr.op("vector", lambda E, dv=dv, bv=bv: E.tensor_single_scalar(bv, dv, -0.5, ALU.is_lt), dvn, bvn)
            tr.op("vector", lambda E, sl=sl, dv=dv, ri=ri: E.scalar_tensor_tensor(sl, dv, 0.5, ri, ALU.is_gt, ALU.add),
                  dvn + rin, w_slices)
            tr.op("vector", lambda E, sl=sl, bv=bv: E.tensor_sub(sl, sl, bv), list(w_slices) + bvn, w_slices)

    FD_RND = 14336

    def prep():
        for nm in ("identf", "ctbl", "stbl", "cF", "gammaF"):
            dma("cx", lambda E, nm=nm: E.dma_start(out=sb[nm][:], in_=dr[nm]), w=[nm])
        dma("cw", lambda E: E.dma_start(out=sb["betaF_bf"][:], in_=dr["betaF"]), w=["betaF_bf"])
        tr.op("vector", lambda E: E.tensor_copy(sb["identb"][:], sb["identf"][:]), ["identf"], ["identb"])
        tr.op("vector", lambda E: E.memset(sb["ones_bf"][:], 1.0), (), ["ones_bf"])
        tr.op("vector", lambda E: E.memset(sb["onesf_col"][:], 1.0), (), ["onesf_col"])
        tr.op("vector", lambda E: E.memset(sb["onesf_row"][:], 1.0), (), ["onesf_row"])
        tr.op("vector", lambda E: E.memset(sb["epsc"][:], EPS), (), ["epsc"])
        tr.op("scalar", lambda E: E.activation(sb["silu_cF"][:], sb["cF"][:], AF.Silu), ["cF"], ["silu_cF"])

        def mtmp(lo, n):
            if lo < 4096:
                assert lo + n <= 4096
                return carve("PF", lo, n, 0, 2)
            return carve("PF", 4096 + (lo - 4096), n, 0, 2)

        wmb, wmbn = carve("PF", 10240, 4096)
        for n in range(12):
            pn, pb = (ps["PA"], "PA") if n % 2 == 0 else (ps["PB"], "PB")
            dma("ct", lambda E, n=n, wmb=wmb: E.dma_start(out=wmb.rearrange("p (c m) -> p c m", m=512), in_=dr["wmod4"][n].transpose([1, 0, 2])),
                r=wmbn, w=wmbn)
            for c in range(8):
                tr.op("tensor", lambda E, c=c, pn=pn, wmb=wmb: E.matmul(
                    pn[0:2, 0:512], sb["silu_cF"][:, 2 * c:2 * c + 2], wmb[:, 512 * c:512 * (c + 1)],
                    start=(c == 0), stop=(c == 7)), ["silu_cF"] + wmbn, [pb])
            mt, mtn = mtmp(512 * n, 512)
            tr.op("vector", lambda E, pn=pn, mt=mt: E.tensor_copy(mt, pn[0:2, 0:512]), [pb], mtn)

        ct, ctn = carve("PF", 8192, 1024, 0, 2)
        cm, cmn = carve("PF", 9216, 1024, 0, 2)
        for lo in (2048, 5120):
            mt, mtn = mtmp(lo, 1024)
            tr.op("vector", lambda E, mt=mt, ct=ct: E.tensor_single_scalar(ct, mt, 0.0, ALU.is_ge), mtn, ctn)
            tr.op("vector", lambda E, ct=ct: E.tensor_scalar(ct, ct, 2.0, -1.0, ALU.mult, ALU.add), ctn, ctn)
            tr.op("scalar", lambda E, mt=mt, cm=cm: E.activation(cm, mt, AF.Abs), mtn, cmn)
            tr.op("vector", lambda E, cm=cm: E.tensor_single_scalar(cm, cm, 0.25, ALU.max), cmn, cmn)
            tr.op("vector", lambda E, mt=mt, ct=ct, cm=cm: E.tensor_mul(mt, ct, cm), ctn + cmn, mtn)

        for vi, base in ((0, 0), (1, 1024)):
            for c in range(8):
                mt, mtn = mtmp(base + TILE * c, TILE)
                tr.op("tensor", lambda E, mt=mt: E.transpose(ps["PC"][:, 0:2], mt, sb["identf"][0:2, 0:2]),
                      mtn + ["identf"], ["PC"])
                tr.op("vector", lambda E, vi=vi, c=c: E.tensor_copy(sb["modF"][:, 16 * vi + 2 * c:16 * vi + 2 * c + 2],
                                                                    ps["PC"][:, 0:2]), ["PC"], ["modF"])
        tr.op("vector", lambda E: E.tensor_copy(sb["saF_bf"][:], sb["modF"][:, 0:16]), ["modF"], ["saF_bf"])
        tr.op("vector", lambda E: E.tensor_scalar_add(sb["caF1p"][:], sb["modF"][:, 16:32], 1.0), ["modF"], ["caF1p"])

        for bi in range(2):
            for seg in range(3):
                mt, mtn = mtmp(2048 * seg, 2048)
                dma("cs", lambda E, bi=bi, seg=seg, mt=mt: E.dma_start(
                    out=dr["modv"][bi:bi + 1, 2048 * seg:2048 * (seg + 1)], in_=mt[bi:bi + 1, :]), r=mtn, w=["modv"])

        pass

    def quant_units():
        nch = {"g": 8, "u": 8, "d": 32}
        wsrc = {"g": "wg4", "u": "wu4", "d": "wd4"}
        wid = {"g": 0, "u": 1, "d": 2}
        ccol = {"g": 0, "u": 8, "d": 16}
        nsl = {"g": 4, "u": 4, "d": 1}
        wcs, wcsn = carve("PF", 12288, 1024)
        ab2, ab2n = carve("PF", 13312, 1024)
        qtb, qtbn = carve("PBF", 28672, 1024)

        for m in ("g", "u", "d"):
            for c in range(nch[m]):
                for sl in range(nsl[m]):
                    def u1(m=m, c=c, sl=sl):
                        dma("ct", lambda E: E.dma_start(out=wcs, in_=dr[wsrc[m]][c][:, 1024 * sl:1024 * (sl + 1)]),
                            r=wcsn, w=wcsn)
                        col = 32 + sl
                        if (c + sl) % 2 == 0:
                            tr.op("scalar", lambda E, col=col: E.activation(
                                ab2, wcs, AF.Abs, accum_out=sb["accw"][:, col:col + 1]),
                                wcsn, ab2n + ["accw"])
                        else:
                            tr.op("vector", lambda E, col=col: E.reduce_sum(
                                sb["accw"][:, col:col + 1], wcs, axis=AX.X, apply_absolute_value=True),
                                wcsn, ["accw"])
                        if sl == nsl[m] - 1:
                            if nsl[m] > 1:
                                tr.op("vector", lambda E: E.tensor_reduce(
                                    sb["accw"][:, ccol[m] + c:ccol[m] + c + 1],
                                    sb["accw"][:, 32:32 + nsl[m]], AX.X, ALU.add), ["accw"], ["accw"])
                            else:
                                tr.op("vector", lambda E: E.tensor_copy(
                                    sb["accw"][:, ccol[m] + c:ccol[m] + c + 1], sb["accw"][:, 32:33]),
                                    ["accw"], ["accw"])
                    yield u1

        for m in ("g", "u", "d"):
            def us(m=m):
                tr.op("tensor", lambda E: E.matmul(ps["PD1"][0:1, 0:nch[m]], sb["onesf_col"][:],
                                                   sb["accw"][:, ccol[m]:ccol[m] + nch[m]], start=True, stop=True),
                      ["onesf_col", "accw"], ["PD1"])
                tr.op("vector", lambda E: E.reduce_sum(sb["stat"][0:1, 84:85], ps["PD1"][0:1, 0:nch[m]], axis=AX.X),
                      ["PD1"], ["statp"])
                tr.op("vector", lambda E: E.tensor_scalar(sb["stat"][0:1, 85:86], sb["stat"][0:1, 84:85],
                                                          1.0 / (INNER * D), 1e-5, ALU.mult, ALU.max),
                      ["statp"], ["statp"])
                tr.op("vector", lambda E: E.reciprocal(sb["stat"][0:1, 86:87], sb["stat"][0:1, 85:86]),
                      ["statp"], ["statp"])
                for (srcc, dst) in ((86, "scw"), (85, "invsw")):
                    tr.op("tensor", lambda E, srcc=srcc: E.matmul(ps["PD2"][:, 0:1], sb["onesf_row"][:],
                                                                  sb["stat"][0:1, srcc:srcc + 1], start=True, stop=True),
                          ["statp", "onesf_row"], ["PD2"])
                    tr.op("vector", lambda E, dst=dst: E.tensor_copy(sb[dst][:, wid[m]:wid[m] + 1], ps["PD2"][:, 0:1]),
                          ["PD2"], [dst])
            yield us

        for m in ("g", "u", "d"):
            for c in range(nch[m]):
                for sl in range(nsl[m]):
                    def u2(m=m, c=c, sl=sl):
                        dma("ct", lambda E: E.dma_start(out=wcs, in_=dr[wsrc[m]][c][:, 1024 * sl:1024 * (sl + 1)]),
                            r=wcsn, w=wcsn)
                        tr.op("vector", lambda E: E.tensor_scalar(wcs, wcs, sb["scw"][:, wid[m]:wid[m] + 1],
                                                                  12582912.0, ALU.mult, ALU.add),
                              wcsn + ["scw"], wcsn)
                        tr.op("vector", lambda E: E.tensor_scalar_add(wcs, wcs, -12582912.0), wcsn, wcsn)
                        tr.op("vector", lambda E: E.tensor_scalar(qtb, wcs, -1.0, 1.0, ALU.max, ALU.min),
                              wcsn, qtbn)
                        if m == "d":
                            dma("cs", lambda E: E.dma_start(out=dr["td"][c], in_=qtb), r=qtbn, w=[])
                        else:
                            tgt = "tg" if m == "g" else "tu"
                            for e2 in range(2):
                                e8 = 2 * sl + e2
                                dma("cs", lambda E, e8=e8, e2=e2, tgt=tgt: E.dma_start(
                                    out=dr[tgt][e8, c], in_=qtb[:, 512 * e2:512 * (e2 + 1)]), r=qtbn, w=[])
                    yield u2

    def wo_prep():
        for c in range(8):
            wv, wvn = carve("PBF", 36864 + D * c, D)
            dma("cw", lambda E, c=c, wv=wv: E.dma_start(out=wv, in_=dr["wo4"][c]), r=wvn, w=wvn)
        for half in range(2):
            for c in range(8):
                wv, wvn = carve("PBF", 36864 + D * c, D)
                tr.op("tensor", lambda E, half=half, c=c, wv=wv: E.matmul(
                    ps["PD1"][0:1, 0:512], sb["betaF_bf"][:, c:c + 1],
                    wv[:, 512 * half:512 * (half + 1)], start=(c == 0), stop=(c == 7)),
                    ["betaF_bf"] + wvn, ["PD1"])
            tr.op("vector", lambda E, half=half: E.tensor_copy(sb["bwo_bf"][0:1, 512 * half:512 * (half + 1)],
                                                               ps["PD1"][0:1, 0:512]), ["PD1"], ["bwo_bf"])
        for c in range(8):
            wv, wvn = carve("PBF", 36864 + D * c, D)
            tr.op("vector", lambda E, c=c, wv=wv: E.tensor_scalar_mul(wv, wv, sb["gammaF"][:, c:c + 1]),
                  wvn + ["gammaF"], wvn)
            dma("cs", lambda E, c=c, wv=wv: E.dma_start(out=dr["wof"][c], in_=wv), r=wvn, w=["wof"])

    def load_wqkv(bi):
        for c in range(8):
            wv, wvn = carve("PBF", 3 * D * c, 3 * D)
            dma("cw", lambda E, c=c, wv=wv: E.dma_start(out=wv, in_=dr["wqkv4"][c]), r=wvn, w=wvn)
        pmap = {0: ("PA", ps["PA"]), 1: ("PB", ps["PB"]), 2: ("PC", ps["PC"])}
        for p in range(3):
            pb, pn = pmap[p]
            for half in range(2):
                for c in range(8):
                    wv, wvn = carve("PBF", 3 * D * c, 3 * D)
                    tr.op("tensor", lambda E, p=p, half=half, c=c, pn=pn, wv=wv: E.matmul(
                        pn[0:1, 512 * half:512 * (half + 1)], sb["saF_bf"][:, 2 * c + bi:2 * c + bi + 1],
                        wv[:, D * p + 512 * half:D * p + 512 * (half + 1)], start=(c == 0), stop=(c == 7)),
                        ["saF_bf"] + wvn, [pb])
                tr.op("vector", lambda E, p=p, half=half, pn=pn: E.tensor_copy(
                    sb["saw_bf"][0:1, D * p + 512 * half:D * p + 512 * (half + 1)],
                    pn[0:1, 512 * half:512 * (half + 1)]), [pb], ["saw_bf"])
        for c in range(8):
            wv, wvn = carve("PBF", 3 * D * c, 3 * D)
            tr.op("vector", lambda E, c=c, wv=wv: E.tensor_scalar_mul(wv, wv, sb["caF1p"][:, 2 * c + bi:2 * c + bi + 1]),
                  wvn + ["caF1p"], wvn)

    def load_wo():
        for c in range(8):
            wv, wvn = carve("PBF", D * c, D)
            dma("cw", lambda E, c=c, wv=wv: E.dma_start(out=wv, in_=dr["wof"][c]), r=["wof"] + wvn, w=wvn)

    def a1_front(gt):
        bi, st = gt // NTS, gt % NTS
        xv, xvn = carve("PF", 10240 + 1024 * (gt % 2), 1024)
        dma("cx", lambda E, bi=bi, st=st, xv=xv: E.dma_start(out=xv, in_=x_h[bi, TILE * st:TILE * (st + 1), :]),
            r=xvn, w=xvn)
        cb0 = 18 * (gt % 2)
        sn = "stat" if gt % 2 == 0 else "stat2"
        stc = sb["stat"][:, cb0:cb0 + 18]
        tr.op("vector", lambda E, xv=xv: E.bn_stats(stc[:, 0:6], xv[:, 0:512]), xvn, [sn])
        tr.op("vector", lambda E, xv=xv: E.bn_stats(stc[:, 6:12], xv[:, 512:1024]), xvn, [sn])
        tr.op("vector", lambda E: E.bn_aggr(stc[:, 12:14], stc[:, 0:12]), [sn], [sn])
        tr.op("scalar", lambda E: E.activation(stc[:, 14:15], stc[:, 13:14], AF.Ln, bias=sb["epsc"][:]), [sn, "epsc"], [sn])
        tr.op("scalar", lambda E: E.activation(stc[:, 15:16], stc[:, 14:15], AF.Exp, scale=-0.5), [sn], [sn])
        tr.op("vector", lambda E: E.tensor_scalar(stc[:, 16:17], stc[:, 12:13], stc[:, 15:16], -1.0, ALU.mult, ALU.mult),
              [sn], [sn])
        xn1, xn1n = carve("PBF", 29696 + 1024 * (gt % 2), 1024)
        tr.op("scalar", lambda E, xv=xv, xn1=xn1: E.activation(xn1, xv, AF.Identity,
                                                               bias=stc[:, 16:17], scale=stc[:, 15:16]),
              xvn + [sn], xn1n)

    def a1_tile(gt):
        bi, st = gt // NTS, gt % NTS
        xn1, xn1n = carve("PBF", 29696 + 1024 * (gt % 2), 1024)
        xn1F, xn1Fn = carve("PBF", 24576, 1024)
        for c in range(8):
            pd, pb = (ps["PD1"], "PD1") if c % 2 == 0 else (ps["PD2"], "PD2")
            tr.op("tensor", lambda E, c=c, pd=pd, xn1=xn1: E.transpose(
                pd.bitcast(BF16)[:, 0:TILE], xn1[:, TILE * c:TILE * (c + 1)],
                sb["identb"][:]), xn1n + ["identb"], [pb])
            tr.op("scalar", lambda E, c=c, pd=pd, xn1F=xn1F: E.activation(xn1F[:, TILE * c:TILE * (c + 1)],
                                                                          pd.bitcast(BF16)[:, 0:TILE], AF.Copy), [pb], xn1Fn)
        pmap = {0: ("PA", ps["PA"]), 1: ("PB", ps["PB"]), 2: ("PC", ps["PC"])}
        for p in range(3):
            pb, pn = pmap[p]
            for half in range(2):
                for c in range(8):
                    wv, wvn = carve("PBF", 3 * D * c, 3 * D)
                    tr.op("tensor", lambda E, p=p, half=half, c=c, pn=pn, wv=wv, xn1F=xn1F: E.matmul(
                        pn[:, 512 * half:512 * (half + 1)], xn1F[:, TILE * c:TILE * (c + 1)],
                        wv[:, D * p + 512 * half:D * p + 512 * (half + 1)], start=(c == 0), stop=False),
                        xn1Fn + wvn, [pb])
                tr.op("tensor", lambda E, p=p, half=half, pn=pn: E.matmul(
                    pn[:, 512 * half:512 * (half + 1)], sb["ones_bf"][0:1, 0:TILE],
                    sb["saw_bf"][0:1, D * p + 512 * half:D * p + 512 * (half + 1)], start=False, stop=True),
                    ["ones_bf", "saw_bf"], [pb])
        if DBG_A1 and gt == 0:
            dma("cs", lambda E, xn1=xn1: E.dma_start(out=dr["dbg1"], in_=xn1), r=xn1n, w=[])
            for pbx, pnx, tag in (("PA", ps["PA"], "dbg2"), ("PB", ps["PB"], "dbg3")):
                tmpd, tmpdn = carve("PF", 2048, 1024)
                tr.op("vector", lambda E, pnx=pnx, tmpd=tmpd: E.tensor_copy(tmpd, pnx[:]), [pbx], tmpdn)
                dma("cs", lambda E, tmpd=tmpd, tag=tag: E.dma_start(out=dr[tag], in_=tmpd), r=tmpdn, w=[])
        for pb, pn, base in (("PA", ps["PA"], 0), ("PB", ps["PB"], 4096)):
            raw, rawn = carve("PF", base + 2048, 1024)
            tr.op("scalar", lambda E, pn=pn, raw=raw: E.activation(raw, pn[:], AF.Copy), [pb], rawn)
            rot, rotn = carve("PF", base, 1024)
            tmp, tmpn = carve("PF", base + 1024, 1024)
            cs = sb["ctbl"][:, DH * st:DH * (st + 1)]
            sn = sb["stbl"][:, DH * st:DH * (st + 1)]
            p3 = raw.rearrange("p (h u) -> p h u", u=DH)
            t3 = tmp.rearrange("p (h u) -> p h u", u=DH)
            d3 = rot.rearrange("p (h u) -> p h u", u=DH)
            sne = sn[:, 0::2].unsqueeze(1).broadcast_to([TILE, H, DH // 2])
            sno = sn[:, 1::2].unsqueeze(1).broadcast_to([TILE, H, DH // 2])
            csf = cs.unsqueeze(1).broadcast_to([TILE, H, DH])
            tr.op("vector", lambda E, t3=t3, sne=sne, p3=p3: E.tensor_mul(t3[:, :, 0::2], sne, p3[:, :, 1::2]),
                  rawn + ["stbl"], tmpn)
            tr.op("vector", lambda E, t3=t3, sno=sno, p3=p3: E.tensor_mul(t3[:, :, 1::2], sno, p3[:, :, 0::2]),
                  rawn + ["stbl"], tmpn)
            tr.op("vector", lambda E, d3=d3, csf=csf, p3=p3: E.tensor_mul(d3, csf, p3), rawn + ["ctbl"], rotn)
            rotb, rotbn = carve("PBF", 31744 if base == 0 else 32768, 1024)
            tr.op("vector", lambda E, rotb=rotb, rot=rot, tmp=tmp: E.tensor_add(rotb, rot, tmp),
                  rotn + tmpn, rotbn)
        vbf, vbfn = carve("PBF", 25600, 1024)
        tr.op("scalar", lambda E, vbf=vbf: E.activation(vbf, ps["PC"][:], AF.Copy), ["PC"], vbfn)
        dma("cs", lambda E, gt=gt, vbf=vbf: E.dma_start(out=dr["sp_kv"][gt, 1], in_=vbf), r=vbfn, w=["sp_kv"])
        for base, dsto, tgt in ((0, 26624, "sp_q"), (4096, 27648, "sp_kv")):
            rotb, rotbn = carve("PBF", 31744 if base == 0 else 32768, 1024)
            dst, dstn = carve("PBF", dsto, 1024)
            for c in range(8):
                pd, pb2 = (ps["PD1"], "PD1") if c % 2 == 0 else (ps["PD2"], "PD2")
                tr.op("tensor", lambda E, c=c, pd=pd, rotb=rotb: E.transpose(
                    pd.bitcast(BF16)[:, 0:TILE], rotb[:, TILE * c:TILE * (c + 1)],
                    sb["identb"][:]), rotbn + ["identb"], [pb2])
                tr.op("vector", lambda E, c=c, pd=pd, dst=dst: E.tensor_copy(dst[:, TILE * c:TILE * (c + 1)],
                                                                             pd.bitcast(BF16)[:, 0:TILE]), [pb2], dstn)
            if tgt == "sp_q":
                dma("cs", lambda E, gt=gt, dst=dst: E.dma_start(out=dr["sp_q"][gt], in_=dst), r=dstn, w=["sp_q"])
            else:
                dma("cs", lambda E, gt=gt, dst=dst: E.dma_start(out=dr["sp_kv"][gt, 0], in_=dst), r=dstn, w=["sp_kv"])

    def a2():
        for qg in range(N_QG):
            ntj = 4 * qg + 4
            qfg, qfgn = carve("PBF", 0, 8192)
            for si in range(2):
                for tq in range(4):
                    gt = si * NTS + 4 * qg + tq
                    if gt >= N_A1:
                        continue
                    sl = qfg[:, 1024 * (4 * si + tq):1024 * (4 * si + tq + 1)]
                    dma("ct", lambda E, gt=gt, sl=sl: E.dma_start(out=sl, in_=dr["sp_q"][gt]),
                        r=["sp_q"] + qfgn, w=qfgn)
            wcols = TILE * (4 * qg + 4 + DPAD)
            for hp in range(H // 2):
                h0, h1 = 2 * hp, 2 * hp + 1
                ds = 2 * (hp % 2)
                dmv0, dmv0n = carve("PF", 2432 * ds, 2432)
                dmv1, dmv1n = carve("PF", 2432 * (ds + 1), 2432)
                dma("ct", lambda E, h0=h0, wcols=wcols, dmv0=dmv0: E.dma_start(
                    out=dmv0[:, 0:wcols], in_=dr["dmt"][h0][:, 0:wcols]), r=dmv0n, w=dmv0n)
                dma("ct", lambda E, h1=h1, wcols=wcols, dmv1=dmv1: E.dma_start(
                    out=dmv1[:, 0:wcols], in_=dr["dmt"][h1][:, 0:wcols]), r=dmv1n, w=dmv1n)
                for si in range(2):
                    if si * NTS + 4 * qg >= N_A1:
                        continue
                    rp0 = (ps["PD1"] if si == 0 else ps["PD2"])[0:DH, 0:512]
                    rp0b = "PD1" if si == 0 else "PD2"
                    rp1 = ps["PC"][0:DH, 0:512] if si == 0 else ps["PC"][0:DH, 512:1024]
                    rp1b = "PC0" if si == 0 else "PC1"
                    for tj in range(ntj):
                        gt = si * NTS + tj
                        ks = tj % 4
                        kvv, kvvn = carve("PBF", 8192 + 2048 * ks, 2048)
                        kv, vv = kvv[:, 0:1024], kvv[:, 1024:2048]
                        kvn = vvn = kvvn
                        dma("ck", lambda E, gt=gt, kvv=kvv: E.dma_start(
                            out=kvv.rearrange("p (a m) -> p a m", m=1024), in_=dr["sp_kv"][gt].transpose([1, 0, 2])),
                            r=["sp_kv"] + kvvn, w=kvvn)
                        off = TILE * (4 * qg - tj + DPAD)
                        for hi, (h, dmv, dmvn, rp, rpb) in enumerate(
                                ((h0, dmv0, dmv0n, rp0, rp0b), (h1, dmv1, dmv1n, rp1, rp1b))):
                            kp = tj % 2
                            sc = (ps["PA"] if hi == 0 else ps["PB"])[:, 512 * kp:512 * (kp + 1)]
                            scb = ("PA" if hi == 0 else "PB") + str(kp)
                            qrhs = qfg.rearrange("p (t m) -> p t m", m=1024)[
                                64 * (h % 2):64 * (h % 2) + 64, 4 * si:4 * si + 4,
                                TILE * (h // 2):TILE * (h // 2 + 1)]
                            tr.op("tensor", lambda E, h=h, kv=kv, sc=sc, qrhs=qrhs: E.matmul(
                                sc, kv[64 * (h % 2):64 * (h % 2) + 64, TILE * (h // 2):TILE * (h // 2 + 1)],
                                qrhs, start=True, stop=True), kvn + qfgn, [scb])
                            smv, smvn = carve("PBF", 16384 + 512 * (2 * (tj % 2) + hi), 512)
                            tr.op("vector", lambda E, sc=sc, smv=smv, dmv=dmv, off=off: E.tensor_mul(
                                smv, sc, dmv[:, off:off + 512]), [scb] + dmvn, smvn)
                            tr.op("tensor", lambda E, h=h, vv=vv, rp=rp, smv=smv, tj=tj, ntj=ntj: E.matmul(
                                rp, vv[:, DH * h:DH * (h + 1)], smv,
                                start=(tj == 0), stop=(tj == ntj - 1)), vvn + smvn, [rpb])
                    for hi, (h, rp, rpb) in enumerate(((h0, rp0, rp0b), (h1, rp1, rp1b))):
                        rfv, rfvn = carve("PF", 14336 + 512 * (2 * si + hi), 512, 0, DH)
                        tr.op("vector", lambda E, rp=rp, rfv=rfv: E.tensor_copy(rfv, rp), [rpb], rfvn)
                        for tq in range(4):
                            gt = si * NTS + 4 * qg + tq
                            if gt >= N_A1:
                                continue
                            dma("cs", lambda E, gt=gt, h=h, rfv=rfv, tq=tq: E.dma_start(
                                out=dr["sp_r"][gt, h], in_=rfv[:, TILE * tq:TILE * (tq + 1)]), r=rfvn, w=["sp_r"])

    def bcast_tile(dst_off, src_b, src_lo, plus1, dstn_out):
        dst, dstn = carve("PF", dst_off, 1024)
        dma("ct", lambda E, dst=dst, src_b=src_b, src_lo=src_lo: E.dma_start(
            out=dst, in_=dr["modv"][src_b:src_b + 1, src_lo:src_lo + 1024].partition_broadcast(TILE).squeeze(1)),
            r=["modv"] + dstn, w=dstn)
        if plus1:
            tr.op("vector", lambda E, dst=dst: E.tensor_scalar_add(dst, dst, 1.0), dstn, dstn)
        dstn_out.extend(dstn)

    def a3_tile(gt, qslot):
        bi, st = gt // NTS, gt % NTS
        stc = sb["stat"]
        rfl, rfln = carve("PF", 4096 + 2048 * (gt % 2), 2048, 0, DH)
        dma("cx", lambda E, gt=gt, rfl=rfl: E.dma_start(out=rfl.rearrange("d (h t) -> d h t", t=128), in_=dr["sp_r"][gt].transpose([1, 0, 2])),
            r=["sp_r"] + rfln, w=rfln)
        xv, xvn = carve("PF", 10240 + 1024 * (gt % 2), 1024)
        dma("cx", lambda E, bi=bi, st=st, xv=xv: E.dma_start(out=xv, in_=x_h[bi, TILE * st:TILE * (st + 1), :]),
            r=xvn, w=xvn)
        rt, rtn = carve("PF", 0, 1024)
        for h in range(H):
            pd, pb = (ps["PD1"], "PD1") if h % 2 == 0 else (ps["PD2"], "PD2")
            tr.op("tensor", lambda E, h=h, pd=pd, rfl=rfl: E.transpose(
                pd[:, 0:DH], rfl[0:DH, TILE * h:TILE * (h + 1)], sb["identf"][0:DH, 0:DH]),
                rfln + ["identf"], [pb])
            tr.op("vector", lambda E, h=h, pd=pd, rt=rt: E.tensor_copy(rt[:, DH * h:DH * (h + 1)], pd[:, 0:DH]),
                  [pb], rtn)
        rt3 = rt.rearrange("p (h u) -> p h u", u=DH)
        tr.op("vector", lambda E, rt3=rt3: E.tensor_reduce(stc[:, 0:16], rt3, AX.X, ALU.add), rtn, ["stat"])
        sq, sqn = carve("PF", 8192, 1024)
        tr.op("scalar", lambda E, rt=rt, sq=sq: E.activation(sq, rt, AF.Square), rtn, sqn)
        tr.op("vector", lambda E, sq=sq: E.tensor_reduce(stc[:, 16:32], sq.rearrange("p (h u) -> p h u", u=DH),
                                                         AX.X, ALU.add), sqn, ["stat"])
        tr.op("vector", lambda E: E.tensor_scalar_mul(stc[:, 0:16], stc[:, 0:16], 1.0 / DH), ["stat"], ["stat"])
        tr.op("vector", lambda E: E.tensor_scalar_mul(stc[:, 16:32], stc[:, 16:32], 1.0 / DH), ["stat"], ["stat"])
        tr.op("vector", lambda E: E.tensor_mul(stc[:, 32:48], stc[:, 0:16], stc[:, 0:16]), ["stat"], ["stat"])
        tr.op("vector", lambda E: E.tensor_sub(stc[:, 16:32], stc[:, 16:32], stc[:, 32:48]), ["stat"], ["stat"])
        tr.op("scalar", lambda E: E.activation(stc[:, 16:32], stc[:, 16:32], AF.Ln, bias=sb["epsc"][:]), ["stat", "epsc"], ["stat"])
        tr.op("scalar", lambda E: E.activation(stc[:, 16:32], stc[:, 16:32], AF.Exp, scale=-0.5), ["stat"], ["stat"])
        if DBG_A1 and gt == 0:
            dma("cs", lambda E, rt=rt: E.dma_start(out=dr["dbg4"], in_=rt), r=rtn, w=[])
        mu_rep = stc[:, 0:16].unsqueeze(2).broadcast_to([TILE, 16, DH])
        rs_rep = stc[:, 16:32].unsqueeze(2).broadcast_to([TILE, 16, DH])
        gn, gnn = carve("PF", 9216, 1024)
        gn3 = gn.rearrange("p (h u) -> p h u", u=DH)
        tr.op("vector", lambda E, gn3=gn3, rt3=rt3, mu_rep=mu_rep: E.tensor_sub(gn3, rt3, mu_rep),
              rtn + ["stat"], gnn)
        gnb, gnbn = carve("PBF", 9216, 1024)
        gnb3 = gnb.rearrange("p (h u) -> p h u", u=DH)
        tr.op("vector", lambda E, gnb3=gnb3, gn3=gn3, rs_rep=rs_rep: E.tensor_mul(gnb3, gn3, rs_rep),
              gnn + ["stat"], gnbn)
        if DBG_A1 and gt == 0:
            dma("cs", lambda E, gn=gn: E.dma_start(out=dr["dbg5"], in_=gn), r=gnn, w=[])
        gnF, gnFn = carve("PBF", 8192, 1024)
        for c in range(8):
            pd, pb = (ps["PD1"], "PD1") if c % 2 == 0 else (ps["PD2"], "PD2")
            tr.op("tensor", lambda E, c=c, pd=pd, gnb=gnb: E.transpose(
                pd.bitcast(BF16)[:, 0:TILE], gnb[:, TILE * c:TILE * (c + 1)],
                sb["identb"][:]), gnbn + ["identb"], [pb])
            tr.op("scalar", lambda E, c=c, pd=pd, gnF=gnF: E.activation(gnF[:, TILE * c:TILE * (c + 1)],
                                                                        pd.bitcast(BF16)[:, 0:TILE], AF.Copy), [pb], gnFn)
        for half in range(2):
            for c in range(8):
                wv, wvn = carve("PBF", D * c, D)
                tr.op("tensor", lambda E, half=half, c=c, wv=wv, gnF=gnF: E.matmul(
                    ps["PA"][:, 512 * half:512 * (half + 1)], gnF[:, TILE * c:TILE * (c + 1)],
                    wv[:, 512 * half:512 * (half + 1)], start=(c == 0), stop=False), gnFn + wvn, ["PA"])
            tr.op("tensor", lambda E, half=half: E.matmul(
                ps["PA"][:, 512 * half:512 * (half + 1)], sb["ones_bf"][0:1, 0:TILE],
                sb["bwo_bf"][0:1, 512 * half:512 * (half + 1)], start=False, stop=True),
                ["ones_bf", "bwo_bf"], ["PA"])
        x1 = sb["x1q"][:, D * qslot:D * (qslot + 1)]
        x1b = f"x1q{qslot}"
        gax, gaxn = carve("PF", 8192, 1024)
        bc1, bc1n = carve("PF", 12288, 1024)
        tr.op("vector", lambda E, gax=gax, bc1=bc1: E.tensor_mul(gax, ps["PA"][:], bc1), ["PA"] + bc1n, gaxn)
        tr.op("vector", lambda E, x1=x1, xv=xv, gax=gax: E.tensor_add(x1, xv, gax), xvn + gaxn, [x1b])
        if DBG_A1 and gt == 0:
            dma("cs", lambda E, x1=x1: E.dma_start(out=dr["dbg6"], in_=x1), r=[x1b], w=[])
        tr.op("vector", lambda E, x1=x1: E.bn_stats(stc[:, 0:6], x1[:, 0:512]), [x1b], ["stat"])
        tr.op("vector", lambda E, x1=x1: E.bn_stats(stc[:, 6:12], x1[:, 512:1024]), [x1b], ["stat"])
        tr.op("vector", lambda E: E.bn_aggr(stc[:, 12:14], stc[:, 0:12]), ["stat"], ["stat"])
        tr.op("scalar", lambda E: E.activation(stc[:, 14:15], stc[:, 13:14], AF.Ln, bias=sb["epsc"][:]), ["stat", "epsc"], ["stat"])
        tr.op("scalar", lambda E, qslot=qslot: E.activation(sb["ln2rstd"][:, qslot:qslot + 1], stc[:, 14:15],
                                                            AF.Exp, scale=-0.5), ["stat"], ["ln2rstd"])
        tr.op("vector", lambda E, qslot=qslot: E.tensor_scalar(sb["ln2nmr"][:, qslot:qslot + 1], stc[:, 12:13],
                                                               sb["ln2rstd"][:, qslot:qslot + 1], -1.0,
                                                               ALU.mult, ALU.mult), ["stat", "ln2rstd"], ["ln2nmr"])

    MAGIC = 12582912.0

    def b1_tile(quad, tq):
        stc = sb["stat"]
        bc1, bc1n = carve("PF", 3072, 1024)
        bc2, bc2n = carve("PF", 13312, 1024)
        x1 = sb["x1q"][:, D * tq:D * (tq + 1)]
        x1b = f"x1q{tq}"
        xn2, xn2n = carve("PF", 2048, 1024)
        tr.op("scalar", lambda E, x1=x1, tq=tq, xn2=xn2: E.activation(
            xn2, x1, AF.Identity, bias=sb["ln2nmr"][:, tq:tq + 1], scale=sb["ln2rstd"][:, tq:tq + 1]),
            [x1b, "ln2rstd", "ln2nmr"], xn2n)
        tr.op("vector", lambda E, xn2=xn2, bc1=bc1: E.tensor_mul(xn2, xn2, bc1), xn2n + bc1n, xn2n)
        tr.op("vector", lambda E, xn2=xn2, bc2=bc2: E.tensor_add(xn2, xn2, bc2), xn2n + bc2n, xn2n)
        tr.op("vector", lambda E, xn2=xn2: E.reduce_max(stc[:, 48:49], xn2, axis=AX.X, apply_absolute_value=True),
              xn2n, ["statb"])
        tr.op("vector", lambda E: E.tensor_single_scalar(stc[:, 49:50], stc[:, 48:49], 1e-5, ALU.max),
              ["statb"], ["statb"])
        tr.op("vector", lambda E: E.reciprocal(stc[:, 50:51], stc[:, 49:50]), ["statb"], ["statb"])
        tr.op("vector", lambda E: E.tensor_scalar_mul(stc[:, 51:52], stc[:, 50:51], 127.0), ["statb"], ["statb"])
        tr.op("vector", lambda E: E.tensor_scalar_mul(stc[:, 52:53], stc[:, 49:50], 1.0 / 127.0), ["statb"], ["statb"])
        tr.op("vector", lambda E, tq=tq: E.tensor_mul(stc[:, 56 + tq:57 + tq], stc[:, 52:53], sb["invsw"][:, 0:1]),
              ["statb", "invsw"], ["esg"])
        tr.op("vector", lambda E, tq=tq: E.tensor_mul(stc[:, 60 + tq:61 + tq], stc[:, 52:53], sb["invsw"][:, 1:2]),
              ["statb", "invsw"], ["esg"])
        if DBG_A1 and quad == 0 and tq == 0:
            dma("cs", lambda E, xn2=xn2: E.dma_start(out=dr["dbg7"], in_=xn2), r=xn2n, w=[])
        q1, q1n = carve("PF", 1024, 1024)
        tr.op("vector", lambda E, xn2=xn2, q1=q1: E.tensor_scalar(q1, xn2, stc[:, 51:52], MAGIC, ALU.mult, ALU.add),
              xn2n + ["statb"], q1n)
        tr.op("vector", lambda E, q1=q1: E.tensor_scalar_add(q1, q1, -MAGIC), q1n, q1n)
        q1b, q1bn = carve("PBF", 47104, 1024)
        tr.op("vector", lambda E, q1=q1, q1b=q1b: E.tensor_scalar(q1b, q1, -128.0, 127.0, ALU.max, ALU.min),
              q1n, q1bn)
        if DBG_A1 and quad == 0 and tq == 0:
            dma("cs", lambda E, q1=q1: E.dma_start(out=dr["dbg8"], in_=q1), r=q1n, w=[])

    def b1_trans(quad, tq):
        q1b, q1bn = carve("PBF", 47104, 1024)
        xqF, xqFn = carve("PBF", 32768 + 1024 * tq, 1024)
        for c in range(8):
            po, pb = (0, "PC0") if c % 2 == 0 else (1024, "PC1")
            tr.op("tensor", lambda E, c=c, po=po, q1b=q1b: E.transpose(
                ps["PC"].bitcast(BF16)[:, po:po + TILE], q1b[:, TILE * c:TILE * (c + 1)],
                sb["identb"][:]), q1bn + ["identb"], [pb])
            tr.op("scalar", lambda E, c=c, po=po, xqF=xqF: E.activation(
                xqF[:, TILE * c:TILE * (c + 1)], ps["PC"].bitcast(BF16)[:, po:po + TILE], AF.Copy), [pb], xqFn)

    def b_quad(quad):
        bi = (4 * quad) // NTS
        stc = sb["stat"]

        for e8 in range(8):
            ks = e8 % 2
            tgv, tgvn = carve("PBF", 0 + 4096 * ks, 4096)
            tuv, tuvn = carve("PBF", 8192 + 4096 * ks, 4096)
            dma("ct", lambda E, e8=e8, tgv=tgv: E.dma_start(out=tgv.rearrange("p (c m) -> p c m", m=512), in_=dr["tg"][e8].transpose([1, 0, 2])),
                r=tgvn, w=tgvn)
            dma("ct", lambda E, e8=e8, tuv=tuv: E.dma_start(out=tuv.rearrange("p (c m) -> p c m", m=512), in_=dr["tu"][e8].transpose([1, 0, 2])),
                r=tuvn, w=tuvn)
            for tq in range(4):
                xqF, xqFn = carve("PBF", 32768 + 1024 * tq, 1024)
                par = (4 * e8 + tq) % 2
                pg, pgb = (ps["PA"], "PA") if par == 0 else (ps["PB"], "PB")
                for c in range(8):
                    tr.op("tensor", lambda E, c=c, tgv=tgv, xqF=xqF, pg=pg: E.matmul(
                        pg[:, 0:512], xqF[:, TILE * c:TILE * (c + 1)], tgv[:, 512 * c:512 * (c + 1)],
                        start=(c == 0), stop=(c == 7)), xqFn + tgvn, [pgb + "0"])
                    tr.op("tensor", lambda E, c=c, tuv=tuv, xqF=xqF, pg=pg: E.matmul(
                        pg[:, 512:1024], xqF[:, TILE * c:TILE * (c + 1)], tuv[:, 512 * c:512 * (c + 1)],
                        start=(c == 0), stop=(c == 7)), xqFn + tuvn, [pgb + "1"])
                sg, sgn = carve("PF", 8192 + 512 * par, 512)
                tr.op("scalar", lambda E, tq=tq, sg=sg, pg=pg: E.activation(sg, pg[:, 0:512], AF.Silu,
                                                                     scale=stc[:, 56 + tq:57 + tq]),
                      [pgb + "0", "esg"], sgn)
                a2v, a2vn = carve("PBF", 16384 + 4096 * tq + 512 * e8, 512)
                tr.op("vector", lambda E, tq=tq, a2v=a2v, sg=sg, pg=pg: E.scalar_tensor_tensor(
                    a2v, pg[:, 512:1024], stc[:, 60 + tq:61 + tq], sg, ALU.mult, ALU.mult),
                    [pgb + "1", "esg"] + sgn, a2vn)
                tr.op("vector", lambda E, tq=tq, e8=e8, a2v=a2v: E.reduce_max(
                    stc[:, 64 + tq:65 + tq] if e8 == 0 else stc[:, 68 + tq:69 + tq], a2v,
                    axis=AX.X, apply_absolute_value=True), a2vn, ["amx"])
                if e8 > 0:
                    tr.op("vector", lambda E, tq=tq: E.tensor_max(stc[:, 64 + tq:65 + tq], stc[:, 64 + tq:65 + tq],
                                                                  stc[:, 68 + tq:69 + tq]), ["amx"], ["amx"])

        if DBG_A1 and quad == 0:
            dma("cs", lambda E: E.dma_start(out=dr["dbg12"], in_=sb["stat"][:, 64:80]), r=["amx"], w=[])
            a2full, a2fn = carve("PBF", 16384, 4096)
            dma("cs", lambda E, a2full=a2full: E.dma_start(out=dr["dbg9"], in_=a2full),
                r=[f"a2_0"] + a2fn, w=[])
        for pair in range(2):
            for tq2 in range(2):
                tq = 2 * pair + tq2
                tr.op("vector", lambda E, tq=tq: E.tensor_single_scalar(stc[:, 72:73], stc[:, 64 + tq:65 + tq],
                                                                        1e-5, ALU.max), ["amx"], ["statq"])
                tr.op("vector", lambda E: E.reciprocal(stc[:, 73:74], stc[:, 72:73]), ["statq"], ["statq"])
                tr.op("vector", lambda E: E.tensor_scalar_mul(stc[:, 74:75], stc[:, 73:74], 127.0), ["statq"], ["statq"])
                tr.op("vector", lambda E: E.tensor_scalar_mul(stc[:, 75:76], stc[:, 72:73], 1.0 / 127.0), ["statq"], ["statq"])
                tr.op("vector", lambda E, tq=tq: E.tensor_mul(stc[:, 80 + tq:81 + tq], stc[:, 75:76], sb["invsw"][:, 2:3]),
                      ["statq", "invsw"], ["es2q"])
                xq2F, xq2Fn = carve("PBF", 36864 + 4096 * tq2, 4096)
                for qtr in range(4):
                    a2v, a2vn = carve("PBF", 16384 + 4096 * tq + 1024 * qtr, 1024)
                    q1, q1n = carve("PF", 1024, 1024)
                    tr.op("vector", lambda E, a2v=a2v, q1=q1: E.tensor_scalar(
                        q1, a2v, stc[:, 74:75], MAGIC, ALU.mult, ALU.add), a2vn + ["statq"], q1n)
                    if DBG_A1 and quad == 0 and tq == 0:
                        dma("cs", lambda E, q1=q1, qtr=qtr: E.dma_start(
                            out=dr["dbg11"][:, 1024 * qtr:1024 * (qtr + 1)], in_=q1), r=q1n, w=[])
                    tr.op("vector", lambda E, q1=q1: E.tensor_scalar_add(q1, q1, -MAGIC), q1n, q1n)
                    q1b, q1bn = carve("PBF", 48128, 1024)
                    tr.op("vector", lambda E, q1=q1, q1b=q1b: E.tensor_scalar(
                        q1b, q1, -128.0, 127.0, ALU.max, ALU.min), q1n, q1bn)
                    if DBG_A1 and quad == 0 and tq == 0:
                        dma("cs", lambda E, q1=q1, qtr=qtr: E.dma_start(
                            out=dr["dbg10"][:, 1024 * qtr:1024 * (qtr + 1)], in_=q1), r=q1n, w=[])
                    for c in range(8):
                        pd, pb = (ps["PD1"], "PD1") if c % 2 == 0 else (ps["PD2"], "PD2")
                        tr.op("tensor", lambda E, c=c, pd=pd, q1b=q1b: E.transpose(
                            pd.bitcast(BF16)[:, 0:TILE], q1b[:, TILE * c:TILE * (c + 1)],
                            sb["identb"][:]), q1bn + ["identb"], [pb])
                        tr.op("scalar", lambda E, c=c, qtr=qtr, pd=pd, xq2F=xq2F: E.activation(
                            xq2F[:, 1024 * qtr + TILE * c:1024 * qtr + TILE * (c + 1)],
                            pd.bitcast(BF16)[:, 0:TILE], AF.Copy), [pb], xq2Fn)
            for k in range(32):
                wv, wvn = carve("PBF", 45056 + 1024 * (k % 2), 1024)
                dma("ck", lambda E, k=k, wv=wv: E.dma_start(out=wv, in_=dr["td"][k]), r=wvn, w=wvn)
                for tq2 in range(2):
                    xq2F, xq2Fn = carve("PBF", 36864 + 4096 * tq2, 4096)
                    pn, pb = (ps["PA"], "PA") if tq2 == 0 else (ps["PB"], "PB")
                    for half in range(2):
                        tr.op("tensor", lambda E, k=k, half=half, pn=pn, xq2F=xq2F, wv=wv: E.matmul(
                            pn[:, 512 * half:512 * (half + 1)], xq2F[:, TILE * k:TILE * (k + 1)],
                            wv[:, 512 * half:512 * (half + 1)], start=(k == 0), stop=(k == 31)),
                            xq2Fn + wvn, [pb + str(half)])
            gb, gbn = carve("PF", 14336, 1024)
            _ = []
            bcast_tile(14336, bi, 5 * D, False, _)
            for tq2 in range(2):
                tq = 2 * pair + tq2
                gt = 4 * quad + tq
                bi3, st3 = gt // NTS, gt % NTS
                pn, pb = (ps["PA"], "PA") if tq2 == 0 else (ps["PB"], "PB")
                x1 = sb["x1q"][:, D * tq:D * (tq + 1)]
                x1b = f"x1q{tq}"
                tmp, tmpn = carve("PF", 2048, 1024)
                for half in range(2):
                    tr.op("vector", lambda E, pn=pn, half=half, tq=tq, tmp=tmp, gb=gb: E.scalar_tensor_tensor(
                        tmp[:, 512 * half:512 * (half + 1)], pn[:, 512 * half:512 * (half + 1)],
                        stc[:, 80 + tq:81 + tq], gb[:, 512 * half:512 * (half + 1)], ALU.mult, ALU.mult),
                        [pb + str(half), "es2q"] + gbn, tmpn)
                tr.op("vector", lambda E, x1=x1, tmp=tmp: E.tensor_add(x1, x1, tmp), [x1b] + tmpn, [x1b])
                dma("co", lambda E, bi3=bi3, st3=st3, x1=x1: E.dma_start(
                    out=out_h[bi3, TILE * st3:TILE * (st3 + 1), :], in_=x1), r=[x1b], w=[])

    prep()
    load_wqkv(0)
    wo_prep()
    qgen = quant_units()
    qdone = False
    if N_A1 > 0:
        a1_front(0)
    if N_A1 > 1:
        a1_front(1)
    for gt in range(N_A1):
        if gt == NTS:
            load_wqkv(1)
        a1_tile(gt)
        if gt + 2 < N_A1:
            a1_front(gt + 2)
        for _ in range(7):
            try:
                next(qgen)()
            except StopIteration:
                qdone = True
                break
    while not qdone:
        try:
            next(qgen)()
        except StopIteration:
            qdone = True
    a2()
    for quad in range(N_QUAD):
        load_wo()
        bi = (4 * quad) // NTS
        _ = []
        bcast_tile(12288, bi, 2 * D, False, _)
        bcast_tile(3072, bi, 4 * D, True, _)
        bcast_tile(13312, bi, 3 * D, False, _)
        for tq in range(4):
            a3_tile(4 * quad + tq, tq)
            if tq > 0:
                b1_trans(quad, tq - 1)
            b1_tile(quad, tq)
        b1_trans(quad, 3)
        b_quad(quad)

    ops = tr.finalize()
    co_waits = [(k, v) for k, v in tr.dcnt.items() if k[1] == "co"]
    if co_waits:
        ops["sync"].append((sorted(co_waits), lambda E: E.nop(), None))
    nds = 0
    for k in tr.dcnt:
        if k not in sems:
            sems[k] = stack.enter_context(nc.semaphore(f"sd{nds}"))
            nds += 1
    assert nds + len(Tr.ENG) <= 100, f"too many semaphores: {nds}"

    with nc.Block() as blk0:
        def clear(E):
            for s in sems.values():
                E.sem_clear(s)
        blk0.sync(clear)

    with nc.Block() as blk:
        def mk(ename):
            myops = ops[ename]

            def run(E):
                for waits, fn, inc in myops:
                    for k, v in waits:
                        E.wait_ge(sems[k], v)
                    inst = fn(E)
                    if inc is not None:
                        inst.then_inc(sems[inc[0]], inc[1])
            return run
        blk.tensor(mk("tensor"))
        blk.vector(mk("vector"))
        blk.scalar(mk("scalar"))
        blk.gpsimd(mk("gpsimd"))
        blk.sync(mk("sync"))

    n_inst = sum(len(v) for v in ops.values())
    build.sb_info = {k: v for k, v in sb.items()}
    return nc, stack, n_inst


def make_inputs(full):
    tabs = host_tables()
    WQ, WK, WV = full["L0_attn_WQ"], full["L0_attn_WK"], full["L0_attn_WV"]
    wqkvT = np.concatenate([w.transpose(2, 0, 1).reshape(D, D) for w in (WQ, WK, WV)], axis=1)
    wqkv4 = np.ascontiguousarray(wqkvT.reshape(8, TILE, 3 * D))
    wo4 = np.ascontiguousarray(full["L0_attn_WO"].reshape(8, TILE, D))
    wmod4 = np.ascontiguousarray(full["L0_wmod"].reshape(8, TILE, 12, 512).transpose(2, 0, 1, 3))
    wg4 = np.ascontiguousarray(full["L0_ffn_w_gate"].T.reshape(8, TILE, INNER))
    wu4 = np.ascontiguousarray(full["L0_ffn_w_up"].T.reshape(8, TILE, INNER))
    wd4 = np.ascontiguousarray(full["L0_ffn_w_down"].T.reshape(32, TILE, D))
    gammaF = np.ascontiguousarray(full["L0_attn_gamma"].reshape(8, TILE).T)
    betaF = np.ascontiguousarray(full["L0_attn_beta"].reshape(8, TILE).T)
    x = full["L0_attn_x"]
    c = full["L0_c"]
    shared = dict(wqkv4=wqkv4, wo4=wo4, wmod4=wmod4, wg4=wg4, wu4=wu4, wd4=wd4,
                  gammaF=gammaF, betaF=betaF, ctbl=tabs["ctbl"], stbl=tabs["stbl"],
                  dmt=tabs["dmt"], identf=tabs["ident"])
    in_maps = []
    for core in range(8):
        cs = c[2 * core:2 * core + 2]
        cF = np.ascontiguousarray(cs.reshape(2, 8, TILE).transpose(2, 1, 0).reshape(TILE, 16))
        m = dict(shared)
        m["x"] = np.ascontiguousarray(x[2 * core:2 * core + 2])
        m["cF"] = cF
        in_maps.append(m)
    return in_maps


def kernel(**inputs):
    in_maps = make_inputs(inputs)
    nc, stack, n_inst = build()
    res = run_bass_kernel_spmd(nc, in_maps, core_ids=list(range(8)))
    outs = [res.results[i]["out"] for i in range(8)]
    full = np.concatenate(outs, axis=0).astype(np.float32)
    if SPILL_OUT:
        return full, res
    stack.close()
    return full



# version 13
# speedup vs baseline: 1.2246x; 1.0102x over previous
import contextlib
import numpy as np
import concourse.bass as bass
import concourse.mybir as mybir
from concourse.bass_utils import run_bass_kernel_spmd

F32 = mybir.dt.float32
BF16 = mybir.dt.bfloat16
I32 = mybir.dt.int32
AF = mybir.ActivationFunctionType
ALU = mybir.AluOpType
AX = mybir.AxisListType

S, D, H, DH = 2048, 1024, 16, 64
TILE = 128
NTS = S // TILE
NT = 2 * NTS
NQG = NTS // 4
INNER = 4096
EPS = 1e-5
DPAD = 3

N_A1 = NT
N_QG = NQG
N_QUAD = NT // 4
SPILL_OUT = False
DBG_A1 = False


def host_tables():
    import jax
    with jax.default_device(jax.devices("cpu")[0]):
        import jax.numpy as jnp
        expoent = jnp.arange(0, DH, 2, dtype=jnp.float32) / DH
        angle = 10.0 ** (-4.0 * expoent)
        pos = jnp.arange(0, S, dtype=jnp.float32)
        nangle = pos[:, None] * angle[None, :]
        sin = np.asarray(jnp.sin(nangle))
        cos = np.asarray(jnp.cos(nangle))
        scale = 1.0 - 2.0 ** (-5.0 - jnp.arange(H, dtype=jnp.float32))
        logs = jnp.log(scale)
        delta = jnp.arange(S, dtype=jnp.float32)
        logd = delta[None, :] * logs[:, None]
        d32 = np.asarray(jnp.exp(logd))

    ctbl = np.zeros((TILE, NTS * DH), np.float32)
    stbl = np.zeros((TILE, NTS * DH), np.float32)
    for st in range(NTS):
        c = cos[st * TILE:(st + 1) * TILE]
        s = sin[st * TILE:(st + 1) * TILE]
        ctbl[:, st * DH:(st + 1) * DH] = np.repeat(c, 2, axis=1)
        blk = np.empty((TILE, DH), np.float32)
        blk[:, 0::2] = -s
        blk[:, 1::2] = s
        stbl[:, st * DH:(st + 1) * DH] = blk

    ndt = DPAD + NTS
    dmt = np.zeros((H, TILE, ndt * TILE), np.float32)
    jj = np.arange(TILE)[:, None]
    ii = np.arange(TILE)[None, :]
    for t in range(ndt):
        dlt = (t - DPAD) * TILE + ii - jj
        valid = (dlt >= 0) & (dlt < S)
        dl = np.clip(dlt, 0, S - 1)
        for h in range(H):
            dmt[h, :, t * TILE:(t + 1) * TILE] = d32[h][dl] * valid
    ident = np.eye(TILE, dtype=np.float32)
    return dict(ctbl=ctbl, stbl=stbl, dmt=dmt, ident=ident)


class Tr:
    ENG = ("tensor", "vector", "scalar", "gpsimd", "sync")
    DMA_Q = {"cx": "sync", "ct": "sync", "ck": "scalar", "cw": "gpsimd", "cs": "gpsimd", "co": "gpsimd"}
    SELFWAIT = {"vector", "scalar", "gpsimd"}

    def __init__(self):
        self.recs = []
        self.lastw = {}
        self.readers = {}
        self.openg = []
        self.opkey = []

    PSPLIT = {"PA": ("PA0", "PA1"), "PB": ("PB0", "PB1"), "PC": ("PC0", "PC1")}

    def _norm(self, bufs):
        out = []
        for b in bufs:
            if b in self.PSPLIT:
                out.extend(self.PSPLIT[b])
            else:
                out.append(b)
        return out

    def op(self, eng, fn, reads=(), writes=(), dma=None):
        reads = self._norm(reads)
        writes = self._norm(writes)
        deps = set()
        myid = len(self.recs)

        def need(pid):
            if pid is None:
                return
            peng = self.openg[pid]
            pkey = self.opkey[pid]
            if pkey is None and peng == eng and eng not in self.SELFWAIT:
                return
            deps.add(pid)

        for b in reads:
            need(self.lastw.get(b))
        for b in writes:
            need(self.lastw.get(b))
            for r in self.readers.get(b, ()):
                need(r)
        key = None
        if dma is not None:
            assert self.DMA_Q[dma] == eng
            slot = tuple(sorted(writes)) if writes else ("st",) + tuple(sorted(reads))
            key = ("D", dma, slot)
        self.openg.append(eng)
        self.opkey.append(key)
        for b in writes:
            self.lastw[b] = myid
            self.readers[b] = []
        for b in reads:
            self.readers.setdefault(b, []).append(myid)
        self.recs.append((eng, fn, key, deps))

    def finalize(self):
        n = len(self.recs)
        needed = [False] * n
        for (_, _, key, deps) in self.recs:
            for d in deps:
                needed[d] = True
        val = [0] * n
        ecnt = {e: 0 for e in self.ENG}
        dcnt = {}
        for i, (eng, fn, key, deps) in enumerate(self.recs):
            if key is not None:
                dcnt[key] = dcnt.get(key, 0) + 16
                val[i] = dcnt[key]
            elif needed[i]:
                ecnt[eng] += 1
                val[i] = ecnt[eng]
        ops = {e: [] for e in self.ENG}
        waited = {e: {} for e in self.ENG}
        for i, (eng, fn, key, deps) in enumerate(self.recs):
            waits = {}
            for d in deps:
                dk = self.opkey[d]
                sk = dk if dk is not None else ("E", self.openg[d])
                v = val[d]
                if waited[eng].get(sk, 0) >= v:
                    continue
                if waits.get(sk, 0) < v:
                    waits[sk] = v
            for k2, v2 in waits.items():
                waited[eng][k2] = v2
            if key is not None:
                inc = (key, 16)
            elif needed[i]:
                inc = (("E", eng), 1)
            else:
                inc = None
            ops[eng].append((sorted(waits.items()), fn, inc))
        self.dcnt = dcnt
        return ops


def build():
    tabs = host_tables()
    nc = bass.Bass("TRN2", target_bir_lowering=False, detect_race_conditions=False)
    stack = contextlib.ExitStack()
    dr = {}

    def din(name, shape, dt=F32):
        dr[name] = nc.dram_tensor(name, list(shape), dt, kind="ExternalInput").ap()

    def dint(name, shape, dt):
        kind = "ExternalOutput" if SPILL_OUT else "Internal"
        dr[name] = nc.dram_tensor(name, list(shape), dt, kind=kind).ap()

    din("x", [2, S, D])
    dr["out"] = nc.dram_tensor("out", [2, S, D], F32, kind="ExternalOutput").ap()
    din("cF", [TILE, 16])
    din("wmod4", [12, 8, TILE, 512])
    din("wqkv4", [8, TILE, 3 * D])
    din("wo4", [8, TILE, D])
    din("gammaF", [TILE, 8])
    din("betaF", [TILE, 8])
    din("wg4", [8, TILE, INNER])
    din("wu4", [8, TILE, INNER])
    din("wd4", [32, TILE, D])
    din("ctbl", [TILE, NTS * DH])
    din("stbl", [TILE, NTS * DH])
    din("dmt", [H, TILE, (DPAD + NTS) * TILE])
    din("identf", [TILE, TILE])
    dint("sp_q", [NT, TILE, D], BF16)
    dint("sp_kv", [NT, 2, TILE, D], BF16)
    dint("sp_r", [NT, H, DH, TILE], BF16)
    dint("modv", [2, 6 * D], F32)
    if DBG_A1:
        for i in (1, 2, 3):
            dint(f"dbg{i}", [TILE, D], F32)
        for i in (4, 5, 6, 7, 8):
            dint(f"dbg{i}", [TILE, D], F32)
        for i in (9, 10, 11):
            dint(f"dbg{i}", [TILE, INNER], F32)
        dint("dbg12", [TILE, 16], F32)
    dint("tg", [8, 8, TILE, 512], BF16)
    dint("tu", [8, 8, TILE, 512], BF16)
    dint("td", [32, TILE, D], BF16)
    dint("wof", [8, TILE, D], BF16)

    sb = {}

    def sbuf(name, shape, dt):
        sb[name] = stack.enter_context(nc.sbuf_tensor("sb_" + name, list(shape), dt))

    sbuf("identf", [TILE, TILE], F32)
    sbuf("ctbl", [TILE, NTS * DH], F32)
    sbuf("stbl", [TILE, NTS * DH], F32)
    sbuf("ones_bf", [1, 512], BF16)
    sbuf("onesf_col", [TILE, 1], F32)
    sbuf("onesf_row", [1, TILE], F32)
    sbuf("cF", [TILE, 16], F32)
    sbuf("silu_cF", [TILE, 16], F32)
    sbuf("gammaF", [TILE, 8], F32)
    sbuf("betaF_bf", [TILE, 8], BF16)
    sbuf("modF", [TILE, 32], F32)
    sbuf("saF_bf", [TILE, 16], BF16)
    sbuf("caF1p", [TILE, 16], F32)
    sbuf("saw_bf", [1, 3 * D], BF16)
    sbuf("bwo_bf", [1, D], BF16)
    sbuf("scw", [TILE, 3], F32)
    sbuf("invsw", [TILE, 3], F32)
    sbuf("accw", [TILE, 48], F32)
    sbuf("stat", [TILE, 96], F32)
    sbuf("ln2rstd", [TILE, 4], F32)
    sbuf("ln2nmr", [TILE, 4], F32)
    sbuf("epsc", [TILE, 1], F32)
    sbuf("x1q", [TILE, 4 * D], F32)
    sbuf("PF", [TILE, 16384], F32)
    sbuf("identb", [TILE, TILE], BF16)
    sbuf("PBF", [TILE, 49152], BF16)
    sbuf("PI", [TILE, 1024], I32)

    SEG = {"PF": 512, "PBF": 1024, "PI": 512}

    def carve(pool, off, n, p0=0, p1=TILE):
        ap = sb[pool][p0:p1, off:off + n]
        s0, s1 = off // SEG[pool], (off + n - 1) // SEG[pool]
        names = [f"{pool}:{i}" for i in range(s0, s1 + 1)]
        return ap, names

    ps = {}
    for n, shp in (("PA", [TILE, 1024]), ("PB", [TILE, 1024]), ("PC", [TILE, 1024]),
                   ("PD1", [TILE, 512]), ("PD2", [TILE, 512])):
        ps[n] = stack.enter_context(nc.psum_tensor(n, shp, F32))

    tr = Tr()
    sems = {}
    for e in Tr.ENG:
        sems[("E", e)] = stack.enter_context(nc.semaphore("se_" + e))

    x_h, out_h = dr["x"], dr["out"]

    def dma(chan, fn, r=(), w=()):
        tr.op(Tr.DMA_Q[chan], fn, r, w, dma=chan)

    def rnd_nearest(src_ap, src_n, w_slices, rbufs):
        n = src_n
        for lo in range(0, n, 1024):
            wdt = min(1024, n - lo)
            sl = src_ap[:, lo:lo + wdt]
            ri, rin = carve("PI", 0, wdt)
            dv, dvn = carve("PF", FD_RND, wdt)
            bv, bvn = carve("PF", FD_RND + 1024, wdt)
            tr.op("vector", lambda E, sl=sl, ri=ri: E.tensor_copy(ri, sl), w_slices, rin)
            tr.op("vector", lambda E, sl=sl, ri=ri, dv=dv: E.tensor_sub(dv, sl, ri), list(w_slices) + rin, dvn)
            tr.op("vector", lambda E, dv=dv, bv=bv: E.tensor_single_scalar(bv, dv, -0.5, ALU.is_lt), dvn, bvn)
            tr.op("vector", lambda E, sl=sl, dv=dv, ri=ri: E.scalar_tensor_tensor(sl, dv, 0.5, ri, ALU.is_gt, ALU.add),
                  dvn + rin, w_slices)
            tr.op("vector", lambda E, sl=sl, bv=bv: E.tensor_sub(sl, sl, bv), list(w_slices) + bvn, w_slices)

    FD_RND = 14336

    def prep():
        for nm in ("identf", "ctbl", "stbl", "cF", "gammaF"):
            dma("cx", lambda E, nm=nm: E.dma_start(out=sb[nm][:], in_=dr[nm]), w=[nm])
        dma("cw", lambda E: E.dma_start(out=sb["betaF_bf"][:], in_=dr["betaF"]), w=["betaF_bf"])
        tr.op("vector", lambda E: E.tensor_copy(sb["identb"][:], sb["identf"][:]), ["identf"], ["identb"])
        tr.op("vector", lambda E: E.memset(sb["ones_bf"][:], 1.0), (), ["ones_bf"])
        tr.op("vector", lambda E: E.memset(sb["onesf_col"][:], 1.0), (), ["onesf_col"])
        tr.op("vector", lambda E: E.memset(sb["onesf_row"][:], 1.0), (), ["onesf_row"])
        tr.op("vector", lambda E: E.memset(sb["epsc"][:], EPS), (), ["epsc"])
        tr.op("scalar", lambda E: E.activation(sb["silu_cF"][:], sb["cF"][:], AF.Silu), ["cF"], ["silu_cF"])

        def mtmp(lo, n):
            if lo < 4096:
                assert lo + n <= 4096
                return carve("PF", lo, n, 0, 2)
            return carve("PF", 4096 + (lo - 4096), n, 0, 2)

        wmb, wmbn = carve("PF", 10240, 4096)
        for n in range(12):
            pn, pb = (ps["PA"], "PA") if n % 2 == 0 else (ps["PB"], "PB")
            dma("ct", lambda E, n=n, wmb=wmb: E.dma_start(out=wmb.rearrange("p (c m) -> p c m", m=512), in_=dr["wmod4"][n].transpose([1, 0, 2])),
                r=wmbn, w=wmbn)
            for c in range(8):
                tr.op("tensor", lambda E, c=c, pn=pn, wmb=wmb: E.matmul(
                    pn[0:2, 0:512], sb["silu_cF"][:, 2 * c:2 * c + 2], wmb[:, 512 * c:512 * (c + 1)],
                    start=(c == 0), stop=(c == 7)), ["silu_cF"] + wmbn, [pb])
            mt, mtn = mtmp(512 * n, 512)
            tr.op("vector", lambda E, pn=pn, mt=mt: E.tensor_copy(mt, pn[0:2, 0:512]), [pb], mtn)

        ct, ctn = carve("PF", 8192, 1024, 0, 2)
        cm, cmn = carve("PF", 9216, 1024, 0, 2)
        for lo in (2048, 5120):
            mt, mtn = mtmp(lo, 1024)
            tr.op("vector", lambda E, mt=mt, ct=ct: E.tensor_single_scalar(ct, mt, 0.0, ALU.is_ge), mtn, ctn)
            tr.op("vector", lambda E, ct=ct: E.tensor_scalar(ct, ct, 2.0, -1.0, ALU.mult, ALU.add), ctn, ctn)
            tr.op("scalar", lambda E, mt=mt, cm=cm: E.activation(cm, mt, AF.Abs), mtn, cmn)
            tr.op("vector", lambda E, cm=cm: E.tensor_single_scalar(cm, cm, 0.25, ALU.max), cmn, cmn)
            tr.op("vector", lambda E, mt=mt, ct=ct, cm=cm: E.tensor_mul(mt, ct, cm), ctn + cmn, mtn)

        for vi, base in ((0, 0), (1, 1024)):
            for c in range(8):
                mt, mtn = mtmp(base + TILE * c, TILE)
                tr.op("tensor", lambda E, mt=mt: E.transpose(ps["PC"][:, 0:2], mt, sb["identf"][0:2, 0:2]),
                      mtn + ["identf"], ["PC"])
                tr.op("vector", lambda E, vi=vi, c=c: E.tensor_copy(sb["modF"][:, 16 * vi + 2 * c:16 * vi + 2 * c + 2],
                                                                    ps["PC"][:, 0:2]), ["PC"], ["modF"])
        tr.op("vector", lambda E: E.tensor_copy(sb["saF_bf"][:], sb["modF"][:, 0:16]), ["modF"], ["saF_bf"])
        tr.op("vector", lambda E: E.tensor_scalar_add(sb["caF1p"][:], sb["modF"][:, 16:32], 1.0), ["modF"], ["caF1p"])

        for bi in range(2):
            for seg in range(3):
                mt, mtn = mtmp(2048 * seg, 2048)
                dma("cs", lambda E, bi=bi, seg=seg, mt=mt: E.dma_start(
                    out=dr["modv"][bi:bi + 1, 2048 * seg:2048 * (seg + 1)], in_=mt[bi:bi + 1, :]), r=mtn, w=["modv"])

        pass

    def quant_units():
        nch = {"g": 8, "u": 8, "d": 32}
        wsrc = {"g": "wg4", "u": "wu4", "d": "wd4"}
        wid = {"g": 0, "u": 1, "d": 2}
        ccol = {"g": 0, "u": 8, "d": 16}
        nsl = {"g": 4, "u": 4, "d": 1}
        wcs, wcsn = carve("PF", 12288, 1024)
        ab2, ab2n = carve("PF", 13312, 1024)
        qtb, qtbn = carve("PBF", 28672, 1024)

        for m in ("g", "u", "d"):
            for c in range(nch[m]):
                for sl in range(nsl[m]):
                    def u1(m=m, c=c, sl=sl):
                        dma("ct", lambda E: E.dma_start(out=wcs, in_=dr[wsrc[m]][c][:, 1024 * sl:1024 * (sl + 1)]),
                            r=wcsn, w=wcsn)
                        col = 32 + sl
                        if (c + sl) % 2 == 0:
                            tr.op("scalar", lambda E, col=col: E.activation(
                                ab2, wcs, AF.Abs, accum_out=sb["accw"][:, col:col + 1]),
                                wcsn, ab2n + ["accw"])
                        else:
                            tr.op("vector", lambda E, col=col: E.reduce_sum(
                                sb["accw"][:, col:col + 1], wcs, axis=AX.X, apply_absolute_value=True),
                                wcsn, ["accw"])
                        if sl == nsl[m] - 1:
                            if nsl[m] > 1:
                                tr.op("vector", lambda E: E.tensor_reduce(
                                    sb["accw"][:, ccol[m] + c:ccol[m] + c + 1],
                                    sb["accw"][:, 32:32 + nsl[m]], AX.X, ALU.add), ["accw"], ["accw"])
                            else:
                                tr.op("vector", lambda E: E.tensor_copy(
                                    sb["accw"][:, ccol[m] + c:ccol[m] + c + 1], sb["accw"][:, 32:33]),
                                    ["accw"], ["accw"])
                    yield u1

        for m in ("g", "u", "d"):
            def us(m=m):
                tr.op("tensor", lambda E: E.matmul(ps["PD1"][0:1, 0:nch[m]], sb["onesf_col"][:],
                                                   sb["accw"][:, ccol[m]:ccol[m] + nch[m]], start=True, stop=True),
                      ["onesf_col", "accw"], ["PD1"])
                tr.op("vector", lambda E: E.reduce_sum(sb["stat"][0:1, 84:85], ps["PD1"][0:1, 0:nch[m]], axis=AX.X),
                      ["PD1"], ["statp"])
                tr.op("vector", lambda E: E.tensor_scalar(sb["stat"][0:1, 85:86], sb["stat"][0:1, 84:85],
                                                          1.0 / (INNER * D), 1e-5, ALU.mult, ALU.max),
                      ["statp"], ["statp"])
                tr.op("vector", lambda E: E.reciprocal(sb["stat"][0:1, 86:87], sb["stat"][0:1, 85:86]),
                      ["statp"], ["statp"])
                for (srcc, dst) in ((86, "scw"), (85, "invsw")):
                    tr.op("tensor", lambda E, srcc=srcc: E.matmul(ps["PD2"][:, 0:1], sb["onesf_row"][:],
                                                                  sb["stat"][0:1, srcc:srcc + 1], start=True, stop=True),
                          ["statp", "onesf_row"], ["PD2"])
                    tr.op("vector", lambda E, dst=dst: E.tensor_copy(sb[dst][:, wid[m]:wid[m] + 1], ps["PD2"][:, 0:1]),
                          ["PD2"], [dst])
            yield us

        for m in ("g", "u", "d"):
            for c in range(nch[m]):
                for sl in range(nsl[m]):
                    def u2(m=m, c=c, sl=sl):
                        dma("ct", lambda E: E.dma_start(out=wcs, in_=dr[wsrc[m]][c][:, 1024 * sl:1024 * (sl + 1)]),
                            r=wcsn, w=wcsn)
                        tr.op("vector", lambda E: E.tensor_scalar(wcs, wcs, sb["scw"][:, wid[m]:wid[m] + 1],
                                                                  12582912.0, ALU.mult, ALU.add),
                              wcsn + ["scw"], wcsn)
                        tr.op("vector", lambda E: E.tensor_scalar_add(wcs, wcs, -12582912.0), wcsn, wcsn)
                        tr.op("vector", lambda E: E.tensor_scalar(qtb, wcs, -1.0, 1.0, ALU.max, ALU.min),
                              wcsn, qtbn)
                        if m == "d":
                            dma("cs", lambda E: E.dma_start(out=dr["td"][c], in_=qtb), r=qtbn, w=[])
                        else:
                            tgt = "tg" if m == "g" else "tu"
                            for e2 in range(2):
                                e8 = 2 * sl + e2
                                dma("cs", lambda E, e8=e8, e2=e2, tgt=tgt: E.dma_start(
                                    out=dr[tgt][e8, c], in_=qtb[:, 512 * e2:512 * (e2 + 1)]), r=qtbn, w=[])
                    yield u2

    def wo_prep():
        for c in range(8):
            wv, wvn = carve("PBF", 36864 + D * c, D)
            dma("cw", lambda E, c=c, wv=wv: E.dma_start(out=wv, in_=dr["wo4"][c]), r=wvn, w=wvn)
        for half in range(2):
            for c in range(8):
                wv, wvn = carve("PBF", 36864 + D * c, D)
                tr.op("tensor", lambda E, half=half, c=c, wv=wv: E.matmul(
                    ps["PD1"][0:1, 0:512], sb["betaF_bf"][:, c:c + 1],
                    wv[:, 512 * half:512 * (half + 1)], start=(c == 0), stop=(c == 7)),
                    ["betaF_bf"] + wvn, ["PD1"])
            tr.op("vector", lambda E, half=half: E.tensor_copy(sb["bwo_bf"][0:1, 512 * half:512 * (half + 1)],
                                                               ps["PD1"][0:1, 0:512]), ["PD1"], ["bwo_bf"])
        for c in range(8):
            wv, wvn = carve("PBF", 36864 + D * c, D)
            tr.op("vector", lambda E, c=c, wv=wv: E.tensor_scalar_mul(wv, wv, sb["gammaF"][:, c:c + 1]),
                  wvn + ["gammaF"], wvn)
            dma("cs", lambda E, c=c, wv=wv: E.dma_start(out=dr["wof"][c], in_=wv), r=wvn, w=["wof"])

    def load_wqkv(bi):
        for c in range(8):
            wv, wvn = carve("PBF", 3 * D * c, 3 * D)
            dma("cw", lambda E, c=c, wv=wv: E.dma_start(out=wv, in_=dr["wqkv4"][c]), r=wvn, w=wvn)
        pmap = {0: ("PA", ps["PA"]), 1: ("PB", ps["PB"]), 2: ("PC", ps["PC"])}
        for p in range(3):
            pb, pn = pmap[p]
            for half in range(2):
                for c in range(8):
                    wv, wvn = carve("PBF", 3 * D * c, 3 * D)
                    tr.op("tensor", lambda E, p=p, half=half, c=c, pn=pn, wv=wv: E.matmul(
                        pn[0:1, 512 * half:512 * (half + 1)], sb["saF_bf"][:, 2 * c + bi:2 * c + bi + 1],
                        wv[:, D * p + 512 * half:D * p + 512 * (half + 1)], start=(c == 0), stop=(c == 7)),
                        ["saF_bf"] + wvn, [pb])
                tr.op("vector", lambda E, p=p, half=half, pn=pn: E.tensor_copy(
                    sb["saw_bf"][0:1, D * p + 512 * half:D * p + 512 * (half + 1)],
                    pn[0:1, 512 * half:512 * (half + 1)]), [pb], ["saw_bf"])
        for c in range(8):
            wv, wvn = carve("PBF", 3 * D * c, 3 * D)
            tr.op("vector", lambda E, c=c, wv=wv: E.tensor_scalar_mul(wv, wv, sb["caF1p"][:, 2 * c + bi:2 * c + bi + 1]),
                  wvn + ["caF1p"], wvn)

    def load_wo():
        for c in range(8):
            wv, wvn = carve("PBF", D * c, D)
            dma("cw", lambda E, c=c, wv=wv: E.dma_start(out=wv, in_=dr["wof"][c]), r=["wof"] + wvn, w=wvn)

    def a1_front(gt):
        bi, st = gt // NTS, gt % NTS
        xv, xvn = carve("PF", 10240 + 1024 * (gt % 2), 1024)
        dma("cx", lambda E, bi=bi, st=st, xv=xv: E.dma_start(out=xv, in_=x_h[bi, TILE * st:TILE * (st + 1), :]),
            r=xvn, w=xvn)
        cb0 = 18 * (gt % 2)
        sn = "stat" if gt % 2 == 0 else "stat2"
        stc = sb["stat"][:, cb0:cb0 + 18]
        tr.op("vector", lambda E, xv=xv: E.bn_stats(stc[:, 0:6], xv[:, 0:512]), xvn, [sn])
        tr.op("vector", lambda E, xv=xv: E.bn_stats(stc[:, 6:12], xv[:, 512:1024]), xvn, [sn])
        tr.op("vector", lambda E: E.bn_aggr(stc[:, 12:14], stc[:, 0:12]), [sn], [sn])
        tr.op("scalar", lambda E: E.activation(stc[:, 14:15], stc[:, 13:14], AF.Ln, bias=sb["epsc"][:]), [sn, "epsc"], [sn])
        tr.op("scalar", lambda E: E.activation(stc[:, 15:16], stc[:, 14:15], AF.Exp, scale=-0.5), [sn], [sn])
        tr.op("vector", lambda E: E.tensor_scalar(stc[:, 16:17], stc[:, 12:13], stc[:, 15:16], -1.0, ALU.mult, ALU.mult),
              [sn], [sn])
        xn1, xn1n = carve("PBF", 29696 + 1024 * (gt % 2), 1024)
        tr.op("scalar", lambda E, xv=xv, xn1=xn1: E.activation(xn1, xv, AF.Identity,
                                                               bias=stc[:, 16:17], scale=stc[:, 15:16]),
              xvn + [sn], xn1n)

    def a1_tile(gt):
        bi, st = gt // NTS, gt % NTS
        xn1, xn1n = carve("PBF", 29696 + 1024 * (gt % 2), 1024)
        xn1F, xn1Fn = carve("PBF", 24576, 1024)
        for c in range(8):
            pd, pb = (ps["PD1"], "PD1") if c % 2 == 0 else (ps["PD2"], "PD2")
            tr.op("tensor", lambda E, c=c, pd=pd, xn1=xn1: E.transpose(
                pd.bitcast(BF16)[:, 0:TILE], xn1[:, TILE * c:TILE * (c + 1)],
                sb["identb"][:]), xn1n + ["identb"], [pb])
            tr.op("scalar", lambda E, c=c, pd=pd, xn1F=xn1F: E.activation(xn1F[:, TILE * c:TILE * (c + 1)],
                                                                          pd.bitcast(BF16)[:, 0:TILE], AF.Copy), [pb], xn1Fn)
        pmap = {0: ("PA", ps["PA"]), 1: ("PB", ps["PB"]), 2: ("PC", ps["PC"])}
        for p in range(3):
            pb, pn = pmap[p]
            for half in range(2):
                for c in range(8):
                    wv, wvn = carve("PBF", 3 * D * c, 3 * D)
                    tr.op("tensor", lambda E, p=p, half=half, c=c, pn=pn, wv=wv, xn1F=xn1F: E.matmul(
                        pn[:, 512 * half:512 * (half + 1)], xn1F[:, TILE * c:TILE * (c + 1)],
                        wv[:, D * p + 512 * half:D * p + 512 * (half + 1)], start=(c == 0), stop=False),
                        xn1Fn + wvn, [pb])
                tr.op("tensor", lambda E, p=p, half=half, pn=pn: E.matmul(
                    pn[:, 512 * half:512 * (half + 1)], sb["ones_bf"][0:1, 0:TILE],
                    sb["saw_bf"][0:1, D * p + 512 * half:D * p + 512 * (half + 1)], start=False, stop=True),
                    ["ones_bf", "saw_bf"], [pb])
        if DBG_A1 and gt == 0:
            dma("cs", lambda E, xn1=xn1: E.dma_start(out=dr["dbg1"], in_=xn1), r=xn1n, w=[])
            for pbx, pnx, tag in (("PA", ps["PA"], "dbg2"), ("PB", ps["PB"], "dbg3")):
                tmpd, tmpdn = carve("PF", 2048, 1024)
                tr.op("vector", lambda E, pnx=pnx, tmpd=tmpd: E.tensor_copy(tmpd, pnx[:]), [pbx], tmpdn)
                dma("cs", lambda E, tmpd=tmpd, tag=tag: E.dma_start(out=dr[tag], in_=tmpd), r=tmpdn, w=[])
        for pb, pn, base in (("PA", ps["PA"], 0), ("PB", ps["PB"], 4096)):
            raw, rawn = carve("PF", base + 2048, 1024)
            tr.op("scalar", lambda E, pn=pn, raw=raw: E.activation(raw, pn[:], AF.Copy), [pb], rawn)
            rot, rotn = carve("PF", base, 1024)
            tmp, tmpn = carve("PF", base + 1024, 1024)
            cs = sb["ctbl"][:, DH * st:DH * (st + 1)]
            sn = sb["stbl"][:, DH * st:DH * (st + 1)]
            p3 = raw.rearrange("p (h u) -> p h u", u=DH)
            t3 = tmp.rearrange("p (h u) -> p h u", u=DH)
            d3 = rot.rearrange("p (h u) -> p h u", u=DH)
            sne = sn[:, 0::2].unsqueeze(1).broadcast_to([TILE, H, DH // 2])
            sno = sn[:, 1::2].unsqueeze(1).broadcast_to([TILE, H, DH // 2])
            csf = cs.unsqueeze(1).broadcast_to([TILE, H, DH])
            tr.op("vector", lambda E, t3=t3, sne=sne, p3=p3: E.tensor_mul(t3[:, :, 0::2], sne, p3[:, :, 1::2]),
                  rawn + ["stbl"], tmpn)
            tr.op("vector", lambda E, t3=t3, sno=sno, p3=p3: E.tensor_mul(t3[:, :, 1::2], sno, p3[:, :, 0::2]),
                  rawn + ["stbl"], tmpn)
            tr.op("vector", lambda E, d3=d3, csf=csf, p3=p3: E.tensor_mul(d3, csf, p3), rawn + ["ctbl"], rotn)
            rotb, rotbn = carve("PBF", 31744 if base == 0 else 32768, 1024)
            tr.op("vector", lambda E, rotb=rotb, rot=rot, tmp=tmp: E.tensor_add(rotb, rot, tmp),
                  rotn + tmpn, rotbn)
        vbf, vbfn = carve("PBF", 25600, 1024)
        tr.op("scalar", lambda E, vbf=vbf: E.activation(vbf, ps["PC"][:], AF.Copy), ["PC"], vbfn)
        dma("cs", lambda E, gt=gt, vbf=vbf: E.dma_start(out=dr["sp_kv"][gt, 1], in_=vbf), r=vbfn, w=["sp_kv"])
        for base, dsto, tgt in ((0, 26624, "sp_q"), (4096, 27648, "sp_kv")):
            rotb, rotbn = carve("PBF", 31744 if base == 0 else 32768, 1024)
            dst, dstn = carve("PBF", dsto, 1024)
            for c in range(8):
                pd, pb2 = (ps["PD1"], "PD1") if c % 2 == 0 else (ps["PD2"], "PD2")
                tr.op("tensor", lambda E, c=c, pd=pd, rotb=rotb: E.transpose(
                    pd.bitcast(BF16)[:, 0:TILE], rotb[:, TILE * c:TILE * (c + 1)],
                    sb["identb"][:]), rotbn + ["identb"], [pb2])
                tr.op("vector", lambda E, c=c, pd=pd, dst=dst: E.tensor_copy(dst[:, TILE * c:TILE * (c + 1)],
                                                                             pd.bitcast(BF16)[:, 0:TILE]), [pb2], dstn)
            if tgt == "sp_q":
                dma("cs", lambda E, gt=gt, dst=dst: E.dma_start(out=dr["sp_q"][gt], in_=dst), r=dstn, w=["sp_q"])
            else:
                dma("cs", lambda E, gt=gt, dst=dst: E.dma_start(out=dr["sp_kv"][gt, 0], in_=dst), r=dstn, w=["sp_kv"])

    def a2():
        for qg in range(N_QG):
            ntj = 4 * qg + 4
            qfg, qfgn = carve("PBF", 0, 8192)
            for si in range(2):
                for tq in range(4):
                    gt = si * NTS + 4 * qg + tq
                    if gt >= N_A1:
                        continue
                    sl = qfg[:, 1024 * (4 * si + tq):1024 * (4 * si + tq + 1)]
                    dma("ct", lambda E, gt=gt, sl=sl: E.dma_start(out=sl, in_=dr["sp_q"][gt]),
                        r=["sp_q"] + qfgn, w=qfgn)
            wcols = TILE * (4 * qg + 4 + DPAD)
            for hp in range(H // 2):
                h0, h1 = 2 * hp, 2 * hp + 1
                ds = 2 * (hp % 2)
                dmv0, dmv0n = carve("PF", 2432 * ds, 2432)
                dmv1, dmv1n = carve("PF", 2432 * (ds + 1), 2432)
                dma("ct", lambda E, h0=h0, wcols=wcols, dmv0=dmv0: E.dma_start(
                    out=dmv0[:, 0:wcols], in_=dr["dmt"][h0][:, 0:wcols]), r=dmv0n, w=dmv0n)
                dma("ct", lambda E, h1=h1, wcols=wcols, dmv1=dmv1: E.dma_start(
                    out=dmv1[:, 0:wcols], in_=dr["dmt"][h1][:, 0:wcols]), r=dmv1n, w=dmv1n)
                for si in range(2):
                    if si * NTS + 4 * qg >= N_A1:
                        continue
                    rp0 = (ps["PD1"] if si == 0 else ps["PD2"])[0:DH, 0:512]
                    rp0b = "PD1" if si == 0 else "PD2"
                    rp1 = ps["PC"][0:DH, 0:512] if si == 0 else ps["PC"][0:DH, 512:1024]
                    rp1b = "PC0" if si == 0 else "PC1"
                    dmp, _dmpn = carve("PF", 2432 * ds, 4864)
                    dmp3a = dmp.rearrange("p (a w) -> p a w", w=2432)
                    for tj in range(ntj):
                        gt = si * NTS + tj
                        ks = tj % 4
                        kvv, kvvn = carve("PBF", 8192 + 2048 * ks, 2048)
                        kv, vv = kvv[:, 0:1024], kvv[:, 1024:2048]
                        kvn = vvn = kvvn
                        dma("ck", lambda E, gt=gt, kvv=kvv: E.dma_start(
                            out=kvv.rearrange("p (a m) -> p a m", m=1024), in_=dr["sp_kv"][gt].transpose([1, 0, 2])),
                            r=["sp_kv"] + kvvn, w=kvvn)
                        off = TILE * (4 * qg - tj + DPAD)
                        kp = tj % 2
                        pg, pgb = (ps["PA"], "PA") if kp == 0 else (ps["PB"], "PB")
                        for hi, h in enumerate((h0, h1)):
                            sc = pg[:, 512 * hi:512 * (hi + 1)]
                            scb = pgb + str(hi)
                            qrhs = qfg.rearrange("p (t m) -> p t m", m=1024)[
                                64 * (h % 2):64 * (h % 2) + 64, 4 * si:4 * si + 4,
                                TILE * (h // 2):TILE * (h // 2 + 1)]
                            tr.op("tensor", lambda E, h=h, kv=kv, sc=sc, qrhs=qrhs: E.matmul(
                                sc, kv[64 * (h % 2):64 * (h % 2) + 64, TILE * (h // 2):TILE * (h // 2 + 1)],
                                qrhs, start=True, stop=True), kvn + qfgn, [scb])
                        smv, smvn = carve("PBF", 16384 + 1024 * kp, 1024)
                        smv3 = smv.rearrange("p (a w) -> p a w", w=512)
                        pg3 = pg.rearrange("p (a w) -> p a w", w=512)
                        tr.op("vector", lambda E, smv3=smv3, pg3=pg3, dmp3a=dmp3a, off=off: E.tensor_mul(
                            smv3, pg3, dmp3a[:, :, off:off + 512]),
                            [pgb + "0", pgb + "1"] + dmv0n + dmv1n, smvn)
                        for hi, (h, rp, rpb) in enumerate(((h0, rp0, rp0b), (h1, rp1, rp1b))):
                            tr.op("tensor", lambda E, h=h, hi=hi, vv=vv, rp=rp, smv=smv, tj=tj, ntj=ntj: E.matmul(
                                rp, vv[:, DH * h:DH * (h + 1)], smv[:, 512 * hi:512 * (hi + 1)],
                                start=(tj == 0), stop=(tj == ntj - 1)), vvn + smvn, [rpb])
                    for hi, (h, rp, rpb) in enumerate(((h0, rp0, rp0b), (h1, rp1, rp1b))):
                        rfv, rfvn = carve("PBF", 18432 + 512 * (2 * si + hi), 512, 0, DH)
                        tr.op("vector", lambda E, rp=rp, rfv=rfv: E.tensor_copy(rfv, rp), [rpb], rfvn)
                        for tq in range(4):
                            gt = si * NTS + 4 * qg + tq
                            if gt >= N_A1:
                                continue
                            dma("cs", lambda E, gt=gt, h=h, rfv=rfv, tq=tq: E.dma_start(
                                out=dr["sp_r"][gt, h], in_=rfv[:, TILE * tq:TILE * (tq + 1)]), r=rfvn, w=["sp_r"])

    def bcast_tile(dst_off, src_b, src_lo, plus1, dstn_out):
        dst, dstn = carve("PF", dst_off, 1024)
        dma("ct", lambda E, dst=dst, src_b=src_b, src_lo=src_lo: E.dma_start(
            out=dst, in_=dr["modv"][src_b:src_b + 1, src_lo:src_lo + 1024].partition_broadcast(TILE).squeeze(1)),
            r=["modv"] + dstn, w=dstn)
        if plus1:
            tr.op("vector", lambda E, dst=dst: E.tensor_scalar_add(dst, dst, 1.0), dstn, dstn)
        dstn_out.extend(dstn)

    def a3_tile(gt, qslot):
        bi, st = gt // NTS, gt % NTS
        stc = sb["stat"]
        rfl, rfln = carve("PBF", 18432 + 2048 * (gt % 2), 2048, 0, DH)
        dma("cx", lambda E, gt=gt, rfl=rfl: E.dma_start(out=rfl.rearrange("d (h t) -> d h t", t=128), in_=dr["sp_r"][gt].transpose([1, 0, 2])),
            r=["sp_r"] + rfln, w=rfln)
        xv, xvn = carve("PF", 10240 + 1024 * (gt % 2), 1024)
        dma("cx", lambda E, bi=bi, st=st, xv=xv: E.dma_start(out=xv, in_=x_h[bi, TILE * st:TILE * (st + 1), :]),
            r=xvn, w=xvn)
        rt, rtn = carve("PF", 0, 1024)
        for h in range(H):
            pd, pb = (ps["PD1"], "PD1") if h % 2 == 0 else (ps["PD2"], "PD2")
            tr.op("tensor", lambda E, h=h, pd=pd, rfl=rfl: E.transpose(
                pd.bitcast(BF16)[:, 0:DH], rfl[0:DH, TILE * h:TILE * (h + 1)], sb["identb"][0:DH, 0:DH]),
                rfln + ["identb"], [pb])
            tr.op("vector", lambda E, h=h, pd=pd, rt=rt: E.tensor_copy(rt[:, DH * h:DH * (h + 1)],
                                                                       pd.bitcast(BF16)[:, 0:DH]), [pb], rtn)
        rt3 = rt.rearrange("p (h u) -> p h u", u=DH)
        tr.op("vector", lambda E, rt3=rt3: E.tensor_reduce(stc[:, 0:16], rt3, AX.X, ALU.add), rtn, ["stat"])
        sq, sqn = carve("PF", 8192, 1024)
        tr.op("scalar", lambda E, rt=rt, sq=sq: E.activation(sq, rt, AF.Square), rtn, sqn)
        tr.op("vector", lambda E, sq=sq: E.tensor_reduce(stc[:, 16:32], sq.rearrange("p (h u) -> p h u", u=DH),
                                                         AX.X, ALU.add), sqn, ["stat"])
        tr.op("vector", lambda E: E.tensor_scalar_mul(stc[:, 0:16], stc[:, 0:16], 1.0 / DH), ["stat"], ["stat"])
        tr.op("vector", lambda E: E.tensor_scalar_mul(stc[:, 16:32], stc[:, 16:32], 1.0 / DH), ["stat"], ["stat"])
        tr.op("vector", lambda E: E.tensor_mul(stc[:, 32:48], stc[:, 0:16], stc[:, 0:16]), ["stat"], ["stat"])
        tr.op("vector", lambda E: E.tensor_sub(stc[:, 16:32], stc[:, 16:32], stc[:, 32:48]), ["stat"], ["stat"])
        tr.op("scalar", lambda E: E.activation(stc[:, 16:32], stc[:, 16:32], AF.Ln, bias=sb["epsc"][:]), ["stat", "epsc"], ["stat"])
        tr.op("scalar", lambda E: E.activation(stc[:, 16:32], stc[:, 16:32], AF.Exp, scale=-0.5), ["stat"], ["stat"])
        if DBG_A1 and gt == 0:
            dma("cs", lambda E, rt=rt: E.dma_start(out=dr["dbg4"], in_=rt), r=rtn, w=[])
        mu_rep = stc[:, 0:16].unsqueeze(2).broadcast_to([TILE, 16, DH])
        rs_rep = stc[:, 16:32].unsqueeze(2).broadcast_to([TILE, 16, DH])
        gn, gnn = carve("PF", 9216, 1024)
        gn3 = gn.rearrange("p (h u) -> p h u", u=DH)
        tr.op("vector", lambda E, gn3=gn3, rt3=rt3, mu_rep=mu_rep: E.tensor_sub(gn3, rt3, mu_rep),
              rtn + ["stat"], gnn)
        gnb, gnbn = carve("PBF", 9216, 1024)
        gnb3 = gnb.rearrange("p (h u) -> p h u", u=DH)
        tr.op("vector", lambda E, gnb3=gnb3, gn3=gn3, rs_rep=rs_rep: E.tensor_mul(gnb3, gn3, rs_rep),
              gnn + ["stat"], gnbn)
        if DBG_A1 and gt == 0:
            dma("cs", lambda E, gn=gn: E.dma_start(out=dr["dbg5"], in_=gn), r=gnn, w=[])
        gnF, gnFn = carve("PBF", 8192, 1024)
        for c in range(8):
            pd, pb = (ps["PD1"], "PD1") if c % 2 == 0 else (ps["PD2"], "PD2")
            tr.op("tensor", lambda E, c=c, pd=pd, gnb=gnb: E.transpose(
                pd.bitcast(BF16)[:, 0:TILE], gnb[:, TILE * c:TILE * (c + 1)],
                sb["identb"][:]), gnbn + ["identb"], [pb])
            tr.op("scalar", lambda E, c=c, pd=pd, gnF=gnF: E.activation(gnF[:, TILE * c:TILE * (c + 1)],
                                                                        pd.bitcast(BF16)[:, 0:TILE], AF.Copy), [pb], gnFn)
        for half in range(2):
            for c in range(8):
                wv, wvn = carve("PBF", D * c, D)
                tr.op("tensor", lambda E, half=half, c=c, wv=wv, gnF=gnF: E.matmul(
                    ps["PA"][:, 512 * half:512 * (half + 1)], gnF[:, TILE * c:TILE * (c + 1)],
                    wv[:, 512 * half:512 * (half + 1)], start=(c == 0), stop=False), gnFn + wvn, ["PA"])
            tr.op("tensor", lambda E, half=half: E.matmul(
                ps["PA"][:, 512 * half:512 * (half + 1)], sb["ones_bf"][0:1, 0:TILE],
                sb["bwo_bf"][0:1, 512 * half:512 * (half + 1)], start=False, stop=True),
                ["ones_bf", "bwo_bf"], ["PA"])
        x1 = sb["x1q"][:, D * qslot:D * (qslot + 1)]
        x1b = f"x1q{qslot}"
        gax, gaxn = carve("PF", 8192, 1024)
        bc1, bc1n = carve("PF", 12288, 1024)
        tr.op("vector", lambda E, gax=gax, bc1=bc1: E.tensor_mul(gax, ps["PA"][:], bc1), ["PA"] + bc1n, gaxn)
        tr.op("vector", lambda E, x1=x1, xv=xv, gax=gax: E.tensor_add(x1, xv, gax), xvn + gaxn, [x1b])
        if DBG_A1 and gt == 0:
            dma("cs", lambda E, x1=x1: E.dma_start(out=dr["dbg6"], in_=x1), r=[x1b], w=[])
        tr.op("vector", lambda E, x1=x1: E.bn_stats(stc[:, 0:6], x1[:, 0:512]), [x1b], ["stat"])
        tr.op("vector", lambda E, x1=x1: E.bn_stats(stc[:, 6:12], x1[:, 512:1024]), [x1b], ["stat"])
        tr.op("vector", lambda E: E.bn_aggr(stc[:, 12:14], stc[:, 0:12]), ["stat"], ["stat"])
        tr.op("scalar", lambda E: E.activation(stc[:, 14:15], stc[:, 13:14], AF.Ln, bias=sb["epsc"][:]), ["stat", "epsc"], ["stat"])
        tr.op("scalar", lambda E, qslot=qslot: E.activation(sb["ln2rstd"][:, qslot:qslot + 1], stc[:, 14:15],
                                                            AF.Exp, scale=-0.5), ["stat"], ["ln2rstd"])
        tr.op("vector", lambda E, qslot=qslot: E.tensor_scalar(sb["ln2nmr"][:, qslot:qslot + 1], stc[:, 12:13],
                                                               sb["ln2rstd"][:, qslot:qslot + 1], -1.0,
                                                               ALU.mult, ALU.mult), ["stat", "ln2rstd"], ["ln2nmr"])

    MAGIC = 12582912.0

    def b1_tile(quad, tq):
        stc = sb["stat"]
        bc1, bc1n = carve("PF", 3072, 1024)
        bc2, bc2n = carve("PF", 13312, 1024)
        x1 = sb["x1q"][:, D * tq:D * (tq + 1)]
        x1b = f"x1q{tq}"
        xn2, xn2n = carve("PF", 2048, 1024)
        tr.op("scalar", lambda E, x1=x1, tq=tq, xn2=xn2: E.activation(
            xn2, x1, AF.Identity, bias=sb["ln2nmr"][:, tq:tq + 1], scale=sb["ln2rstd"][:, tq:tq + 1]),
            [x1b, "ln2rstd", "ln2nmr"], xn2n)
        tr.op("vector", lambda E, xn2=xn2, bc1=bc1: E.tensor_mul(xn2, xn2, bc1), xn2n + bc1n, xn2n)
        tr.op("vector", lambda E, xn2=xn2, bc2=bc2: E.tensor_add(xn2, xn2, bc2), xn2n + bc2n, xn2n)
        tr.op("vector", lambda E, xn2=xn2: E.reduce_max(stc[:, 48:49], xn2, axis=AX.X, apply_absolute_value=True),
              xn2n, ["statb"])
        tr.op("vector", lambda E: E.tensor_single_scalar(stc[:, 49:50], stc[:, 48:49], 1e-5, ALU.max),
              ["statb"], ["statb"])
        tr.op("vector", lambda E: E.reciprocal(stc[:, 50:51], stc[:, 49:50]), ["statb"], ["statb"])
        tr.op("vector", lambda E: E.tensor_scalar_mul(stc[:, 51:52], stc[:, 50:51], 127.0), ["statb"], ["statb"])
        tr.op("vector", lambda E: E.tensor_scalar_mul(stc[:, 52:53], stc[:, 49:50], 1.0 / 127.0), ["statb"], ["statb"])
        tr.op("vector", lambda E, tq=tq: E.tensor_mul(stc[:, 56 + tq:57 + tq], stc[:, 52:53], sb["invsw"][:, 0:1]),
              ["statb", "invsw"], ["esg"])
        tr.op("vector", lambda E, tq=tq: E.tensor_mul(stc[:, 60 + tq:61 + tq], stc[:, 52:53], sb["invsw"][:, 1:2]),
              ["statb", "invsw"], ["esg"])
        if DBG_A1 and quad == 0 and tq == 0:
            dma("cs", lambda E, xn2=xn2: E.dma_start(out=dr["dbg7"], in_=xn2), r=xn2n, w=[])
        q1, q1n = carve("PF", 1024, 1024)
        tr.op("vector", lambda E, xn2=xn2, q1=q1: E.tensor_scalar(q1, xn2, stc[:, 51:52], MAGIC, ALU.mult, ALU.add),
              xn2n + ["statb"], q1n)
        tr.op("vector", lambda E, q1=q1: E.tensor_scalar_add(q1, q1, -MAGIC), q1n, q1n)
        q1b, q1bn = carve("PBF", 47104, 1024)
        tr.op("vector", lambda E, q1=q1, q1b=q1b: E.tensor_scalar(q1b, q1, -128.0, 127.0, ALU.max, ALU.min),
              q1n, q1bn)
        if DBG_A1 and quad == 0 and tq == 0:
            dma("cs", lambda E, q1=q1: E.dma_start(out=dr["dbg8"], in_=q1), r=q1n, w=[])

    def b1_trans(quad, tq):
        q1b, q1bn = carve("PBF", 47104, 1024)
        xqF, xqFn = carve("PBF", 32768 + 1024 * tq, 1024)
        for c in range(8):
            po, pb = (0, "PC0") if c % 2 == 0 else (1024, "PC1")
            tr.op("tensor", lambda E, c=c, po=po, q1b=q1b: E.transpose(
                ps["PC"].bitcast(BF16)[:, po:po + TILE], q1b[:, TILE * c:TILE * (c + 1)],
                sb["identb"][:]), q1bn + ["identb"], [pb])
            tr.op("scalar", lambda E, c=c, po=po, xqF=xqF: E.activation(
                xqF[:, TILE * c:TILE * (c + 1)], ps["PC"].bitcast(BF16)[:, po:po + TILE], AF.Copy), [pb], xqFn)

    def b_quad(quad):
        bi = (4 * quad) // NTS
        stc = sb["stat"]

        for e8 in range(8):
            ks = e8 % 2
            tgv, tgvn = carve("PBF", 0 + 4096 * ks, 4096)
            tuv, tuvn = carve("PBF", 8192 + 4096 * ks, 4096)
            dma("ct", lambda E, e8=e8, tgv=tgv: E.dma_start(out=tgv.rearrange("p (c m) -> p c m", m=512), in_=dr["tg"][e8].transpose([1, 0, 2])),
                r=tgvn, w=tgvn)
            dma("ct", lambda E, e8=e8, tuv=tuv: E.dma_start(out=tuv.rearrange("p (c m) -> p c m", m=512), in_=dr["tu"][e8].transpose([1, 0, 2])),
                r=tuvn, w=tuvn)
            for tq in range(4):
                xqF, xqFn = carve("PBF", 32768 + 1024 * tq, 1024)
                par = (4 * e8 + tq) % 2
                pg, pgb = (ps["PA"], "PA") if par == 0 else (ps["PB"], "PB")
                for c in range(8):
                    tr.op("tensor", lambda E, c=c, tgv=tgv, xqF=xqF, pg=pg: E.matmul(
                        pg[:, 0:512], xqF[:, TILE * c:TILE * (c + 1)], tgv[:, 512 * c:512 * (c + 1)],
                        start=(c == 0), stop=(c == 7)), xqFn + tgvn, [pgb + "0"])
                    tr.op("tensor", lambda E, c=c, tuv=tuv, xqF=xqF, pg=pg: E.matmul(
                        pg[:, 512:1024], xqF[:, TILE * c:TILE * (c + 1)], tuv[:, 512 * c:512 * (c + 1)],
                        start=(c == 0), stop=(c == 7)), xqFn + tuvn, [pgb + "1"])
                sg, sgn = carve("PF", 8192 + 512 * par, 512)
                tr.op("scalar", lambda E, tq=tq, sg=sg, pg=pg: E.activation(sg, pg[:, 0:512], AF.Silu,
                                                                     scale=stc[:, 56 + tq:57 + tq]),
                      [pgb + "0", "esg"], sgn)
                a2v, a2vn = carve("PBF", 16384 + 4096 * tq + 512 * e8, 512)
                tr.op("vector", lambda E, tq=tq, a2v=a2v, sg=sg, pg=pg: E.scalar_tensor_tensor(
                    a2v, pg[:, 512:1024], stc[:, 60 + tq:61 + tq], sg, ALU.mult, ALU.mult),
                    [pgb + "1", "esg"] + sgn, a2vn)
                tr.op("vector", lambda E, tq=tq, e8=e8, a2v=a2v: E.reduce_max(
                    stc[:, 64 + tq:65 + tq] if e8 == 0 else stc[:, 68 + tq:69 + tq], a2v,
                    axis=AX.X, apply_absolute_value=True), a2vn, ["amx"])
                if e8 > 0:
                    tr.op("vector", lambda E, tq=tq: E.tensor_max(stc[:, 64 + tq:65 + tq], stc[:, 64 + tq:65 + tq],
                                                                  stc[:, 68 + tq:69 + tq]), ["amx"], ["amx"])

        if DBG_A1 and quad == 0:
            dma("cs", lambda E: E.dma_start(out=dr["dbg12"], in_=sb["stat"][:, 64:80]), r=["amx"], w=[])
            a2full, a2fn = carve("PBF", 16384, 4096)
            dma("cs", lambda E, a2full=a2full: E.dma_start(out=dr["dbg9"], in_=a2full),
                r=[f"a2_0"] + a2fn, w=[])
        for pair in range(2):
            for tq2 in range(2):
                tq = 2 * pair + tq2
                tr.op("vector", lambda E, tq=tq: E.tensor_single_scalar(stc[:, 72:73], stc[:, 64 + tq:65 + tq],
                                                                        1e-5, ALU.max), ["amx"], ["statq"])
                tr.op("vector", lambda E: E.reciprocal(stc[:, 73:74], stc[:, 72:73]), ["statq"], ["statq"])
                tr.op("vector", lambda E: E.tensor_scalar_mul(stc[:, 74:75], stc[:, 73:74], 127.0), ["statq"], ["statq"])
                tr.op("vector", lambda E: E.tensor_scalar_mul(stc[:, 75:76], stc[:, 72:73], 1.0 / 127.0), ["statq"], ["statq"])
                tr.op("vector", lambda E, tq=tq: E.tensor_mul(stc[:, 80 + tq:81 + tq], stc[:, 75:76], sb["invsw"][:, 2:3]),
                      ["statq", "invsw"], ["es2q"])
                xq2F, xq2Fn = carve("PBF", 36864 + 4096 * tq2, 4096)
                for qtr in range(4):
                    a2v, a2vn = carve("PBF", 16384 + 4096 * tq + 1024 * qtr, 1024)
                    q1, q1n = carve("PF", 1024, 1024)
                    tr.op("vector", lambda E, a2v=a2v, q1=q1: E.tensor_scalar(
                        q1, a2v, stc[:, 74:75], MAGIC, ALU.mult, ALU.add), a2vn + ["statq"], q1n)
                    if DBG_A1 and quad == 0 and tq == 0:
                        dma("cs", lambda E, q1=q1, qtr=qtr: E.dma_start(
                            out=dr["dbg11"][:, 1024 * qtr:1024 * (qtr + 1)], in_=q1), r=q1n, w=[])
                    tr.op("vector", lambda E, q1=q1: E.tensor_scalar_add(q1, q1, -MAGIC), q1n, q1n)
                    q1b, q1bn = carve("PBF", 48128, 1024)
                    tr.op("vector", lambda E, q1=q1, q1b=q1b: E.tensor_scalar(
                        q1b, q1, -128.0, 127.0, ALU.max, ALU.min), q1n, q1bn)
                    if DBG_A1 and quad == 0 and tq == 0:
                        dma("cs", lambda E, q1=q1, qtr=qtr: E.dma_start(
                            out=dr["dbg10"][:, 1024 * qtr:1024 * (qtr + 1)], in_=q1), r=q1n, w=[])
                    for c in range(8):
                        pd, pb = (ps["PD1"], "PD1") if c % 2 == 0 else (ps["PD2"], "PD2")
                        tr.op("tensor", lambda E, c=c, pd=pd, q1b=q1b: E.transpose(
                            pd.bitcast(BF16)[:, 0:TILE], q1b[:, TILE * c:TILE * (c + 1)],
                            sb["identb"][:]), q1bn + ["identb"], [pb])
                        tr.op("scalar", lambda E, c=c, qtr=qtr, pd=pd, xq2F=xq2F: E.activation(
                            xq2F[:, 1024 * qtr + TILE * c:1024 * qtr + TILE * (c + 1)],
                            pd.bitcast(BF16)[:, 0:TILE], AF.Copy), [pb], xq2Fn)
            for k in range(32):
                wv, wvn = carve("PBF", 45056 + 1024 * (k % 2), 1024)
                dma("ck", lambda E, k=k, wv=wv: E.dma_start(out=wv, in_=dr["td"][k]), r=wvn, w=wvn)
                for tq2 in range(2):
                    xq2F, xq2Fn = carve("PBF", 36864 + 4096 * tq2, 4096)
                    pn, pb = (ps["PA"], "PA") if tq2 == 0 else (ps["PB"], "PB")
                    for half in range(2):
                        tr.op("tensor", lambda E, k=k, half=half, pn=pn, xq2F=xq2F, wv=wv: E.matmul(
                            pn[:, 512 * half:512 * (half + 1)], xq2F[:, TILE * k:TILE * (k + 1)],
                            wv[:, 512 * half:512 * (half + 1)], start=(k == 0), stop=(k == 31)),
                            xq2Fn + wvn, [pb + str(half)])
            gb, gbn = carve("PF", 14336, 1024)
            _ = []
            bcast_tile(14336, bi, 5 * D, False, _)
            for tq2 in range(2):
                tq = 2 * pair + tq2
                gt = 4 * quad + tq
                bi3, st3 = gt // NTS, gt % NTS
                pn, pb = (ps["PA"], "PA") if tq2 == 0 else (ps["PB"], "PB")
                x1 = sb["x1q"][:, D * tq:D * (tq + 1)]
                x1b = f"x1q{tq}"
                tmp, tmpn = carve("PF", 2048, 1024)
                for half in range(2):
                    tr.op("vector", lambda E, pn=pn, half=half, tq=tq, tmp=tmp, gb=gb: E.scalar_tensor_tensor(
                        tmp[:, 512 * half:512 * (half + 1)], pn[:, 512 * half:512 * (half + 1)],
                        stc[:, 80 + tq:81 + tq], gb[:, 512 * half:512 * (half + 1)], ALU.mult, ALU.mult),
                        [pb + str(half), "es2q"] + gbn, tmpn)
                tr.op("vector", lambda E, x1=x1, tmp=tmp: E.tensor_add(x1, x1, tmp), [x1b] + tmpn, [x1b])
                dma("co", lambda E, bi3=bi3, st3=st3, x1=x1: E.dma_start(
                    out=out_h[bi3, TILE * st3:TILE * (st3 + 1), :], in_=x1), r=[x1b], w=[])

    prep()
    load_wqkv(0)
    wo_prep()
    qgen = quant_units()
    qdone = False
    if N_A1 > 0:
        a1_front(0)
    if N_A1 > 1:
        a1_front(1)
    for gt in range(N_A1):
        if gt == NTS:
            load_wqkv(1)
        a1_tile(gt)
        if gt + 2 < N_A1:
            a1_front(gt + 2)
        for _ in range(7):
            try:
                next(qgen)()
            except StopIteration:
                qdone = True
                break
    while not qdone:
        try:
            next(qgen)()
        except StopIteration:
            qdone = True
    a2()
    for quad in range(N_QUAD):
        load_wo()
        bi = (4 * quad) // NTS
        _ = []
        bcast_tile(12288, bi, 2 * D, False, _)
        bcast_tile(3072, bi, 4 * D, True, _)
        bcast_tile(13312, bi, 3 * D, False, _)
        for tq in range(4):
            a3_tile(4 * quad + tq, tq)
            if tq > 0:
                b1_trans(quad, tq - 1)
            b1_tile(quad, tq)
        b1_trans(quad, 3)
        b_quad(quad)

    ops = tr.finalize()
    co_waits = [(k, v) for k, v in tr.dcnt.items() if k[1] == "co"]
    if co_waits:
        ops["sync"].append((sorted(co_waits), lambda E: E.nop(), None))
    nds = 0
    for k in tr.dcnt:
        if k not in sems:
            sems[k] = stack.enter_context(nc.semaphore(f"sd{nds}"))
            nds += 1
    assert nds + len(Tr.ENG) <= 100, f"too many semaphores: {nds}"

    with nc.Block() as blk0:
        def clear(E):
            for s in sems.values():
                E.sem_clear(s)
        blk0.sync(clear)

    with nc.Block() as blk:
        def mk(ename):
            myops = ops[ename]

            def run(E):
                for waits, fn, inc in myops:
                    for k, v in waits:
                        E.wait_ge(sems[k], v)
                    inst = fn(E)
                    if inc is not None:
                        inst.then_inc(sems[inc[0]], inc[1])
            return run
        blk.tensor(mk("tensor"))
        blk.vector(mk("vector"))
        blk.scalar(mk("scalar"))
        blk.gpsimd(mk("gpsimd"))
        blk.sync(mk("sync"))

    n_inst = sum(len(v) for v in ops.values())
    build.sb_info = {k: v for k, v in sb.items()}
    return nc, stack, n_inst


def make_inputs(full):
    tabs = host_tables()
    WQ, WK, WV = full["L0_attn_WQ"], full["L0_attn_WK"], full["L0_attn_WV"]
    wqkvT = np.concatenate([w.transpose(2, 0, 1).reshape(D, D) for w in (WQ, WK, WV)], axis=1)
    wqkv4 = np.ascontiguousarray(wqkvT.reshape(8, TILE, 3 * D))
    wo4 = np.ascontiguousarray(full["L0_attn_WO"].reshape(8, TILE, D))
    wmod4 = np.ascontiguousarray(full["L0_wmod"].reshape(8, TILE, 12, 512).transpose(2, 0, 1, 3))
    wg4 = np.ascontiguousarray(full["L0_ffn_w_gate"].T.reshape(8, TILE, INNER))
    wu4 = np.ascontiguousarray(full["L0_ffn_w_up"].T.reshape(8, TILE, INNER))
    wd4 = np.ascontiguousarray(full["L0_ffn_w_down"].T.reshape(32, TILE, D))
    gammaF = np.ascontiguousarray(full["L0_attn_gamma"].reshape(8, TILE).T)
    betaF = np.ascontiguousarray(full["L0_attn_beta"].reshape(8, TILE).T)
    x = full["L0_attn_x"]
    c = full["L0_c"]
    shared = dict(wqkv4=wqkv4, wo4=wo4, wmod4=wmod4, wg4=wg4, wu4=wu4, wd4=wd4,
                  gammaF=gammaF, betaF=betaF, ctbl=tabs["ctbl"], stbl=tabs["stbl"],
                  dmt=tabs["dmt"], identf=tabs["ident"])
    in_maps = []
    for core in range(8):
        cs = c[2 * core:2 * core + 2]
        cF = np.ascontiguousarray(cs.reshape(2, 8, TILE).transpose(2, 1, 0).reshape(TILE, 16))
        m = dict(shared)
        m["x"] = np.ascontiguousarray(x[2 * core:2 * core + 2])
        m["cF"] = cF
        in_maps.append(m)
    return in_maps


def kernel(**inputs):
    in_maps = make_inputs(inputs)
    nc, stack, n_inst = build()
    res = run_bass_kernel_spmd(nc, in_maps, core_ids=list(range(8)))
    outs = [res.results[i]["out"] for i in range(8)]
    full = np.concatenate(outs, axis=0).astype(np.float32)
    if SPILL_OUT:
        return full, res
    stack.close()
    return full

